# version 05
# baseline (speedup 1.0000x reference)
import numpy as np
import concourse.bass as bass
import concourse.mybir as mybir

F32 = mybir.dt.float32
BF16 = mybir.dt.bfloat16
I32 = mybir.dt.int32
U8 = mybir.dt.uint8
AF = mybir.ActivationFunctionType
ALU = mybir.AluOpType
AX = mybir.AxisListType.X

DM = 4096
NKC = DM // 128


class Emit:

    ENG = ("t", "v", "s", "g")

    def __init__(self, nc):
        self.nc = nc
        self.e = {"t": nc.tensor, "v": nc.vector, "s": nc.scalar,
                  "g": nc.gpsimd, "y": nc.sync,
                  "dg": nc.gpsimd, "dy": nc.sync}
        self.sem = {k: nc.alloc_semaphore("sem_" + k) for k in self.ENG}
        self.cnt = {k: 0 for k in self.ENG}
        self.hw = {}

    def _dsem(self, name):
        key = "D:" + name
        if key not in self.sem:
            self.sem[key] = self.nc.alloc_semaphore("sem_" + name)
            self.cnt[key] = 0
        return key

    def wait(self, consumer, *toks):
        for tok in toks:
            if tok is None:
                continue
            k, v = tok
            if v <= 0 or self.hw.get((consumer, k), 0) >= v:
                continue
            self.e[consumer].wait_ge(self.sem[k], v)
            self.hw[(consumer, k)] = v

    def inc(self, inst, key, n=1):
        inst.then_inc(self.sem[key], n)
        self.cnt[key] += n
        return (key, self.cnt[key])

    def tok(self, key):
        return (key, self.cnt[key])

    def op(self, eng, name, *args, waits=(), **kw):
        self.wait(eng, *[w for w in waits if w])
        inst = getattr(self.e[eng], name)(*args, **kw)
        return self.inc(inst, eng, 1)

    def op_(self, eng, name, *args, waits=(), **kw):
        self.wait(eng, *[w for w in waits if w])
        getattr(self.e[eng], name)(*args, **kw)
        return None

    def dma(self, q, stream, out, in_, *, waits=(), **kw):
        key = self._dsem(stream)
        self.wait(q, *[w for w in waits if w])
        inst = self.e[q].dma_start(out, in_, **kw)
        return self.inc(inst, key, 16)

    def barrier(self):
        toks = [self.tok(k) for k in self.cnt]
        for consumer in ("t", "v", "s", "g", "y"):
            self.wait(consumer, *toks)

    def dma_totals(self):
        return [self.tok(k) for k in self.cnt if k.startswith("D:")]


class Arena:

    def __init__(self, nc, name, nbytes):
        self.t = nc.alloc_sbuf_tensor(name, [128, nbytes], U8)
        self.nbytes = nbytes
        self.off = 0

    def reset(self):
        self.off = 0

    def mark(self):
        return self.off

    def rewind(self, m):
        self.off = m

    def view(self, off, part, free, dtype):
        sz = free * np.dtype(mybir.dt.np(dtype)).itemsize
        assert off + sz <= self.nbytes
        return self.t.ap()[0:part, off:off + sz].bitcast(dtype)

    def take(self, part, free, dtype):
        sz = free * np.dtype(mybir.dt.np(dtype)).itemsize
        off = (self.off + 511) & ~511
        assert off + sz <= self.nbytes, f"arena overflow {off}+{sz}>{self.nbytes}"
        self.off = off + sz
        return self.t.ap()[0:part, off:off + sz].bitcast(dtype)


def ln_stats(E, x_ap, s1_ap, s2_ap, sq_ap, waits=()):
    t1 = E.op("v", "reduce_sum", s1_ap, x_ap, axis=AX, waits=waits)
    t2 = E.op("s", "activation", sq_ap, x_ap, AF.Square, accum_out=s2_ap,
              waits=waits)
    return t1, t2


def ln_chain(E, s1_ap, s2_ap, r_ap, nmr_ap, tmp_ap, D, tok_s1, tok_s2):
    t1 = E.op("v", "tensor_tensor", tmp_ap, s1_ap, s1_ap, ALU.mult,
              waits=(tok_s1,))
    t2 = E.op("v", "scalar_tensor_tensor", tmp_ap, tmp_ap, -1.0 / D, s2_ap,
              ALU.mult, ALU.add, waits=(t1, tok_s2))
    t3 = E.op("v", "tensor_scalar", tmp_ap, tmp_ap, 1.0 / D, 1e-5,
              ALU.mult, ALU.add, waits=(t2,))
    sd = E.op("s", "activation", r_ap, tmp_ap, AF.Sqrt, waits=(t3,))
    rt = E.op("v", "reciprocal", r_ap, r_ap, waits=(sd,))
    nm = E.op("v", "scalar_tensor_tensor", nmr_ap, s1_ap, -1.0 / D, r_ap,
              ALU.mult, ALU.mult, waits=(rt, tok_s1))
    return rt, nm


class TPool:

    def __init__(self, E, psts, ident):
        self.E = E
        self.psts = psts
        self.free = [None] * len(psts)
        self.ident = ident
        self.i = 0
        self.flip = 0

    def transpose(self, src_ap, dst_ap, waits=(), evac_eng=None):
        E = self.E
        i = self.i
        self.i = (i + 1) % len(self.psts)
        eng = evac_eng or ("v" if self.flip == 0 else "s")
        self.flip ^= 1
        po = self.psts[i][0:src_ap.shape[1], 0:src_ap.shape[0]]
        tt = E.op("t", "transpose", po, src_ap, self.ident,
                  waits=tuple(waits) + (self.free[i],))
        meth = "copy" if eng == "s" else "tensor_copy"
        te = E.op(eng, meth, dst_ap, po, waits=(tt,))
        self.free[i] = te
        return tt, te




class TRot:

    def __init__(self, E, views, ident):
        self.E = E
        self.views = views
        self.ident = ident
        self.free = [None] * len(views)
        self.i = 0
        self.flip = 0

    def go(self, src_ap, dst_ap, waits=()):
        E = self.E
        i = self.i
        self.i = (i + 1) % len(self.views)
        eng = "v" if self.flip == 0 else "s"
        self.flip ^= 1
        p = src_ap.shape[0]
        po = self.views[i][0:src_ap.shape[1], 0:p]
        fr = self.free[i]
        if fr is None:
            fr = ()
        elif isinstance(fr[0], str):
            fr = (fr,)
        tt = E.op("t", "transpose", po, src_ap, self.ident[0:p, 0:p],
                  waits=tuple(w for w in waits if w) + fr)
        meth = "copy" if eng == "s" else "tensor_copy"
        te = E.op(eng, meth, dst_ap, po, waits=(tt,))
        self.free[i] = te
        return tt, te


class Ctx:
    pass


def make_ctx(nc):
    c = Ctx()
    c.nc = nc
    c.E = Emit(nc)
    E = c.E
    c.ident = nc.alloc_sbuf_tensor("ident", [128, 128], BF16)
    c.ones = nc.alloc_sbuf_tensor("ones", [1, 128], F32)
    c.s1a = nc.alloc_sbuf_tensor("s1a", [128, 16], F32)
    c.s2a = nc.alloc_sbuf_tensor("s2a", [128, 16], F32)
    c.ra = nc.alloc_sbuf_tensor("ra", [128, 16], F32)
    c.nmra = nc.alloc_sbuf_tensor("nmra", [128, 16], F32)
    c.tmpa = nc.alloc_sbuf_tensor("tmpa", [128, 16], F32)
    c.p_s1 = nc.alloc_sbuf_tensor("p_s1", [128, 16], F32)
    c.p_s2 = nc.alloc_sbuf_tensor("p_s2", [128, 16], F32)
    c.p_r = nc.alloc_sbuf_tensor("p_r", [128, 16], F32)
    c.p_nmr = nc.alloc_sbuf_tensor("p_nmr", [128, 16], F32)
    c.p_tmp = nc.alloc_sbuf_tensor("p_tmp", [128, 16], F32)
    c.nm = nc.alloc_sbuf_tensor("nm", [128, 1], F32)
    c.nms = nc.alloc_sbuf_tensor("nms", [128, 1], F32)
    c.rsum = nc.alloc_sbuf_tensor("rsum", [128, 1], F32)
    c.rrec = nc.alloc_sbuf_tensor("rrec", [128, 1], F32)
    c.PSA = nc.alloc_psum_tensor("PSA", [128, 2048], F32)
    c.PSB = nc.alloc_psum_tensor("PSB", [128, 2048], F32)
    c.TVIEWS = [c.PSB[:, 0:512].bitcast(BF16)[:, 0:128],
                c.PSB[:, 512:1024].bitcast(BF16)[:, 0:128]]
    rem = nc.sbuf_bytes_remaining
    c.ar = Arena(nc, "arena", rem - 1536)
    tms = E.op("g", "memset", c.ident[:, :], 0.0)
    E.op("g", "affine_select", out=c.ident[:, :], in_=c.ident[:, :],
         compare_op=ALU.not_equal, fill=1.0, base=0,
         pattern=[[-1, 128]], channel_multiplier=1, waits=(tms,))
    E.op("g", "memset", c.ones[0:1, :], 1.0)
    return c


def emit_l0_attn(c, xo, xr, wqkv, bqkv, wo, bo, x1_out):
    nc, E, ar = c.nc, c.E, c.ar
    PSA, PSB = c.PSA, c.PSB
    ident, ones = c.ident, c.ones
    s1a, s2a, ra, nmra, tmpa = c.s1a, c.s2a, c.ra, c.nmra, c.tmpa
    p_s1, p_s2, p_r, p_nmr, p_tmp = c.p_s1, c.p_s2, c.p_r, c.p_nmr, c.p_tmp
    nm, nms, rsum, rrec = c.nm, c.nms, c.rsum, c.rrec

    ar.reset()
    K_fm = ar.take(128, 8 * 2048, BF16)
    V_tm = ar.take(128, 16 * 1024, BF16)
    Q_fm = ar.take(128, 8 * 1024, BF16)
    at_fm_all = ar.take(128, 8 * 1024, BF16)
    m0 = ar.mark()
    x_st = ar.take(128, DM, F32)
    ln_tm = ar.take(128, DM, BF16)
    lnfm = ar.take(128, 4 * DM, BF16)
    sqs = ar.take(128, DM, F32)
    wpc = ar.take(128, 2 * 2048, BF16)
    ttm = ar.take(128, 1024, BF16)
    bqv = ar.take(1, 3072, F32)
    gend = ar.off
    ar.rewind(m0)
    P = ar.take(128, 2048, BF16)
    PTs = ar.take(128, 2048, BF16)
    at_tm = ar.take(128, 1024, BF16)
    aend = ar.off
    ar.rewind(m0)
    x0s = ar.take(128, DM, F32)
    x1s = ar.take(128, DM, F32)
    sqp = ar.take(128, DM, F32)
    wop = ar.take(128, 4 * 512, BF16)
    bor = ar.take(1, DM, F32)
    dens = ar.view(0, 128, 4 * DM, F32)
    ar.off = max(ar.off, gend, aend)

    PSO = [PSB[:, 1024:1536], PSB[:, 1536:2048]]
    TP = TRot(E, c.TVIEWS, ident)
    tok_bqv = E.dma("dy", "a_bias1", bqv, bqkv)

    groups = [
        (xo, [0, 1, 2, 3], 0, range(0, 6)),
        (xo, [4, 5, 6, 7], 4, range(0, 6)),
        (xr, [0, 1, 2, 3], 8, range(2, 6)),
        (xr, [4, 5, 6, 7], 12, range(2, 6)),
    ]
    tok_xst_free = None
    tok_lntm_free = None
    piece_free = [None] * 4
    ttm_free = [None, None]
    psa_free = [None] * 4
    jglob = 0
    for (src, tiles, t16b, ntiles) in groups:
        lnfm_ready = [None] * 4
        for i, tile in enumerate(tiles):
            j = jglob
            jglob += 1
            jj = j % 16
            td = E.dma("dy", "a_x", x_st, src[tile * 128:(tile + 1) * 128, :],
                       waits=(tok_xst_free,))
            t1, t2 = ln_stats(E, x_st, s1a[:, jj:jj + 1], s2a[:, jj:jj + 1],
                              sqs, waits=(td,))
            tr, tn = ln_chain(E, s1a[:, jj:jj + 1], s2a[:, jj:jj + 1],
                              ra[:, jj:jj + 1], nmra[:, jj:jj + 1],
                              tmpa[:, jj:jj + 1], DM, t1, t2)
            tok_apply = E.op("v", "tensor_scalar", ln_tm, x_st,
                             ra[:, jj:jj + 1], nmra[:, jj:jj + 1],
                             ALU.mult, ALU.add,
                             waits=(tr, tn, tok_lntm_free))
            tok_xst_free = tok_apply
            last_tt = None
            for ch in range(NKC):
                tt, te = TP.go(ln_tm[:, ch * 128:(ch + 1) * 128],
                               lnfm[:, i * DM + ch * 128:i * DM + (ch + 1) * 128],
                               waits=(tok_apply,))
                last_tt = tt
            tok_lntm_free = last_tt
            lnfm_ready[i] = (E.tok("v"), E.tok("s"))

        for n in ntiles:
            c0 = n * 512
            for k4 in range(NKC // 4):
                buf = k4 % 2
                tw = E.dma("dg", f"a_w{buf}",
                           wpc[:, buf * 2048:(buf + 1) * 2048]
                           .rearrange("p (a n) -> p a n", a=4),
                           wqkv[k4 * 512:(k4 + 1) * 512, c0:c0 + 512]
                           .rearrange("(a p) n -> p a n", p=128),
                           waits=(piece_free[buf],))
                for sub in range(4):
                    k = k4 * 4 + sub
                    for i in range(4):
                        w = [tw]
                        if k == 0:
                            w += list(lnfm_ready[i]) + [psa_free[i]]
                        lhs = lnfm[:, i * DM + k * 128:i * DM + (k + 1) * 128]
                        rhs = wpc[:, buf * 2048 + sub * 512:buf * 2048 + (sub + 1) * 512]
                        psl = PSA[:, i * 512:(i + 1) * 512]
                        if i == 3 and sub == 3:
                            piece_free[buf] = E.op("t", "matmul", psl, lhs, rhs,
                                                   start=(k == 0), stop=False,
                                                   waits=w)
                        else:
                            E.op_("t", "matmul", psl, lhs, rhs,
                                  start=(k == 0), stop=False, waits=w)
            for i in range(4):
                tok_done = E.op("t", "matmul", PSA[:, i * 512:(i + 1) * 512],
                                ones[0:1, :], bqv[0:1, c0:c0 + 512],
                                start=False, stop=True, waits=(tok_bqv,))
                t16 = t16b + i
                tloc = tiles[i]
                if n < 2:
                    ev = E.op("s", "copy", ttm[:, (i % 2) * 512:(i % 2 + 1) * 512],
                              PSA[:, i * 512:(i + 1) * 512],
                              waits=(tok_done, ttm_free[i % 2]))
                    psa_free[i] = ev
                    last = None
                    for s4 in range(4):
                        chunk = n * 4 + s4
                        tt, te = TP.go(
                            ttm[:, (i % 2) * 512 + s4 * 128:(i % 2) * 512 + (s4 + 1) * 128],
                            Q_fm[:, chunk * 1024 + tloc * 128:chunk * 1024 + (tloc + 1) * 128],
                            waits=(ev,))
                        last = tt
                    ttm_free[i % 2] = last
                elif n < 4:
                    ev = E.op("s", "copy", ttm[:, (i % 2) * 512:(i % 2 + 1) * 512],
                              PSA[:, i * 512:(i + 1) * 512],
                              waits=(tok_done, ttm_free[i % 2]))
                    psa_free[i] = ev
                    last = None
                    for s4 in range(4):
                        chunk = (n - 2) * 4 + s4
                        tt, te = TP.go(
                            ttm[:, (i % 2) * 512 + s4 * 128:(i % 2) * 512 + (s4 + 1) * 128],
                            K_fm[:, chunk * 2048 + t16 * 128:chunk * 2048 + (t16 + 1) * 128],
                            waits=(ev,))
                        last = tt
                    ttm_free[i % 2] = last
                else:
                    ev = E.op("v", "tensor_copy",
                              V_tm[:, t16 * 1024 + (n - 4) * 512:t16 * 1024 + (n - 4) * 512 + 512],
                              PSA[:, i * 512:(i + 1) * 512], waits=(tok_done,))
                    psa_free[i] = ev

    E.barrier()
    tok_psa_attn = None
    tok_pso_free = [None, None]
    for m in range(8):
        tok_scores = None
        for ch in range(8):
            for kk in range(4):
                w = []
                if ch == 0:
                    w = [tok_psa_attn, psa_free[kk]]
                lhs = Q_fm[:, ch * 1024 + m * 128:ch * 1024 + (m + 1) * 128]
                rhs = K_fm[:, ch * 2048 + kk * 512:ch * 2048 + (kk + 1) * 512]
                psl = PSA[:, kk * 512:(kk + 1) * 512]
                if ch == 7 and kk == 3:
                    tok_scores = E.op("t", "matmul", psl, lhs, rhs,
                                      start=(ch == 0), stop=(ch == 7), waits=w)
                else:
                    E.op_("t", "matmul", psl, lhs, rhs,
                          start=(ch == 0), stop=(ch == 7), waits=w)
        psa_free = [None] * 4
        tmax = E.op("v", "reduce_max", nm[:, :], PSA[:, :], axis=AX,
                    negate=True, waits=(tok_scores,))
        tnms = E.op("v", "tensor_scalar_mul", nms[:, :], nm[:, :], 1.0 / 32.0,
                    waits=(tmax,))
        tok_exp = E.op("s", "activation", P, PSA[:, :], AF.Exp,
                       bias=nms[:, 0:1], scale=1.0 / 32.0,
                       accum_out=rsum[:, 0:1], waits=(tnms,))
        tok_psa_attn = tok_exp
        trr = E.op("v", "reciprocal", rrec[:, :], rsum[:, :], waits=(tok_exp,))
        tok_pv = None
        for ch in range(16):
            tt, te = TP.go(P[:, ch * 128:(ch + 1) * 128],
                           PTs[:, ch * 128:(ch + 1) * 128],
                           waits=(tok_exp,))
            for n2 in range(2):
                w = [te]
                if ch == 0:
                    w += [tok_pso_free[0], tok_pso_free[1]]
                psl = PSO[n2]
                lhs = PTs[:, ch * 128:(ch + 1) * 128]
                rhs = V_tm[:, ch * 1024 + n2 * 512:ch * 1024 + (n2 + 1) * 512]
                if ch == 15 and n2 == 1:
                    tok_pv = E.op("t", "matmul", psl, lhs, rhs,
                                  start=(ch == 0), stop=(ch == 15), waits=w)
                else:
                    E.op_("t", "matmul", psl, lhs, rhs,
                          start=(ch == 0), stop=(ch == 15), waits=w)
        tok_at = E.op("s", "activation", at_tm, PSB[:, 1024:2048], AF.Copy,
                      scale=rrec[:, 0:1], waits=(tok_pv, trr))
        for ch in range(8):
            TP.go(at_tm[:, ch * 128:(ch + 1) * 128],
                  at_fm_all[:, m * 1024 + ch * 128:m * 1024 + (ch + 1) * 128],
                  waits=(tok_at,))
        tok_pso_free = [tok_at, tok_at]

    guard = (E.tok("t"), E.tok("v"), E.tok("s"))
    tok_bor = E.dma("dy", "a_bias2", bor, bo, waits=guard)
    at_ready = guard
    wop_free = [None] * 4
    psa_free2 = [None] * 4
    tok_x0_free = None
    tok_x1_free = None
    for g2 in range(2):
        tiles2 = [g2 * 4 + i for i in range(4)]
        dens_done = [[None] * 8 for _ in range(4)]
        for n in range(8):
            c0 = n * 512
            for k in range(8):
                buf = k % 4
                tw = E.dma("dg", f"a_wo{buf}", wop[:, buf * 512:(buf + 1) * 512],
                           wo[k * 128:(k + 1) * 128, c0:c0 + 512],
                           waits=(wop_free[buf], guard[0]))
                for i in range(4):
                    m = tiles2[i]
                    w = [tw]
                    if k == 0:
                        w += list(at_ready) + [psa_free2[i], tok_psa_attn]
                    lhs = at_fm_all[:, m * 1024 + k * 128:m * 1024 + (k + 1) * 128]
                    rhs = wop[:, buf * 512:(buf + 1) * 512]
                    psl = PSA[:, i * 512:(i + 1) * 512]
                    if i == 3:
                        wop_free[buf] = E.op("t", "matmul", psl, lhs, rhs,
                                             start=(k == 0), stop=False,
                                             waits=w)
                    else:
                        E.op_("t", "matmul", psl, lhs, rhs,
                              start=(k == 0), stop=False, waits=w)
            for i in range(4):
                tok_done = E.op("t", "matmul", PSA[:, i * 512:(i + 1) * 512],
                                ones[0:1, :], bor[0:1, c0:c0 + 512],
                                start=False, stop=True, waits=(tok_bor,))
                eng = "v" if i % 2 == 0 else "s"
                meth = "tensor_copy" if eng == "v" else "copy"
                tev = E.op(eng, meth, dens[:, i * DM + c0:i * DM + c0 + 512],
                           PSA[:, i * 512:(i + 1) * 512], waits=(tok_done,))
                psa_free2[i] = tev
                dens_done[i][n] = tev
        tok_out_last = None
        for i in range(4):
            m = tiles2[i]
            dview = dens[:, i * DM:(i + 1) * DM]
            t1, t2 = ln_stats(E, dview, p_s1[:, m:m + 1], p_s2[:, m:m + 1],
                              sqp, waits=tuple(dens_done[i]))
            tr, tn = ln_chain(E, p_s1[:, m:m + 1], p_s2[:, m:m + 1],
                              p_r[:, m:m + 1], p_nmr[:, m:m + 1],
                              p_tmp[:, m:m + 1], DM, t1, t2)
            tx = E.dma("dy", "a_x0", x0s, xo[m * 128:(m + 1) * 128, :],
                       waits=(tok_x0_free,) + guard)
            tap = E.op("v", "tensor_scalar", x1s, dview, p_r[:, m:m + 1],
                       p_nmr[:, m:m + 1], ALU.mult, ALU.add,
                       waits=(tr, tn, tok_x1_free))
            tadd = E.op("v", "tensor_tensor", x1s, x1s, x0s, ALU.add,
                        waits=(tap, tx))
            tok_x0_free = tadd
            tok_x1_free = E.dma("dy", "a_out", x1_out[m * 128:(m + 1) * 128, :],
                                x1s, waits=(tadd,))
        E.wait("v", tok_x1_free)
        E.wait("s", tok_x1_free)
    E.barrier()


def emit_l0_ffn(c, x1i, w0, b0, cent, assign, cc_in, cc_out, x2_out):
    nc, E, ar = c.nc, c.E, c.ar
    PSA, PSB = c.PSA, c.PSB
    ones = c.ones
    s1a, s2a, ra, nmra, tmpa = c.s1a, c.s2a, c.ra, c.nmra, c.tmpa
    p_s1, p_s2, p_r, p_nmr, p_tmp = c.p_s1, c.p_s2, c.p_r, c.p_nmr, c.p_tmp
    NCLUST, THR, INV_TEMP = 64, 0.01, 10.0
    nmx, nmx10, dsum, drec = c.nm, c.nms, c.rsum, c.rrec

    ar.reset()
    ident32 = ar.take(128, 128, F32)
    ones64 = ar.take(64, 128, F32)
    iota64 = ar.take(64, 1, F32)
    cent_sb = ar.take(128, NKC * NCLUST, F32)
    lnbf = ar.take(128, 8 * DM, BF16)
    qmask = ar.take(128, 8, F32)
    qmx = ar.take(128, 1, F32)
    cmax = ar.take(128, NCLUST, F32)
    cmT = ar.take(64, 128, F32)
    cm_col = ar.take(64, 1, F32)
    cs_col = ar.take(64, 1, F32)
    cs_gt = ar.take(64, 1, F32)
    dots = ar.take(128, NCLUST, F32)
    x_st = ar.take(128, DM, F32)
    lnf32 = ar.take(128, DM, F32)
    lnfm32 = ar.take(128, DM, F32)
    dens = ar.take(128, 4 * DM, F32)
    b0pc = ar.take(1, 1024, F32)
    sqs = dens[:, 0:DM]
    sqs3 = lnf32
    asg = lnf32[0:64, :].bitcast(I32)
    asgf = lnfm32[0:64, :]
    ohcs = x_st[0:64, :]
    rmask = lnfm32
    wpc = cent_sb[:, 0:2048].bitcast(BF16)[:, 0:4096]

    T32 = [PSB[:, 0:128], PSB[:, 512:640]]
    PSD = PSB[:, 1024:1024 + NCLUST]
    TP32 = TRot(E, T32, ident32)

    tms = E.op("g", "memset", ident32[0:128, :], 0.0)
    E.op("g", "affine_select", out=ident32[0:128, :], in_=ident32[0:128, :],
         compare_op=ALU.not_equal, fill=1.0, base=0,
         pattern=[[-1, 128]], channel_multiplier=1, waits=(tms,))
    tmo = E.op("g", "memset", ones64[0:64, :], 1.0)
    tio = E.op("g", "iota", iota64[0:64, 0:1], pattern=[[0, 1]], base=0,
               channel_multiplier=1, allow_small_or_imprecise_dtypes=True)
    tok_cmz = E.op("v", "memset", cmax[:, :], 0.0)
    tok_cent = E.dma("dy", "b_cent",
                     cent_sb[:, :].rearrange("p (c n) -> p c n", n=NCLUST),
                     cent.rearrange("(c p) n -> p c n", p=128))

    tok_xst_free = None
    tok_lnf_free = None
    tok_lnfm_free = None
    tok_psd_free = None
    tok_cmax = tok_cmz
    tok_ident_done = E.tok("g")
    for m in range(8):
        td = E.dma("dy", "b_x", x_st, x1i[m * 128:(m + 1) * 128, :],
                   waits=(tok_xst_free,))
        t1, t2 = ln_stats(E, x_st, s1a[:, m:m + 1], s2a[:, m:m + 1], sqs,
                          waits=(td,))
        tr, tn = ln_chain(E, s1a[:, m:m + 1], s2a[:, m:m + 1],
                          ra[:, m:m + 1], nmra[:, m:m + 1],
                          tmpa[:, m:m + 1], DM, t1, t2)
        tok_ap = E.op("v", "tensor_scalar", lnf32, x_st, ra[:, m:m + 1],
                      nmra[:, m:m + 1], ALU.mult, ALU.add,
                      waits=(tr, tn, tok_lnf_free))
        tok_xst_free = tok_ap
        last_tt = None
        for ch in range(NKC):
            tt, te = TP32.go(lnf32[:, ch * 128:(ch + 1) * 128],
                             lnfm32[:, ch * 128:(ch + 1) * 128],
                             waits=(tok_ap, tok_lnfm_free, tok_ident_done))
            last_tt = tt
        tok_lnf_free = last_tt
        fm_ready = (E.tok("v"), E.tok("s"))
        tok_bf = E.op("v", "tensor_copy", lnbf[:, m * DM:(m + 1) * DM],
                      lnfm32, waits=fm_ready)
        tok_dots = None
        for k in range(NKC):
            w = list(fm_ready) + [tok_cent]
            if k == 0:
                w.append(tok_psd_free)
            tok_dots = E.op("t", "matmul", PSD,
                            lnfm32[:, k * 128:(k + 1) * 128],
                            cent_sb[:, k * NCLUST:(k + 1) * NCLUST],
                            start=(k == 0), stop=(k == NKC - 1), waits=w)
        tok_lnfm_free = tok_dots
        tmx = E.op("v", "reduce_max", nmx[:, :], PSD, axis=AX,
                   negate=True, waits=(tok_dots,))
        tx10 = E.op("v", "tensor_scalar_mul", nmx10[:, :], nmx[:, :],
                    INV_TEMP, waits=(tmx,))
        tex = E.op("s", "activation", dots[:, :], PSD, AF.Exp,
                   bias=nmx10[:, 0:1], scale=INV_TEMP,
                   accum_out=dsum[:, 0:1], waits=(tx10,))
        tok_psd_free = tex
        trc = E.op("v", "reciprocal", drec[:, :], dsum[:, :], waits=(tex,))
        tdn = E.op("v", "tensor_scalar", dots[:, :], dots[:, :],
                   drec[:, 0:1], None, ALU.mult, waits=(trc, tex))
        tqm = E.op("v", "reduce_max", qmx[:, 0:1], dots[:, :], axis=AX,
                   waits=(tdn,))
        E.op("v", "tensor_single_scalar", qmask[:, m:m + 1], qmx[:, 0:1],
             THR, ALU.is_gt, waits=(tqm,))
        tok_cmax = E.op("v", "tensor_tensor", cmax[:, :], cmax[:, :],
                        dots[:, :], ALU.max, waits=(tok_cmax, tdn))

    tt, te = TP32.go(cmax[:, 0:64], cmT[0:64, 0:128], waits=(tok_cmax,))
    tcm = E.op("v", "reduce_max", cm_col[0:64, 0:1], cmT[0:64, 0:128],
               axis=AX, waits=(te,))
    tdm = E.dma("dg", "b_ccin", cc_in.ap(), cm_col[0:64, 0:1],
                waits=(tcm,), allow_slow_non_contiguous=True)
    E.wait("g", tdm)
    cc_inst = nc.gpsimd.collective_compute(
        "AllReduce", ALU.max,
        replica_groups=[list(range(8))],
        ins=[cc_in.ap().opt()], outs=[cc_out.ap().opt()],
    )
    tok_cc = E.inc(cc_inst, "g", 1)
    tcs = E.dma("dg", "b_ccout", cs_col[0:64, 0:1], cc_out.ap(),
                waits=(tok_cc,), allow_slow_non_contiguous=True)
    tgt = E.op("v", "tensor_single_scalar", cs_gt[0:64, 0:1],
               cs_col[0:64, 0:1], THR, ALU.is_gt, waits=(tcs,))
    tas = E.dma("dg", "b_asg", asg, assign.partition_broadcast(64),
                waits=(E.tok("t"), E.tok("v"), E.tok("s")))
    taf = E.op("g", "tensor_copy", asgf, asg, waits=(tas,))
    toh = E.op("v", "tensor_scalar", ohcs, asgf, iota64[0:64, 0:1],
               cs_gt[0:64, 0:1], ALU.is_equal, ALU.mult, waits=(taf, tgt))
    rmask_done = []
    for n in range(8):
        sl = n % 4
        w = [toh, tok_psd_free]
        tmm = E.op("t", "matmul", PSA[:, sl * 512:(sl + 1) * 512],
                   ones64[0:64, :], ohcs[0:64, n * 512:(n + 1) * 512],
                   start=True, stop=True,
                   waits=w + ([rmask_done[n - 4]] if n >= 4 else []))
        tev = E.op("v", "tensor_copy", rmask[:, n * 512:(n + 1) * 512],
                   PSA[:, sl * 512:(sl + 1) * 512], waits=(tmm,))
        rmask_done.append(tev)
    tok_rmask = rmask_done[-1]

    E.wait("v", E.tok("s"))
    piece_free = [None] * 4
    b0_free = [None, None]
    psa_free = [None] * 4
    tok_x3_free = tok_rmask
    for grp in range(2):
        tiles = [grp * 4 + i for i in range(4)]
        dens_done = [[None] * 8 for _ in range(4)]
        for n in range(8):
            c0 = n * 512
            for k4 in range(NKC // 4):
                buf = k4 % 2
                tw = E.dma("dg", f"b_w{buf}",
                           wpc[:, buf * 2048:(buf + 1) * 2048]
                           .rearrange("p (a n) -> p a n", a=4),
                           w0[k4 * 512:(k4 + 1) * 512, c0:c0 + 512]
                           .rearrange("(a p) n -> p a n", p=128),
                           waits=(piece_free[buf],))
                for sub in range(4):
                    k = k4 * 4 + sub
                    for i in range(4):
                        m = tiles[i]
                        w = [tw]
                        if k == 0:
                            w += [psa_free[i], tok_rmask]
                        lhs = lnbf[:, m * DM + k * 128:m * DM + (k + 1) * 128]
                        rhs = wpc[:, buf * 2048 + sub * 512:buf * 2048 + (sub + 1) * 512]
                        psl = PSA[:, i * 512:(i + 1) * 512]
                        if i == 3 and sub == 3:
                            piece_free[buf] = E.op("t", "matmul", psl, lhs, rhs,
                                                   start=(k == 0), stop=False,
                                                   waits=w)
                        else:
                            E.op_("t", "matmul", psl, lhs, rhs,
                                  start=(k == 0), stop=False, waits=w)
            bb = n % 2
            tb0 = E.dma("dy", f"b_b0{bb}", b0pc[0:1, bb * 512:bb * 512 + 512],
                        b0[0:1, c0:c0 + 512], waits=(b0_free[bb],))
            for i in range(4):
                m = tiles[i]
                tok_done = E.op("t", "matmul", PSA[:, i * 512:(i + 1) * 512],
                                ones[0:1, :], b0pc[0:1, bb * 512:bb * 512 + 512],
                                start=False, stop=True, waits=(tb0,))
                if i == 3:
                    b0_free[bb] = tok_done
                tev = E.op("v", "scalar_tensor_tensor",
                           dens[:, i * DM + c0:i * DM + c0 + 512],
                           PSA[:, i * 512:(i + 1) * 512],
                           qmask[:, m:m + 1], rmask[:, c0:c0 + 512],
                           ALU.mult, ALU.mult, waits=(tok_done,))
                psa_free[i] = tev
                dens_done[i][n] = tev
        tok_out_last = None
        for i in range(4):
            m = tiles[i]
            dview = dens[:, i * DM:(i + 1) * DM]
            t1, t2 = ln_stats(E, dview, p_s1[:, m:m + 1], p_s2[:, m:m + 1],
                              sqs3, waits=tuple(dens_done[i]))
            tr, tn = ln_chain(E, p_s1[:, m:m + 1], p_s2[:, m:m + 1],
                              p_r[:, m:m + 1], p_nmr[:, m:m + 1],
                              p_tmp[:, m:m + 1], DM, t1, t2)
            tx = E.dma("dy", "b_x3", x_st, x1i[m * 128:(m + 1) * 128, :],
                       waits=(tok_x3_free,))
            tap = E.op("v", "tensor_scalar", dview, dview, p_r[:, m:m + 1],
                       p_nmr[:, m:m + 1], ALU.mult, ALU.add, waits=(tr, tn))
            tadd = E.op("v", "tensor_tensor", dview, dview, x_st, ALU.add,
                        waits=(tap, tx))
            tok_x3_free = tadd
            tok_out_last = E.dma("dy", "b_out",
                                 x2_out[m * 128:(m + 1) * 128, :],
                                 dview, waits=(tadd,))
        E.wait("v", tok_out_last)
        E.wait("s", tok_out_last)
    E.barrier()


def emit_l1_attn(c, x2i, cw, cb, wq, wk, wv, bqc, bkc, bvr, wo2, bo2, pw, pb,
                 halo_sel, halo_mask, hg_in, hg_out, x3_out):
    nc, E, ar = c.nc, c.E, c.ar
    PSA, PSB = c.PSA, c.PSB
    ident, ones = c.ident, c.ones
    s1a, s2a, ra, nmra, tmpa = c.s1a, c.s2a, c.ra, c.nmra, c.tmpa
    p_s1, p_s2, p_r, p_nmr, p_tmp = c.p_s1, c.p_s2, c.p_r, c.p_nmr, c.p_tmp
    nm, rsum, rrec = c.nm, c.rsum, c.rrec
    CW = 1032
    NH = 16
    SCALE = 0.125
    NEG_INF = -1e9

    ar.reset()
    cmask = ar.take(128, 128, F32)
    qkv_tm = ar.take(128, 8 * 3072, BF16)
    ao_tm = ar.take(128, 8 * 1024, BF16)
    bqc_sb = ar.take(128, 8, F32)
    bkc_sb = ar.take(128, 8, F32)
    hmask_sb = ar.take(2, 1, F32)
    hsel_sb = ar.take(16, 2, F32)
    mseg = ar.mark()
    x_st = ar.take(128, DM, F32)
    sqs = ar.take(128, DM, F32)
    ln_tm = ar.take(128, DM, BF16)
    ln3fm = ar.take(128, NKC * CW, BF16)
    cpc = ar.take(128, 4 * 512, BF16)
    cbr = ar.take(1, 3072, F32)
    hg_sb = x_st[0:16, :]
    lnh = ln_tm[0:2, :]
    x2h = qkv_tm[0:2, 0:8192].bitcast(F32)
    segA_end = ar.off
    ar.rewind(mseg)
    wq_sb = ar.take(128, 8 * 1024, BF16)
    wk_sb = ar.take(128, 8 * 1024, BF16)
    wv_sb = ar.take(128, 8 * 1024, BF16)
    wo2_sb = ar.take(128, 8 * 1024, BF16)
    q_fm = ar.take(128, 1024, BF16)
    k_fm = ar.take(128, 1024, BF16)
    v_fm = ar.take(128, 1024, BF16)
    qh_fm = ar.take(128, 1024, BF16)
    kh_fm = ar.take(128, 1024, BF16)
    vh_tm = ar.take(128, 1024, BF16)
    s_sb = ar.take(128, 128, F32)
    P = ar.take(128, 128, BF16)
    PT_sb = ar.take(128, 128, BF16)
    o_tm = ar.take(128, 1024, BF16)
    o_fm = ar.take(128, 1024, BF16)
    bvr_sb = ar.take(1, 1024, F32)
    bo2_sb = ar.take(1, 1024, F32)
    segB_end = ar.off
    ar.rewind(mseg)
    at_fm = ar.take(128, 4 * 1024, BF16)
    dens = ar.take(128, 4 * DM, F32)
    x_st4 = ar.take(128, DM, F32)
    sqs4 = ar.take(128, DM, F32)
    ppc = ar.take(128, 4 * 512, BF16)
    pbr = ar.take(1, DM, F32)
    ar.off = max(ar.off, segA_end, segB_end)

    TRB = TRot(E, c.TVIEWS, ident)

    tcm = E.op("g", "memset", cmask[:, :], 0.0)
    E.op("g", "affine_select", out=cmask[:, :], in_=cmask[:, :],
         compare_op=ALU.is_ge, fill=NEG_INF, base=0,
         pattern=[[-1, 128]], channel_multiplier=1, waits=(tcm,))
    tok_cb = E.dma("dy", "c_cbias", cbr, cb)
    tok_hm = E.dma("dy", "c_hmask", hmask_sb[0:2, 0:1], halo_mask,
                   allow_slow_non_contiguous=True)
    tok_hs = E.dma("dy", "c_hsel", hsel_sb[0:16, 0:2], halo_sel,
                   allow_slow_non_contiguous=True)

    tok_xst_free = None
    tok_lntm_free = None
    for m in range(8):
        r0 = m * 128
        td = E.dma("dy", "c_x", x_st, x2i[r0:r0 + 128, :],
                   waits=(tok_xst_free,))
        t1, t2 = ln_stats(E, x_st, s1a[:, m:m + 1], s2a[:, m:m + 1], sqs,
                          waits=(td,))
        tr, tn = ln_chain(E, s1a[:, m:m + 1], s2a[:, m:m + 1],
                          ra[:, m:m + 1], nmra[:, m:m + 1],
                          tmpa[:, m:m + 1], DM, t1, t2)
        tok_ap = E.op("v", "tensor_scalar", ln_tm, x_st, ra[:, m:m + 1],
                      nmra[:, m:m + 1], ALU.mult, ALU.add,
                      waits=(tr, tn, tok_lntm_free))
        tok_xst_free = tok_ap
        last_tt = None
        for ch in range(NKC):
            tt, te = TRB.go(ln_tm[:, ch * 128:(ch + 1) * 128],
                            ln3fm[:, ch * CW + r0 + 1:ch * CW + r0 + 129],
                            waits=(tok_ap,))
            last_tt = tt
        tok_lntm_free = last_tt

    th1 = E.dma("dg", "c_hgin", hg_in.ap()[0:1, :], x2i[0:1, :])
    th2 = E.dma("dg", "c_hgin", hg_in.ap()[1:2, :], x2i[1023:1024, :])
    E.wait("g", th2)
    cc_inst = nc.gpsimd.collective_compute(
        "AllGather", ALU.bypass,
        replica_groups=[list(range(8))],
        ins=[hg_in.ap().opt()], outs=[hg_out.ap().opt()],
    )
    tok_ag = E.inc(cc_inst, "g", 1)
    thg = E.dma("dg", "c_hg", hg_sb[0:16, :], hg_out.ap(),
                waits=(tok_ag, tok_xst_free))
    ev_h = []
    for n in range(8):
        sl = n % 4
        w = [thg, tok_hs]
        tmm = E.op("t", "matmul", PSA[0:2, sl * 512:sl * 512 + 512],
                   hsel_sb[0:16, 0:2], hg_sb[0:16, n * 512:(n + 1) * 512],
                   start=True, stop=True,
                   waits=w + ([ev_h[n - 4]] if n >= 4 else []))
        tev = E.op("v", "tensor_copy", x2h[0:2, n * 512:(n + 1) * 512],
                   PSA[0:2, sl * 512:sl * 512 + 512], waits=(tmm,))
        ev_h.append(tev)
    t1, t2 = ln_stats(E, x2h[0:2, :], s1a[0:2, 8:9], s2a[0:2, 8:9],
                      sqs[0:2, :], waits=tuple(ev_h))
    tr, tn = ln_chain(E, s1a[0:2, 8:9], s2a[0:2, 8:9], ra[0:2, 8:9],
                      nmra[0:2, 8:9], tmpa[0:2, 8:9], DM, t1, t2)
    tok_lnh = E.op("v", "tensor_scalar", lnh[0:2, :], x2h[0:2, :],
                   ra[0:2, 8:9], nmra[0:2, 8:9], ALU.mult, ALU.add,
                   waits=(tr, tn, tok_lntm_free))
    tok_lnhm = E.op("v", "tensor_scalar", lnh[0:2, :], lnh[0:2, :],
                    hmask_sb[0:2, 0:1], None, ALU.mult,
                    waits=(tok_lnh, tok_hm))
    for ch in range(NKC):
        i2 = ch % 2
        po = c.TVIEWS[i2][0:128, 0:2]
        fr = TRB.free[i2]
        fr = () if fr is None else ((fr,) if isinstance(fr[0], str) else fr)
        tt = E.op("t", "transpose", po, lnh[0:2, ch * 128:(ch + 1) * 128],
                  ident[0:2, 0:2], waits=(tok_lnhm,) + fr)
        e1 = E.op("v", "tensor_copy", ln3fm[:, ch * CW:ch * CW + 1],
                  po[:, 0:1], waits=(tt,))
        e2 = E.op("s", "copy", ln3fm[:, ch * CW + 1025:ch * CW + 1026],
                  po[:, 1:2], waits=(tt,))
        TRB.free[i2] = (e1, e2)
    ph1_done = (E.tok("v"), E.tok("s"))
    def conv_slice(m):
        if m < 4:
            return PSA[:, m * 512:(m + 1) * 512]
        return PSB[:, (m - 4) * 512:(m - 3) * 512]

    piece_free = [None] * 4
    slice_free = [None] * 8
    for n in range(6):
        c0 = n * 512
        step = 0
        for k in range(NKC):
            for tap in range(3):
                buf = step % 4
                step += 1
                tw = E.dma("dg", f"c_cw{buf}", cpc[:, buf * 512:(buf + 1) * 512],
                           cw[tap][k * 128:(k + 1) * 128, c0:c0 + 512],
                           waits=(piece_free[buf],))
                for m in range(8):
                    w = [tw]
                    if step == 1:
                        w += list(ph1_done) + [slice_free[m]]
                    lhs = ln3fm[:, k * CW + m * 128 + tap:
                                k * CW + m * 128 + tap + 128]
                    rhs = cpc[:, buf * 512:(buf + 1) * 512]
                    tok = E.op("t", "matmul", conv_slice(m), lhs, rhs,
                               start=(step == 1), stop=False, waits=w)
                    if m == 7:
                        piece_free[buf] = tok
        for m in range(8):
            tok_done = E.op("t", "matmul", conv_slice(m), ones[0:1, :],
                            cbr[0:1, c0:c0 + 512], start=False, stop=True,
                            waits=(tok_cb,))
            eng = "v" if m % 2 == 0 else "s"
            meth = "tensor_copy" if eng == "v" else "copy"
            slice_free[m] = E.op(eng, meth,
                                 qkv_tm[:, m * 3072 + c0:m * 3072 + c0 + 512],
                                 conv_slice(m), waits=(tok_done,))
    E.barrier()

    tok_wq = E.dma("dg", "c_wq", wq_sb[:, :].rearrange("p (c n) -> p c n", n=1024),
                   wq.rearrange("(c p) n -> p c n", p=128))
    tok_wk = E.dma("dg", "c_wk", wk_sb[:, :].rearrange("p (c n) -> p c n", n=1024),
                   wk.rearrange("(c p) n -> p c n", p=128))
    tok_wv = E.dma("dg", "c_wv", wv_sb[:, :].rearrange("p (c n) -> p c n", n=1024),
                   wv.rearrange("(c p) n -> p c n", p=128))
    tok_wo2 = E.dma("dg", "c_wo2", wo2_sb[:, :].rearrange("p (c n) -> p c n", n=1024),
                    wo2.rearrange("(c p) n -> p c n", p=128))
    tok_bqc = E.dma("dy", "c_bqc", bqc_sb[:, :], bqc)
    tok_bkc = E.dma("dy", "c_bkc", bkc_sb[:, :], bkc)
    tok_bvr = E.dma("dy", "c_bvr", bvr_sb, bvr)
    tok_bo2 = E.dma("dy", "c_bo2", bo2_sb, bo2)

    PH = [PSA[:, 0:128], PSA[:, 512:640]]
    PV = [PSA[:, 1024:1536], PSA[:, 1536:2048]]
    PS = [PSB[:, 1024:1152], PSB[:, 1536:1664]]
    ph_free = [None, None]
    pv_free = [None, None]
    ps_free = [None, None]

    for m in range(8):
        base = m * 3072
        qf = []
        for c3 in range(8):
            tt, te = TRB.go(qkv_tm[:, base + c3 * 128:base + (c3 + 1) * 128],
                            q_fm[:, c3 * 128:(c3 + 1) * 128], waits=())
        for c3 in range(8):
            tt, te = TRB.go(qkv_tm[:, base + 1024 + c3 * 128:base + 1024 + (c3 + 1) * 128],
                            k_fm[:, c3 * 128:(c3 + 1) * 128], waits=())
        for c3 in range(8):
            tt, te = TRB.go(qkv_tm[:, base + 2048 + c3 * 128:base + 2048 + (c3 + 1) * 128],
                            v_fm[:, c3 * 128:(c3 + 1) * 128], waits=())
        fm_ready = (E.tok("v"), E.tok("s"))
        for (wsb, bias_sb, dst, tok_w, tok_b) in (
                (wq_sb, bqc_sb, qh_fm, tok_wq, tok_bqc),
                (wk_sb, bkc_sb, kh_fm, tok_wk, tok_bkc)):
            for hc in range(8):
                pview = PH[hc % 2]
                tok_mm = None
                for wc in range(8):
                    w = [tok_w] + list(fm_ready)
                    if wc == 0:
                        w.append(ph_free[hc % 2])
                    if wc == 7:
                        tok_mm = E.op("t", "matmul", pview[:, 0:128],
                                      wsb[:, wc * 1024 + hc * 128:wc * 1024 + (hc + 1) * 128],
                                      (q_fm if dst is qh_fm else k_fm)[:, wc * 128:(wc + 1) * 128],
                                      start=(wc == 0), stop=(wc == 7), waits=w)
                    else:
                        E.op_("t", "matmul", pview[:, 0:128],
                              wsb[:, wc * 1024 + hc * 128:wc * 1024 + (hc + 1) * 128],
                              (q_fm if dst is qh_fm else k_fm)[:, wc * 128:(wc + 1) * 128],
                              start=(wc == 0), stop=(wc == 7), waits=w)
                tok_ev = E.op("v", "tensor_scalar_add",
                              dst[:, hc * 128:(hc + 1) * 128],
                              pview[:, 0:128], bias_sb[:, hc:hc + 1],
                              waits=(tok_mm, tok_b))
                ph_free[hc % 2] = tok_ev
        for n2 in range(2):
            pview = PV[n2]
            tok_mm = None
            for wc in range(8):
                w = [tok_wv] + list(fm_ready)
                if wc == 0:
                    w.append(pv_free[n2])
                E.op_("t", "matmul", pview,
                      v_fm[:, wc * 128:(wc + 1) * 128],
                      wv_sb[:, wc * 1024 + n2 * 512:wc * 1024 + n2 * 512 + 512],
                      start=(wc == 0), stop=False, waits=w)
            tok_mm = E.op("t", "matmul", pview, ones[0:1, :],
                          bvr_sb[0:1, n2 * 512:n2 * 512 + 512],
                          start=False, stop=True, waits=(tok_bvr,))
            tok_ev = E.op("s", "copy", vh_tm[:, n2 * 512:n2 * 512 + 512],
                          pview, waits=(tok_mm,))
            pv_free[n2] = tok_ev
        vh_ready = pv_free[:]
        qh_ready = (E.tok("v"),)
        tok_o_ev = [None, None]
        tok_exp_prev = None
        tok_ptt_prev = None
        for h in range(NH):
            hc = h // 2
            pb_ = (h % 2) * 64
            psc = PS[h % 2]
            tok_sc = E.op("t", "matmul", psc[:, 0:128],
                          qh_fm[pb_:pb_ + 64, hc * 128:(hc + 1) * 128],
                          kh_fm[pb_:pb_ + 64, hc * 128:(hc + 1) * 128],
                          start=True, stop=True,
                          waits=list(qh_ready) + [ps_free[h % 2]])
            tok_stt = E.op("v", "scalar_tensor_tensor", s_sb[:, :],
                           psc[:, 0:128], SCALE, cmask[:, :],
                           ALU.mult, ALU.add, waits=(tok_sc, tok_exp_prev))
            ps_free[h % 2] = tok_stt
            tok_mx = E.op("v", "reduce_max", nm[:, :], s_sb[:, :], axis=AX,
                          negate=True, waits=(tok_stt,))
            tok_exp = E.op("s", "activation", P, s_sb[:, :], AF.Exp,
                           bias=nm[:, 0:1], scale=1.0,
                           accum_out=rsum[:, 0:1],
                           waits=(tok_mx, tok_ptt_prev))
            tok_exp_prev = tok_exp
            tok_rr = E.op("v", "reciprocal", rrec[:, :], rsum[:, :],
                          waits=(tok_exp,))
            tt, te = TRB.go(P, PT_sb, waits=(tok_exp,))
            tok_ptt_prev = tt
            tok_pv = E.op("t", "matmul", PH[h % 2][:, 0:64], PT_sb,
                          vh_tm[:, h * 64:(h + 1) * 64],
                          start=True, stop=True,
                          waits=[te, ph_free[h % 2]] + vh_ready)
            tok_oev = E.op("s", "activation", o_tm[:, h * 64:(h + 1) * 64],
                           PH[h % 2][:, 0:64], AF.Copy, scale=rrec[:, 0:1],
                           waits=(tok_pv, tok_rr))
            ph_free[h % 2] = tok_oev
            tok_o_ev[h % 2] = tok_oev
        for c3 in range(8):
            tt, te = TRB.go(o_tm[:, c3 * 128:(c3 + 1) * 128],
                            o_fm[:, c3 * 128:(c3 + 1) * 128],
                            waits=[tok_o_ev[0], tok_o_ev[1]])
        ofm_ready = (E.tok("v"), E.tok("s"))
        for n2 in range(2):
            pview = PV[n2]
            tok_mm = None
            for hc in range(8):
                w = [tok_wo2] + list(ofm_ready)
                if hc == 0:
                    w.append(pv_free[n2])
                E.op_("t", "matmul", pview,
                      o_fm[:, hc * 128:(hc + 1) * 128],
                      wo2_sb[:, hc * 1024 + n2 * 512:hc * 1024 + n2 * 512 + 512],
                      start=(hc == 0), stop=False, waits=w)
            tok_mm = E.op("t", "matmul", pview, ones[0:1, :],
                          bo2_sb[0:1, n2 * 512:n2 * 512 + 512],
                          start=False, stop=True, waits=(tok_bo2,))
            tok_ev = E.op("v", "tensor_copy",
                          ao_tm[:, m * 1024 + n2 * 512:m * 1024 + n2 * 512 + 512],
                          pview, waits=(tok_mm,))
            pv_free[n2] = tok_ev
    E.barrier()

    tok_pb = E.dma("dy", "c_pbias", pbr, pb)
    piece_free = [None] * 4
    psa_free = [None] * 4
    tok_x4_free = None
    for grp in range(2):
        tiles = [grp * 4 + i for i in range(4)]
        for i, m in enumerate(tiles):
            for c3 in range(8):
                TRB.go(ao_tm[:, m * 1024 + c3 * 128:m * 1024 + (c3 + 1) * 128],
                       at_fm[:, i * 1024 + c3 * 128:i * 1024 + (c3 + 1) * 128],
                       waits=())
        at_ready = (E.tok("v"), E.tok("s"))
        dens_done = [[None] * 8 for _ in range(4)]
        for n in range(8):
            c0 = n * 512
            for k in range(8):
                buf = k % 4
                tw = E.dma("dg", f"c_pw{buf}", ppc[:, buf * 512:(buf + 1) * 512],
                           pw[k * 128:(k + 1) * 128, c0:c0 + 512],
                           waits=(piece_free[buf],))
                for i in range(4):
                    w = [tw]
                    if k == 0:
                        w += list(at_ready) + [psa_free[i]]
                    if i == 3:
                        piece_free[buf] = E.op("t", "matmul",
                                               PSA[:, i * 512:(i + 1) * 512],
                                               at_fm[:, i * 1024 + k * 128:i * 1024 + (k + 1) * 128],
                                               ppc[:, buf * 512:(buf + 1) * 512],
                                               start=(k == 0), stop=False,
                                               waits=w)
                    else:
                        E.op_("t", "matmul", PSA[:, i * 512:(i + 1) * 512],
                              at_fm[:, i * 1024 + k * 128:i * 1024 + (k + 1) * 128],
                              ppc[:, buf * 512:(buf + 1) * 512],
                              start=(k == 0), stop=False, waits=w)
            for i in range(4):
                m = tiles[i]
                tok_done = E.op("t", "matmul", PSA[:, i * 512:(i + 1) * 512],
                                ones[0:1, :], pbr[0:1, c0:c0 + 512],
                                start=False, stop=True, waits=(tok_pb,))
                eng = "v" if i % 2 == 0 else "s"
                meth = "tensor_copy" if eng == "v" else "copy"
                tev = E.op(eng, meth, dens[:, i * DM + c0:i * DM + c0 + 512],
                           PSA[:, i * 512:(i + 1) * 512], waits=(tok_done,))
                psa_free[i] = tev
                dens_done[i][n] = tev
        tok_out_last = None
        for i in range(4):
            m = tiles[i]
            dview = dens[:, i * DM:(i + 1) * DM]
            t1, t2 = ln_stats(E, dview, p_s1[:, m:m + 1], p_s2[:, m:m + 1],
                              sqs4, waits=tuple(dens_done[i]))
            tr, tn = ln_chain(E, p_s1[:, m:m + 1], p_s2[:, m:m + 1],
                              p_r[:, m:m + 1], p_nmr[:, m:m + 1],
                              p_tmp[:, m:m + 1], DM, t1, t2)
            tx = E.dma("dy", "c_x4", x_st4, x2i[m * 128:(m + 1) * 128, :],
                       waits=(tok_x4_free,))
            tap = E.op("v", "tensor_scalar", dview, dview, p_r[:, m:m + 1],
                       p_nmr[:, m:m + 1], ALU.mult, ALU.add,
                       waits=(tr, tn))
            tadd = E.op("v", "tensor_tensor", dview, dview, x_st4, ALU.add,
                        waits=(tap, tx))
            tok_x4_free = tadd
            tok_out_last = E.dma("dy", "c_out", x3_out[m * 128:(m + 1) * 128, :],
                                 dview, waits=(tadd,))
        E.wait("v", tok_out_last)
        E.wait("s", tok_out_last)

    E.barrier()


def emit_l1_ffn(c, x3i, w1, w3, w2, a1, b1, a3, b3, a2, b2, out_ap):
    nc, E, ar = c.nc, c.E, c.ar
    PSA, PSB = c.PSA, c.PSB
    ident, ones = c.ident, c.ones
    s1a, s2a, ra, nmra, tmpa = c.s1a, c.s2a, c.ra, c.nmra, c.tmpa
    p_s1, p_s2, p_r, p_nmr, p_tmp = c.p_s1, c.p_s2, c.p_r, c.p_nmr, c.p_tmp
    HID = 11008
    NHC = 86
    HCS = [1024] * 10 + [768]
    SCALING = 2.0

    def spans(L):
        return [(0, 512), (512, L)] if L > 512 else [(0, L)]

    ar.reset()
    a1_sb = ar.take(128, 32 * 16, BF16)
    a3_sb = ar.take(128, 32 * 16, BF16)
    a2_sb = ar.take(128, NHC * 16, BF16)
    b2_sb = ar.take(16, DM, BF16)
    x_st = ar.take(128, DM, F32)
    ln_tm = ar.take(128, DM, BF16)
    lnfm = ar.take(128, 4 * DM, BF16)
    dens = ar.take(128, 4 * DM, F32)
    sqs = dens[:, 0:DM]
    sq_post = lnfm[:, 0:2 * DM].bitcast(F32)
    h1_sb = ar.take(128, 4 * 1024, BF16)
    sl_sb = ar.take(128, 4 * 1024, BF16)
    g_sb = ar.take(128, 4 * 1024, BF16)
    g_fm = ar.take(128, 4 * 1024, BF16)
    wpc = ar.take(128, 4 * 1024, BF16)
    w2pc = ar.take(128, 4 * 1024, BF16)
    b1s = ar.take(16, 1024, BF16)
    b3s = ar.take(16, 1024, BF16)
    xa_bf = ar.take(128, 16, BF16)
    xa1_fm = ar.take(16, 4 * 128, BF16)
    xa3_fm = ar.take(16, 4 * 128, BF16)
    ga2_sb = ar.take(128, 4 * 16, F32)
    ga2_bf = ar.take(128, 16, BF16)
    ga2_fm = ar.take(16, 4 * 128, BF16)

    TVIEWS = c.TVIEWS
    PH = [PSA[:, 0:1024], PSA[:, 1024:2048],
          PSB[:, 0:1024], PSB[:, 1024:2048]]
    PGA = PSA[:, 0:16]

    TR = TRot(E, TVIEWS, ident)

    E.dma("dg", "d_a1", a1_sb[:, :].rearrange("p (c n) -> p c n", n=16),
          a1.rearrange("(c p) n -> p c n", p=128))
    E.dma("dg", "d_a3", a3_sb[:, :].rearrange("p (c n) -> p c n", n=16),
          a3.rearrange("(c p) n -> p c n", p=128))
    E.dma("dg", "d_a2", a2_sb[:, :].rearrange("p (c n) -> p c n", n=16),
          a2.rearrange("(c p) n -> p c n", p=128))
    E.dma("dg", "d_b2", b2_sb[0:16, :], b2)
    tok_a1 = E.tok("D:d_a1")
    tok_a3 = E.tok("D:d_a3")
    tok_a2 = E.tok("D:d_a2")
    tok_b2 = E.tok("D:d_b2")
    E.barrier()

    tok_xst_free = None
    tok_lntm_free = None
    tok_out_last = None
    tok_b1_used = None
    tok_b3_used = None
    for grp in range(2):
        tiles = [grp * 4 + i for i in range(4)]
        for i, m in enumerate(tiles):
            td = E.dma("dy", "d_x", x_st, x3i[m * 128:(m + 1) * 128, :],
                       waits=(tok_xst_free,))
            t1, t2 = ln_stats(E, x_st, s1a[:, m:m + 1], s2a[:, m:m + 1], sqs,
                              waits=(td,))
            tr, tn = ln_chain(E, s1a[:, m:m + 1], s2a[:, m:m + 1],
                              ra[:, m:m + 1], nmra[:, m:m + 1],
                              tmpa[:, m:m + 1], DM, t1, t2)
            tok_ap = E.op("v", "tensor_scalar", ln_tm, x_st, ra[:, m:m + 1],
                          nmra[:, m:m + 1], ALU.mult, ALU.add,
                          waits=(tr, tn, tok_lntm_free))
            tok_xst_free = tok_ap
            last_tt = None
            for c in range(NKC):
                tt, te = TR.go(ln_tm[:, c * 128:(c + 1) * 128],
                               lnfm[:, i * DM + c * 128:i * DM + (c + 1) * 128],
                               waits=(tok_ap,))
                last_tt = tt
            tok_lntm_free = last_tt
        lnfm_ready = (E.tok("v"), E.tok("s"))

        tok_pga_free = None
        for (asb, xfm, tok_a) in ((a1_sb, xa1_fm, tok_a1),
                                  (a3_sb, xa3_fm, tok_a3)):
            for i in range(4):
                tok_mm = None
                for k in range(NKC):
                    w = list(lnfm_ready) + [tok_a]
                    if k == 0:
                        w.append(tok_pga_free)
                    tok_mm = E.op("t", "matmul", PGA,
                                  lnfm[:, i * DM + k * 128:i * DM + (k + 1) * 128],
                                  asb[:, k * 16:(k + 1) * 16],
                                  start=(k == 0), stop=(k == 31), waits=w)
                tok_ev = E.op("s", "activation", xa_bf[:, :], PGA, AF.Copy,
                              scale=SCALING, waits=(tok_mm,))
                tok_pga_free = tok_ev
                tt, te = TR.go(xa_bf[:, :], xfm[0:16, i * 128:(i + 1) * 128],
                               waits=(tok_ev,))
        xa_ready = (E.tok("v"), E.tok("s"))
        tok_ga2_z = E.op("v", "memset", ga2_sb[:, :], 0.0)
        tok_ga2 = [tok_ga2_z] * 4

        E.wait("v", E.tok("s"))
        dens_tok = [[None] * 4 for _ in range(4)]
        ph_free = [xa_ready] * 4
        piece_free = [None] * 4
        w2_free = [None] * 4
        hc0 = 0
        for ci, L in enumerate(HCS):
            khs = L // 128
            tb1 = E.dma("dg", "d_b1s", b1s[0:16, 0:L], b1[0:16, hc0:hc0 + L],
                        waits=(tok_b1_used,))
            h1_done = [None] * 4
            for k2 in range(NKC // 2):
                buf = k2 % 2
                tw = E.dma("dg", f"d_wp{buf}",
                           wpc[:, buf * 2048:buf * 2048 + 2 * L]
                           .rearrange("p (a n) -> p a n", a=2),
                           w1[k2 * 256:(k2 + 1) * 256, hc0:hc0 + L]
                           .rearrange("(a p) n -> p a n", p=128),
                           waits=(piece_free[buf],))
                for sub in range(2):
                    k = k2 * 2 + sub
                    for i in range(4):
                        w = [tw]
                        if k == 0:
                            w += list(lnfm_ready) + list(ph_free[i])
                        for si, (s0, s1_) in enumerate(spans(L)):
                            last = (i == 3 and sub == 1 and si == len(spans(L)) - 1)
                            if last:
                                piece_free[buf] = E.op(
                                    "t", "matmul", PH[i][:, s0:s1_],
                                    lnfm[:, i * DM + k * 128:i * DM + (k + 1) * 128],
                                    wpc[:, buf * 2048 + sub * L + s0:buf * 2048 + sub * L + s1_],
                                    start=(k == 0), stop=False, waits=w)
                            else:
                                E.op_("t", "matmul", PH[i][:, s0:s1_],
                                      lnfm[:, i * DM + k * 128:i * DM + (k + 1) * 128],
                                      wpc[:, buf * 2048 + sub * L + s0:buf * 2048 + sub * L + s1_],
                                      start=(k == 0), stop=False, waits=w)
            for i in range(4):
                tok = None
                for (s0, s1_) in spans(L):
                    tok = E.op("t", "matmul", PH[i][:, s0:s1_],
                               xa1_fm[0:16, i * 128:(i + 1) * 128],
                               b1s[0:16, s0:s1_],
                               start=False, stop=True,
                               waits=[tb1] + list(xa_ready))
                if i == 3:
                    tok_b1_used = tok
                eng = "v" if i % 2 == 0 else "s"
                meth = "tensor_copy" if eng == "v" else "copy"
                h1_done[i] = E.op(eng, meth, h1_sb[:, i * 1024:i * 1024 + L],
                                  PH[i][0:128, 0:L], waits=(tok,))
                ph_free[i] = (h1_done[i],)
            sl_done = [None] * 4
            for i in range(4):
                sl_done[i] = E.op("s", "activation",
                                  sl_sb[:, i * 1024:i * 1024 + L],
                                  h1_sb[:, i * 1024:i * 1024 + L], AF.Silu,
                                  waits=(h1_done[i],))
            tb3 = E.dma("dg", "d_b3s", b3s[0:16, 0:L], b3[0:16, hc0:hc0 + L],
                        waits=(tok_b3_used,))
            g_done = [None] * 4
            for k2 in range(NKC // 2):
                buf = k2 % 2
                tw = E.dma("dg", f"d_wp{buf}",
                           wpc[:, buf * 2048:buf * 2048 + 2 * L]
                           .rearrange("p (a n) -> p a n", a=2),
                           w3[k2 * 256:(k2 + 1) * 256, hc0:hc0 + L]
                           .rearrange("(a p) n -> p a n", p=128),
                           waits=(piece_free[buf],))
                for sub in range(2):
                    k = k2 * 2 + sub
                    for i in range(4):
                        w = [tw]
                        if k == 0:
                            w += list(ph_free[i])
                        for si, (s0, s1_) in enumerate(spans(L)):
                            last = (i == 3 and sub == 1 and si == len(spans(L)) - 1)
                            if last:
                                piece_free[buf] = E.op(
                                    "t", "matmul", PH[i][:, s0:s1_],
                                    lnfm[:, i * DM + k * 128:i * DM + (k + 1) * 128],
                                    wpc[:, buf * 2048 + sub * L + s0:buf * 2048 + sub * L + s1_],
                                    start=(k == 0), stop=False, waits=w)
                            else:
                                E.op_("t", "matmul", PH[i][:, s0:s1_],
                                      lnfm[:, i * DM + k * 128:i * DM + (k + 1) * 128],
                                      wpc[:, buf * 2048 + sub * L + s0:buf * 2048 + sub * L + s1_],
                                      start=(k == 0), stop=False, waits=w)
            for i in range(4):
                tok = None
                for (s0, s1_) in spans(L):
                    tok = E.op("t", "matmul", PH[i][:, s0:s1_],
                               xa3_fm[0:16, i * 128:(i + 1) * 128],
                               b3s[0:16, s0:s1_],
                               start=False, stop=True,
                               waits=[tb3] + list(xa_ready))
                if i == 3:
                    tok_b3_used = tok
                g_done[i] = E.op("v", "tensor_tensor",
                                 g_sb[:, i * 1024:i * 1024 + L],
                                 PH[i][0:128, 0:L],
                                 sl_sb[:, i * 1024:i * 1024 + L],
                                 ALU.mult, waits=(tok, sl_done[i]))
                ph_free[i] = (g_done[i],)
            gfm_tok = [None] * 4
            for i in range(4):
                for kc in range(khs):
                    TR.go(g_sb[:, i * 1024 + kc * 128:i * 1024 + (kc + 1) * 128],
                          g_fm[:, i * 1024 + kc * 128:i * 1024 + (kc + 1) * 128],
                          waits=tuple(g_done))
                gfm_tok[i] = (E.tok("v"), E.tok("s"))
            for i in range(4):
                tok = None
                for kc in range(khs):
                    w = list(gfm_tok[i]) + [tok_a2]
                    if kc == 0:
                        w += [tok_pga_free] + list(ph_free[0])
                    kglob = (hc0 // 128) + kc
                    tok = E.op("t", "matmul", PGA,
                               g_fm[:, i * 1024 + kc * 128:i * 1024 + (kc + 1) * 128],
                               a2_sb[:, kglob * 16:(kglob + 1) * 16],
                               start=(kc == 0), stop=(kc == khs - 1), waits=w)
                tok_ga2[i] = E.op("v", "tensor_tensor",
                                  ga2_sb[:, i * 16:(i + 1) * 16],
                                  ga2_sb[:, i * 16:(i + 1) * 16], PGA,
                                  ALU.add, waits=(tok, tok_ga2[i]))
                tok_pga_free = tok_ga2[i]
            for dmq in range(4):
                c0 = dmq * 1024
                tokmm = [None] * 4
                for kc2 in range(khs // 2):
                    buf = kc2 % 2
                    kg0 = (hc0 // 128) + kc2 * 2
                    tw2 = E.dma("dg", f"d_w2{buf}",
                                w2pc[:, buf * 2048:(buf + 1) * 2048]
                                .rearrange("p (a n) -> p a n", a=2),
                                w2[kg0 * 128:(kg0 + 2) * 128, c0:c0 + 1024]
                                .rearrange("(a p) n -> p a n", p=128),
                                waits=(w2_free[buf],))
                    for sub in range(2):
                        kc = kc2 * 2 + sub
                        for i in range(4):
                            w = [tw2] + list(gfm_tok[i])
                            if kc == 0:
                                w += [dens_tok[i][dmq], tok_pga_free] + list(ph_free[i])
                            for n5 in range(2):
                                need = (n5 == 1 and sub == 1 and (i == 3 or kc == khs - 1))
                                if kc == khs - 1 and n5 == 1:
                                    need = True
                                if need:
                                    tok = E.op("t", "matmul",
                                               PH[i][:, n5 * 512:(n5 + 1) * 512],
                                               g_fm[:, i * 1024 + kc * 128:i * 1024 + (kc + 1) * 128],
                                               w2pc[:, buf * 2048 + sub * 1024 + n5 * 512:buf * 2048 + sub * 1024 + (n5 + 1) * 512],
                                               start=(kc == 0), stop=(kc == khs - 1),
                                               waits=w)
                                    if kc == khs - 1:
                                        tokmm[i] = tok
                                    if i == 3:
                                        w2_free[buf] = tok
                                else:
                                    E.op_("t", "matmul",
                                          PH[i][:, n5 * 512:(n5 + 1) * 512],
                                          g_fm[:, i * 1024 + kc * 128:i * 1024 + (kc + 1) * 128],
                                          w2pc[:, buf * 2048 + sub * 1024 + n5 * 512:buf * 2048 + sub * 1024 + (n5 + 1) * 512],
                                          start=(kc == 0), stop=(kc == khs - 1),
                                          waits=w)
                for i in range(4):
                    eng = "v" if i % 2 == 0 else "s"
                    dsl = dens[:, i * DM + c0:i * DM + c0 + 1024]
                    if ci == 0:
                        meth = "tensor_copy" if eng == "v" else "copy"
                        tev = E.op(eng, meth, dsl, PH[i],
                                   waits=(tokmm[i], dens_tok[i][dmq]))
                    else:
                        tev = E.op("v", "tensor_tensor", dsl, dsl, PH[i],
                                   ALU.add, waits=(tokmm[i], dens_tok[i][dmq]))
                    dens_tok[i][dmq] = tev
                    ph_free[i] = (tev,)
            hc0 += L

        all_dens = tuple(t for row in dens_tok for t in row)
        prev_tt = None
        for i in range(4):
            m = tiles[i]
            tcv = E.op("s", "activation", ga2_bf[:, :],
                       ga2_sb[:, i * 16:(i + 1) * 16],
                       AF.Copy, scale=SCALING, waits=(tok_ga2[i], prev_tt))
            tt, te = TR.go(ga2_bf[:, :], ga2_fm[0:16, i * 128:(i + 1) * 128],
                           waits=(tcv,) + all_dens)
            prev_tt = tt
            for dmq in range(4):
                tok = None
                for n5 in range(2):
                    c0 = dmq * 1024 + n5 * 512
                    tok = E.op("t", "matmul",
                               PH[i][:, n5 * 512:(n5 + 1) * 512],
                               ga2_fm[0:16, i * 128:(i + 1) * 128],
                               b2_sb[0:16, c0:c0 + 512],
                               start=True, stop=True,
                               waits=[te, tok_b2] + list(ph_free[i]))
                tev = E.op("v", "tensor_tensor",
                           dens[:, i * DM + dmq * 1024:i * DM + (dmq + 1) * 1024],
                           dens[:, i * DM + dmq * 1024:i * DM + (dmq + 1) * 1024],
                           PH[i], ALU.add, waits=(tok, dens_tok[i][dmq]))
                dens_tok[i][dmq] = tev
                ph_free[i] = (tev,)
            dview = dens[:, i * DM:(i + 1) * DM]
            t1, t2 = ln_stats(E, dview, p_s1[:, m:m + 1], p_s2[:, m:m + 1],
                              sq_post, waits=tuple(dens_tok[i]))
            tr, tn = ln_chain(E, p_s1[:, m:m + 1], p_s2[:, m:m + 1],
                              p_r[:, m:m + 1], p_nmr[:, m:m + 1],
                              p_tmp[:, m:m + 1], DM, t1, t2)
            tx = E.dma("dy", "d_x", x_st, x3i[m * 128:(m + 1) * 128, :],
                       waits=(tok_xst_free,))
            tap = E.op("v", "tensor_scalar", dview, dview, p_r[:, m:m + 1],
                       p_nmr[:, m:m + 1], ALU.mult, ALU.add, waits=(tr, tn))
            tadd = E.op("v", "tensor_tensor", dview, dview, x_st, ALU.add,
                        waits=(tap, tx))
            tok_xst_free = tadd
            tok_out_last = E.dma("dy", "d_out", out_ap[m * 128:(m + 1) * 128, :],
                                 dview, waits=(tadd,))
        E.barrier()

    E.barrier()




def build_fused():
    nc = bass.Bass("TRN2", target_bir_lowering=False)

    xo = nc.dram_tensor("xo", [1024, DM], F32, kind="ExternalInput").ap()
    xr = nc.dram_tensor("xr", [1024, DM], F32, kind="ExternalInput").ap()
    wqkv = nc.dram_tensor("wqkv", [DM, 3072], F32, kind="ExternalInput").ap()
    bqkv = nc.dram_tensor("bqkv", [1, 3072], F32, kind="ExternalInput").ap()
    wo = nc.dram_tensor("wo", [1024, DM], F32, kind="ExternalInput").ap()
    bo = nc.dram_tensor("bo", [1, DM], F32, kind="ExternalInput").ap()
    w0 = nc.dram_tensor("w0", [DM, DM], F32, kind="ExternalInput").ap()
    b0 = nc.dram_tensor("b0", [1, DM], F32, kind="ExternalInput").ap()
    cent = nc.dram_tensor("cent", [DM, 64], F32, kind="ExternalInput").ap()
    assign = nc.dram_tensor("assign", [1, DM], I32, kind="ExternalInput").ap()
    cw = [nc.dram_tensor(f"cw{t}", [DM, 3072], F32, kind="ExternalInput").ap()
          for t in range(3)]
    cb = nc.dram_tensor("cb", [1, 3072], F32, kind="ExternalInput").ap()
    wq = nc.dram_tensor("wq", [1024, 1024], F32, kind="ExternalInput").ap()
    wk = nc.dram_tensor("wk", [1024, 1024], F32, kind="ExternalInput").ap()
    wv = nc.dram_tensor("wv", [1024, 1024], F32, kind="ExternalInput").ap()
    bqc = nc.dram_tensor("bqc", [128, 8], F32, kind="ExternalInput").ap()
    bkc = nc.dram_tensor("bkc", [128, 8], F32, kind="ExternalInput").ap()
    bvr = nc.dram_tensor("bvr", [1, 1024], F32, kind="ExternalInput").ap()
    wo2 = nc.dram_tensor("wo2", [1024, 1024], F32, kind="ExternalInput").ap()
    bo2 = nc.dram_tensor("bo2", [1, 1024], F32, kind="ExternalInput").ap()
    pw = nc.dram_tensor("pw", [1024, DM], F32, kind="ExternalInput").ap()
    pb = nc.dram_tensor("pb", [1, DM], F32, kind="ExternalInput").ap()
    halo_sel = nc.dram_tensor("halo_sel", [16, 2], F32, kind="ExternalInput").ap()
    halo_mask = nc.dram_tensor("halo_mask", [2, 1], F32, kind="ExternalInput").ap()
    w1 = nc.dram_tensor("w1", [DM, 11008], F32, kind="ExternalInput").ap()
    w3 = nc.dram_tensor("w3", [DM, 11008], F32, kind="ExternalInput").ap()
    w2 = nc.dram_tensor("w2", [11008, DM], F32, kind="ExternalInput").ap()
    a1 = nc.dram_tensor("a1", [DM, 16], F32, kind="ExternalInput").ap()
    b1 = nc.dram_tensor("b1", [16, 11008], F32, kind="ExternalInput").ap()
    a3 = nc.dram_tensor("a3", [DM, 16], F32, kind="ExternalInput").ap()
    b3 = nc.dram_tensor("b3", [16, 11008], F32, kind="ExternalInput").ap()
    a2 = nc.dram_tensor("a2", [11008, 16], F32, kind="ExternalInput").ap()
    b2 = nc.dram_tensor("b2", [16, DM], F32, kind="ExternalInput").ap()
    out = nc.dram_tensor("out", [1024, DM], F32, kind="ExternalOutput").ap()

    xs1 = nc.dram_tensor("xs1", [1024, DM], F32)
    xs2 = nc.dram_tensor("xs2", [1024, DM], F32)
    xs3 = nc.dram_tensor("xs3", [1024, DM], F32)
    cc_in = nc.dram_tensor("cc_in", [64, 1], F32)
    cc_out = nc.dram_tensor("cc_out", [64, 1], F32)
    hg_in = nc.dram_tensor("hg_in", [2, DM], F32)
    hg_out = nc.dram_tensor("hg_out", [16, DM], F32)

    c = make_ctx(nc)
    c.E.barrier()
    emit_l0_attn(c, xo, xr, wqkv, bqkv, wo, bo, xs1.ap())
    emit_l0_ffn(c, xs1.ap(), w0, b0, cent, assign, cc_in, cc_out, xs2.ap())
    emit_l1_attn(c, xs2.ap(), cw, cb, wq, wk, wv, bqc, bkc, bvr, wo2, bo2,
                 pw, pb, halo_sel, halo_mask, hg_in, hg_out, xs3.ap())
    emit_l1_ffn(c, xs3.ap(), w1, w3, w2, a1, b1, a3, b3, a2, b2, out)
    c.E.wait("y", *c.E.dma_totals())
    return nc


def host_inputs_fused(d):
    x0 = np.ascontiguousarray(d["L0_attn_x"])
    w0T = np.ascontiguousarray(d["L0_ffn_weight"].T)
    centT = np.ascontiguousarray(d["L0_ffn_centroids"].T)
    cwf = d["L1_attn_conv_w"]
    shared = {
        "wqkv": np.ascontiguousarray(d["L0_attn_Wqkv"]),
        "bqkv": np.ascontiguousarray(d["L0_attn_bqkv"][None, :]),
        "wo": np.ascontiguousarray(d["L0_attn_Wo"]),
        "bo": np.ascontiguousarray(d["L0_attn_bo"][None, :]),
        "w0": w0T,
        "b0": np.ascontiguousarray(d["L0_ffn_bias"][None, :]),
        "cent": centT,
        "assign": np.ascontiguousarray(d["L0_ffn_assignments"][None, :]),
        "cw0": np.ascontiguousarray(cwf[0]),
        "cw1": np.ascontiguousarray(cwf[1]),
        "cw2": np.ascontiguousarray(cwf[2]),
        "cb": np.ascontiguousarray(d["L1_attn_conv_b"][None, :]),
        "wq": np.ascontiguousarray(d["L1_attn_wq"].reshape(1024, 1024)),
        "wk": np.ascontiguousarray(d["L1_attn_wk"].reshape(1024, 1024)),
        "wv": np.ascontiguousarray(d["L1_attn_wv"].reshape(1024, 1024)),
        "bqc": np.ascontiguousarray(d["L1_attn_bq"].reshape(8, 128).T),
        "bkc": np.ascontiguousarray(d["L1_attn_bk"].reshape(8, 128).T),
        "bvr": np.ascontiguousarray(d["L1_attn_bv"].reshape(1, 1024)),
        "wo2": np.ascontiguousarray(d["L1_attn_wo"].reshape(1024, 1024)),
        "bo2": np.ascontiguousarray(d["L1_attn_bo"][None, :]),
        "pw": np.ascontiguousarray(d["L1_attn_proj_w"]),
        "pb": np.ascontiguousarray(d["L1_attn_proj_b"][None, :]),
        "w1": np.ascontiguousarray(d["L1_ffn_w1"].T),
        "w3": np.ascontiguousarray(d["L1_ffn_w3"].T),
        "w2": np.ascontiguousarray(d["L1_ffn_w2"].T),
        "a1": np.ascontiguousarray(d["L1_ffn_lora_A1"].T),
        "b1": np.ascontiguousarray(d["L1_ffn_lora_B1"].T),
        "a3": np.ascontiguousarray(d["L1_ffn_lora_A3"].T),
        "b3": np.ascontiguousarray(d["L1_ffn_lora_B3"].T),
        "a2": np.ascontiguousarray(d["L1_ffn_lora_A2"].T),
        "b2": np.ascontiguousarray(d["L1_ffn_lora_B2"].T),
    }
    maps = []
    for cid in range(8):
        b, h = cid // 2, cid % 2
        hs = np.zeros((16, 2), np.float32)
        hm = np.zeros((2, 1), np.float32)
        if h == 1:
            hs[2 * cid - 1, 0] = 1.0
            hm[0, 0] = 1.0
        if h == 0:
            hs[2 * cid + 2, 1] = 1.0
            hm[1, 0] = 1.0
        m = dict(shared)
        m["xo"] = np.ascontiguousarray(x0[b, h * 1024:(h + 1) * 1024])
        m["xr"] = np.ascontiguousarray(x0[b, (1 - h) * 1024:(2 - h) * 1024])
        m["halo_sel"] = hs
        m["halo_mask"] = hm
        maps.append(m)
    return maps


def gather_out_fused(results):
    out = np.zeros((4, 2048, 4096), np.float32)
    for cid in range(8):
        b, h = cid // 2, cid % 2
        out[b, h * 1024:(h + 1) * 1024] = results[cid]["out"]
    return out




from concourse.bass_utils import run_bass_kernel_spmd


def kernel(**inputs):
    nc = build_fused()
    res = run_bass_kernel_spmd(nc, host_inputs_fused(inputs),
                               core_ids=list(range(8)))
    return gather_out_fused(res.results)


# version 06
# speedup vs baseline: 1.1379x; 1.1379x over previous
import numpy as np
import concourse.bass as bass
import concourse.mybir as mybir

F32 = mybir.dt.float32
BF16 = mybir.dt.bfloat16
I32 = mybir.dt.int32
U8 = mybir.dt.uint8
AF = mybir.ActivationFunctionType
ALU = mybir.AluOpType
AX = mybir.AxisListType.X

DM = 4096
NKC = DM // 128


class Emit:

    ENG = ("t", "v", "s", "g")

    def __init__(self, nc):
        self.nc = nc
        self.e = {"t": nc.tensor, "v": nc.vector, "s": nc.scalar,
                  "g": nc.gpsimd, "y": nc.sync,
                  "dg": nc.gpsimd, "dy": nc.sync}
        self.sem = {k: nc.alloc_semaphore("sem_" + k) for k in self.ENG}
        self.cnt = {k: 0 for k in self.ENG}
        self.hw = {}

    def _dsem(self, name):
        key = "D:" + name
        if key not in self.sem:
            self.sem[key] = self.nc.alloc_semaphore("sem_" + name)
            self.cnt[key] = 0
        return key

    def wait(self, consumer, *toks):
        for tok in toks:
            if tok is None:
                continue
            k, v = tok
            if v <= 0 or self.hw.get((consumer, k), 0) >= v:
                continue
            self.e[consumer].wait_ge(self.sem[k], v)
            self.hw[(consumer, k)] = v

    def inc(self, inst, key, n=1):
        inst.then_inc(self.sem[key], n)
        self.cnt[key] += n
        return (key, self.cnt[key])

    def tok(self, key):
        return (key, self.cnt[key])

    def op(self, eng, name, *args, waits=(), **kw):
        self.wait(eng, *[w for w in waits if w])
        inst = getattr(self.e[eng], name)(*args, **kw)
        return self.inc(inst, eng, 1)

    def op_(self, eng, name, *args, waits=(), **kw):
        self.wait(eng, *[w for w in waits if w])
        getattr(self.e[eng], name)(*args, **kw)
        return None

    def dma(self, q, stream, out, in_, *, waits=(), **kw):
        key = self._dsem(stream)
        self.wait(q, *[w for w in waits if w])
        inst = self.e[q].dma_start(out, in_, **kw)
        return self.inc(inst, key, 16)

    def barrier(self):
        toks = [self.tok(k) for k in self.cnt]
        for consumer in ("t", "v", "s", "g", "y"):
            self.wait(consumer, *toks)

    def dma_totals(self):
        return [self.tok(k) for k in self.cnt if k.startswith("D:")]


class Arena:

    def __init__(self, nc, name, nbytes):
        self.t = nc.alloc_sbuf_tensor(name, [128, nbytes], U8)
        self.nbytes = nbytes
        self.off = 0

    def reset(self):
        self.off = 0

    def mark(self):
        return self.off

    def rewind(self, m):
        self.off = m

    def view(self, off, part, free, dtype):
        sz = free * np.dtype(mybir.dt.np(dtype)).itemsize
        assert off + sz <= self.nbytes
        return self.t.ap()[0:part, off:off + sz].bitcast(dtype)

    def take(self, part, free, dtype):
        sz = free * np.dtype(mybir.dt.np(dtype)).itemsize
        off = (self.off + 511) & ~511
        assert off + sz <= self.nbytes, f"arena overflow {off}+{sz}>{self.nbytes}"
        self.off = off + sz
        return self.t.ap()[0:part, off:off + sz].bitcast(dtype)


def ln_stats(E, x_ap, s1_ap, s2_ap, sq_ap, waits=()):
    t1 = E.op("v", "reduce_sum", s1_ap, x_ap, axis=AX, waits=waits)
    t2 = E.op("s", "activation", sq_ap, x_ap, AF.Square, accum_out=s2_ap,
              waits=waits)
    return t1, t2


def ln_chain(E, s1_ap, s2_ap, r_ap, nmr_ap, tmp_ap, D, tok_s1, tok_s2):
    t1 = E.op("v", "tensor_tensor", tmp_ap, s1_ap, s1_ap, ALU.mult,
              waits=(tok_s1,))
    t2 = E.op("v", "scalar_tensor_tensor", tmp_ap, tmp_ap, -1.0 / D, s2_ap,
              ALU.mult, ALU.add, waits=(t1, tok_s2))
    t3 = E.op("v", "tensor_scalar", tmp_ap, tmp_ap, 1.0 / D, 1e-5,
              ALU.mult, ALU.add, waits=(t2,))
    sd = E.op("s", "activation", r_ap, tmp_ap, AF.Sqrt, waits=(t3,))
    rt = E.op("v", "reciprocal", r_ap, r_ap, waits=(sd,))
    nm = E.op("v", "scalar_tensor_tensor", nmr_ap, s1_ap, -1.0 / D, r_ap,
              ALU.mult, ALU.mult, waits=(rt, tok_s1))
    return rt, nm


class TPool:

    def __init__(self, E, psts, ident):
        self.E = E
        self.psts = psts
        self.free = [None] * len(psts)
        self.ident = ident
        self.i = 0
        self.flip = 0

    def transpose(self, src_ap, dst_ap, waits=(), evac_eng=None):
        E = self.E
        i = self.i
        self.i = (i + 1) % len(self.psts)
        eng = evac_eng or ("v" if self.flip == 0 else "s")
        self.flip ^= 1
        po = self.psts[i][0:src_ap.shape[1], 0:src_ap.shape[0]]
        tt = E.op("t", "transpose", po, src_ap, self.ident,
                  waits=tuple(waits) + (self.free[i],))
        meth = "copy" if eng == "s" else "tensor_copy"
        te = E.op(eng, meth, dst_ap, po, waits=(tt,))
        self.free[i] = te
        return tt, te




class TRot:

    def __init__(self, E, views, ident):
        self.E = E
        self.views = views
        self.ident = ident
        self.free = [None] * len(views)
        self.i = 0
        self.flip = 0

    def go(self, src_ap, dst_ap, waits=()):
        E = self.E
        i = self.i
        self.i = (i + 1) % len(self.views)
        eng = "v" if self.flip == 0 else "s"
        self.flip ^= 1
        p = src_ap.shape[0]
        po = self.views[i][0:src_ap.shape[1], 0:p]
        fr = self.free[i]
        if fr is None:
            fr = ()
        elif isinstance(fr[0], str):
            fr = (fr,)
        tt = E.op("t", "transpose", po, src_ap, self.ident[0:p, 0:p],
                  waits=tuple(w for w in waits if w) + fr)
        meth = "copy" if eng == "s" else "tensor_copy"
        te = E.op(eng, meth, dst_ap, po, waits=(tt,))
        self.free[i] = te
        return tt, te


class Ctx:
    pass


def make_ctx(nc):
    c = Ctx()
    c.nc = nc
    c.E = Emit(nc)
    E = c.E
    c.ident = nc.alloc_sbuf_tensor("ident", [128, 128], BF16)
    c.ones = nc.alloc_sbuf_tensor("ones", [1, 128], F32)
    c.ones_bf = nc.alloc_sbuf_tensor("ones_bf", [1, 128], BF16)
    c.s1a = nc.alloc_sbuf_tensor("s1a", [128, 16], F32)
    c.s2a = nc.alloc_sbuf_tensor("s2a", [128, 16], F32)
    c.ra = nc.alloc_sbuf_tensor("ra", [128, 16], F32)
    c.nmra = nc.alloc_sbuf_tensor("nmra", [128, 16], F32)
    c.tmpa = nc.alloc_sbuf_tensor("tmpa", [128, 16], F32)
    c.p_s1 = nc.alloc_sbuf_tensor("p_s1", [128, 16], F32)
    c.p_s2 = nc.alloc_sbuf_tensor("p_s2", [128, 16], F32)
    c.p_r = nc.alloc_sbuf_tensor("p_r", [128, 16], F32)
    c.p_nmr = nc.alloc_sbuf_tensor("p_nmr", [128, 16], F32)
    c.p_tmp = nc.alloc_sbuf_tensor("p_tmp", [128, 16], F32)
    c.nm = nc.alloc_sbuf_tensor("nm", [128, 1], F32)
    c.nms = nc.alloc_sbuf_tensor("nms", [128, 1], F32)
    c.rsum = nc.alloc_sbuf_tensor("rsum", [128, 1], F32)
    c.rrec = nc.alloc_sbuf_tensor("rrec", [128, 1], F32)
    c.PSA = nc.alloc_psum_tensor("PSA", [128, 2048], F32)
    c.PSB = nc.alloc_psum_tensor("PSB", [128, 2048], F32)
    c.TVIEWS = [c.PSB[:, 0:512].bitcast(BF16)[:, 0:128],
                c.PSB[:, 512:1024].bitcast(BF16)[:, 0:128]]
    rem = nc.sbuf_bytes_remaining
    c.ar = Arena(nc, "arena", rem - 1536)
    tms = E.op("g", "memset", c.ident[:, :], 0.0)
    E.op("g", "affine_select", out=c.ident[:, :], in_=c.ident[:, :],
         compare_op=ALU.not_equal, fill=1.0, base=0,
         pattern=[[-1, 128]], channel_multiplier=1, waits=(tms,))
    E.op("g", "memset", c.ones[0:1, :], 1.0)
    E.op("g", "memset", c.ones_bf[0:1, :], 1.0)
    return c


def emit_l0_attn(c, xo, xr, wqkv, bqkv, wo, bo, x1_out):
    nc, E, ar = c.nc, c.E, c.ar
    PSA, PSB = c.PSA, c.PSB
    ident, ones = c.ident, c.ones_bf
    s1a, s2a, ra, nmra, tmpa = c.s1a, c.s2a, c.ra, c.nmra, c.tmpa
    p_s1, p_s2, p_r, p_nmr, p_tmp = c.p_s1, c.p_s2, c.p_r, c.p_nmr, c.p_tmp
    nm, nms, rsum, rrec = c.nm, c.nms, c.rsum, c.rrec

    ar.reset()
    K_fm = ar.take(128, 8 * 2048, BF16)
    V_tm = ar.take(128, 16 * 1024, BF16)
    Q_fm = ar.take(128, 8 * 1024, BF16)
    at_fm_all = ar.take(128, 8 * 1024, BF16)
    m0 = ar.mark()
    x_st = ar.take(128, DM, F32)
    ln_tm = ar.take(128, DM, BF16)
    lnfm = ar.take(128, 4 * DM, BF16)
    sqs = ar.take(128, DM, F32)
    wpc = ar.take(128, 4 * 512, BF16)
    ttm = ar.take(128, 1024, BF16)
    bqv = ar.take(1, 3072, BF16)
    gend = ar.off
    ar.rewind(m0)
    P = ar.take(128, 2048, BF16)
    PTs = ar.take(128, 2048, BF16)
    at_tm = ar.take(128, 1024, BF16)
    aend = ar.off
    ar.rewind(m0)
    x0s = ar.take(128, DM, F32)
    x1s = ar.take(128, DM, F32)
    sqp = ar.take(128, DM, F32)
    wop = ar.take(128, 4 * 512, BF16)
    bor = ar.take(1, DM, BF16)
    dens = ar.view(0, 128, 4 * DM, F32)
    ar.off = max(ar.off, gend, aend)

    PSO = [PSB[:, 1024:1536], PSB[:, 1536:2048]]
    TP = TRot(E, c.TVIEWS, ident)
    tok_bqv = E.dma("dg", "a_bias1", bqv, bqkv)

    groups = [
        (xo, [0, 1, 2, 3], 0, range(0, 6)),
        (xo, [4, 5, 6, 7], 4, range(0, 6)),
        (xr, [0, 1, 2, 3], 8, range(2, 6)),
        (xr, [4, 5, 6, 7], 12, range(2, 6)),
    ]
    tok_xst_free = None
    tok_lntm_free = None
    piece_free = [None] * 4
    ttm_free = [None, None]
    psa_free = [None] * 4
    jglob = 0
    for (src, tiles, t16b, ntiles) in groups:
        lnfm_ready = [None] * 4
        for i, tile in enumerate(tiles):
            j = jglob
            jglob += 1
            jj = j % 16
            td = E.dma("dy", "a_x", x_st, src[tile * 128:(tile + 1) * 128, :],
                       waits=(tok_xst_free,))
            t1, t2 = ln_stats(E, x_st, s1a[:, jj:jj + 1], s2a[:, jj:jj + 1],
                              sqs, waits=(td,))
            tr, tn = ln_chain(E, s1a[:, jj:jj + 1], s2a[:, jj:jj + 1],
                              ra[:, jj:jj + 1], nmra[:, jj:jj + 1],
                              tmpa[:, jj:jj + 1], DM, t1, t2)
            tok_apply = E.op("v", "tensor_scalar", ln_tm, x_st,
                             ra[:, jj:jj + 1], nmra[:, jj:jj + 1],
                             ALU.mult, ALU.add,
                             waits=(tr, tn, tok_lntm_free))
            tok_xst_free = tok_apply
            last_tt = None
            for ch in range(NKC):
                tt, te = TP.go(ln_tm[:, ch * 128:(ch + 1) * 128],
                               lnfm[:, i * DM + ch * 128:i * DM + (ch + 1) * 128],
                               waits=(tok_apply,))
                last_tt = tt
            tok_lntm_free = last_tt
            lnfm_ready[i] = (E.tok("v"), E.tok("s"))

        for n in ntiles:
            c0 = n * 512
            for k in range(NKC):
                buf = k % 4
                tw = E.dma("dg", f"a_w{buf}", wpc[:, buf * 512:(buf + 1) * 512],
                           wqkv[k * 128:(k + 1) * 128, c0:c0 + 512],
                           waits=(piece_free[buf],))
                for i in range(4):
                    w = [tw]
                    if k == 0:
                        w += list(lnfm_ready[i]) + [psa_free[i]]
                    lhs = lnfm[:, i * DM + k * 128:i * DM + (k + 1) * 128]
                    rhs = wpc[:, buf * 512:(buf + 1) * 512]
                    psl = PSA[:, i * 512:(i + 1) * 512]
                    if i == 3:
                        piece_free[buf] = E.op("t", "matmul", psl, lhs, rhs,
                                               start=(k == 0), stop=False,
                                               waits=w)
                    else:
                        E.op_("t", "matmul", psl, lhs, rhs,
                              start=(k == 0), stop=False, waits=w)
            for i in range(4):
                tok_done = E.op("t", "matmul", PSA[:, i * 512:(i + 1) * 512],
                                ones[0:1, :], bqv[0:1, c0:c0 + 512],
                                start=False, stop=True, waits=(tok_bqv,))
                t16 = t16b + i
                tloc = tiles[i]
                if n < 2:
                    ev = E.op("s", "copy", ttm[:, (i % 2) * 512:(i % 2 + 1) * 512],
                              PSA[:, i * 512:(i + 1) * 512],
                              waits=(tok_done, ttm_free[i % 2]))
                    psa_free[i] = ev
                    last = None
                    for s4 in range(4):
                        chunk = n * 4 + s4
                        tt, te = TP.go(
                            ttm[:, (i % 2) * 512 + s4 * 128:(i % 2) * 512 + (s4 + 1) * 128],
                            Q_fm[:, chunk * 1024 + tloc * 128:chunk * 1024 + (tloc + 1) * 128],
                            waits=(ev,))
                        last = tt
                    ttm_free[i % 2] = last
                elif n < 4:
                    ev = E.op("s", "copy", ttm[:, (i % 2) * 512:(i % 2 + 1) * 512],
                              PSA[:, i * 512:(i + 1) * 512],
                              waits=(tok_done, ttm_free[i % 2]))
                    psa_free[i] = ev
                    last = None
                    for s4 in range(4):
                        chunk = (n - 2) * 4 + s4
                        tt, te = TP.go(
                            ttm[:, (i % 2) * 512 + s4 * 128:(i % 2) * 512 + (s4 + 1) * 128],
                            K_fm[:, chunk * 2048 + t16 * 128:chunk * 2048 + (t16 + 1) * 128],
                            waits=(ev,))
                        last = tt
                    ttm_free[i % 2] = last
                else:
                    ev = E.op("v", "tensor_copy",
                              V_tm[:, t16 * 1024 + (n - 4) * 512:t16 * 1024 + (n - 4) * 512 + 512],
                              PSA[:, i * 512:(i + 1) * 512], waits=(tok_done,))
                    psa_free[i] = ev

    E.barrier()
    tok_psa_attn = None
    tok_pso_free = [None, None]
    for m in range(8):
        tok_scores = None
        for ch in range(8):
            for kk in range(4):
                w = []
                if ch == 0:
                    w = [tok_psa_attn, psa_free[kk]]
                lhs = Q_fm[:, ch * 1024 + m * 128:ch * 1024 + (m + 1) * 128]
                rhs = K_fm[:, ch * 2048 + kk * 512:ch * 2048 + (kk + 1) * 512]
                psl = PSA[:, kk * 512:(kk + 1) * 512]
                if ch == 7 and kk == 3:
                    tok_scores = E.op("t", "matmul", psl, lhs, rhs,
                                      start=(ch == 0), stop=(ch == 7), waits=w)
                else:
                    E.op_("t", "matmul", psl, lhs, rhs,
                          start=(ch == 0), stop=(ch == 7), waits=w)
        psa_free = [None] * 4
        tmax = E.op("v", "reduce_max", nm[:, :], PSA[:, :], axis=AX,
                    negate=True, waits=(tok_scores,))
        tnms = E.op("v", "tensor_scalar_mul", nms[:, :], nm[:, :], 1.0 / 32.0,
                    waits=(tmax,))
        tok_exp = E.op("s", "activation", P, PSA[:, :], AF.Exp,
                       bias=nms[:, 0:1], scale=1.0 / 32.0,
                       accum_out=rsum[:, 0:1], waits=(tnms,))
        tok_psa_attn = tok_exp
        trr = E.op("v", "reciprocal", rrec[:, :], rsum[:, :], waits=(tok_exp,))
        tok_pv = None
        for ch in range(16):
            tt, te = TP.go(P[:, ch * 128:(ch + 1) * 128],
                           PTs[:, ch * 128:(ch + 1) * 128],
                           waits=(tok_exp,))
            for n2 in range(2):
                w = [te]
                if ch == 0:
                    w += [tok_pso_free[0], tok_pso_free[1]]
                psl = PSO[n2]
                lhs = PTs[:, ch * 128:(ch + 1) * 128]
                rhs = V_tm[:, ch * 1024 + n2 * 512:ch * 1024 + (n2 + 1) * 512]
                if ch == 15 and n2 == 1:
                    tok_pv = E.op("t", "matmul", psl, lhs, rhs,
                                  start=(ch == 0), stop=(ch == 15), waits=w)
                else:
                    E.op_("t", "matmul", psl, lhs, rhs,
                          start=(ch == 0), stop=(ch == 15), waits=w)
        tok_at = E.op("s", "activation", at_tm, PSB[:, 1024:2048], AF.Copy,
                      scale=rrec[:, 0:1], waits=(tok_pv, trr))
        for ch in range(8):
            TP.go(at_tm[:, ch * 128:(ch + 1) * 128],
                  at_fm_all[:, m * 1024 + ch * 128:m * 1024 + (ch + 1) * 128],
                  waits=(tok_at,))
        tok_pso_free = [tok_at, tok_at]

    guard = (E.tok("t"), E.tok("v"), E.tok("s"))
    tok_bor = E.dma("dg", "a_bias2", bor, bo, waits=guard)
    at_ready = guard
    wop_free = [None] * 4
    psa_free2 = [None] * 4
    tok_x0_free = None
    tok_x1_free = None
    for g2 in range(2):
        tiles2 = [g2 * 4 + i for i in range(4)]
        dens_done = [[None] * 8 for _ in range(4)]
        for n in range(8):
            c0 = n * 512
            for k in range(8):
                buf = k % 4
                tw = E.dma("dg", f"a_wo{buf}", wop[:, buf * 512:(buf + 1) * 512],
                           wo[k * 128:(k + 1) * 128, c0:c0 + 512],
                           waits=(wop_free[buf], guard[0]))
                for i in range(4):
                    m = tiles2[i]
                    w = [tw]
                    if k == 0:
                        w += list(at_ready) + [psa_free2[i], tok_psa_attn]
                    lhs = at_fm_all[:, m * 1024 + k * 128:m * 1024 + (k + 1) * 128]
                    rhs = wop[:, buf * 512:(buf + 1) * 512]
                    psl = PSA[:, i * 512:(i + 1) * 512]
                    if i == 3:
                        wop_free[buf] = E.op("t", "matmul", psl, lhs, rhs,
                                             start=(k == 0), stop=False,
                                             waits=w)
                    else:
                        E.op_("t", "matmul", psl, lhs, rhs,
                              start=(k == 0), stop=False, waits=w)
            for i in range(4):
                tok_done = E.op("t", "matmul", PSA[:, i * 512:(i + 1) * 512],
                                ones[0:1, :], bor[0:1, c0:c0 + 512],
                                start=False, stop=True, waits=(tok_bor,))
                eng = "v" if i % 2 == 0 else "s"
                meth = "tensor_copy" if eng == "v" else "copy"
                tev = E.op(eng, meth, dens[:, i * DM + c0:i * DM + c0 + 512],
                           PSA[:, i * 512:(i + 1) * 512], waits=(tok_done,))
                psa_free2[i] = tev
                dens_done[i][n] = tev
        tok_out_last = None
        for i in range(4):
            m = tiles2[i]
            dview = dens[:, i * DM:(i + 1) * DM]
            t1, t2 = ln_stats(E, dview, p_s1[:, m:m + 1], p_s2[:, m:m + 1],
                              sqp, waits=tuple(dens_done[i]))
            tr, tn = ln_chain(E, p_s1[:, m:m + 1], p_s2[:, m:m + 1],
                              p_r[:, m:m + 1], p_nmr[:, m:m + 1],
                              p_tmp[:, m:m + 1], DM, t1, t2)
            tx = E.dma("dy", "a_x0", x0s, xo[m * 128:(m + 1) * 128, :],
                       waits=(tok_x0_free,) + guard)
            tap = E.op("v", "tensor_scalar", x1s, dview, p_r[:, m:m + 1],
                       p_nmr[:, m:m + 1], ALU.mult, ALU.add,
                       waits=(tr, tn, tok_x1_free))
            tadd = E.op("v", "tensor_tensor", x1s, x1s, x0s, ALU.add,
                        waits=(tap, tx))
            tok_x0_free = tadd
            tok_x1_free = E.dma("dy", "a_out", x1_out[m * 128:(m + 1) * 128, :],
                                x1s, waits=(tadd,))
        E.wait("v", tok_x1_free)
        E.wait("s", tok_x1_free)
    E.barrier()


def emit_l0_ffn(c, x1i, w0, b0, cent, assign, cc_in, cc_out, x2_out):
    nc, E, ar = c.nc, c.E, c.ar
    PSA, PSB = c.PSA, c.PSB
    ones = c.ones
    s1a, s2a, ra, nmra, tmpa = c.s1a, c.s2a, c.ra, c.nmra, c.tmpa
    p_s1, p_s2, p_r, p_nmr, p_tmp = c.p_s1, c.p_s2, c.p_r, c.p_nmr, c.p_tmp
    NCLUST, THR, INV_TEMP = 64, 0.01, 10.0
    nmx, nmx10, dsum, drec = c.nm, c.nms, c.rsum, c.rrec

    ar.reset()
    ident32 = ar.take(128, 128, F32)
    ones64 = ar.take(64, 128, F32)
    iota64 = ar.take(64, 1, F32)
    cent_sb = ar.take(128, NKC * NCLUST, F32)
    lnbf = ar.take(128, 8 * DM, BF16)
    qmask = ar.take(128, 8, F32)
    qmx = ar.take(128, 1, F32)
    cmax = ar.take(128, NCLUST, F32)
    cmT = ar.take(64, 128, F32)
    cm_col = ar.take(64, 1, F32)
    cs_col = ar.take(64, 1, F32)
    cs_gt = ar.take(64, 1, F32)
    dots = ar.take(128, NCLUST, F32)
    x_st = ar.take(128, DM, F32)
    lnf32 = ar.take(128, DM, F32)
    lnfm32 = ar.take(128, DM, F32)
    dens = ar.take(128, 4 * DM, F32)
    b0pc = ar.take(1, 1024, BF16)
    sqs = dens[:, 0:DM]
    sqs3 = lnf32
    asg = lnf32[0:64, :].bitcast(I32)
    asgf = lnfm32[0:64, :]
    ohcs = x_st[0:64, :]
    rmask = lnfm32
    wpc = cent_sb[:, 0:1024].bitcast(BF16)[:, 0:2048]

    T32 = [PSB[:, 0:128], PSB[:, 512:640]]
    PSD = PSB[:, 1024:1024 + NCLUST]
    TP32 = TRot(E, T32, ident32)

    tms = E.op("g", "memset", ident32[0:128, :], 0.0)
    E.op("g", "affine_select", out=ident32[0:128, :], in_=ident32[0:128, :],
         compare_op=ALU.not_equal, fill=1.0, base=0,
         pattern=[[-1, 128]], channel_multiplier=1, waits=(tms,))
    tmo = E.op("g", "memset", ones64[0:64, :], 1.0)
    tio = E.op("g", "iota", iota64[0:64, 0:1], pattern=[[0, 1]], base=0,
               channel_multiplier=1, allow_small_or_imprecise_dtypes=True)
    tok_cmz = E.op("v", "memset", cmax[:, :], 0.0)
    tok_cent = E.dma("dy", "b_cent",
                     cent_sb[:, :].rearrange("p (c n) -> p c n", n=NCLUST),
                     cent.rearrange("(c p) n -> p c n", p=128))

    tok_xst_free = None
    tok_lnf_free = None
    tok_lnfm_free = None
    tok_psd_free = None
    tok_cmax = tok_cmz
    tok_ident_done = E.tok("g")
    for m in range(8):
        td = E.dma("dy", "b_x", x_st, x1i[m * 128:(m + 1) * 128, :],
                   waits=(tok_xst_free,))
        t1, t2 = ln_stats(E, x_st, s1a[:, m:m + 1], s2a[:, m:m + 1], sqs,
                          waits=(td,))
        tr, tn = ln_chain(E, s1a[:, m:m + 1], s2a[:, m:m + 1],
                          ra[:, m:m + 1], nmra[:, m:m + 1],
                          tmpa[:, m:m + 1], DM, t1, t2)
        tok_ap = E.op("v", "tensor_scalar", lnf32, x_st, ra[:, m:m + 1],
                      nmra[:, m:m + 1], ALU.mult, ALU.add,
                      waits=(tr, tn, tok_lnf_free))
        tok_xst_free = tok_ap
        last_tt = None
        for ch in range(NKC):
            tt, te = TP32.go(lnf32[:, ch * 128:(ch + 1) * 128],
                             lnfm32[:, ch * 128:(ch + 1) * 128],
                             waits=(tok_ap, tok_lnfm_free, tok_ident_done))
            last_tt = tt
        tok_lnf_free = last_tt
        fm_ready = (E.tok("v"), E.tok("s"))
        tok_bf = E.op("v", "tensor_copy", lnbf[:, m * DM:(m + 1) * DM],
                      lnfm32, waits=fm_ready)
        tok_dots = None
        for k in range(NKC):
            w = list(fm_ready) + [tok_cent]
            if k == 0:
                w.append(tok_psd_free)
            tok_dots = E.op("t", "matmul", PSD,
                            lnfm32[:, k * 128:(k + 1) * 128],
                            cent_sb[:, k * NCLUST:(k + 1) * NCLUST],
                            start=(k == 0), stop=(k == NKC - 1), waits=w)
        tok_lnfm_free = tok_dots
        tmx = E.op("v", "reduce_max", nmx[:, :], PSD, axis=AX,
                   negate=True, waits=(tok_dots,))
        tx10 = E.op("v", "tensor_scalar_mul", nmx10[:, :], nmx[:, :],
                    INV_TEMP, waits=(tmx,))
        tex = E.op("s", "activation", dots[:, :], PSD, AF.Exp,
                   bias=nmx10[:, 0:1], scale=INV_TEMP,
                   accum_out=dsum[:, 0:1], waits=(tx10,))
        tok_psd_free = tex
        trc = E.op("v", "reciprocal", drec[:, :], dsum[:, :], waits=(tex,))
        tdn = E.op("v", "tensor_scalar", dots[:, :], dots[:, :],
                   drec[:, 0:1], None, ALU.mult, waits=(trc, tex))
        tqm = E.op("v", "reduce_max", qmx[:, 0:1], dots[:, :], axis=AX,
                   waits=(tdn,))
        E.op("v", "tensor_single_scalar", qmask[:, m:m + 1], qmx[:, 0:1],
             THR, ALU.is_gt, waits=(tqm,))
        tok_cmax = E.op("v", "tensor_tensor", cmax[:, :], cmax[:, :],
                        dots[:, :], ALU.max, waits=(tok_cmax, tdn))

    tt, te = TP32.go(cmax[:, 0:64], cmT[0:64, 0:128], waits=(tok_cmax,))
    tcm = E.op("v", "reduce_max", cm_col[0:64, 0:1], cmT[0:64, 0:128],
               axis=AX, waits=(te,))
    tdm = E.dma("dg", "b_ccin", cc_in.ap(), cm_col[0:64, 0:1],
                waits=(tcm,), allow_slow_non_contiguous=True)
    E.wait("g", tdm)
    cc_inst = nc.gpsimd.collective_compute(
        "AllReduce", ALU.max,
        replica_groups=[list(range(8))],
        ins=[cc_in.ap().opt()], outs=[cc_out.ap().opt()],
    )
    tok_cc = E.inc(cc_inst, "g", 1)
    tcs = E.dma("dg", "b_ccout", cs_col[0:64, 0:1], cc_out.ap(),
                waits=(tok_cc,), allow_slow_non_contiguous=True)
    tgt = E.op("v", "tensor_single_scalar", cs_gt[0:64, 0:1],
               cs_col[0:64, 0:1], THR, ALU.is_gt, waits=(tcs,))
    tas = E.dma("dg", "b_asg", asg, assign.partition_broadcast(64),
                waits=(E.tok("t"), E.tok("v"), E.tok("s")))
    taf = E.op("g", "tensor_copy", asgf, asg, waits=(tas,))
    toh = E.op("v", "tensor_scalar", ohcs, asgf, iota64[0:64, 0:1],
               cs_gt[0:64, 0:1], ALU.is_equal, ALU.mult, waits=(taf, tgt))
    rmask_done = []
    for n in range(8):
        sl = n % 4
        w = [toh, tok_psd_free]
        tmm = E.op("t", "matmul", PSA[:, sl * 512:(sl + 1) * 512],
                   ones64[0:64, :], ohcs[0:64, n * 512:(n + 1) * 512],
                   start=True, stop=True,
                   waits=w + ([rmask_done[n - 4]] if n >= 4 else []))
        tev = E.op("v", "tensor_copy", rmask[:, n * 512:(n + 1) * 512],
                   PSA[:, sl * 512:(sl + 1) * 512], waits=(tmm,))
        rmask_done.append(tev)
    tok_rmask = rmask_done[-1]

    E.wait("v", E.tok("s"))
    piece_free = [None] * 4
    b0_free = [None, None]
    psa_free = [None] * 4
    tok_x3_free = tok_rmask
    for grp in range(2):
        tiles = [grp * 4 + i for i in range(4)]
        dens_done = [[None] * 8 for _ in range(4)]
        for n in range(8):
            c0 = n * 512
            for k in range(NKC):
                buf = k % 4
                tw = E.dma("dg", f"b_w{buf}", wpc[:, buf * 512:(buf + 1) * 512],
                           w0[k * 128:(k + 1) * 128, c0:c0 + 512],
                           waits=(piece_free[buf],))
                for i in range(4):
                    m = tiles[i]
                    w = [tw]
                    if k == 0:
                        w += [psa_free[i], tok_rmask]
                    lhs = lnbf[:, m * DM + k * 128:m * DM + (k + 1) * 128]
                    rhs = wpc[:, buf * 512:(buf + 1) * 512]
                    psl = PSA[:, i * 512:(i + 1) * 512]
                    if i == 3:
                        piece_free[buf] = E.op("t", "matmul", psl, lhs, rhs,
                                               start=(k == 0), stop=False,
                                               waits=w)
                    else:
                        E.op_("t", "matmul", psl, lhs, rhs,
                              start=(k == 0), stop=False, waits=w)
            bb = n % 2
            tb0 = E.dma("dg", f"b_b0{bb}", b0pc[0:1, bb * 512:bb * 512 + 512],
                        b0[0:1, c0:c0 + 512], waits=(b0_free[bb],))
            for i in range(4):
                m = tiles[i]
                tok_done = E.op("t", "matmul", PSA[:, i * 512:(i + 1) * 512],
                                c.ones_bf[0:1, :], b0pc[0:1, bb * 512:bb * 512 + 512],
                                start=False, stop=True, waits=(tb0,))
                if i == 3:
                    b0_free[bb] = tok_done
                tev = E.op("v", "scalar_tensor_tensor",
                           dens[:, i * DM + c0:i * DM + c0 + 512],
                           PSA[:, i * 512:(i + 1) * 512],
                           qmask[:, m:m + 1], rmask[:, c0:c0 + 512],
                           ALU.mult, ALU.mult, waits=(tok_done,))
                psa_free[i] = tev
                dens_done[i][n] = tev
        tok_out_last = None
        for i in range(4):
            m = tiles[i]
            dview = dens[:, i * DM:(i + 1) * DM]
            t1, t2 = ln_stats(E, dview, p_s1[:, m:m + 1], p_s2[:, m:m + 1],
                              sqs3, waits=tuple(dens_done[i]))
            tr, tn = ln_chain(E, p_s1[:, m:m + 1], p_s2[:, m:m + 1],
                              p_r[:, m:m + 1], p_nmr[:, m:m + 1],
                              p_tmp[:, m:m + 1], DM, t1, t2)
            tx = E.dma("dy", "b_x3", x_st, x1i[m * 128:(m + 1) * 128, :],
                       waits=(tok_x3_free,))
            tap = E.op("v", "tensor_scalar", dview, dview, p_r[:, m:m + 1],
                       p_nmr[:, m:m + 1], ALU.mult, ALU.add, waits=(tr, tn))
            tadd = E.op("v", "tensor_tensor", dview, dview, x_st, ALU.add,
                        waits=(tap, tx))
            tok_x3_free = tadd
            tok_out_last = E.dma("dy", "b_out",
                                 x2_out[m * 128:(m + 1) * 128, :],
                                 dview, waits=(tadd,))
        E.wait("v", tok_out_last)
        E.wait("s", tok_out_last)
    E.barrier()


def emit_l1_attn(c, x2i, cw, cb, wq, wk, wv, bqc, bkc, bvr, wo2, bo2, pw, pb,
                 halo_sel, halo_mask, hg_in, hg_out, x3_out):
    nc, E, ar = c.nc, c.E, c.ar
    PSA, PSB = c.PSA, c.PSB
    ident, ones = c.ident, c.ones_bf
    s1a, s2a, ra, nmra, tmpa = c.s1a, c.s2a, c.ra, c.nmra, c.tmpa
    p_s1, p_s2, p_r, p_nmr, p_tmp = c.p_s1, c.p_s2, c.p_r, c.p_nmr, c.p_tmp
    nm, rsum, rrec = c.nm, c.rsum, c.rrec
    CW = 1032
    NH = 16
    SCALE = 0.125
    NEG_INF = -1e9

    ar.reset()
    cmask = ar.take(128, 128, F32)
    qkv_tm = ar.take(128, 8 * 3072, BF16)
    ao_tm = ar.take(128, 8 * 1024, BF16)
    bqc_sb = ar.take(128, 8, F32)
    bkc_sb = ar.take(128, 8, F32)
    hmask_sb = ar.take(2, 1, F32)
    hsel_sb = ar.take(16, 2, F32)
    mseg = ar.mark()
    x_st = ar.take(128, DM, F32)
    sqs = ar.take(128, DM, F32)
    ln_tm = ar.take(128, DM, BF16)
    ln3fm = ar.take(128, NKC * CW, BF16)
    cpc = ar.take(128, 4 * 512, BF16)
    cbr = ar.take(1, 3072, BF16)
    hg_sb = x_st[0:16, :]
    lnh = ln_tm[0:2, :]
    x2h = qkv_tm[0:2, 0:8192].bitcast(F32)
    segA_end = ar.off
    ar.rewind(mseg)
    wq_sb = ar.take(128, 8 * 1024, BF16)
    wk_sb = ar.take(128, 8 * 1024, BF16)
    wv_sb = ar.take(128, 8 * 1024, BF16)
    wo2_sb = ar.take(128, 8 * 1024, BF16)
    q_fm = ar.take(128, 1024, BF16)
    k_fm = ar.take(128, 1024, BF16)
    v_fm = ar.take(128, 1024, BF16)
    qh_fm = ar.take(128, 1024, BF16)
    kh_fm = ar.take(128, 1024, BF16)
    vh_tm = ar.take(128, 1024, BF16)
    s_sb = ar.take(128, 128, F32)
    P = ar.take(128, 128, BF16)
    PT_sb = ar.take(128, 128, BF16)
    o_tm = ar.take(128, 1024, BF16)
    o_fm = ar.take(128, 1024, BF16)
    bvr_sb = ar.take(1, 1024, BF16)
    bo2_sb = ar.take(1, 1024, BF16)
    segB_end = ar.off
    ar.rewind(mseg)
    at_fm = ar.take(128, 4 * 1024, BF16)
    dens = ar.take(128, 4 * DM, F32)
    x_st4 = ar.take(128, DM, F32)
    sqs4 = ar.take(128, DM, F32)
    ppc = ar.take(128, 4 * 512, BF16)
    pbr = ar.take(1, DM, BF16)
    ar.off = max(ar.off, segA_end, segB_end)

    TRB = TRot(E, c.TVIEWS, ident)

    tcm = E.op("g", "memset", cmask[:, :], 0.0)
    E.op("g", "affine_select", out=cmask[:, :], in_=cmask[:, :],
         compare_op=ALU.is_ge, fill=NEG_INF, base=0,
         pattern=[[-1, 128]], channel_multiplier=1, waits=(tcm,))
    tok_cb = E.dma("dg", "c_cbias", cbr, cb)
    tok_hm = E.dma("dy", "c_hmask", hmask_sb[0:2, 0:1], halo_mask,
                   allow_slow_non_contiguous=True)
    tok_hs = E.dma("dy", "c_hsel", hsel_sb[0:16, 0:2], halo_sel,
                   allow_slow_non_contiguous=True)

    tok_xst_free = None
    tok_lntm_free = None
    for m in range(8):
        r0 = m * 128
        td = E.dma("dy", "c_x", x_st, x2i[r0:r0 + 128, :],
                   waits=(tok_xst_free,))
        t1, t2 = ln_stats(E, x_st, s1a[:, m:m + 1], s2a[:, m:m + 1], sqs,
                          waits=(td,))
        tr, tn = ln_chain(E, s1a[:, m:m + 1], s2a[:, m:m + 1],
                          ra[:, m:m + 1], nmra[:, m:m + 1],
                          tmpa[:, m:m + 1], DM, t1, t2)
        tok_ap = E.op("v", "tensor_scalar", ln_tm, x_st, ra[:, m:m + 1],
                      nmra[:, m:m + 1], ALU.mult, ALU.add,
                      waits=(tr, tn, tok_lntm_free))
        tok_xst_free = tok_ap
        last_tt = None
        for ch in range(NKC):
            tt, te = TRB.go(ln_tm[:, ch * 128:(ch + 1) * 128],
                            ln3fm[:, ch * CW + r0 + 1:ch * CW + r0 + 129],
                            waits=(tok_ap,))
            last_tt = tt
        tok_lntm_free = last_tt

    th1 = E.dma("dg", "c_hgin", hg_in.ap()[0:1, :], x2i[0:1, :])
    th2 = E.dma("dg", "c_hgin", hg_in.ap()[1:2, :], x2i[1023:1024, :])
    E.wait("g", th2)
    cc_inst = nc.gpsimd.collective_compute(
        "AllGather", ALU.bypass,
        replica_groups=[list(range(8))],
        ins=[hg_in.ap().opt()], outs=[hg_out.ap().opt()],
    )
    tok_ag = E.inc(cc_inst, "g", 1)
    thg = E.dma("dg", "c_hg", hg_sb[0:16, :], hg_out.ap(),
                waits=(tok_ag, tok_xst_free))
    ev_h = []
    for n in range(8):
        sl = n % 4
        w = [thg, tok_hs]
        tmm = E.op("t", "matmul", PSA[0:2, sl * 512:sl * 512 + 512],
                   hsel_sb[0:16, 0:2], hg_sb[0:16, n * 512:(n + 1) * 512],
                   start=True, stop=True,
                   waits=w + ([ev_h[n - 4]] if n >= 4 else []))
        tev = E.op("v", "tensor_copy", x2h[0:2, n * 512:(n + 1) * 512],
                   PSA[0:2, sl * 512:sl * 512 + 512], waits=(tmm,))
        ev_h.append(tev)
    t1, t2 = ln_stats(E, x2h[0:2, :], s1a[0:2, 8:9], s2a[0:2, 8:9],
                      sqs[0:2, :], waits=tuple(ev_h))
    tr, tn = ln_chain(E, s1a[0:2, 8:9], s2a[0:2, 8:9], ra[0:2, 8:9],
                      nmra[0:2, 8:9], tmpa[0:2, 8:9], DM, t1, t2)
    tok_lnh = E.op("v", "tensor_scalar", lnh[0:2, :], x2h[0:2, :],
                   ra[0:2, 8:9], nmra[0:2, 8:9], ALU.mult, ALU.add,
                   waits=(tr, tn, tok_lntm_free))
    tok_lnhm = E.op("v", "tensor_scalar", lnh[0:2, :], lnh[0:2, :],
                    hmask_sb[0:2, 0:1], None, ALU.mult,
                    waits=(tok_lnh, tok_hm))
    for ch in range(NKC):
        i2 = ch % 2
        po = c.TVIEWS[i2][0:128, 0:2]
        fr = TRB.free[i2]
        fr = () if fr is None else ((fr,) if isinstance(fr[0], str) else fr)
        tt = E.op("t", "transpose", po, lnh[0:2, ch * 128:(ch + 1) * 128],
                  ident[0:2, 0:2], waits=(tok_lnhm,) + fr)
        e1 = E.op("v", "tensor_copy", ln3fm[:, ch * CW:ch * CW + 1],
                  po[:, 0:1], waits=(tt,))
        e2 = E.op("s", "copy", ln3fm[:, ch * CW + 1025:ch * CW + 1026],
                  po[:, 1:2], waits=(tt,))
        TRB.free[i2] = (e1, e2)
    ph1_done = (E.tok("v"), E.tok("s"))
    def conv_slice(m):
        if m < 4:
            return PSA[:, m * 512:(m + 1) * 512]
        return PSB[:, (m - 4) * 512:(m - 3) * 512]

    piece_free = [None] * 4
    slice_free = [None] * 8
    for n in range(6):
        c0 = n * 512
        step = 0
        for k in range(NKC):
            for tap in range(3):
                buf = step % 4
                step += 1
                tw = E.dma("dg", f"c_cw{buf}", cpc[:, buf * 512:(buf + 1) * 512],
                           cw[tap][k * 128:(k + 1) * 128, c0:c0 + 512],
                           waits=(piece_free[buf],))
                for m in range(8):
                    w = [tw]
                    if step == 1:
                        w += list(ph1_done) + [slice_free[m]]
                    lhs = ln3fm[:, k * CW + m * 128 + tap:
                                k * CW + m * 128 + tap + 128]
                    rhs = cpc[:, buf * 512:(buf + 1) * 512]
                    tok = E.op("t", "matmul", conv_slice(m), lhs, rhs,
                               start=(step == 1), stop=False, waits=w)
                    if m == 7:
                        piece_free[buf] = tok
        for m in range(8):
            tok_done = E.op("t", "matmul", conv_slice(m), ones[0:1, :],
                            cbr[0:1, c0:c0 + 512], start=False, stop=True,
                            waits=(tok_cb,))
            eng = "v" if m % 2 == 0 else "s"
            meth = "tensor_copy" if eng == "v" else "copy"
            slice_free[m] = E.op(eng, meth,
                                 qkv_tm[:, m * 3072 + c0:m * 3072 + c0 + 512],
                                 conv_slice(m), waits=(tok_done,))
    E.barrier()

    tok_wq = E.dma("dg", "c_wq", wq_sb[:, :].rearrange("p (c n) -> p c n", n=1024),
                   wq.rearrange("(c p) n -> p c n", p=128))
    tok_wk = E.dma("dg", "c_wk", wk_sb[:, :].rearrange("p (c n) -> p c n", n=1024),
                   wk.rearrange("(c p) n -> p c n", p=128))
    tok_wv = E.dma("dg", "c_wv", wv_sb[:, :].rearrange("p (c n) -> p c n", n=1024),
                   wv.rearrange("(c p) n -> p c n", p=128))
    tok_wo2 = E.dma("dg", "c_wo2", wo2_sb[:, :].rearrange("p (c n) -> p c n", n=1024),
                    wo2.rearrange("(c p) n -> p c n", p=128))
    tok_bqc = E.dma("dy", "c_bqc", bqc_sb[:, :], bqc)
    tok_bkc = E.dma("dy", "c_bkc", bkc_sb[:, :], bkc)
    tok_bvr = E.dma("dg", "c_bvr", bvr_sb, bvr)
    tok_bo2 = E.dma("dg", "c_bo2", bo2_sb, bo2)

    PH = [PSA[:, 0:128], PSA[:, 512:640]]
    PV = [PSA[:, 1024:1536], PSA[:, 1536:2048]]
    PS = [PSB[:, 1024:1152], PSB[:, 1536:1664]]
    ph_free = [None, None]
    pv_free = [None, None]
    ps_free = [None, None]

    for m in range(8):
        base = m * 3072
        qf = []
        for c3 in range(8):
            tt, te = TRB.go(qkv_tm[:, base + c3 * 128:base + (c3 + 1) * 128],
                            q_fm[:, c3 * 128:(c3 + 1) * 128], waits=())
        for c3 in range(8):
            tt, te = TRB.go(qkv_tm[:, base + 1024 + c3 * 128:base + 1024 + (c3 + 1) * 128],
                            k_fm[:, c3 * 128:(c3 + 1) * 128], waits=())
        for c3 in range(8):
            tt, te = TRB.go(qkv_tm[:, base + 2048 + c3 * 128:base + 2048 + (c3 + 1) * 128],
                            v_fm[:, c3 * 128:(c3 + 1) * 128], waits=())
        fm_ready = (E.tok("v"), E.tok("s"))
        for (wsb, bias_sb, dst, tok_w, tok_b) in (
                (wq_sb, bqc_sb, qh_fm, tok_wq, tok_bqc),
                (wk_sb, bkc_sb, kh_fm, tok_wk, tok_bkc)):
            for hc in range(8):
                pview = PH[hc % 2]
                tok_mm = None
                for wc in range(8):
                    w = [tok_w] + list(fm_ready)
                    if wc == 0:
                        w.append(ph_free[hc % 2])
                    if wc == 7:
                        tok_mm = E.op("t", "matmul", pview[:, 0:128],
                                      wsb[:, wc * 1024 + hc * 128:wc * 1024 + (hc + 1) * 128],
                                      (q_fm if dst is qh_fm else k_fm)[:, wc * 128:(wc + 1) * 128],
                                      start=(wc == 0), stop=(wc == 7), waits=w)
                    else:
                        E.op_("t", "matmul", pview[:, 0:128],
                              wsb[:, wc * 1024 + hc * 128:wc * 1024 + (hc + 1) * 128],
                              (q_fm if dst is qh_fm else k_fm)[:, wc * 128:(wc + 1) * 128],
                              start=(wc == 0), stop=(wc == 7), waits=w)
                tok_ev = E.op("v", "tensor_scalar_add",
                              dst[:, hc * 128:(hc + 1) * 128],
                              pview[:, 0:128], bias_sb[:, hc:hc + 1],
                              waits=(tok_mm, tok_b))
                ph_free[hc % 2] = tok_ev
        for n2 in range(2):
            pview = PV[n2]
            tok_mm = None
            for wc in range(8):
                w = [tok_wv] + list(fm_ready)
                if wc == 0:
                    w.append(pv_free[n2])
                E.op_("t", "matmul", pview,
                      v_fm[:, wc * 128:(wc + 1) * 128],
                      wv_sb[:, wc * 1024 + n2 * 512:wc * 1024 + n2 * 512 + 512],
                      start=(wc == 0), stop=False, waits=w)
            tok_mm = E.op("t", "matmul", pview, ones[0:1, :],
                          bvr_sb[0:1, n2 * 512:n2 * 512 + 512],
                          start=False, stop=True, waits=(tok_bvr,))
            tok_ev = E.op("s", "copy", vh_tm[:, n2 * 512:n2 * 512 + 512],
                          pview, waits=(tok_mm,))
            pv_free[n2] = tok_ev
        vh_ready = pv_free[:]
        qh_ready = (E.tok("v"),)
        tok_o_ev = [None, None]
        tok_exp_prev = None
        tok_ptt_prev = None
        for h in range(NH):
            hc = h // 2
            pb_ = (h % 2) * 64
            psc = PS[h % 2]
            tok_sc = E.op("t", "matmul", psc[:, 0:128],
                          qh_fm[pb_:pb_ + 64, hc * 128:(hc + 1) * 128],
                          kh_fm[pb_:pb_ + 64, hc * 128:(hc + 1) * 128],
                          start=True, stop=True,
                          waits=list(qh_ready) + [ps_free[h % 2]])
            tok_stt = E.op("v", "scalar_tensor_tensor", s_sb[:, :],
                           psc[:, 0:128], SCALE, cmask[:, :],
                           ALU.mult, ALU.add, waits=(tok_sc, tok_exp_prev))
            ps_free[h % 2] = tok_stt
            tok_mx = E.op("v", "reduce_max", nm[:, :], s_sb[:, :], axis=AX,
                          negate=True, waits=(tok_stt,))
            tok_exp = E.op("s", "activation", P, s_sb[:, :], AF.Exp,
                           bias=nm[:, 0:1], scale=1.0,
                           accum_out=rsum[:, 0:1],
                           waits=(tok_mx, tok_ptt_prev))
            tok_exp_prev = tok_exp
            tok_rr = E.op("v", "reciprocal", rrec[:, :], rsum[:, :],
                          waits=(tok_exp,))
            tt, te = TRB.go(P, PT_sb, waits=(tok_exp,))
            tok_ptt_prev = tt
            tok_pv = E.op("t", "matmul", PH[h % 2][:, 0:64], PT_sb,
                          vh_tm[:, h * 64:(h + 1) * 64],
                          start=True, stop=True,
                          waits=[te, ph_free[h % 2]] + vh_ready)
            tok_oev = E.op("s", "activation", o_tm[:, h * 64:(h + 1) * 64],
                           PH[h % 2][:, 0:64], AF.Copy, scale=rrec[:, 0:1],
                           waits=(tok_pv, tok_rr))
            ph_free[h % 2] = tok_oev
            tok_o_ev[h % 2] = tok_oev
        for c3 in range(8):
            tt, te = TRB.go(o_tm[:, c3 * 128:(c3 + 1) * 128],
                            o_fm[:, c3 * 128:(c3 + 1) * 128],
                            waits=[tok_o_ev[0], tok_o_ev[1]])
        ofm_ready = (E.tok("v"), E.tok("s"))
        for n2 in range(2):
            pview = PV[n2]
            tok_mm = None
            for hc in range(8):
                w = [tok_wo2] + list(ofm_ready)
                if hc == 0:
                    w.append(pv_free[n2])
                E.op_("t", "matmul", pview,
                      o_fm[:, hc * 128:(hc + 1) * 128],
                      wo2_sb[:, hc * 1024 + n2 * 512:hc * 1024 + n2 * 512 + 512],
                      start=(hc == 0), stop=False, waits=w)
            tok_mm = E.op("t", "matmul", pview, ones[0:1, :],
                          bo2_sb[0:1, n2 * 512:n2 * 512 + 512],
                          start=False, stop=True, waits=(tok_bo2,))
            tok_ev = E.op("v", "tensor_copy",
                          ao_tm[:, m * 1024 + n2 * 512:m * 1024 + n2 * 512 + 512],
                          pview, waits=(tok_mm,))
            pv_free[n2] = tok_ev
    E.barrier()

    tok_pb = E.dma("dg", "c_pbias", pbr, pb)
    piece_free = [None] * 4
    psa_free = [None] * 4
    tok_x4_free = None
    for grp in range(2):
        tiles = [grp * 4 + i for i in range(4)]
        for i, m in enumerate(tiles):
            for c3 in range(8):
                TRB.go(ao_tm[:, m * 1024 + c3 * 128:m * 1024 + (c3 + 1) * 128],
                       at_fm[:, i * 1024 + c3 * 128:i * 1024 + (c3 + 1) * 128],
                       waits=())
        at_ready = (E.tok("v"), E.tok("s"))
        dens_done = [[None] * 8 for _ in range(4)]
        for n in range(8):
            c0 = n * 512
            for k in range(8):
                buf = k % 4
                tw = E.dma("dg", f"c_pw{buf}", ppc[:, buf * 512:(buf + 1) * 512],
                           pw[k * 128:(k + 1) * 128, c0:c0 + 512],
                           waits=(piece_free[buf],))
                for i in range(4):
                    w = [tw]
                    if k == 0:
                        w += list(at_ready) + [psa_free[i]]
                    if i == 3:
                        piece_free[buf] = E.op("t", "matmul",
                                               PSA[:, i * 512:(i + 1) * 512],
                                               at_fm[:, i * 1024 + k * 128:i * 1024 + (k + 1) * 128],
                                               ppc[:, buf * 512:(buf + 1) * 512],
                                               start=(k == 0), stop=False,
                                               waits=w)
                    else:
                        E.op_("t", "matmul", PSA[:, i * 512:(i + 1) * 512],
                              at_fm[:, i * 1024 + k * 128:i * 1024 + (k + 1) * 128],
                              ppc[:, buf * 512:(buf + 1) * 512],
                              start=(k == 0), stop=False, waits=w)
            for i in range(4):
                m = tiles[i]
                tok_done = E.op("t", "matmul", PSA[:, i * 512:(i + 1) * 512],
                                ones[0:1, :], pbr[0:1, c0:c0 + 512],
                                start=False, stop=True, waits=(tok_pb,))
                eng = "v" if i % 2 == 0 else "s"
                meth = "tensor_copy" if eng == "v" else "copy"
                tev = E.op(eng, meth, dens[:, i * DM + c0:i * DM + c0 + 512],
                           PSA[:, i * 512:(i + 1) * 512], waits=(tok_done,))
                psa_free[i] = tev
                dens_done[i][n] = tev
        tok_out_last = None
        for i in range(4):
            m = tiles[i]
            dview = dens[:, i * DM:(i + 1) * DM]
            t1, t2 = ln_stats(E, dview, p_s1[:, m:m + 1], p_s2[:, m:m + 1],
                              sqs4, waits=tuple(dens_done[i]))
            tr, tn = ln_chain(E, p_s1[:, m:m + 1], p_s2[:, m:m + 1],
                              p_r[:, m:m + 1], p_nmr[:, m:m + 1],
                              p_tmp[:, m:m + 1], DM, t1, t2)
            tx = E.dma("dy", "c_x4", x_st4, x2i[m * 128:(m + 1) * 128, :],
                       waits=(tok_x4_free,))
            tap = E.op("v", "tensor_scalar", dview, dview, p_r[:, m:m + 1],
                       p_nmr[:, m:m + 1], ALU.mult, ALU.add,
                       waits=(tr, tn))
            tadd = E.op("v", "tensor_tensor", dview, dview, x_st4, ALU.add,
                        waits=(tap, tx))
            tok_x4_free = tadd
            tok_out_last = E.dma("dy", "c_out", x3_out[m * 128:(m + 1) * 128, :],
                                 dview, waits=(tadd,))
        E.wait("v", tok_out_last)
        E.wait("s", tok_out_last)

    E.barrier()


def emit_l1_ffn(c, x3i, w1, w3, w2, a1, b1, a3, b3, a2, b2, out_ap):
    nc, E, ar = c.nc, c.E, c.ar
    PSA, PSB = c.PSA, c.PSB
    ident, ones = c.ident, c.ones
    s1a, s2a, ra, nmra, tmpa = c.s1a, c.s2a, c.ra, c.nmra, c.tmpa
    p_s1, p_s2, p_r, p_nmr, p_tmp = c.p_s1, c.p_s2, c.p_r, c.p_nmr, c.p_tmp
    HID = 11008
    NHC = 86
    HCS = [1024] * 10 + [768]
    SCALING = 2.0

    def spans(L):
        return [(0, 512), (512, L)] if L > 512 else [(0, L)]

    ar.reset()
    a1_sb = ar.take(128, 32 * 16, BF16)
    a3_sb = ar.take(128, 32 * 16, BF16)
    a2_sb = ar.take(128, NHC * 16, BF16)
    b2_sb = ar.take(16, DM, BF16)
    x_st = ar.take(128, DM, F32)
    ln_tm = ar.take(128, DM, BF16)
    lnfm = ar.take(128, 4 * DM, BF16)
    dens = ar.take(128, 4 * DM, F32)
    sqs = dens[:, 0:DM]
    sq_post = lnfm[:, 0:2 * DM].bitcast(F32)
    h1_sb = ar.take(128, 4 * 1024, BF16)
    sl_sb = ar.take(128, 4 * 1024, BF16)
    g_sb = ar.take(128, 4 * 1024, BF16)
    g_fm = ar.take(128, 4 * 1024, BF16)
    wpc = ar.take(128, 4 * 1024, BF16)
    w2pc = ar.take(128, 4 * 1024, BF16)
    b1s = ar.take(16, 1024, BF16)
    b3s = ar.take(16, 1024, BF16)
    xa_bf = ar.take(128, 16, BF16)
    xa1_fm = ar.take(16, 4 * 128, BF16)
    xa3_fm = ar.take(16, 4 * 128, BF16)
    ga2_sb = ar.take(128, 4 * 16, F32)
    ga2_bf = ar.take(128, 16, BF16)
    ga2_fm = ar.take(16, 4 * 128, BF16)

    TVIEWS = c.TVIEWS
    PH = [PSA[:, 0:1024], PSA[:, 1024:2048],
          PSB[:, 0:1024], PSB[:, 1024:2048]]
    PGA = PSA[:, 0:16]

    TR = TRot(E, TVIEWS, ident)

    E.dma("dg", "d_a1", a1_sb[:, :].rearrange("p (c n) -> p c n", n=16),
          a1.rearrange("(c p) n -> p c n", p=128))
    E.dma("dg", "d_a3", a3_sb[:, :].rearrange("p (c n) -> p c n", n=16),
          a3.rearrange("(c p) n -> p c n", p=128))
    E.dma("dg", "d_a2", a2_sb[:, :].rearrange("p (c n) -> p c n", n=16),
          a2.rearrange("(c p) n -> p c n", p=128))
    E.dma("dg", "d_b2", b2_sb[0:16, :], b2)
    tok_a1 = E.tok("D:d_a1")
    tok_a3 = E.tok("D:d_a3")
    tok_a2 = E.tok("D:d_a2")
    tok_b2 = E.tok("D:d_b2")
    E.barrier()

    tok_xst_free = None
    tok_lntm_free = None
    tok_out_last = None
    tok_b1_used = None
    tok_b3_used = None
    for grp in range(2):
        tiles = [grp * 4 + i for i in range(4)]
        for i, m in enumerate(tiles):
            td = E.dma("dy", "d_x", x_st, x3i[m * 128:(m + 1) * 128, :],
                       waits=(tok_xst_free,))
            t1, t2 = ln_stats(E, x_st, s1a[:, m:m + 1], s2a[:, m:m + 1], sqs,
                              waits=(td,))
            tr, tn = ln_chain(E, s1a[:, m:m + 1], s2a[:, m:m + 1],
                              ra[:, m:m + 1], nmra[:, m:m + 1],
                              tmpa[:, m:m + 1], DM, t1, t2)
            tok_ap = E.op("v", "tensor_scalar", ln_tm, x_st, ra[:, m:m + 1],
                          nmra[:, m:m + 1], ALU.mult, ALU.add,
                          waits=(tr, tn, tok_lntm_free))
            tok_xst_free = tok_ap
            last_tt = None
            for c in range(NKC):
                tt, te = TR.go(ln_tm[:, c * 128:(c + 1) * 128],
                               lnfm[:, i * DM + c * 128:i * DM + (c + 1) * 128],
                               waits=(tok_ap,))
                last_tt = tt
            tok_lntm_free = last_tt
        lnfm_ready = (E.tok("v"), E.tok("s"))

        tok_pga_free = None
        for (asb, xfm, tok_a) in ((a1_sb, xa1_fm, tok_a1),
                                  (a3_sb, xa3_fm, tok_a3)):
            for i in range(4):
                tok_mm = None
                for k in range(NKC):
                    w = list(lnfm_ready) + [tok_a]
                    if k == 0:
                        w.append(tok_pga_free)
                    tok_mm = E.op("t", "matmul", PGA,
                                  lnfm[:, i * DM + k * 128:i * DM + (k + 1) * 128],
                                  asb[:, k * 16:(k + 1) * 16],
                                  start=(k == 0), stop=(k == 31), waits=w)
                tok_ev = E.op("s", "activation", xa_bf[:, :], PGA, AF.Copy,
                              scale=SCALING, waits=(tok_mm,))
                tok_pga_free = tok_ev
                tt, te = TR.go(xa_bf[:, :], xfm[0:16, i * 128:(i + 1) * 128],
                               waits=(tok_ev,))
        xa_ready = (E.tok("v"), E.tok("s"))
        tok_ga2_z = E.op("v", "memset", ga2_sb[:, :], 0.0)
        tok_ga2 = [tok_ga2_z] * 4

        E.wait("v", E.tok("s"))
        dens_tok = [[None] * 4 for _ in range(4)]
        ph_free = [xa_ready] * 4
        piece_free = [None] * 4
        w2_free = [None] * 4
        hc0 = 0
        for ci, L in enumerate(HCS):
            khs = L // 128
            tb1 = E.dma("dg", "d_b1s", b1s[0:16, 0:L], b1[0:16, hc0:hc0 + L],
                        waits=(tok_b1_used,))
            h1_done = [None] * 4
            for k in range(NKC):
                buf = k % 4
                tw = E.dma("dg", f"d_wp{buf}", wpc[:, buf * 1024:buf * 1024 + L],
                           w1[k * 128:(k + 1) * 128, hc0:hc0 + L],
                           waits=(piece_free[buf],))
                for i in range(4):
                    w = [tw]
                    if k == 0:
                        w += list(lnfm_ready) + list(ph_free[i])
                    for si, (s0, s1_) in enumerate(spans(L)):
                        last = (i == 3 and si == len(spans(L)) - 1)
                        if last:
                            piece_free[buf] = E.op(
                                "t", "matmul", PH[i][:, s0:s1_],
                                lnfm[:, i * DM + k * 128:i * DM + (k + 1) * 128],
                                wpc[:, buf * 1024 + s0:buf * 1024 + s1_],
                                start=(k == 0), stop=False, waits=w)
                        else:
                            E.op_("t", "matmul", PH[i][:, s0:s1_],
                                  lnfm[:, i * DM + k * 128:i * DM + (k + 1) * 128],
                                  wpc[:, buf * 1024 + s0:buf * 1024 + s1_],
                                  start=(k == 0), stop=False, waits=w)
            for i in range(4):
                tok = None
                for (s0, s1_) in spans(L):
                    tok = E.op("t", "matmul", PH[i][:, s0:s1_],
                               xa1_fm[0:16, i * 128:(i + 1) * 128],
                               b1s[0:16, s0:s1_],
                               start=False, stop=True,
                               waits=[tb1] + list(xa_ready))
                if i == 3:
                    tok_b1_used = tok
                eng = "v" if i % 2 == 0 else "s"
                meth = "tensor_copy" if eng == "v" else "copy"
                h1_done[i] = E.op(eng, meth, h1_sb[:, i * 1024:i * 1024 + L],
                                  PH[i][0:128, 0:L], waits=(tok,))
                ph_free[i] = (h1_done[i],)
            sl_done = [None] * 4
            for i in range(4):
                sl_done[i] = E.op("s", "activation",
                                  sl_sb[:, i * 1024:i * 1024 + L],
                                  h1_sb[:, i * 1024:i * 1024 + L], AF.Silu,
                                  waits=(h1_done[i],))
            tb3 = E.dma("dg", "d_b3s", b3s[0:16, 0:L], b3[0:16, hc0:hc0 + L],
                        waits=(tok_b3_used,))
            g_done = [None] * 4
            for k in range(NKC):
                buf = k % 4
                tw = E.dma("dg", f"d_wp{buf}", wpc[:, buf * 1024:buf * 1024 + L],
                           w3[k * 128:(k + 1) * 128, hc0:hc0 + L],
                           waits=(piece_free[buf],))
                for i in range(4):
                    w = [tw]
                    if k == 0:
                        w += list(ph_free[i])
                    for si, (s0, s1_) in enumerate(spans(L)):
                        last = (i == 3 and si == len(spans(L)) - 1)
                        if last:
                            piece_free[buf] = E.op(
                                "t", "matmul", PH[i][:, s0:s1_],
                                lnfm[:, i * DM + k * 128:i * DM + (k + 1) * 128],
                                wpc[:, buf * 1024 + s0:buf * 1024 + s1_],
                                start=(k == 0), stop=False, waits=w)
                        else:
                            E.op_("t", "matmul", PH[i][:, s0:s1_],
                                  lnfm[:, i * DM + k * 128:i * DM + (k + 1) * 128],
                                  wpc[:, buf * 1024 + s0:buf * 1024 + s1_],
                                  start=(k == 0), stop=False, waits=w)
            for i in range(4):
                tok = None
                for (s0, s1_) in spans(L):
                    tok = E.op("t", "matmul", PH[i][:, s0:s1_],
                               xa3_fm[0:16, i * 128:(i + 1) * 128],
                               b3s[0:16, s0:s1_],
                               start=False, stop=True,
                               waits=[tb3] + list(xa_ready))
                if i == 3:
                    tok_b3_used = tok
                g_done[i] = E.op("v", "tensor_tensor",
                                 g_sb[:, i * 1024:i * 1024 + L],
                                 PH[i][0:128, 0:L],
                                 sl_sb[:, i * 1024:i * 1024 + L],
                                 ALU.mult, waits=(tok, sl_done[i]))
                ph_free[i] = (g_done[i],)
            gfm_tok = [None] * 4
            for i in range(4):
                for kc in range(khs):
                    TR.go(g_sb[:, i * 1024 + kc * 128:i * 1024 + (kc + 1) * 128],
                          g_fm[:, i * 1024 + kc * 128:i * 1024 + (kc + 1) * 128],
                          waits=tuple(g_done))
                gfm_tok[i] = (E.tok("v"), E.tok("s"))
            for i in range(4):
                tok = None
                for kc in range(khs):
                    w = list(gfm_tok[i]) + [tok_a2]
                    if kc == 0:
                        w += [tok_pga_free] + list(ph_free[0])
                    kglob = (hc0 // 128) + kc
                    tok = E.op("t", "matmul", PGA,
                               g_fm[:, i * 1024 + kc * 128:i * 1024 + (kc + 1) * 128],
                               a2_sb[:, kglob * 16:(kglob + 1) * 16],
                               start=(kc == 0), stop=(kc == khs - 1), waits=w)
                tok_ga2[i] = E.op("v", "tensor_tensor",
                                  ga2_sb[:, i * 16:(i + 1) * 16],
                                  ga2_sb[:, i * 16:(i + 1) * 16], PGA,
                                  ALU.add, waits=(tok, tok_ga2[i]))
                tok_pga_free = tok_ga2[i]
            for dmq in range(4):
                c0 = dmq * 1024
                tokmm = [None] * 4
                for kc in range(khs):
                    buf = kc % 4
                    kglob = (hc0 // 128) + kc
                    tw2 = E.dma("dg", f"d_w2{buf}",
                                w2pc[:, buf * 1024:(buf + 1) * 1024],
                                w2[kglob * 128:(kglob + 1) * 128, c0:c0 + 1024],
                                waits=(w2_free[buf],))
                    for i in range(4):
                        w = [tw2] + list(gfm_tok[i])
                        if kc == 0:
                            w += [dens_tok[i][dmq], tok_pga_free] + list(ph_free[i])
                        for n5 in range(2):
                            need = (n5 == 1 and (i == 3 or kc == khs - 1))
                            if need:
                                tok = E.op("t", "matmul",
                                           PH[i][:, n5 * 512:(n5 + 1) * 512],
                                           g_fm[:, i * 1024 + kc * 128:i * 1024 + (kc + 1) * 128],
                                           w2pc[:, buf * 1024 + n5 * 512:buf * 1024 + (n5 + 1) * 512],
                                           start=(kc == 0), stop=(kc == khs - 1),
                                           waits=w)
                                if kc == khs - 1:
                                    tokmm[i] = tok
                                if i == 3:
                                    w2_free[buf] = tok
                            else:
                                E.op_("t", "matmul",
                                      PH[i][:, n5 * 512:(n5 + 1) * 512],
                                      g_fm[:, i * 1024 + kc * 128:i * 1024 + (kc + 1) * 128],
                                      w2pc[:, buf * 1024 + n5 * 512:buf * 1024 + (n5 + 1) * 512],
                                      start=(kc == 0), stop=(kc == khs - 1),
                                      waits=w)
                for i in range(4):
                    eng = "v" if i % 2 == 0 else "s"
                    dsl = dens[:, i * DM + c0:i * DM + c0 + 1024]
                    if ci == 0:
                        meth = "tensor_copy" if eng == "v" else "copy"
                        tev = E.op(eng, meth, dsl, PH[i],
                                   waits=(tokmm[i], dens_tok[i][dmq]))
                    else:
                        tev = E.op("v", "tensor_tensor", dsl, dsl, PH[i],
                                   ALU.add, waits=(tokmm[i], dens_tok[i][dmq]))
                    dens_tok[i][dmq] = tev
                    ph_free[i] = (tev,)
            hc0 += L

        all_dens = tuple(t for row in dens_tok for t in row)
        prev_tt = None
        for i in range(4):
            m = tiles[i]
            tcv = E.op("s", "activation", ga2_bf[:, :],
                       ga2_sb[:, i * 16:(i + 1) * 16],
                       AF.Copy, scale=SCALING, waits=(tok_ga2[i], prev_tt))
            tt, te = TR.go(ga2_bf[:, :], ga2_fm[0:16, i * 128:(i + 1) * 128],
                           waits=(tcv,) + all_dens)
            prev_tt = tt
            for dmq in range(4):
                tok = None
                for n5 in range(2):
                    c0 = dmq * 1024 + n5 * 512
                    tok = E.op("t", "matmul",
                               PH[i][:, n5 * 512:(n5 + 1) * 512],
                               ga2_fm[0:16, i * 128:(i + 1) * 128],
                               b2_sb[0:16, c0:c0 + 512],
                               start=True, stop=True,
                               waits=[te, tok_b2] + list(ph_free[i]))
                tev = E.op("v", "tensor_tensor",
                           dens[:, i * DM + dmq * 1024:i * DM + (dmq + 1) * 1024],
                           dens[:, i * DM + dmq * 1024:i * DM + (dmq + 1) * 1024],
                           PH[i], ALU.add, waits=(tok, dens_tok[i][dmq]))
                dens_tok[i][dmq] = tev
                ph_free[i] = (tev,)
            dview = dens[:, i * DM:(i + 1) * DM]
            t1, t2 = ln_stats(E, dview, p_s1[:, m:m + 1], p_s2[:, m:m + 1],
                              sq_post, waits=tuple(dens_tok[i]))
            tr, tn = ln_chain(E, p_s1[:, m:m + 1], p_s2[:, m:m + 1],
                              p_r[:, m:m + 1], p_nmr[:, m:m + 1],
                              p_tmp[:, m:m + 1], DM, t1, t2)
            tx = E.dma("dy", "d_x", x_st, x3i[m * 128:(m + 1) * 128, :],
                       waits=(tok_xst_free,))
            tap = E.op("v", "tensor_scalar", dview, dview, p_r[:, m:m + 1],
                       p_nmr[:, m:m + 1], ALU.mult, ALU.add, waits=(tr, tn))
            tadd = E.op("v", "tensor_tensor", dview, dview, x_st, ALU.add,
                        waits=(tap, tx))
            tok_xst_free = tadd
            tok_out_last = E.dma("dy", "d_out", out_ap[m * 128:(m + 1) * 128, :],
                                 dview, waits=(tadd,))
        E.barrier()

    E.barrier()




def build_fused():
    nc = bass.Bass("TRN2", target_bir_lowering=False)

    xo = nc.dram_tensor("xo", [1024, DM], F32, kind="ExternalInput").ap()
    xr = nc.dram_tensor("xr", [1024, DM], F32, kind="ExternalInput").ap()
    wqkv = nc.dram_tensor("wqkv", [DM, 3072], F32, kind="ExternalInput").ap()
    bqkv = nc.dram_tensor("bqkv", [1, 3072], F32, kind="ExternalInput").ap()
    wo = nc.dram_tensor("wo", [1024, DM], F32, kind="ExternalInput").ap()
    bo = nc.dram_tensor("bo", [1, DM], F32, kind="ExternalInput").ap()
    w0 = nc.dram_tensor("w0", [DM, DM], F32, kind="ExternalInput").ap()
    b0 = nc.dram_tensor("b0", [1, DM], F32, kind="ExternalInput").ap()
    cent = nc.dram_tensor("cent", [DM, 64], F32, kind="ExternalInput").ap()
    assign = nc.dram_tensor("assign", [1, DM], I32, kind="ExternalInput").ap()
    cw = [nc.dram_tensor(f"cw{t}", [DM, 3072], F32, kind="ExternalInput").ap()
          for t in range(3)]
    cb = nc.dram_tensor("cb", [1, 3072], F32, kind="ExternalInput").ap()
    wq = nc.dram_tensor("wq", [1024, 1024], F32, kind="ExternalInput").ap()
    wk = nc.dram_tensor("wk", [1024, 1024], F32, kind="ExternalInput").ap()
    wv = nc.dram_tensor("wv", [1024, 1024], F32, kind="ExternalInput").ap()
    bqc = nc.dram_tensor("bqc", [128, 8], F32, kind="ExternalInput").ap()
    bkc = nc.dram_tensor("bkc", [128, 8], F32, kind="ExternalInput").ap()
    bvr = nc.dram_tensor("bvr", [1, 1024], F32, kind="ExternalInput").ap()
    wo2 = nc.dram_tensor("wo2", [1024, 1024], F32, kind="ExternalInput").ap()
    bo2 = nc.dram_tensor("bo2", [1, 1024], F32, kind="ExternalInput").ap()
    pw = nc.dram_tensor("pw", [1024, DM], F32, kind="ExternalInput").ap()
    pb = nc.dram_tensor("pb", [1, DM], F32, kind="ExternalInput").ap()
    halo_sel = nc.dram_tensor("halo_sel", [16, 2], F32, kind="ExternalInput").ap()
    halo_mask = nc.dram_tensor("halo_mask", [2, 1], F32, kind="ExternalInput").ap()
    w1 = nc.dram_tensor("w1", [DM, 11008], F32, kind="ExternalInput").ap()
    w3 = nc.dram_tensor("w3", [DM, 11008], F32, kind="ExternalInput").ap()
    w2 = nc.dram_tensor("w2", [11008, DM], F32, kind="ExternalInput").ap()
    a1 = nc.dram_tensor("a1", [DM, 16], F32, kind="ExternalInput").ap()
    b1 = nc.dram_tensor("b1", [16, 11008], F32, kind="ExternalInput").ap()
    a3 = nc.dram_tensor("a3", [DM, 16], F32, kind="ExternalInput").ap()
    b3 = nc.dram_tensor("b3", [16, 11008], F32, kind="ExternalInput").ap()
    a2 = nc.dram_tensor("a2", [11008, 16], F32, kind="ExternalInput").ap()
    b2 = nc.dram_tensor("b2", [16, DM], F32, kind="ExternalInput").ap()
    out = nc.dram_tensor("out", [1024, DM], F32, kind="ExternalOutput").ap()

    xs1 = nc.dram_tensor("xs1", [1024, DM], F32)
    xs2 = nc.dram_tensor("xs2", [1024, DM], F32)
    xs3 = nc.dram_tensor("xs3", [1024, DM], F32)
    cc_in = nc.dram_tensor("cc_in", [64, 1], F32)
    cc_out = nc.dram_tensor("cc_out", [64, 1], F32)
    hg_in = nc.dram_tensor("hg_in", [2, DM], F32)
    hg_out = nc.dram_tensor("hg_out", [16, DM], F32)

    c = make_ctx(nc)
    c.E.barrier()
    emit_l0_attn(c, xo, xr, wqkv, bqkv, wo, bo, xs1.ap())
    emit_l0_ffn(c, xs1.ap(), w0, b0, cent, assign, cc_in, cc_out, xs2.ap())
    emit_l1_attn(c, xs2.ap(), cw, cb, wq, wk, wv, bqc, bkc, bvr, wo2, bo2,
                 pw, pb, halo_sel, halo_mask, hg_in, hg_out, xs3.ap())
    emit_l1_ffn(c, xs3.ap(), w1, w3, w2, a1, b1, a3, b3, a2, b2, out)
    c.E.wait("y", *c.E.dma_totals())
    return nc


def host_inputs_fused(d):
    x0 = np.ascontiguousarray(d["L0_attn_x"])
    w0T = np.ascontiguousarray(d["L0_ffn_weight"].T)
    centT = np.ascontiguousarray(d["L0_ffn_centroids"].T)
    cwf = d["L1_attn_conv_w"]
    shared = {
        "wqkv": np.ascontiguousarray(d["L0_attn_Wqkv"]),
        "bqkv": np.ascontiguousarray(d["L0_attn_bqkv"][None, :]),
        "wo": np.ascontiguousarray(d["L0_attn_Wo"]),
        "bo": np.ascontiguousarray(d["L0_attn_bo"][None, :]),
        "w0": w0T,
        "b0": np.ascontiguousarray(d["L0_ffn_bias"][None, :]),
        "cent": centT,
        "assign": np.ascontiguousarray(d["L0_ffn_assignments"][None, :]),
        "cw0": np.ascontiguousarray(cwf[0]),
        "cw1": np.ascontiguousarray(cwf[1]),
        "cw2": np.ascontiguousarray(cwf[2]),
        "cb": np.ascontiguousarray(d["L1_attn_conv_b"][None, :]),
        "wq": np.ascontiguousarray(d["L1_attn_wq"].reshape(1024, 1024)),
        "wk": np.ascontiguousarray(d["L1_attn_wk"].reshape(1024, 1024)),
        "wv": np.ascontiguousarray(d["L1_attn_wv"].reshape(1024, 1024)),
        "bqc": np.ascontiguousarray(d["L1_attn_bq"].reshape(8, 128).T),
        "bkc": np.ascontiguousarray(d["L1_attn_bk"].reshape(8, 128).T),
        "bvr": np.ascontiguousarray(d["L1_attn_bv"].reshape(1, 1024)),
        "wo2": np.ascontiguousarray(d["L1_attn_wo"].reshape(1024, 1024)),
        "bo2": np.ascontiguousarray(d["L1_attn_bo"][None, :]),
        "pw": np.ascontiguousarray(d["L1_attn_proj_w"]),
        "pb": np.ascontiguousarray(d["L1_attn_proj_b"][None, :]),
        "w1": np.ascontiguousarray(d["L1_ffn_w1"].T),
        "w3": np.ascontiguousarray(d["L1_ffn_w3"].T),
        "w2": np.ascontiguousarray(d["L1_ffn_w2"].T),
        "a1": np.ascontiguousarray(d["L1_ffn_lora_A1"].T),
        "b1": np.ascontiguousarray(d["L1_ffn_lora_B1"].T),
        "a3": np.ascontiguousarray(d["L1_ffn_lora_A3"].T),
        "b3": np.ascontiguousarray(d["L1_ffn_lora_B3"].T),
        "a2": np.ascontiguousarray(d["L1_ffn_lora_A2"].T),
        "b2": np.ascontiguousarray(d["L1_ffn_lora_B2"].T),
    }
    maps = []
    for cid in range(8):
        b, h = cid // 2, cid % 2
        hs = np.zeros((16, 2), np.float32)
        hm = np.zeros((2, 1), np.float32)
        if h == 1:
            hs[2 * cid - 1, 0] = 1.0
            hm[0, 0] = 1.0
        if h == 0:
            hs[2 * cid + 2, 1] = 1.0
            hm[1, 0] = 1.0
        m = dict(shared)
        m["xo"] = np.ascontiguousarray(x0[b, h * 1024:(h + 1) * 1024])
        m["xr"] = np.ascontiguousarray(x0[b, (1 - h) * 1024:(2 - h) * 1024])
        m["halo_sel"] = hs
        m["halo_mask"] = hm
        maps.append(m)
    return maps


def gather_out_fused(results):
    out = np.zeros((4, 2048, 4096), np.float32)
    for cid in range(8):
        b, h = cid // 2, cid % 2
        out[b, h * 1024:(h + 1) * 1024] = results[cid]["out"]
    return out




from concourse.bass_utils import run_bass_kernel_spmd


def kernel(**inputs):
    nc = build_fused()
    res = run_bass_kernel_spmd(nc, host_inputs_fused(inputs),
                               core_ids=list(range(8)))
    return gather_out_fused(res.results)


# version 07
# speedup vs baseline: 1.1404x; 1.0023x over previous
import numpy as np
import concourse.bass as bass
import concourse.mybir as mybir

F32 = mybir.dt.float32
BF16 = mybir.dt.bfloat16
I32 = mybir.dt.int32
U8 = mybir.dt.uint8
AF = mybir.ActivationFunctionType
ALU = mybir.AluOpType
AX = mybir.AxisListType.X

DM = 4096
NKC = DM // 128


class Emit:

    ENG = ("t", "v", "s", "g")

    def __init__(self, nc):
        self.nc = nc
        self.e = {"t": nc.tensor, "v": nc.vector, "s": nc.scalar,
                  "g": nc.gpsimd, "y": nc.sync,
                  "dg": nc.gpsimd, "dy": nc.sync}
        self.sem = {k: nc.alloc_semaphore("sem_" + k) for k in self.ENG}
        self.cnt = {k: 0 for k in self.ENG}
        self.hw = {}

    def _dsem(self, name):
        key = "D:" + name
        if key not in self.sem:
            self.sem[key] = self.nc.alloc_semaphore("sem_" + name)
            self.cnt[key] = 0
        return key

    def wait(self, consumer, *toks):
        for tok in toks:
            if tok is None:
                continue
            k, v = tok
            if v <= 0 or self.hw.get((consumer, k), 0) >= v:
                continue
            self.e[consumer].wait_ge(self.sem[k], v)
            self.hw[(consumer, k)] = v

    def inc(self, inst, key, n=1):
        inst.then_inc(self.sem[key], n)
        self.cnt[key] += n
        return (key, self.cnt[key])

    def tok(self, key):
        return (key, self.cnt[key])

    def op(self, eng, name, *args, waits=(), **kw):
        self.wait(eng, *[w for w in waits if w])
        inst = getattr(self.e[eng], name)(*args, **kw)
        return self.inc(inst, eng, 1)

    def op_(self, eng, name, *args, waits=(), **kw):
        self.wait(eng, *[w for w in waits if w])
        getattr(self.e[eng], name)(*args, **kw)
        return None

    def dma(self, q, stream, out, in_, *, waits=(), **kw):
        key = self._dsem(stream)
        self.wait(q, *[w for w in waits if w])
        inst = self.e[q].dma_start(out, in_, **kw)
        return self.inc(inst, key, 16)

    def barrier(self):
        toks = [self.tok(k) for k in self.cnt]
        for consumer in ("t", "v", "s", "g", "y"):
            self.wait(consumer, *toks)

    def dma_totals(self):
        return [self.tok(k) for k in self.cnt if k.startswith("D:")]


class Arena:

    def __init__(self, nc, name, nbytes):
        self.t = nc.alloc_sbuf_tensor(name, [128, nbytes], U8)
        self.nbytes = nbytes
        self.off = 0

    def reset(self):
        self.off = 0

    def mark(self):
        return self.off

    def rewind(self, m):
        self.off = m

    def view(self, off, part, free, dtype):
        sz = free * np.dtype(mybir.dt.np(dtype)).itemsize
        assert off + sz <= self.nbytes
        return self.t.ap()[0:part, off:off + sz].bitcast(dtype)

    def take(self, part, free, dtype):
        sz = free * np.dtype(mybir.dt.np(dtype)).itemsize
        off = (self.off + 511) & ~511
        assert off + sz <= self.nbytes, f"arena overflow {off}+{sz}>{self.nbytes}"
        self.off = off + sz
        return self.t.ap()[0:part, off:off + sz].bitcast(dtype)


def ln_stats(E, x_ap, s1_ap, s2_ap, sq_ap, waits=()):
    t1 = E.op("v", "reduce_sum", s1_ap, x_ap, axis=AX, waits=waits)
    t2 = E.op("s", "activation", sq_ap, x_ap, AF.Square, accum_out=s2_ap,
              waits=waits)
    return t1, t2


def ln_chain(E, s1_ap, s2_ap, r_ap, nmr_ap, tmp_ap, D, tok_s1, tok_s2):
    t1 = E.op("v", "tensor_tensor", tmp_ap, s1_ap, s1_ap, ALU.mult,
              waits=(tok_s1,))
    t2 = E.op("v", "scalar_tensor_tensor", tmp_ap, tmp_ap, -1.0 / D, s2_ap,
              ALU.mult, ALU.add, waits=(t1, tok_s2))
    t3 = E.op("v", "tensor_scalar", tmp_ap, tmp_ap, 1.0 / D, 1e-5,
              ALU.mult, ALU.add, waits=(t2,))
    sd = E.op("s", "activation", r_ap, tmp_ap, AF.Sqrt, waits=(t3,))
    rt = E.op("v", "reciprocal", r_ap, r_ap, waits=(sd,))
    nm = E.op("v", "scalar_tensor_tensor", nmr_ap, s1_ap, -1.0 / D, r_ap,
              ALU.mult, ALU.mult, waits=(rt, tok_s1))
    return rt, nm


class TPool:

    def __init__(self, E, psts, ident):
        self.E = E
        self.psts = psts
        self.free = [None] * len(psts)
        self.ident = ident
        self.i = 0
        self.flip = 0

    def transpose(self, src_ap, dst_ap, waits=(), evac_eng=None):
        E = self.E
        i = self.i
        self.i = (i + 1) % len(self.psts)
        eng = evac_eng or ("v" if self.flip == 0 else "s")
        self.flip ^= 1
        po = self.psts[i][0:src_ap.shape[1], 0:src_ap.shape[0]]
        tt = E.op("t", "transpose", po, src_ap, self.ident,
                  waits=tuple(waits) + (self.free[i],))
        meth = "copy" if eng == "s" else "tensor_copy"
        te = E.op(eng, meth, dst_ap, po, waits=(tt,))
        self.free[i] = te
        return tt, te




class TRot:

    def __init__(self, E, views, ident):
        self.E = E
        self.views = views
        self.ident = ident
        self.free = [None] * len(views)
        self.i = 0
        self.flip = 0

    def go(self, src_ap, dst_ap, waits=()):
        E = self.E
        i = self.i
        self.i = (i + 1) % len(self.views)
        eng = "v" if self.flip == 0 else "s"
        self.flip ^= 1
        p = src_ap.shape[0]
        po = self.views[i][0:src_ap.shape[1], 0:p]
        fr = self.free[i]
        if fr is None:
            fr = ()
        elif isinstance(fr[0], str):
            fr = (fr,)
        tt = E.op("t", "transpose", po, src_ap, self.ident[0:p, 0:p],
                  waits=tuple(w for w in waits if w) + fr)
        meth = "copy" if eng == "s" else "tensor_copy"
        te = E.op(eng, meth, dst_ap, po, waits=(tt,))
        self.free[i] = te
        return tt, te


class Ctx:
    pass


def make_ctx(nc):
    c = Ctx()
    c.nc = nc
    c.E = Emit(nc)
    E = c.E
    c.ident = nc.alloc_sbuf_tensor("ident", [128, 128], BF16)
    c.ones = nc.alloc_sbuf_tensor("ones", [1, 128], F32)
    c.ones_bf = nc.alloc_sbuf_tensor("ones_bf", [1, 128], BF16)
    c.s1a = nc.alloc_sbuf_tensor("s1a", [128, 16], F32)
    c.s2a = nc.alloc_sbuf_tensor("s2a", [128, 16], F32)
    c.ra = nc.alloc_sbuf_tensor("ra", [128, 16], F32)
    c.nmra = nc.alloc_sbuf_tensor("nmra", [128, 16], F32)
    c.tmpa = nc.alloc_sbuf_tensor("tmpa", [128, 16], F32)
    c.p_s1 = nc.alloc_sbuf_tensor("p_s1", [128, 16], F32)
    c.p_s2 = nc.alloc_sbuf_tensor("p_s2", [128, 16], F32)
    c.p_r = nc.alloc_sbuf_tensor("p_r", [128, 16], F32)
    c.p_nmr = nc.alloc_sbuf_tensor("p_nmr", [128, 16], F32)
    c.p_tmp = nc.alloc_sbuf_tensor("p_tmp", [128, 16], F32)
    c.nm = nc.alloc_sbuf_tensor("nm", [128, 1], F32)
    c.nms = nc.alloc_sbuf_tensor("nms", [128, 1], F32)
    c.rsum = nc.alloc_sbuf_tensor("rsum", [128, 1], F32)
    c.rrec = nc.alloc_sbuf_tensor("rrec", [128, 1], F32)
    c.PSA = nc.alloc_psum_tensor("PSA", [128, 2048], F32)
    c.PSB = nc.alloc_psum_tensor("PSB", [128, 2048], F32)
    c.TVIEWS = [c.PSB[:, 0:512].bitcast(BF16)[:, 0:128],
                c.PSB[:, 512:1024].bitcast(BF16)[:, 0:128]]
    rem = nc.sbuf_bytes_remaining
    c.ar = Arena(nc, "arena", rem - 1536)
    tms = E.op("g", "memset", c.ident[:, :], 0.0)
    E.op("g", "affine_select", out=c.ident[:, :], in_=c.ident[:, :],
         compare_op=ALU.not_equal, fill=1.0, base=0,
         pattern=[[-1, 128]], channel_multiplier=1, waits=(tms,))
    E.op("g", "memset", c.ones[0:1, :], 1.0)
    E.op("g", "memset", c.ones_bf[0:1, :], 1.0)
    return c


def emit_l0_attn(c, xo, xr, wqkv, bqkv, wo, bo, x1_out):
    nc, E, ar = c.nc, c.E, c.ar
    PSA, PSB = c.PSA, c.PSB
    ident, ones = c.ident, c.ones_bf
    s1a, s2a, ra, nmra, tmpa = c.s1a, c.s2a, c.ra, c.nmra, c.tmpa
    p_s1, p_s2, p_r, p_nmr, p_tmp = c.p_s1, c.p_s2, c.p_r, c.p_nmr, c.p_tmp
    nm, nms, rsum, rrec = c.nm, c.nms, c.rsum, c.rrec

    ar.reset()
    K_fm = ar.take(128, 8 * 2048, BF16)
    V_tm = ar.take(128, 16 * 1024, BF16)
    Q_fm = ar.take(128, 8 * 1024, BF16)
    at_fm_all = ar.take(128, 8 * 1024, BF16)
    m0 = ar.mark()
    x_st = ar.take(128, DM, F32)
    ln_tm = ar.take(128, DM, BF16)
    lnfm = ar.take(128, 4 * DM, BF16)
    sqs = ar.take(128, DM, F32)
    wpc = ar.take(128, 4 * 512, BF16)
    ttm = ar.take(128, 1024, BF16)
    bqv = ar.take(1, 3072, BF16)
    gend = ar.off
    ar.rewind(m0)
    P = ar.take(128, 2048, BF16)
    PTs = ar.take(128, 2048, BF16)
    at_tm = ar.take(128, 1024, BF16)
    aend = ar.off
    ar.rewind(m0)
    x0s = ar.take(128, DM, F32)
    x1s = ar.take(128, DM, F32)
    sqp = ar.take(128, DM, F32)
    wop = ar.take(128, 4 * 512, BF16)
    bor = ar.take(1, DM, BF16)
    dens = ar.view(0, 128, 4 * DM, F32)
    ar.off = max(ar.off, gend, aend)

    PSO = [PSB[:, 1024:1536], PSB[:, 1536:2048]]
    TP = TRot(E, c.TVIEWS, ident)
    tok_bqv = E.dma("dg", "a_bias1", bqv, bqkv)

    groups = [
        (xo, [0, 1, 2, 3], 0, range(0, 6)),
        (xo, [4, 5, 6, 7], 4, range(0, 6)),
        (xr, [0, 1, 2, 3], 8, range(2, 6)),
        (xr, [4, 5, 6, 7], 12, range(2, 6)),
    ]
    tok_xst_free = None
    tok_lntm_free = None
    piece_free = [None] * 4
    ttm_free = [None, None]
    psa_free = [None] * 4
    jglob = 0
    for (src, tiles, t16b, ntiles) in groups:
        lnfm_ready = [None] * 4
        for i, tile in enumerate(tiles):
            j = jglob
            jglob += 1
            jj = j % 16
            td = E.dma("dy", "a_x", x_st, src[tile * 128:(tile + 1) * 128, :],
                       waits=(tok_xst_free,))
            t1, t2 = ln_stats(E, x_st, s1a[:, jj:jj + 1], s2a[:, jj:jj + 1],
                              sqs, waits=(td,))
            tr, tn = ln_chain(E, s1a[:, jj:jj + 1], s2a[:, jj:jj + 1],
                              ra[:, jj:jj + 1], nmra[:, jj:jj + 1],
                              tmpa[:, jj:jj + 1], DM, t1, t2)
            tok_apply = E.op("v", "tensor_scalar", ln_tm, x_st,
                             ra[:, jj:jj + 1], nmra[:, jj:jj + 1],
                             ALU.mult, ALU.add,
                             waits=(tr, tn, tok_lntm_free))
            tok_xst_free = tok_apply
            last_tt = None
            for ch in range(NKC):
                tt, te = TP.go(ln_tm[:, ch * 128:(ch + 1) * 128],
                               lnfm[:, i * DM + ch * 128:i * DM + (ch + 1) * 128],
                               waits=(tok_apply,))
                last_tt = tt
            tok_lntm_free = last_tt
            lnfm_ready[i] = (E.tok("v"), E.tok("s"))

        for n in ntiles:
            c0 = n * 512
            for k in range(NKC):
                buf = k % 4
                tw = E.dma("dg", f"a_w{buf}", wpc[:, buf * 512:(buf + 1) * 512],
                           wqkv[k * 128:(k + 1) * 128, c0:c0 + 512],
                           waits=(piece_free[buf],))
                for i in range(4):
                    w = [tw]
                    if k == 0:
                        w += list(lnfm_ready[i]) + [psa_free[i]]
                    lhs = lnfm[:, i * DM + k * 128:i * DM + (k + 1) * 128]
                    rhs = wpc[:, buf * 512:(buf + 1) * 512]
                    psl = PSA[:, i * 512:(i + 1) * 512]
                    if i == 3:
                        piece_free[buf] = E.op("t", "matmul", psl, lhs, rhs,
                                               start=(k == 0), stop=False,
                                               waits=w)
                    else:
                        E.op_("t", "matmul", psl, lhs, rhs,
                              start=(k == 0), stop=False, waits=w)
            for i in range(4):
                tok_done = E.op("t", "matmul", PSA[:, i * 512:(i + 1) * 512],
                                ones[0:1, :], bqv[0:1, c0:c0 + 512],
                                start=False, stop=True, waits=(tok_bqv,))
                t16 = t16b + i
                tloc = tiles[i]
                if n < 2:
                    ev = E.op("s", "copy", ttm[:, (i % 2) * 512:(i % 2 + 1) * 512],
                              PSA[:, i * 512:(i + 1) * 512],
                              waits=(tok_done, ttm_free[i % 2]))
                    psa_free[i] = ev
                    last = None
                    for s4 in range(4):
                        chunk = n * 4 + s4
                        tt, te = TP.go(
                            ttm[:, (i % 2) * 512 + s4 * 128:(i % 2) * 512 + (s4 + 1) * 128],
                            Q_fm[:, chunk * 1024 + tloc * 128:chunk * 1024 + (tloc + 1) * 128],
                            waits=(ev,))
                        last = tt
                    ttm_free[i % 2] = last
                elif n < 4:
                    ev = E.op("s", "copy", ttm[:, (i % 2) * 512:(i % 2 + 1) * 512],
                              PSA[:, i * 512:(i + 1) * 512],
                              waits=(tok_done, ttm_free[i % 2]))
                    psa_free[i] = ev
                    last = None
                    for s4 in range(4):
                        chunk = (n - 2) * 4 + s4
                        tt, te = TP.go(
                            ttm[:, (i % 2) * 512 + s4 * 128:(i % 2) * 512 + (s4 + 1) * 128],
                            K_fm[:, chunk * 2048 + t16 * 128:chunk * 2048 + (t16 + 1) * 128],
                            waits=(ev,))
                        last = tt
                    ttm_free[i % 2] = last
                else:
                    ev = E.op("v", "tensor_copy",
                              V_tm[:, t16 * 1024 + (n - 4) * 512:t16 * 1024 + (n - 4) * 512 + 512],
                              PSA[:, i * 512:(i + 1) * 512], waits=(tok_done,))
                    psa_free[i] = ev

    E.barrier()
    tok_psa_attn = None
    tok_pso_free = [None, None]
    for m in range(8):
        tok_scores = None
        for ch in range(8):
            for kk in range(4):
                w = []
                if ch == 0:
                    w = [tok_psa_attn, psa_free[kk]]
                lhs = Q_fm[:, ch * 1024 + m * 128:ch * 1024 + (m + 1) * 128]
                rhs = K_fm[:, ch * 2048 + kk * 512:ch * 2048 + (kk + 1) * 512]
                psl = PSA[:, kk * 512:(kk + 1) * 512]
                if ch == 7 and kk == 3:
                    tok_scores = E.op("t", "matmul", psl, lhs, rhs,
                                      start=(ch == 0), stop=(ch == 7), waits=w)
                else:
                    E.op_("t", "matmul", psl, lhs, rhs,
                          start=(ch == 0), stop=(ch == 7), waits=w)
        psa_free = [None] * 4
        tmax = E.op("v", "reduce_max", nm[:, :], PSA[:, :], axis=AX,
                    negate=True, waits=(tok_scores,))
        tnms = E.op("v", "tensor_scalar_mul", nms[:, :], nm[:, :], 1.0 / 32.0,
                    waits=(tmax,))
        tok_exp = E.op("s", "activation", P, PSA[:, :], AF.Exp,
                       bias=nms[:, 0:1], scale=1.0 / 32.0,
                       accum_out=rsum[:, 0:1], waits=(tnms,))
        tok_psa_attn = tok_exp
        trr = E.op("v", "reciprocal", rrec[:, :], rsum[:, :], waits=(tok_exp,))
        tok_pv = None
        for ch in range(16):
            tt, te = TP.go(P[:, ch * 128:(ch + 1) * 128],
                           PTs[:, ch * 128:(ch + 1) * 128],
                           waits=(tok_exp,))
            for n2 in range(2):
                w = [te]
                if ch == 0:
                    w += [tok_pso_free[0], tok_pso_free[1]]
                psl = PSO[n2]
                lhs = PTs[:, ch * 128:(ch + 1) * 128]
                rhs = V_tm[:, ch * 1024 + n2 * 512:ch * 1024 + (n2 + 1) * 512]
                if ch == 15 and n2 == 1:
                    tok_pv = E.op("t", "matmul", psl, lhs, rhs,
                                  start=(ch == 0), stop=(ch == 15), waits=w)
                else:
                    E.op_("t", "matmul", psl, lhs, rhs,
                          start=(ch == 0), stop=(ch == 15), waits=w)
        tok_at = E.op("s", "activation", at_tm, PSB[:, 1024:2048], AF.Copy,
                      scale=rrec[:, 0:1], waits=(tok_pv, trr))
        for ch in range(8):
            TP.go(at_tm[:, ch * 128:(ch + 1) * 128],
                  at_fm_all[:, m * 1024 + ch * 128:m * 1024 + (ch + 1) * 128],
                  waits=(tok_at,))
        tok_pso_free = [tok_at, tok_at]

    guard = (E.tok("t"), E.tok("v"), E.tok("s"))
    tok_bor = E.dma("dg", "a_bias2", bor, bo, waits=guard)
    at_ready = guard
    wop_free = [None] * 4
    psa_free2 = [None] * 4
    tok_x0_free = None
    tok_x1_free = None
    for g2 in range(2):
        tiles2 = [g2 * 4 + i for i in range(4)]
        dens_done = [[None] * 8 for _ in range(4)]
        for n in range(8):
            c0 = n * 512
            for k in range(8):
                buf = k % 4
                tw = E.dma("dg", f"a_wo{buf}", wop[:, buf * 512:(buf + 1) * 512],
                           wo[k * 128:(k + 1) * 128, c0:c0 + 512],
                           waits=(wop_free[buf], guard[0]))
                for i in range(4):
                    m = tiles2[i]
                    w = [tw]
                    if k == 0:
                        w += list(at_ready) + [psa_free2[i], tok_psa_attn]
                    lhs = at_fm_all[:, m * 1024 + k * 128:m * 1024 + (k + 1) * 128]
                    rhs = wop[:, buf * 512:(buf + 1) * 512]
                    psl = PSA[:, i * 512:(i + 1) * 512]
                    if i == 3:
                        wop_free[buf] = E.op("t", "matmul", psl, lhs, rhs,
                                             start=(k == 0), stop=False,
                                             waits=w)
                    else:
                        E.op_("t", "matmul", psl, lhs, rhs,
                              start=(k == 0), stop=False, waits=w)
            for i in range(4):
                tok_done = E.op("t", "matmul", PSA[:, i * 512:(i + 1) * 512],
                                ones[0:1, :], bor[0:1, c0:c0 + 512],
                                start=False, stop=True, waits=(tok_bor,))
                eng = "v" if i % 2 == 0 else "s"
                meth = "tensor_copy" if eng == "v" else "copy"
                tev = E.op(eng, meth, dens[:, i * DM + c0:i * DM + c0 + 512],
                           PSA[:, i * 512:(i + 1) * 512], waits=(tok_done,))
                psa_free2[i] = tev
                dens_done[i][n] = tev
        tok_out_last = None
        for i in range(4):
            m = tiles2[i]
            dview = dens[:, i * DM:(i + 1) * DM]
            t1, t2 = ln_stats(E, dview, p_s1[:, m:m + 1], p_s2[:, m:m + 1],
                              sqp, waits=tuple(dens_done[i]))
            tr, tn = ln_chain(E, p_s1[:, m:m + 1], p_s2[:, m:m + 1],
                              p_r[:, m:m + 1], p_nmr[:, m:m + 1],
                              p_tmp[:, m:m + 1], DM, t1, t2)
            tx = E.dma("dy", "a_x0", x0s, xo[m * 128:(m + 1) * 128, :],
                       waits=(tok_x0_free,) + guard)
            tap = E.op("v", "tensor_scalar", x1s, dview, p_r[:, m:m + 1],
                       p_nmr[:, m:m + 1], ALU.mult, ALU.add,
                       waits=(tr, tn, tok_x1_free))
            tadd = E.op("v", "tensor_tensor", x1s, x1s, x0s, ALU.add,
                        waits=(tap, tx))
            tok_x0_free = tadd
            tok_x1_free = E.dma("dy", "a_out", x1_out[m * 128:(m + 1) * 128, :],
                                x1s, waits=(tadd,))
        E.wait("v", tok_x1_free)
        E.wait("s", tok_x1_free)
    E.barrier()


def emit_l0_ffn(c, x1i, w0, b0, cent, assign, cc_in, cc_out, x2_out):
    nc, E, ar = c.nc, c.E, c.ar
    PSA, PSB = c.PSA, c.PSB
    ones = c.ones
    s1a, s2a, ra, nmra, tmpa = c.s1a, c.s2a, c.ra, c.nmra, c.tmpa
    p_s1, p_s2, p_r, p_nmr, p_tmp = c.p_s1, c.p_s2, c.p_r, c.p_nmr, c.p_tmp
    NCLUST, THR, INV_TEMP = 64, 0.01, 10.0
    nmx, nmx10, dsum, drec = c.nm, c.nms, c.rsum, c.rrec

    ar.reset()
    ident32 = ar.take(128, 128, F32)
    ones64 = ar.take(64, 128, F32)
    iota64 = ar.take(64, 1, F32)
    cent_sb = ar.take(128, NKC * NCLUST, F32)
    lnbf = ar.take(128, 8 * DM, BF16)
    qmask = ar.take(128, 8, F32)
    qmx = ar.take(128, 1, F32)
    cmax = ar.take(128, NCLUST, F32)
    cmT = ar.take(64, 128, F32)
    cm_col = ar.take(64, 1, F32)
    cs_col = ar.take(64, 1, F32)
    cs_gt = ar.take(64, 1, F32)
    dots = ar.take(128, NCLUST, F32)
    x_st = ar.take(128, DM, F32)
    lnf32 = ar.take(128, DM, F32)
    lnfm32 = ar.take(128, DM, F32)
    dens = ar.take(128, 4 * DM, F32)
    b0pc = ar.take(1, 1024, BF16)
    sqs = dens[:, 0:DM]
    sqs3 = lnf32
    asg = lnf32[0:64, :].bitcast(I32)
    asgf = lnfm32[0:64, :]
    ohcs = x_st[0:64, :]
    rmask = lnfm32
    wpc = cent_sb[:, 0:1024].bitcast(BF16)[:, 0:2048]

    T32 = [PSB[:, 0:128], PSB[:, 512:640]]
    PSD = PSB[:, 1024:1024 + NCLUST]
    TP32 = TRot(E, T32, ident32)

    tms = E.op("g", "memset", ident32[0:128, :], 0.0)
    E.op("g", "affine_select", out=ident32[0:128, :], in_=ident32[0:128, :],
         compare_op=ALU.not_equal, fill=1.0, base=0,
         pattern=[[-1, 128]], channel_multiplier=1, waits=(tms,))
    tmo = E.op("g", "memset", ones64[0:64, :], 1.0)
    tio = E.op("g", "iota", iota64[0:64, 0:1], pattern=[[0, 1]], base=0,
               channel_multiplier=1, allow_small_or_imprecise_dtypes=True)
    tok_cmz = E.op("v", "memset", cmax[:, :], 0.0)
    tok_cent = E.dma("dy", "b_cent",
                     cent_sb[:, :].rearrange("p (c n) -> p c n", n=NCLUST),
                     cent.rearrange("(c p) n -> p c n", p=128))

    tok_xst_free = None
    tok_lnf_free = None
    tok_lnfm_free = None
    tok_psd_free = None
    tok_cmax = tok_cmz
    tok_ident_done = E.tok("g")
    for m in range(8):
        td = E.dma("dy", "b_x", x_st, x1i[m * 128:(m + 1) * 128, :],
                   waits=(tok_xst_free,))
        t1, t2 = ln_stats(E, x_st, s1a[:, m:m + 1], s2a[:, m:m + 1], sqs,
                          waits=(td,))
        tr, tn = ln_chain(E, s1a[:, m:m + 1], s2a[:, m:m + 1],
                          ra[:, m:m + 1], nmra[:, m:m + 1],
                          tmpa[:, m:m + 1], DM, t1, t2)
        tok_ap = E.op("v", "tensor_scalar", lnf32, x_st, ra[:, m:m + 1],
                      nmra[:, m:m + 1], ALU.mult, ALU.add,
                      waits=(tr, tn, tok_lnf_free))
        tok_xst_free = tok_ap
        last_tt = None
        for ch in range(NKC):
            tt, te = TP32.go(lnf32[:, ch * 128:(ch + 1) * 128],
                             lnfm32[:, ch * 128:(ch + 1) * 128],
                             waits=(tok_ap, tok_lnfm_free, tok_ident_done))
            last_tt = tt
        tok_lnf_free = last_tt
        fm_ready = (E.tok("v"), E.tok("s"))
        tok_bf = E.op("v", "tensor_copy", lnbf[:, m * DM:(m + 1) * DM],
                      lnfm32, waits=fm_ready)
        tok_dots = None
        for k in range(NKC):
            w = list(fm_ready) + [tok_cent]
            if k == 0:
                w.append(tok_psd_free)
            tok_dots = E.op("t", "matmul", PSD,
                            lnfm32[:, k * 128:(k + 1) * 128],
                            cent_sb[:, k * NCLUST:(k + 1) * NCLUST],
                            start=(k == 0), stop=(k == NKC - 1), waits=w)
        tok_lnfm_free = tok_dots
        tmx = E.op("v", "reduce_max", nmx[:, :], PSD, axis=AX,
                   negate=True, waits=(tok_dots,))
        tx10 = E.op("v", "tensor_scalar_mul", nmx10[:, :], nmx[:, :],
                    INV_TEMP, waits=(tmx,))
        tex = E.op("s", "activation", dots[:, :], PSD, AF.Exp,
                   bias=nmx10[:, 0:1], scale=INV_TEMP,
                   accum_out=dsum[:, 0:1], waits=(tx10,))
        tok_psd_free = tex
        trc = E.op("v", "reciprocal", drec[:, :], dsum[:, :], waits=(tex,))
        tdn = E.op("v", "tensor_scalar", dots[:, :], dots[:, :],
                   drec[:, 0:1], None, ALU.mult, waits=(trc, tex))
        tqm = E.op("v", "reduce_max", qmx[:, 0:1], dots[:, :], axis=AX,
                   waits=(tdn,))
        E.op("v", "tensor_single_scalar", qmask[:, m:m + 1], qmx[:, 0:1],
             THR, ALU.is_gt, waits=(tqm,))
        tok_cmax = E.op("v", "tensor_tensor", cmax[:, :], cmax[:, :],
                        dots[:, :], ALU.max, waits=(tok_cmax, tdn))

    tt, te = TP32.go(cmax[:, 0:64], cmT[0:64, 0:128], waits=(tok_cmax,))
    tcm = E.op("v", "reduce_max", cm_col[0:64, 0:1], cmT[0:64, 0:128],
               axis=AX, waits=(te,))
    tdm = E.dma("dg", "b_ccin", cc_in.ap(), cm_col[0:64, 0:1],
                waits=(tcm,), allow_slow_non_contiguous=True)
    E.wait("g", tdm)
    cc_inst = nc.gpsimd.collective_compute(
        "AllReduce", ALU.max,
        replica_groups=[list(range(8))],
        ins=[cc_in.ap().opt()], outs=[cc_out.ap().opt()],
    )
    tok_cc = E.inc(cc_inst, "g", 1)
    tcs = E.dma("dg", "b_ccout", cs_col[0:64, 0:1], cc_out.ap(),
                waits=(tok_cc,), allow_slow_non_contiguous=True)
    tgt = E.op("v", "tensor_single_scalar", cs_gt[0:64, 0:1],
               cs_col[0:64, 0:1], THR, ALU.is_gt, waits=(tcs,))
    tas = E.dma("dg", "b_asg", asg, assign.partition_broadcast(64),
                waits=(E.tok("t"), E.tok("v"), E.tok("s")))
    taf = E.op("g", "tensor_copy", asgf, asg, waits=(tas,))
    toh = E.op("v", "tensor_scalar", ohcs, asgf, iota64[0:64, 0:1],
               cs_gt[0:64, 0:1], ALU.is_equal, ALU.mult, waits=(taf, tgt))
    rmask_done = []
    for n in range(8):
        w = [toh]
        tmm = E.op("t", "matmul", PSB[:, 1536:2048],
                   ones64[0:64, :], ohcs[0:64, n * 512:(n + 1) * 512],
                   start=True, stop=True,
                   waits=w + ([rmask_done[n - 1]] if n >= 1 else []))
        tev = E.op("v", "tensor_copy", rmask[:, n * 512:(n + 1) * 512],
                   PSB[:, 1536:2048], waits=(tmm,))
        rmask_done.append(tev)
    tok_rmask = rmask_done[-1]

    E.wait("v", E.tok("s"))
    piece_free = [None] * 4
    b0_free = [None, None]
    psa_free = [None] * 4
    tok_x3_free = tok_rmask
    for grp in range(2):
        tiles = [grp * 4 + i for i in range(4)]
        dens_done = [[None] * 8 for _ in range(4)]
        for n in range(8):
            c0 = n * 512
            for k in range(NKC):
                buf = k % 4
                tw = E.dma("dg", f"b_w{buf}", wpc[:, buf * 512:(buf + 1) * 512],
                           w0[k * 128:(k + 1) * 128, c0:c0 + 512],
                           waits=(piece_free[buf],))
                for i in range(4):
                    m = tiles[i]
                    w = [tw]
                    if k == 0:
                        w += [psa_free[i]]
                    lhs = lnbf[:, m * DM + k * 128:m * DM + (k + 1) * 128]
                    rhs = wpc[:, buf * 512:(buf + 1) * 512]
                    psl = PSA[:, i * 512:(i + 1) * 512]
                    if i == 3:
                        piece_free[buf] = E.op("t", "matmul", psl, lhs, rhs,
                                               start=(k == 0), stop=False,
                                               waits=w)
                    else:
                        E.op_("t", "matmul", psl, lhs, rhs,
                              start=(k == 0), stop=False, waits=w)
            bb = n % 2
            tb0 = E.dma("dg", f"b_b0{bb}", b0pc[0:1, bb * 512:bb * 512 + 512],
                        b0[0:1, c0:c0 + 512], waits=(b0_free[bb],))
            for i in range(4):
                m = tiles[i]
                tok_done = E.op("t", "matmul", PSA[:, i * 512:(i + 1) * 512],
                                c.ones_bf[0:1, :], b0pc[0:1, bb * 512:bb * 512 + 512],
                                start=False, stop=True, waits=(tb0,))
                if i == 3:
                    b0_free[bb] = tok_done
                tev = E.op("v", "scalar_tensor_tensor",
                           dens[:, i * DM + c0:i * DM + c0 + 512],
                           PSA[:, i * 512:(i + 1) * 512],
                           qmask[:, m:m + 1], rmask[:, c0:c0 + 512],
                           ALU.mult, ALU.mult, waits=(tok_done, tok_rmask))
                psa_free[i] = tev
                dens_done[i][n] = tev
        tok_out_last = None
        for i in range(4):
            m = tiles[i]
            dview = dens[:, i * DM:(i + 1) * DM]
            t1, t2 = ln_stats(E, dview, p_s1[:, m:m + 1], p_s2[:, m:m + 1],
                              sqs3, waits=tuple(dens_done[i]))
            tr, tn = ln_chain(E, p_s1[:, m:m + 1], p_s2[:, m:m + 1],
                              p_r[:, m:m + 1], p_nmr[:, m:m + 1],
                              p_tmp[:, m:m + 1], DM, t1, t2)
            tx = E.dma("dy", "b_x3", x_st, x1i[m * 128:(m + 1) * 128, :],
                       waits=(tok_x3_free,))
            tap = E.op("v", "tensor_scalar", dview, dview, p_r[:, m:m + 1],
                       p_nmr[:, m:m + 1], ALU.mult, ALU.add, waits=(tr, tn))
            tadd = E.op("v", "tensor_tensor", dview, dview, x_st, ALU.add,
                        waits=(tap, tx))
            tok_x3_free = tadd
            tok_out_last = E.dma("dy", "b_out",
                                 x2_out[m * 128:(m + 1) * 128, :],
                                 dview, waits=(tadd,))
        E.wait("v", tok_out_last)
        E.wait("s", tok_out_last)
    E.barrier()


def emit_l1_attn(c, x2i, cw, cb, wq, wk, wv, bqc, bkc, bvr, wo2, bo2, pw, pb,
                 halo_sel, halo_mask, hg_in, hg_out, x3_out):
    nc, E, ar = c.nc, c.E, c.ar
    PSA, PSB = c.PSA, c.PSB
    ident, ones = c.ident, c.ones_bf
    s1a, s2a, ra, nmra, tmpa = c.s1a, c.s2a, c.ra, c.nmra, c.tmpa
    p_s1, p_s2, p_r, p_nmr, p_tmp = c.p_s1, c.p_s2, c.p_r, c.p_nmr, c.p_tmp
    nm, rsum, rrec = c.nm, c.rsum, c.rrec
    CW = 1032
    NH = 16
    SCALE = 0.125
    NEG_INF = -1e9

    ar.reset()
    cmask = ar.take(128, 128, F32)
    qkv_tm = ar.take(128, 8 * 3072, BF16)
    ao_tm = ar.take(128, 8 * 1024, BF16)
    bqc_sb = ar.take(128, 8, F32)
    bkc_sb = ar.take(128, 8, F32)
    hmask_sb = ar.take(2, 1, F32)
    hsel_sb = ar.take(16, 2, F32)
    mseg = ar.mark()
    x_st = ar.take(128, DM, F32)
    sqs = ar.take(128, DM, F32)
    ln_tm = ar.take(128, DM, BF16)
    ln3fm = ar.take(128, NKC * CW, BF16)
    cpc = ar.take(128, 4 * 512, BF16)
    cbr = ar.take(1, 3072, BF16)
    hg_sb = x_st[0:16, :]
    lnh = ln_tm[0:2, :]
    x2h = qkv_tm[0:2, 0:8192].bitcast(F32)
    segA_end = ar.off
    ar.rewind(mseg)
    wq_sb = ar.take(128, 8 * 1024, BF16)
    wk_sb = ar.take(128, 8 * 1024, BF16)
    wv_sb = ar.take(128, 8 * 1024, BF16)
    wo2_sb = ar.take(128, 8 * 1024, BF16)
    q_fm = ar.take(128, 1024, BF16)
    k_fm = ar.take(128, 1024, BF16)
    v_fm = ar.take(128, 1024, BF16)
    qh_fm = ar.take(128, 1024, BF16)
    kh_fm = ar.take(128, 1024, BF16)
    vh_tm = ar.take(128, 1024, BF16)
    s_sb = ar.take(128, 128, F32)
    P = ar.take(128, 128, BF16)
    PT_sb = ar.take(128, 128, BF16)
    o_tm = ar.take(128, 1024, BF16)
    o_fm = ar.take(128, 1024, BF16)
    bvr_sb = ar.take(1, 1024, BF16)
    bo2_sb = ar.take(1, 1024, BF16)
    segB_end = ar.off
    ar.rewind(mseg)
    at_fm = ar.take(128, 4 * 1024, BF16)
    dens = ar.take(128, 4 * DM, F32)
    x_st4 = ar.take(128, DM, F32)
    sqs4 = ar.take(128, DM, F32)
    ppc = ar.take(128, 4 * 512, BF16)
    pbr = ar.take(1, DM, BF16)
    ar.off = max(ar.off, segA_end, segB_end)

    TRB = TRot(E, c.TVIEWS, ident)

    tcm = E.op("g", "memset", cmask[:, :], 0.0)
    E.op("g", "affine_select", out=cmask[:, :], in_=cmask[:, :],
         compare_op=ALU.is_ge, fill=NEG_INF, base=0,
         pattern=[[-1, 128]], channel_multiplier=1, waits=(tcm,))
    tok_cb = E.dma("dg", "c_cbias", cbr, cb)
    tok_hm = E.dma("dy", "c_hmask", hmask_sb[0:2, 0:1], halo_mask,
                   allow_slow_non_contiguous=True)
    tok_hs = E.dma("dy", "c_hsel", hsel_sb[0:16, 0:2], halo_sel,
                   allow_slow_non_contiguous=True)

    tok_xst_free = None
    tok_lntm_free = None
    for m in range(8):
        r0 = m * 128
        td = E.dma("dy", "c_x", x_st, x2i[r0:r0 + 128, :],
                   waits=(tok_xst_free,))
        t1, t2 = ln_stats(E, x_st, s1a[:, m:m + 1], s2a[:, m:m + 1], sqs,
                          waits=(td,))
        tr, tn = ln_chain(E, s1a[:, m:m + 1], s2a[:, m:m + 1],
                          ra[:, m:m + 1], nmra[:, m:m + 1],
                          tmpa[:, m:m + 1], DM, t1, t2)
        tok_ap = E.op("v", "tensor_scalar", ln_tm, x_st, ra[:, m:m + 1],
                      nmra[:, m:m + 1], ALU.mult, ALU.add,
                      waits=(tr, tn, tok_lntm_free))
        tok_xst_free = tok_ap
        last_tt = None
        for ch in range(NKC):
            tt, te = TRB.go(ln_tm[:, ch * 128:(ch + 1) * 128],
                            ln3fm[:, ch * CW + r0 + 1:ch * CW + r0 + 129],
                            waits=(tok_ap,))
            last_tt = tt
        tok_lntm_free = last_tt

    th1 = E.dma("dg", "c_hgin", hg_in.ap()[0:1, :], x2i[0:1, :])
    th2 = E.dma("dg", "c_hgin", hg_in.ap()[1:2, :], x2i[1023:1024, :])
    E.wait("g", th2)
    cc_inst = nc.gpsimd.collective_compute(
        "AllGather", ALU.bypass,
        replica_groups=[list(range(8))],
        ins=[hg_in.ap().opt()], outs=[hg_out.ap().opt()],
    )
    tok_ag = E.inc(cc_inst, "g", 1)
    thg = E.dma("dg", "c_hg", hg_sb[0:16, :], hg_out.ap(),
                waits=(tok_ag, tok_xst_free))
    ev_h = []
    for n in range(8):
        sl = n % 4
        w = [thg, tok_hs]
        tmm = E.op("t", "matmul", PSA[0:2, sl * 512:sl * 512 + 512],
                   hsel_sb[0:16, 0:2], hg_sb[0:16, n * 512:(n + 1) * 512],
                   start=True, stop=True,
                   waits=w + ([ev_h[n - 4]] if n >= 4 else []))
        tev = E.op("v", "tensor_copy", x2h[0:2, n * 512:(n + 1) * 512],
                   PSA[0:2, sl * 512:sl * 512 + 512], waits=(tmm,))
        ev_h.append(tev)
    t1, t2 = ln_stats(E, x2h[0:2, :], s1a[0:2, 8:9], s2a[0:2, 8:9],
                      sqs[0:2, :], waits=tuple(ev_h))
    tr, tn = ln_chain(E, s1a[0:2, 8:9], s2a[0:2, 8:9], ra[0:2, 8:9],
                      nmra[0:2, 8:9], tmpa[0:2, 8:9], DM, t1, t2)
    tok_lnh = E.op("v", "tensor_scalar", lnh[0:2, :], x2h[0:2, :],
                   ra[0:2, 8:9], nmra[0:2, 8:9], ALU.mult, ALU.add,
                   waits=(tr, tn, tok_lntm_free))
    tok_lnhm = E.op("v", "tensor_scalar", lnh[0:2, :], lnh[0:2, :],
                    hmask_sb[0:2, 0:1], None, ALU.mult,
                    waits=(tok_lnh, tok_hm))
    for ch in range(NKC):
        i2 = ch % 2
        po = c.TVIEWS[i2][0:128, 0:2]
        fr = TRB.free[i2]
        fr = () if fr is None else ((fr,) if isinstance(fr[0], str) else fr)
        tt = E.op("t", "transpose", po, lnh[0:2, ch * 128:(ch + 1) * 128],
                  ident[0:2, 0:2], waits=(tok_lnhm,) + fr)
        e1 = E.op("v", "tensor_copy", ln3fm[:, ch * CW:ch * CW + 1],
                  po[:, 0:1], waits=(tt,))
        e2 = E.op("s", "copy", ln3fm[:, ch * CW + 1025:ch * CW + 1026],
                  po[:, 1:2], waits=(tt,))
        TRB.free[i2] = (e1, e2)
    ph1_done = (E.tok("v"), E.tok("s"))
    def conv_slice(m):
        if m < 4:
            return PSA[:, m * 512:(m + 1) * 512]
        return PSB[:, (m - 4) * 512:(m - 3) * 512]

    piece_free = [None] * 4
    slice_free = [None] * 8
    for n in range(6):
        c0 = n * 512
        step = 0
        for k in range(NKC):
            for tap in range(3):
                buf = step % 4
                step += 1
                tw = E.dma("dg", f"c_cw{buf}", cpc[:, buf * 512:(buf + 1) * 512],
                           cw[tap][k * 128:(k + 1) * 128, c0:c0 + 512],
                           waits=(piece_free[buf],))
                for m in range(8):
                    w = [tw]
                    if step == 1:
                        w += list(ph1_done) + [slice_free[m]]
                    lhs = ln3fm[:, k * CW + m * 128 + tap:
                                k * CW + m * 128 + tap + 128]
                    rhs = cpc[:, buf * 512:(buf + 1) * 512]
                    tok = E.op("t", "matmul", conv_slice(m), lhs, rhs,
                               start=(step == 1), stop=False, waits=w)
                    if m == 7:
                        piece_free[buf] = tok
        for m in range(8):
            tok_done = E.op("t", "matmul", conv_slice(m), ones[0:1, :],
                            cbr[0:1, c0:c0 + 512], start=False, stop=True,
                            waits=(tok_cb,))
            eng = "v" if m % 2 == 0 else "s"
            meth = "tensor_copy" if eng == "v" else "copy"
            slice_free[m] = E.op(eng, meth,
                                 qkv_tm[:, m * 3072 + c0:m * 3072 + c0 + 512],
                                 conv_slice(m), waits=(tok_done,))
    E.barrier()

    tok_wq = E.dma("dg", "c_wq", wq_sb[:, :].rearrange("p (c n) -> p c n", n=1024),
                   wq.rearrange("(c p) n -> p c n", p=128))
    tok_wk = E.dma("dg", "c_wk", wk_sb[:, :].rearrange("p (c n) -> p c n", n=1024),
                   wk.rearrange("(c p) n -> p c n", p=128))
    tok_wv = E.dma("dg", "c_wv", wv_sb[:, :].rearrange("p (c n) -> p c n", n=1024),
                   wv.rearrange("(c p) n -> p c n", p=128))
    tok_wo2 = E.dma("dg", "c_wo2", wo2_sb[:, :].rearrange("p (c n) -> p c n", n=1024),
                    wo2.rearrange("(c p) n -> p c n", p=128))
    tok_bqc = E.dma("dy", "c_bqc", bqc_sb[:, :], bqc)
    tok_bkc = E.dma("dy", "c_bkc", bkc_sb[:, :], bkc)
    tok_bvr = E.dma("dg", "c_bvr", bvr_sb, bvr)
    tok_bo2 = E.dma("dg", "c_bo2", bo2_sb, bo2)

    PH = [PSA[:, 0:128], PSA[:, 512:640]]
    PV = [PSA[:, 1024:1536], PSA[:, 1536:2048]]
    PS = [PSB[:, 1024:1152], PSB[:, 1536:1664]]
    ph_free = [None, None]
    pv_free = [None, None]
    ps_free = [None, None]

    for m in range(8):
        base = m * 3072
        qf = []
        for c3 in range(8):
            tt, te = TRB.go(qkv_tm[:, base + c3 * 128:base + (c3 + 1) * 128],
                            q_fm[:, c3 * 128:(c3 + 1) * 128], waits=())
        for c3 in range(8):
            tt, te = TRB.go(qkv_tm[:, base + 1024 + c3 * 128:base + 1024 + (c3 + 1) * 128],
                            k_fm[:, c3 * 128:(c3 + 1) * 128], waits=())
        for c3 in range(8):
            tt, te = TRB.go(qkv_tm[:, base + 2048 + c3 * 128:base + 2048 + (c3 + 1) * 128],
                            v_fm[:, c3 * 128:(c3 + 1) * 128], waits=())
        fm_ready = (E.tok("v"), E.tok("s"))
        for (wsb, bias_sb, dst, tok_w, tok_b) in (
                (wq_sb, bqc_sb, qh_fm, tok_wq, tok_bqc),
                (wk_sb, bkc_sb, kh_fm, tok_wk, tok_bkc)):
            for hc in range(8):
                pview = PH[hc % 2]
                tok_mm = None
                for wc in range(8):
                    w = [tok_w] + list(fm_ready)
                    if wc == 0:
                        w.append(ph_free[hc % 2])
                    if wc == 7:
                        tok_mm = E.op("t", "matmul", pview[:, 0:128],
                                      wsb[:, wc * 1024 + hc * 128:wc * 1024 + (hc + 1) * 128],
                                      (q_fm if dst is qh_fm else k_fm)[:, wc * 128:(wc + 1) * 128],
                                      start=(wc == 0), stop=(wc == 7), waits=w)
                    else:
                        E.op_("t", "matmul", pview[:, 0:128],
                              wsb[:, wc * 1024 + hc * 128:wc * 1024 + (hc + 1) * 128],
                              (q_fm if dst is qh_fm else k_fm)[:, wc * 128:(wc + 1) * 128],
                              start=(wc == 0), stop=(wc == 7), waits=w)
                tok_ev = E.op("v", "tensor_scalar_add",
                              dst[:, hc * 128:(hc + 1) * 128],
                              pview[:, 0:128], bias_sb[:, hc:hc + 1],
                              waits=(tok_mm, tok_b))
                ph_free[hc % 2] = tok_ev
        for n2 in range(2):
            pview = PV[n2]
            tok_mm = None
            for wc in range(8):
                w = [tok_wv] + list(fm_ready)
                if wc == 0:
                    w.append(pv_free[n2])
                E.op_("t", "matmul", pview,
                      v_fm[:, wc * 128:(wc + 1) * 128],
                      wv_sb[:, wc * 1024 + n2 * 512:wc * 1024 + n2 * 512 + 512],
                      start=(wc == 0), stop=False, waits=w)
            tok_mm = E.op("t", "matmul", pview, ones[0:1, :],
                          bvr_sb[0:1, n2 * 512:n2 * 512 + 512],
                          start=False, stop=True, waits=(tok_bvr,))
            tok_ev = E.op("s", "copy", vh_tm[:, n2 * 512:n2 * 512 + 512],
                          pview, waits=(tok_mm,))
            pv_free[n2] = tok_ev
        vh_ready = pv_free[:]
        qh_ready = (E.tok("v"),)
        tok_o_ev = [None, None]
        tok_exp_prev = None
        tok_ptt_prev = None
        for h in range(NH):
            hc = h // 2
            pb_ = (h % 2) * 64
            psc = PS[h % 2]
            tok_sc = E.op("t", "matmul", psc[:, 0:128],
                          qh_fm[pb_:pb_ + 64, hc * 128:(hc + 1) * 128],
                          kh_fm[pb_:pb_ + 64, hc * 128:(hc + 1) * 128],
                          start=True, stop=True,
                          waits=list(qh_ready) + [ps_free[h % 2]])
            tok_stt = E.op("v", "scalar_tensor_tensor", s_sb[:, :],
                           psc[:, 0:128], SCALE, cmask[:, :],
                           ALU.mult, ALU.add, waits=(tok_sc, tok_exp_prev))
            ps_free[h % 2] = tok_stt
            tok_mx = E.op("v", "reduce_max", nm[:, :], s_sb[:, :], axis=AX,
                          negate=True, waits=(tok_stt,))
            tok_exp = E.op("s", "activation", P, s_sb[:, :], AF.Exp,
                           bias=nm[:, 0:1], scale=1.0,
                           accum_out=rsum[:, 0:1],
                           waits=(tok_mx, tok_ptt_prev))
            tok_exp_prev = tok_exp
            tok_rr = E.op("v", "reciprocal", rrec[:, :], rsum[:, :],
                          waits=(tok_exp,))
            tt, te = TRB.go(P, PT_sb, waits=(tok_exp,))
            tok_ptt_prev = tt
            tok_pv = E.op("t", "matmul", PH[h % 2][:, 0:64], PT_sb,
                          vh_tm[:, h * 64:(h + 1) * 64],
                          start=True, stop=True,
                          waits=[te, ph_free[h % 2]] + vh_ready)
            tok_oev = E.op("s", "activation", o_tm[:, h * 64:(h + 1) * 64],
                           PH[h % 2][:, 0:64], AF.Copy, scale=rrec[:, 0:1],
                           waits=(tok_pv, tok_rr))
            ph_free[h % 2] = tok_oev
            tok_o_ev[h % 2] = tok_oev
        for c3 in range(8):
            tt, te = TRB.go(o_tm[:, c3 * 128:(c3 + 1) * 128],
                            o_fm[:, c3 * 128:(c3 + 1) * 128],
                            waits=[tok_o_ev[0], tok_o_ev[1]])
        ofm_ready = (E.tok("v"), E.tok("s"))
        for n2 in range(2):
            pview = PV[n2]
            tok_mm = None
            for hc in range(8):
                w = [tok_wo2] + list(ofm_ready)
                if hc == 0:
                    w.append(pv_free[n2])
                E.op_("t", "matmul", pview,
                      o_fm[:, hc * 128:(hc + 1) * 128],
                      wo2_sb[:, hc * 1024 + n2 * 512:hc * 1024 + n2 * 512 + 512],
                      start=(hc == 0), stop=False, waits=w)
            tok_mm = E.op("t", "matmul", pview, ones[0:1, :],
                          bo2_sb[0:1, n2 * 512:n2 * 512 + 512],
                          start=False, stop=True, waits=(tok_bo2,))
            tok_ev = E.op("v", "tensor_copy",
                          ao_tm[:, m * 1024 + n2 * 512:m * 1024 + n2 * 512 + 512],
                          pview, waits=(tok_mm,))
            pv_free[n2] = tok_ev
    E.barrier()

    tok_pb = E.dma("dg", "c_pbias", pbr, pb)
    piece_free = [None] * 4
    psa_free = [None] * 4
    tok_x4_free = None
    for grp in range(2):
        tiles = [grp * 4 + i for i in range(4)]
        for i, m in enumerate(tiles):
            for c3 in range(8):
                TRB.go(ao_tm[:, m * 1024 + c3 * 128:m * 1024 + (c3 + 1) * 128],
                       at_fm[:, i * 1024 + c3 * 128:i * 1024 + (c3 + 1) * 128],
                       waits=())
        at_ready = (E.tok("v"), E.tok("s"))
        dens_done = [[None] * 8 for _ in range(4)]
        for n in range(8):
            c0 = n * 512
            for k in range(8):
                buf = k % 4
                tw = E.dma("dg", f"c_pw{buf}", ppc[:, buf * 512:(buf + 1) * 512],
                           pw[k * 128:(k + 1) * 128, c0:c0 + 512],
                           waits=(piece_free[buf],))
                for i in range(4):
                    w = [tw]
                    if k == 0:
                        w += list(at_ready) + [psa_free[i]]
                    if i == 3:
                        piece_free[buf] = E.op("t", "matmul",
                                               PSA[:, i * 512:(i + 1) * 512],
                                               at_fm[:, i * 1024 + k * 128:i * 1024 + (k + 1) * 128],
                                               ppc[:, buf * 512:(buf + 1) * 512],
                                               start=(k == 0), stop=False,
                                               waits=w)
                    else:
                        E.op_("t", "matmul", PSA[:, i * 512:(i + 1) * 512],
                              at_fm[:, i * 1024 + k * 128:i * 1024 + (k + 1) * 128],
                              ppc[:, buf * 512:(buf + 1) * 512],
                              start=(k == 0), stop=False, waits=w)
            for i in range(4):
                m = tiles[i]
                tok_done = E.op("t", "matmul", PSA[:, i * 512:(i + 1) * 512],
                                ones[0:1, :], pbr[0:1, c0:c0 + 512],
                                start=False, stop=True, waits=(tok_pb,))
                eng = "v" if i % 2 == 0 else "s"
                meth = "tensor_copy" if eng == "v" else "copy"
                tev = E.op(eng, meth, dens[:, i * DM + c0:i * DM + c0 + 512],
                           PSA[:, i * 512:(i + 1) * 512], waits=(tok_done,))
                psa_free[i] = tev
                dens_done[i][n] = tev
        tok_out_last = None
        for i in range(4):
            m = tiles[i]
            dview = dens[:, i * DM:(i + 1) * DM]
            t1, t2 = ln_stats(E, dview, p_s1[:, m:m + 1], p_s2[:, m:m + 1],
                              sqs4, waits=tuple(dens_done[i]))
            tr, tn = ln_chain(E, p_s1[:, m:m + 1], p_s2[:, m:m + 1],
                              p_r[:, m:m + 1], p_nmr[:, m:m + 1],
                              p_tmp[:, m:m + 1], DM, t1, t2)
            tx = E.dma("dy", "c_x4", x_st4, x2i[m * 128:(m + 1) * 128, :],
                       waits=(tok_x4_free,))
            tap = E.op("v", "tensor_scalar", dview, dview, p_r[:, m:m + 1],
                       p_nmr[:, m:m + 1], ALU.mult, ALU.add,
                       waits=(tr, tn))
            tadd = E.op("v", "tensor_tensor", dview, dview, x_st4, ALU.add,
                        waits=(tap, tx))
            tok_x4_free = tadd
            tok_out_last = E.dma("dy", "c_out", x3_out[m * 128:(m + 1) * 128, :],
                                 dview, waits=(tadd,))
        E.wait("v", tok_out_last)
        E.wait("s", tok_out_last)

    E.barrier()


def emit_l1_ffn(c, x3i, w1, w3, w2, a1, b1, a3, b3, a2, b2, out_ap):
    nc, E, ar = c.nc, c.E, c.ar
    PSA, PSB = c.PSA, c.PSB
    ident, ones = c.ident, c.ones
    s1a, s2a, ra, nmra, tmpa = c.s1a, c.s2a, c.ra, c.nmra, c.tmpa
    p_s1, p_s2, p_r, p_nmr, p_tmp = c.p_s1, c.p_s2, c.p_r, c.p_nmr, c.p_tmp
    HID = 11008
    NHC = 86
    HCS = [1024] * 10 + [768]
    SCALING = 2.0

    def spans(L):
        return [(0, 512), (512, L)] if L > 512 else [(0, L)]

    ar.reset()
    a1_sb = ar.take(128, 32 * 16, BF16)
    a3_sb = ar.take(128, 32 * 16, BF16)
    a2_sb = ar.take(128, NHC * 16, BF16)
    b2_sb = ar.take(16, DM, BF16)
    x_st = ar.take(128, DM, F32)
    ln_tm = ar.take(128, DM, BF16)
    lnfm = ar.take(128, 4 * DM, BF16)
    dens = ar.take(128, 4 * DM, F32)
    sqs = dens[:, 0:DM]
    sq_post = lnfm[:, 0:2 * DM].bitcast(F32)
    h1_sb = ar.take(128, 4 * 1024, BF16)
    sl_sb = ar.take(128, 4 * 1024, BF16)
    g_sb = ar.take(128, 4 * 1024, BF16)
    g_fm = ar.take(128, 4 * 1024, BF16)
    wpc = ar.take(128, 4 * 1024, BF16)
    w2pc = ar.take(128, 4 * 1024, BF16)
    b1s = ar.take(16, 1024, BF16)
    b3s = ar.take(16, 1024, BF16)
    xa_bf = ar.take(128, 16, BF16)
    xa1_fm = ar.take(16, 4 * 128, BF16)
    xa3_fm = ar.take(16, 4 * 128, BF16)
    ga2_sb = ar.take(128, 4 * 16, F32)
    ga2_bf = ar.take(128, 16, BF16)
    ga2_fm = ar.take(16, 4 * 128, BF16)

    TVIEWS = c.TVIEWS
    PH = [PSA[:, 0:1024], PSA[:, 1024:2048],
          PSB[:, 0:1024], PSB[:, 1024:2048]]
    PGA = PSA[:, 0:16]

    TR = TRot(E, TVIEWS, ident)

    E.dma("dg", "d_a1", a1_sb[:, :].rearrange("p (c n) -> p c n", n=16),
          a1.rearrange("(c p) n -> p c n", p=128))
    E.dma("dg", "d_a3", a3_sb[:, :].rearrange("p (c n) -> p c n", n=16),
          a3.rearrange("(c p) n -> p c n", p=128))
    E.dma("dg", "d_a2", a2_sb[:, :].rearrange("p (c n) -> p c n", n=16),
          a2.rearrange("(c p) n -> p c n", p=128))
    E.dma("dg", "d_b2", b2_sb[0:16, :], b2)
    tok_a1 = E.tok("D:d_a1")
    tok_a3 = E.tok("D:d_a3")
    tok_a2 = E.tok("D:d_a2")
    tok_b2 = E.tok("D:d_b2")
    E.barrier()

    tok_xst_free = None
    tok_lntm_free = None
    tok_out_last = None
    tok_b1_used = None
    tok_b3_used = None
    for grp in range(2):
        tiles = [grp * 4 + i for i in range(4)]
        for i, m in enumerate(tiles):
            td = E.dma("dy", "d_x", x_st, x3i[m * 128:(m + 1) * 128, :],
                       waits=(tok_xst_free,))
            t1, t2 = ln_stats(E, x_st, s1a[:, m:m + 1], s2a[:, m:m + 1], sqs,
                              waits=(td,))
            tr, tn = ln_chain(E, s1a[:, m:m + 1], s2a[:, m:m + 1],
                              ra[:, m:m + 1], nmra[:, m:m + 1],
                              tmpa[:, m:m + 1], DM, t1, t2)
            tok_ap = E.op("v", "tensor_scalar", ln_tm, x_st, ra[:, m:m + 1],
                          nmra[:, m:m + 1], ALU.mult, ALU.add,
                          waits=(tr, tn, tok_lntm_free))
            tok_xst_free = tok_ap
            last_tt = None
            for c in range(NKC):
                tt, te = TR.go(ln_tm[:, c * 128:(c + 1) * 128],
                               lnfm[:, i * DM + c * 128:i * DM + (c + 1) * 128],
                               waits=(tok_ap,))
                last_tt = tt
            tok_lntm_free = last_tt
        lnfm_ready = (E.tok("v"), E.tok("s"))

        tok_pga_free = None
        for (asb, xfm, tok_a) in ((a1_sb, xa1_fm, tok_a1),
                                  (a3_sb, xa3_fm, tok_a3)):
            for i in range(4):
                tok_mm = None
                for k in range(NKC):
                    w = list(lnfm_ready) + [tok_a]
                    if k == 0:
                        w.append(tok_pga_free)
                    tok_mm = E.op("t", "matmul", PGA,
                                  lnfm[:, i * DM + k * 128:i * DM + (k + 1) * 128],
                                  asb[:, k * 16:(k + 1) * 16],
                                  start=(k == 0), stop=(k == 31), waits=w)
                tok_ev = E.op("s", "activation", xa_bf[:, :], PGA, AF.Copy,
                              scale=SCALING, waits=(tok_mm,))
                tok_pga_free = tok_ev
                tt, te = TR.go(xa_bf[:, :], xfm[0:16, i * 128:(i + 1) * 128],
                               waits=(tok_ev,))
        xa_ready = (E.tok("v"), E.tok("s"))
        tok_ga2_z = E.op("v", "memset", ga2_sb[:, :], 0.0)
        tok_ga2 = [tok_ga2_z] * 4

        E.wait("v", E.tok("s"))
        dens_tok = [[None] * 4 for _ in range(4)]
        ph_free = [xa_ready] * 4
        piece_free = [None] * 4
        w2_free = [None] * 4
        hc0 = 0
        for ci, L in enumerate(HCS):
            khs = L // 128
            tb1 = E.dma("dg", "d_b1s", b1s[0:16, 0:L], b1[0:16, hc0:hc0 + L],
                        waits=(tok_b1_used,))
            h1_done = [None] * 4
            for k in range(NKC):
                buf = k % 4
                tw = E.dma("dg", f"d_wp{buf}", wpc[:, buf * 1024:buf * 1024 + L],
                           w1[k * 128:(k + 1) * 128, hc0:hc0 + L],
                           waits=(piece_free[buf],))
                for i in range(4):
                    w = [tw]
                    if k == 0:
                        w += list(lnfm_ready) + list(ph_free[i])
                    for si, (s0, s1_) in enumerate(spans(L)):
                        last = (i == 3 and si == len(spans(L)) - 1)
                        if last:
                            piece_free[buf] = E.op(
                                "t", "matmul", PH[i][:, s0:s1_],
                                lnfm[:, i * DM + k * 128:i * DM + (k + 1) * 128],
                                wpc[:, buf * 1024 + s0:buf * 1024 + s1_],
                                start=(k == 0), stop=False, waits=w)
                        else:
                            E.op_("t", "matmul", PH[i][:, s0:s1_],
                                  lnfm[:, i * DM + k * 128:i * DM + (k + 1) * 128],
                                  wpc[:, buf * 1024 + s0:buf * 1024 + s1_],
                                  start=(k == 0), stop=False, waits=w)
            for i in range(4):
                tok = None
                for (s0, s1_) in spans(L):
                    tok = E.op("t", "matmul", PH[i][:, s0:s1_],
                               xa1_fm[0:16, i * 128:(i + 1) * 128],
                               b1s[0:16, s0:s1_],
                               start=False, stop=True,
                               waits=[tb1] + list(xa_ready))
                if i == 3:
                    tok_b1_used = tok
                eng = "v" if i % 2 == 0 else "s"
                meth = "tensor_copy" if eng == "v" else "copy"
                h1_done[i] = E.op(eng, meth, h1_sb[:, i * 1024:i * 1024 + L],
                                  PH[i][0:128, 0:L], waits=(tok,))
                ph_free[i] = (h1_done[i],)
            sl_done = [None] * 4
            for i in range(4):
                sl_done[i] = E.op("s", "activation",
                                  sl_sb[:, i * 1024:i * 1024 + L],
                                  h1_sb[:, i * 1024:i * 1024 + L], AF.Silu,
                                  waits=(h1_done[i],))
            tb3 = E.dma("dg", "d_b3s", b3s[0:16, 0:L], b3[0:16, hc0:hc0 + L],
                        waits=(tok_b3_used,))
            g_done = [None] * 4
            for k in range(NKC):
                buf = k % 4
                tw = E.dma("dg", f"d_wp{buf}", wpc[:, buf * 1024:buf * 1024 + L],
                           w3[k * 128:(k + 1) * 128, hc0:hc0 + L],
                           waits=(piece_free[buf],))
                for i in range(4):
                    w = [tw]
                    if k == 0:
                        w += list(ph_free[i])
                    for si, (s0, s1_) in enumerate(spans(L)):
                        last = (i == 3 and si == len(spans(L)) - 1)
                        if last:
                            piece_free[buf] = E.op(
                                "t", "matmul", PH[i][:, s0:s1_],
                                lnfm[:, i * DM + k * 128:i * DM + (k + 1) * 128],
                                wpc[:, buf * 1024 + s0:buf * 1024 + s1_],
                                start=(k == 0), stop=False, waits=w)
                        else:
                            E.op_("t", "matmul", PH[i][:, s0:s1_],
                                  lnfm[:, i * DM + k * 128:i * DM + (k + 1) * 128],
                                  wpc[:, buf * 1024 + s0:buf * 1024 + s1_],
                                  start=(k == 0), stop=False, waits=w)
            for i in range(4):
                tok = None
                for (s0, s1_) in spans(L):
                    tok = E.op("t", "matmul", PH[i][:, s0:s1_],
                               xa3_fm[0:16, i * 128:(i + 1) * 128],
                               b3s[0:16, s0:s1_],
                               start=False, stop=True,
                               waits=[tb3] + list(xa_ready))
                if i == 3:
                    tok_b3_used = tok
                g_done[i] = E.op("v", "tensor_tensor",
                                 g_sb[:, i * 1024:i * 1024 + L],
                                 PH[i][0:128, 0:L],
                                 sl_sb[:, i * 1024:i * 1024 + L],
                                 ALU.mult, waits=(tok, sl_done[i]))
                ph_free[i] = (g_done[i],)
            gfm_tok = [None] * 4
            for i in range(4):
                for kc in range(khs):
                    TR.go(g_sb[:, i * 1024 + kc * 128:i * 1024 + (kc + 1) * 128],
                          g_fm[:, i * 1024 + kc * 128:i * 1024 + (kc + 1) * 128],
                          waits=tuple(g_done))
                gfm_tok[i] = (E.tok("v"), E.tok("s"))
            for i in range(4):
                tok = None
                for kc in range(khs):
                    w = list(gfm_tok[i]) + [tok_a2]
                    if kc == 0:
                        w += [tok_pga_free] + list(ph_free[0])
                    kglob = (hc0 // 128) + kc
                    tok = E.op("t", "matmul", PGA,
                               g_fm[:, i * 1024 + kc * 128:i * 1024 + (kc + 1) * 128],
                               a2_sb[:, kglob * 16:(kglob + 1) * 16],
                               start=(kc == 0), stop=(kc == khs - 1), waits=w)
                tok_ga2[i] = E.op("v", "tensor_tensor",
                                  ga2_sb[:, i * 16:(i + 1) * 16],
                                  ga2_sb[:, i * 16:(i + 1) * 16], PGA,
                                  ALU.add, waits=(tok, tok_ga2[i]))
                tok_pga_free = tok_ga2[i]
            for dmq in range(4):
                c0 = dmq * 1024
                tokmm = [None] * 4
                for kc in range(khs):
                    buf = kc % 4
                    kglob = (hc0 // 128) + kc
                    tw2 = E.dma("dg", f"d_w2{buf}",
                                w2pc[:, buf * 1024:(buf + 1) * 1024],
                                w2[kglob * 128:(kglob + 1) * 128, c0:c0 + 1024],
                                waits=(w2_free[buf],))
                    for i in range(4):
                        w = [tw2] + list(gfm_tok[i])
                        if kc == 0:
                            w += [dens_tok[i][dmq], tok_pga_free] + list(ph_free[i])
                        for n5 in range(2):
                            need = (n5 == 1 and (i == 3 or kc == khs - 1))
                            if need:
                                tok = E.op("t", "matmul",
                                           PH[i][:, n5 * 512:(n5 + 1) * 512],
                                           g_fm[:, i * 1024 + kc * 128:i * 1024 + (kc + 1) * 128],
                                           w2pc[:, buf * 1024 + n5 * 512:buf * 1024 + (n5 + 1) * 512],
                                           start=(kc == 0), stop=(kc == khs - 1),
                                           waits=w)
                                if kc == khs - 1:
                                    tokmm[i] = tok
                                if i == 3:
                                    w2_free[buf] = tok
                            else:
                                E.op_("t", "matmul",
                                      PH[i][:, n5 * 512:(n5 + 1) * 512],
                                      g_fm[:, i * 1024 + kc * 128:i * 1024 + (kc + 1) * 128],
                                      w2pc[:, buf * 1024 + n5 * 512:buf * 1024 + (n5 + 1) * 512],
                                      start=(kc == 0), stop=(kc == khs - 1),
                                      waits=w)
                for i in range(4):
                    eng = "v" if i % 2 == 0 else "s"
                    dsl = dens[:, i * DM + c0:i * DM + c0 + 1024]
                    if ci == 0:
                        meth = "tensor_copy" if eng == "v" else "copy"
                        tev = E.op(eng, meth, dsl, PH[i],
                                   waits=(tokmm[i], dens_tok[i][dmq]))
                    else:
                        tev = E.op("v", "tensor_tensor", dsl, dsl, PH[i],
                                   ALU.add, waits=(tokmm[i], dens_tok[i][dmq]))
                    dens_tok[i][dmq] = tev
                    ph_free[i] = (tev,)
            hc0 += L

        all_dens = tuple(t for row in dens_tok for t in row)
        prev_tt = None
        for i in range(4):
            m = tiles[i]
            tcv = E.op("s", "activation", ga2_bf[:, :],
                       ga2_sb[:, i * 16:(i + 1) * 16],
                       AF.Copy, scale=SCALING, waits=(tok_ga2[i], prev_tt))
            tt, te = TR.go(ga2_bf[:, :], ga2_fm[0:16, i * 128:(i + 1) * 128],
                           waits=(tcv,) + all_dens)
            prev_tt = tt
            for dmq in range(4):
                tok = None
                for n5 in range(2):
                    c0 = dmq * 1024 + n5 * 512
                    tok = E.op("t", "matmul",
                               PH[i][:, n5 * 512:(n5 + 1) * 512],
                               ga2_fm[0:16, i * 128:(i + 1) * 128],
                               b2_sb[0:16, c0:c0 + 512],
                               start=True, stop=True,
                               waits=[te, tok_b2] + list(ph_free[i]))
                tev = E.op("v", "tensor_tensor",
                           dens[:, i * DM + dmq * 1024:i * DM + (dmq + 1) * 1024],
                           dens[:, i * DM + dmq * 1024:i * DM + (dmq + 1) * 1024],
                           PH[i], ALU.add, waits=(tok, dens_tok[i][dmq]))
                dens_tok[i][dmq] = tev
                ph_free[i] = (tev,)
            dview = dens[:, i * DM:(i + 1) * DM]
            t1, t2 = ln_stats(E, dview, p_s1[:, m:m + 1], p_s2[:, m:m + 1],
                              sq_post, waits=tuple(dens_tok[i]))
            tr, tn = ln_chain(E, p_s1[:, m:m + 1], p_s2[:, m:m + 1],
                              p_r[:, m:m + 1], p_nmr[:, m:m + 1],
                              p_tmp[:, m:m + 1], DM, t1, t2)
            tx = E.dma("dy", "d_x", x_st, x3i[m * 128:(m + 1) * 128, :],
                       waits=(tok_xst_free,))
            tap = E.op("v", "tensor_scalar", dview, dview, p_r[:, m:m + 1],
                       p_nmr[:, m:m + 1], ALU.mult, ALU.add, waits=(tr, tn))
            tadd = E.op("v", "tensor_tensor", dview, dview, x_st, ALU.add,
                        waits=(tap, tx))
            tok_xst_free = tadd
            tok_out_last = E.dma("dy", "d_out", out_ap[m * 128:(m + 1) * 128, :],
                                 dview, waits=(tadd,))
        E.barrier()

    E.barrier()




def build_fused():
    nc = bass.Bass("TRN2", target_bir_lowering=False)

    xo = nc.dram_tensor("xo", [1024, DM], F32, kind="ExternalInput").ap()
    xr = nc.dram_tensor("xr", [1024, DM], F32, kind="ExternalInput").ap()
    wqkv = nc.dram_tensor("wqkv", [DM, 3072], F32, kind="ExternalInput").ap()
    bqkv = nc.dram_tensor("bqkv", [1, 3072], F32, kind="ExternalInput").ap()
    wo = nc.dram_tensor("wo", [1024, DM], F32, kind="ExternalInput").ap()
    bo = nc.dram_tensor("bo", [1, DM], F32, kind="ExternalInput").ap()
    w0 = nc.dram_tensor("w0", [DM, DM], F32, kind="ExternalInput").ap()
    b0 = nc.dram_tensor("b0", [1, DM], F32, kind="ExternalInput").ap()
    cent = nc.dram_tensor("cent", [DM, 64], F32, kind="ExternalInput").ap()
    assign = nc.dram_tensor("assign", [1, DM], I32, kind="ExternalInput").ap()
    cw = [nc.dram_tensor(f"cw{t}", [DM, 3072], F32, kind="ExternalInput").ap()
          for t in range(3)]
    cb = nc.dram_tensor("cb", [1, 3072], F32, kind="ExternalInput").ap()
    wq = nc.dram_tensor("wq", [1024, 1024], F32, kind="ExternalInput").ap()
    wk = nc.dram_tensor("wk", [1024, 1024], F32, kind="ExternalInput").ap()
    wv = nc.dram_tensor("wv", [1024, 1024], F32, kind="ExternalInput").ap()
    bqc = nc.dram_tensor("bqc", [128, 8], F32, kind="ExternalInput").ap()
    bkc = nc.dram_tensor("bkc", [128, 8], F32, kind="ExternalInput").ap()
    bvr = nc.dram_tensor("bvr", [1, 1024], F32, kind="ExternalInput").ap()
    wo2 = nc.dram_tensor("wo2", [1024, 1024], F32, kind="ExternalInput").ap()
    bo2 = nc.dram_tensor("bo2", [1, 1024], F32, kind="ExternalInput").ap()
    pw = nc.dram_tensor("pw", [1024, DM], F32, kind="ExternalInput").ap()
    pb = nc.dram_tensor("pb", [1, DM], F32, kind="ExternalInput").ap()
    halo_sel = nc.dram_tensor("halo_sel", [16, 2], F32, kind="ExternalInput").ap()
    halo_mask = nc.dram_tensor("halo_mask", [2, 1], F32, kind="ExternalInput").ap()
    w1 = nc.dram_tensor("w1", [DM, 11008], F32, kind="ExternalInput").ap()
    w3 = nc.dram_tensor("w3", [DM, 11008], F32, kind="ExternalInput").ap()
    w2 = nc.dram_tensor("w2", [11008, DM], F32, kind="ExternalInput").ap()
    a1 = nc.dram_tensor("a1", [DM, 16], F32, kind="ExternalInput").ap()
    b1 = nc.dram_tensor("b1", [16, 11008], F32, kind="ExternalInput").ap()
    a3 = nc.dram_tensor("a3", [DM, 16], F32, kind="ExternalInput").ap()
    b3 = nc.dram_tensor("b3", [16, 11008], F32, kind="ExternalInput").ap()
    a2 = nc.dram_tensor("a2", [11008, 16], F32, kind="ExternalInput").ap()
    b2 = nc.dram_tensor("b2", [16, DM], F32, kind="ExternalInput").ap()
    out = nc.dram_tensor("out", [1024, DM], F32, kind="ExternalOutput").ap()

    xs1 = nc.dram_tensor("xs1", [1024, DM], F32)
    xs2 = nc.dram_tensor("xs2", [1024, DM], F32)
    xs3 = nc.dram_tensor("xs3", [1024, DM], F32)
    cc_in = nc.dram_tensor("cc_in", [64, 1], F32)
    cc_out = nc.dram_tensor("cc_out", [64, 1], F32)
    hg_in = nc.dram_tensor("hg_in", [2, DM], F32)
    hg_out = nc.dram_tensor("hg_out", [16, DM], F32)

    c = make_ctx(nc)
    c.E.barrier()
    emit_l0_attn(c, xo, xr, wqkv, bqkv, wo, bo, xs1.ap())
    emit_l0_ffn(c, xs1.ap(), w0, b0, cent, assign, cc_in, cc_out, xs2.ap())
    emit_l1_attn(c, xs2.ap(), cw, cb, wq, wk, wv, bqc, bkc, bvr, wo2, bo2,
                 pw, pb, halo_sel, halo_mask, hg_in, hg_out, xs3.ap())
    emit_l1_ffn(c, xs3.ap(), w1, w3, w2, a1, b1, a3, b3, a2, b2, out)
    c.E.wait("y", *c.E.dma_totals())
    return nc


def host_inputs_fused(d):
    x0 = np.ascontiguousarray(d["L0_attn_x"])
    w0T = np.ascontiguousarray(d["L0_ffn_weight"].T)
    centT = np.ascontiguousarray(d["L0_ffn_centroids"].T)
    cwf = d["L1_attn_conv_w"]
    shared = {
        "wqkv": np.ascontiguousarray(d["L0_attn_Wqkv"]),
        "bqkv": np.ascontiguousarray(d["L0_attn_bqkv"][None, :]),
        "wo": np.ascontiguousarray(d["L0_attn_Wo"]),
        "bo": np.ascontiguousarray(d["L0_attn_bo"][None, :]),
        "w0": w0T,
        "b0": np.ascontiguousarray(d["L0_ffn_bias"][None, :]),
        "cent": centT,
        "assign": np.ascontiguousarray(d["L0_ffn_assignments"][None, :]),
        "cw0": np.ascontiguousarray(cwf[0]),
        "cw1": np.ascontiguousarray(cwf[1]),
        "cw2": np.ascontiguousarray(cwf[2]),
        "cb": np.ascontiguousarray(d["L1_attn_conv_b"][None, :]),
        "wq": np.ascontiguousarray(d["L1_attn_wq"].reshape(1024, 1024)),
        "wk": np.ascontiguousarray(d["L1_attn_wk"].reshape(1024, 1024)),
        "wv": np.ascontiguousarray(d["L1_attn_wv"].reshape(1024, 1024)),
        "bqc": np.ascontiguousarray(d["L1_attn_bq"].reshape(8, 128).T),
        "bkc": np.ascontiguousarray(d["L1_attn_bk"].reshape(8, 128).T),
        "bvr": np.ascontiguousarray(d["L1_attn_bv"].reshape(1, 1024)),
        "wo2": np.ascontiguousarray(d["L1_attn_wo"].reshape(1024, 1024)),
        "bo2": np.ascontiguousarray(d["L1_attn_bo"][None, :]),
        "pw": np.ascontiguousarray(d["L1_attn_proj_w"]),
        "pb": np.ascontiguousarray(d["L1_attn_proj_b"][None, :]),
        "w1": np.ascontiguousarray(d["L1_ffn_w1"].T),
        "w3": np.ascontiguousarray(d["L1_ffn_w3"].T),
        "w2": np.ascontiguousarray(d["L1_ffn_w2"].T),
        "a1": np.ascontiguousarray(d["L1_ffn_lora_A1"].T),
        "b1": np.ascontiguousarray(d["L1_ffn_lora_B1"].T),
        "a3": np.ascontiguousarray(d["L1_ffn_lora_A3"].T),
        "b3": np.ascontiguousarray(d["L1_ffn_lora_B3"].T),
        "a2": np.ascontiguousarray(d["L1_ffn_lora_A2"].T),
        "b2": np.ascontiguousarray(d["L1_ffn_lora_B2"].T),
    }
    maps = []
    for cid in range(8):
        b, h = cid // 2, cid % 2
        hs = np.zeros((16, 2), np.float32)
        hm = np.zeros((2, 1), np.float32)
        if h == 1:
            hs[2 * cid - 1, 0] = 1.0
            hm[0, 0] = 1.0
        if h == 0:
            hs[2 * cid + 2, 1] = 1.0
            hm[1, 0] = 1.0
        m = dict(shared)
        m["xo"] = np.ascontiguousarray(x0[b, h * 1024:(h + 1) * 1024])
        m["xr"] = np.ascontiguousarray(x0[b, (1 - h) * 1024:(2 - h) * 1024])
        m["halo_sel"] = hs
        m["halo_mask"] = hm
        maps.append(m)
    return maps


def gather_out_fused(results):
    out = np.zeros((4, 2048, 4096), np.float32)
    for cid in range(8):
        b, h = cid // 2, cid % 2
        out[b, h * 1024:(h + 1) * 1024] = results[cid]["out"]
    return out




from concourse.bass_utils import run_bass_kernel_spmd


def kernel(**inputs):
    nc = build_fused()
    res = run_bass_kernel_spmd(nc, host_inputs_fused(inputs),
                               core_ids=list(range(8)))
    return gather_out_fused(res.results)


# version 08
# speedup vs baseline: 1.1539x; 1.0118x over previous
import numpy as np
import concourse.bass as bass
import concourse.mybir as mybir

F32 = mybir.dt.float32
BF16 = mybir.dt.bfloat16
I32 = mybir.dt.int32
U8 = mybir.dt.uint8
AF = mybir.ActivationFunctionType
ALU = mybir.AluOpType
AX = mybir.AxisListType.X

DM = 4096
NKC = DM // 128


class Emit:

    ENG = ("t", "v", "s", "g")

    def __init__(self, nc):
        self.nc = nc
        self.e = {"t": nc.tensor, "v": nc.vector, "s": nc.scalar,
                  "g": nc.gpsimd, "y": nc.sync,
                  "dg": nc.gpsimd, "dy": nc.sync}
        self.sem = {k: nc.alloc_semaphore("sem_" + k) for k in self.ENG}
        self.cnt = {k: 0 for k in self.ENG}
        self.hw = {}

    def _dsem(self, name):
        key = "D:" + name
        if key not in self.sem:
            self.sem[key] = self.nc.alloc_semaphore("sem_" + name)
            self.cnt[key] = 0
        return key

    def wait(self, consumer, *toks):
        for tok in toks:
            if tok is None:
                continue
            k, v = tok
            if v <= 0 or self.hw.get((consumer, k), 0) >= v:
                continue
            self.e[consumer].wait_ge(self.sem[k], v)
            self.hw[(consumer, k)] = v

    def inc(self, inst, key, n=1):
        inst.then_inc(self.sem[key], n)
        self.cnt[key] += n
        return (key, self.cnt[key])

    def tok(self, key):
        return (key, self.cnt[key])

    def op(self, eng, name, *args, waits=(), **kw):
        self.wait(eng, *[w for w in waits if w])
        inst = getattr(self.e[eng], name)(*args, **kw)
        return self.inc(inst, eng, 1)

    def op_(self, eng, name, *args, waits=(), **kw):
        self.wait(eng, *[w for w in waits if w])
        getattr(self.e[eng], name)(*args, **kw)
        return None

    def dma(self, q, stream, out, in_, *, waits=(), **kw):
        key = self._dsem(stream)
        self.wait(q, *[w for w in waits if w])
        inst = self.e[q].dma_start(out, in_, **kw)
        return self.inc(inst, key, 16)

    def barrier(self):
        toks = [self.tok(k) for k in self.cnt]
        for consumer in ("t", "v", "s", "g", "y"):
            self.wait(consumer, *toks)

    def dma_totals(self):
        return [self.tok(k) for k in self.cnt if k.startswith("D:")]


class Arena:

    def __init__(self, nc, name, nbytes):
        self.t = nc.alloc_sbuf_tensor(name, [128, nbytes], U8)
        self.nbytes = nbytes
        self.off = 0

    def reset(self):
        self.off = 0

    def mark(self):
        return self.off

    def rewind(self, m):
        self.off = m

    def view(self, off, part, free, dtype):
        sz = free * np.dtype(mybir.dt.np(dtype)).itemsize
        assert off + sz <= self.nbytes
        return self.t.ap()[0:part, off:off + sz].bitcast(dtype)

    def take(self, part, free, dtype):
        sz = free * np.dtype(mybir.dt.np(dtype)).itemsize
        off = (self.off + 511) & ~511
        assert off + sz <= self.nbytes, f"arena overflow {off}+{sz}>{self.nbytes}"
        self.off = off + sz
        return self.t.ap()[0:part, off:off + sz].bitcast(dtype)


def ln_stats(E, x_ap, s1_ap, s2_ap, sq_ap, waits=()):
    t1 = E.op("v", "reduce_sum", s1_ap, x_ap, axis=AX, waits=waits)
    t2 = E.op("s", "activation", sq_ap, x_ap, AF.Square, accum_out=s2_ap,
              waits=waits)
    return t1, t2


def ln_chain(E, s1_ap, s2_ap, r_ap, nmr_ap, tmp_ap, D, tok_s1, tok_s2):
    t1 = E.op("v", "tensor_tensor", tmp_ap, s1_ap, s1_ap, ALU.mult,
              waits=(tok_s1,))
    t2 = E.op("v", "scalar_tensor_tensor", tmp_ap, tmp_ap, -1.0 / D, s2_ap,
              ALU.mult, ALU.add, waits=(t1, tok_s2))
    t3 = E.op("v", "tensor_scalar", tmp_ap, tmp_ap, 1.0 / D, 1e-5,
              ALU.mult, ALU.add, waits=(t2,))
    sd = E.op("s", "activation", r_ap, tmp_ap, AF.Sqrt, waits=(t3,))
    rt = E.op("v", "reciprocal", r_ap, r_ap, waits=(sd,))
    nm = E.op("v", "scalar_tensor_tensor", nmr_ap, s1_ap, -1.0 / D, r_ap,
              ALU.mult, ALU.mult, waits=(rt, tok_s1))
    return rt, nm


class TPool:

    def __init__(self, E, psts, ident):
        self.E = E
        self.psts = psts
        self.free = [None] * len(psts)
        self.ident = ident
        self.i = 0
        self.flip = 0

    def transpose(self, src_ap, dst_ap, waits=(), evac_eng=None):
        E = self.E
        i = self.i
        self.i = (i + 1) % len(self.psts)
        eng = evac_eng or ("v" if self.flip == 0 else "s")
        self.flip ^= 1
        po = self.psts[i][0:src_ap.shape[1], 0:src_ap.shape[0]]
        tt = E.op("t", "transpose", po, src_ap, self.ident,
                  waits=tuple(waits) + (self.free[i],))
        meth = "copy" if eng == "s" else "tensor_copy"
        te = E.op(eng, meth, dst_ap, po, waits=(tt,))
        self.free[i] = te
        return tt, te




class TRot:

    def __init__(self, E, views, ident):
        self.E = E
        self.views = views
        self.ident = ident
        self.free = [None] * len(views)
        self.i = 0
        self.flip = 0

    def go(self, src_ap, dst_ap, waits=()):
        E = self.E
        i = self.i
        self.i = (i + 1) % len(self.views)
        eng = "v" if self.flip == 0 else "s"
        self.flip ^= 1
        p = src_ap.shape[0]
        po = self.views[i][0:src_ap.shape[1], 0:p]
        fr = self.free[i]
        if fr is None:
            fr = ()
        elif isinstance(fr[0], str):
            fr = (fr,)
        tt = E.op("t", "transpose", po, src_ap, self.ident[0:p, 0:p],
                  waits=tuple(w for w in waits if w) + fr)
        meth = "copy" if eng == "s" else "tensor_copy"
        te = E.op(eng, meth, dst_ap, po, waits=(tt,))
        self.free[i] = te
        return tt, te


class Ctx:
    pass


def make_ctx(nc):
    c = Ctx()
    c.nc = nc
    c.E = Emit(nc)
    E = c.E
    c.ident = nc.alloc_sbuf_tensor("ident", [128, 128], BF16)
    c.ones = nc.alloc_sbuf_tensor("ones", [1, 128], F32)
    c.ones_bf = nc.alloc_sbuf_tensor("ones_bf", [1, 128], BF16)
    c.s1a = nc.alloc_sbuf_tensor("s1a", [128, 16], F32)
    c.s2a = nc.alloc_sbuf_tensor("s2a", [128, 16], F32)
    c.ra = nc.alloc_sbuf_tensor("ra", [128, 16], F32)
    c.nmra = nc.alloc_sbuf_tensor("nmra", [128, 16], F32)
    c.tmpa = nc.alloc_sbuf_tensor("tmpa", [128, 16], F32)
    c.p_s1 = nc.alloc_sbuf_tensor("p_s1", [128, 16], F32)
    c.p_s2 = nc.alloc_sbuf_tensor("p_s2", [128, 16], F32)
    c.p_r = nc.alloc_sbuf_tensor("p_r", [128, 16], F32)
    c.p_nmr = nc.alloc_sbuf_tensor("p_nmr", [128, 16], F32)
    c.p_tmp = nc.alloc_sbuf_tensor("p_tmp", [128, 16], F32)
    c.nm = nc.alloc_sbuf_tensor("nm", [128, 1], F32)
    c.nms = nc.alloc_sbuf_tensor("nms", [128, 1], F32)
    c.rsum = nc.alloc_sbuf_tensor("rsum", [128, 1], F32)
    c.rrec = nc.alloc_sbuf_tensor("rrec", [128, 1], F32)
    c.PSA = nc.alloc_psum_tensor("PSA", [128, 2048], F32)
    c.PSB = nc.alloc_psum_tensor("PSB", [128, 2048], F32)
    c.TVIEWS = [c.PSB[:, 0:512].bitcast(BF16)[:, 0:128],
                c.PSB[:, 512:1024].bitcast(BF16)[:, 0:128]]
    rem = nc.sbuf_bytes_remaining
    c.ar = Arena(nc, "arena", rem - 1536)
    tms = E.op("g", "memset", c.ident[:, :], 0.0)
    E.op("g", "affine_select", out=c.ident[:, :], in_=c.ident[:, :],
         compare_op=ALU.not_equal, fill=1.0, base=0,
         pattern=[[-1, 128]], channel_multiplier=1, waits=(tms,))
    E.op("g", "memset", c.ones[0:1, :], 1.0)
    E.op("g", "memset", c.ones_bf[0:1, :], 1.0)
    return c


def emit_l0_attn(c, xo, xr, wqkv, bqkv, wo, bo, x1_out):
    nc, E, ar = c.nc, c.E, c.ar
    PSA, PSB = c.PSA, c.PSB
    ident, ones = c.ident, c.ones_bf
    s1a, s2a, ra, nmra, tmpa = c.s1a, c.s2a, c.ra, c.nmra, c.tmpa
    p_s1, p_s2, p_r, p_nmr, p_tmp = c.p_s1, c.p_s2, c.p_r, c.p_nmr, c.p_tmp
    nm, nms, rsum, rrec = c.nm, c.nms, c.rsum, c.rrec

    ar.reset()
    K_fm = ar.take(128, 8 * 2048, BF16)
    V_tm = ar.take(128, 16 * 1024, BF16)
    Q_fm = ar.take(128, 8 * 1024, BF16)
    at_fm_all = ar.take(128, 8 * 1024, BF16)
    m0 = ar.mark()
    x_st = ar.take(128, DM, F32)
    ln_tm = ar.take(128, DM, BF16)
    lnfm = ar.take(128, 4 * DM, BF16)
    sqs = ar.take(128, DM, F32)
    wpc = ar.take(128, 4 * 512, BF16)
    ttm = ar.take(128, 1024, BF16)
    bqv = ar.take(1, 3072, BF16)
    gend = ar.off
    ar.rewind(m0)
    P = ar.take(128, 2048, BF16)
    PTs = ar.take(128, 2048, BF16)
    at_tm = ar.take(128, 1024, BF16)
    aend = ar.off
    ar.rewind(m0)
    x0s = ar.take(128, DM, F32)
    x1s = ar.take(128, DM, F32)
    sqp = ar.take(128, DM, F32)
    wop = ar.take(128, 4 * 512, BF16)
    bor = ar.take(1, DM, BF16)
    dens = ar.view(0, 128, 4 * DM, F32)
    ar.off = max(ar.off, gend, aend)

    PSO = [PSB[:, 1024:1536], PSB[:, 1536:2048]]
    TP = TRot(E, c.TVIEWS, ident)
    tok_bqv = E.dma("dg", "a_bias1", bqv, bqkv)

    groups = [
        (xo, [0, 1, 2, 3], 0, range(0, 6)),
        (xo, [4, 5, 6, 7], 4, range(0, 6)),
        (xr, [0, 1, 2, 3], 8, range(2, 6)),
        (xr, [4, 5, 6, 7], 12, range(2, 6)),
    ]
    tok_xst_free = None
    tok_lntm_free = None
    piece_free = [None] * 4
    ttm_free = [None, None]
    psa_free = [None] * 4
    jglob = 0
    for (src, tiles, t16b, ntiles) in groups:
        lnfm_ready = [None] * 4
        for i, tile in enumerate(tiles):
            j = jglob
            jglob += 1
            jj = j % 16
            td = E.dma("dy", "a_x", x_st, src[tile * 128:(tile + 1) * 128, :],
                       waits=(tok_xst_free,))
            t1, t2 = ln_stats(E, x_st, s1a[:, jj:jj + 1], s2a[:, jj:jj + 1],
                              sqs, waits=(td,))
            tr, tn = ln_chain(E, s1a[:, jj:jj + 1], s2a[:, jj:jj + 1],
                              ra[:, jj:jj + 1], nmra[:, jj:jj + 1],
                              tmpa[:, jj:jj + 1], DM, t1, t2)
            tok_apply = E.op("v", "tensor_scalar", ln_tm, x_st,
                             ra[:, jj:jj + 1], nmra[:, jj:jj + 1],
                             ALU.mult, ALU.add,
                             waits=(tr, tn, tok_lntm_free))
            tok_xst_free = tok_apply
            last_tt = None
            for ch in range(NKC):
                tt, te = TP.go(ln_tm[:, ch * 128:(ch + 1) * 128],
                               lnfm[:, i * DM + ch * 128:i * DM + (ch + 1) * 128],
                               waits=(tok_apply,))
                last_tt = tt
            tok_lntm_free = last_tt
            lnfm_ready[i] = (E.tok("v"), E.tok("s"))

        for n in ntiles:
            c0 = n * 512
            for k in range(NKC):
                buf = k % 4
                tw = E.dma("dg", f"a_w{buf}", wpc[:, buf * 512:(buf + 1) * 512],
                           wqkv[k * 128:(k + 1) * 128, c0:c0 + 512],
                           waits=(piece_free[buf],))
                for i in range(4):
                    w = [tw]
                    if k == 0:
                        w += list(lnfm_ready[i]) + [psa_free[i]]
                    lhs = lnfm[:, i * DM + k * 128:i * DM + (k + 1) * 128]
                    rhs = wpc[:, buf * 512:(buf + 1) * 512]
                    psl = PSA[:, i * 512:(i + 1) * 512]
                    if i == 3:
                        piece_free[buf] = E.op("t", "matmul", psl, lhs, rhs,
                                               start=(k == 0), stop=False,
                                               waits=w)
                    else:
                        E.op_("t", "matmul", psl, lhs, rhs,
                              start=(k == 0), stop=False, waits=w)
            for i in range(4):
                tok_done = E.op("t", "matmul", PSA[:, i * 512:(i + 1) * 512],
                                ones[0:1, :], bqv[0:1, c0:c0 + 512],
                                start=False, stop=True, waits=(tok_bqv,))
                t16 = t16b + i
                tloc = tiles[i]
                if n < 2:
                    ev = E.op("s", "copy", ttm[:, (i % 2) * 512:(i % 2 + 1) * 512],
                              PSA[:, i * 512:(i + 1) * 512],
                              waits=(tok_done, ttm_free[i % 2]))
                    psa_free[i] = ev
                    last = None
                    for s4 in range(4):
                        chunk = n * 4 + s4
                        tt, te = TP.go(
                            ttm[:, (i % 2) * 512 + s4 * 128:(i % 2) * 512 + (s4 + 1) * 128],
                            Q_fm[:, chunk * 1024 + tloc * 128:chunk * 1024 + (tloc + 1) * 128],
                            waits=(ev,))
                        last = tt
                    ttm_free[i % 2] = last
                elif n < 4:
                    ev = E.op("s", "copy", ttm[:, (i % 2) * 512:(i % 2 + 1) * 512],
                              PSA[:, i * 512:(i + 1) * 512],
                              waits=(tok_done, ttm_free[i % 2]))
                    psa_free[i] = ev
                    last = None
                    for s4 in range(4):
                        chunk = (n - 2) * 4 + s4
                        tt, te = TP.go(
                            ttm[:, (i % 2) * 512 + s4 * 128:(i % 2) * 512 + (s4 + 1) * 128],
                            K_fm[:, chunk * 2048 + t16 * 128:chunk * 2048 + (t16 + 1) * 128],
                            waits=(ev,))
                        last = tt
                    ttm_free[i % 2] = last
                else:
                    ev = E.op("v", "tensor_copy",
                              V_tm[:, t16 * 1024 + (n - 4) * 512:t16 * 1024 + (n - 4) * 512 + 512],
                              PSA[:, i * 512:(i + 1) * 512], waits=(tok_done,))
                    psa_free[i] = ev

    E.barrier()
    tok_psa_attn = None
    tok_pso_free = [None, None]
    for m in range(8):
        tok_scores = None
        for ch in range(8):
            for kk in range(4):
                w = []
                if ch == 0:
                    w = [tok_psa_attn, psa_free[kk]]
                lhs = Q_fm[:, ch * 1024 + m * 128:ch * 1024 + (m + 1) * 128]
                rhs = K_fm[:, ch * 2048 + kk * 512:ch * 2048 + (kk + 1) * 512]
                psl = PSA[:, kk * 512:(kk + 1) * 512]
                if ch == 7 and kk == 3:
                    tok_scores = E.op("t", "matmul", psl, lhs, rhs,
                                      start=(ch == 0), stop=(ch == 7), waits=w)
                else:
                    E.op_("t", "matmul", psl, lhs, rhs,
                          start=(ch == 0), stop=(ch == 7), waits=w)
        psa_free = [None] * 4
        tmax = E.op("v", "reduce_max", nm[:, :], PSA[:, :], axis=AX,
                    negate=True, waits=(tok_scores,))
        tnms = E.op("v", "tensor_scalar_mul", nms[:, :], nm[:, :], 1.0 / 32.0,
                    waits=(tmax,))
        tok_exp = E.op("s", "activation", P, PSA[:, :], AF.Exp,
                       bias=nms[:, 0:1], scale=1.0 / 32.0,
                       accum_out=rsum[:, 0:1], waits=(tnms,))
        tok_psa_attn = tok_exp
        trr = E.op("v", "reciprocal", rrec[:, :], rsum[:, :], waits=(tok_exp,))
        tok_pv = None
        for ch in range(16):
            tt, te = TP.go(P[:, ch * 128:(ch + 1) * 128],
                           PTs[:, ch * 128:(ch + 1) * 128],
                           waits=(tok_exp,))
            for n2 in range(2):
                w = [te]
                if ch == 0:
                    w += [tok_pso_free[0], tok_pso_free[1]]
                psl = PSO[n2]
                lhs = PTs[:, ch * 128:(ch + 1) * 128]
                rhs = V_tm[:, ch * 1024 + n2 * 512:ch * 1024 + (n2 + 1) * 512]
                if ch == 15 and n2 == 1:
                    tok_pv = E.op("t", "matmul", psl, lhs, rhs,
                                  start=(ch == 0), stop=(ch == 15), waits=w)
                else:
                    E.op_("t", "matmul", psl, lhs, rhs,
                          start=(ch == 0), stop=(ch == 15), waits=w)
        tok_at = E.op("s", "activation", at_tm, PSB[:, 1024:2048], AF.Copy,
                      scale=rrec[:, 0:1], waits=(tok_pv, trr))
        for ch in range(8):
            TP.go(at_tm[:, ch * 128:(ch + 1) * 128],
                  at_fm_all[:, m * 1024 + ch * 128:m * 1024 + (ch + 1) * 128],
                  waits=(tok_at,))
        tok_pso_free = [tok_at, tok_at]

    guard = (E.tok("t"), E.tok("v"), E.tok("s"))
    tok_bor = E.dma("dg", "a_bias2", bor, bo, waits=guard)
    at_ready = guard
    wop_free = [None] * 4
    psa_free2 = [None] * 4
    tok_x0_free = None
    tok_x1_free = None
    for g2 in range(2):
        tiles2 = [g2 * 4 + i for i in range(4)]
        dens_done = [[None] * 8 for _ in range(4)]
        for n in range(8):
            c0 = n * 512
            for k in range(8):
                buf = k % 4
                tw = E.dma("dg", f"a_wo{buf}", wop[:, buf * 512:(buf + 1) * 512],
                           wo[k * 128:(k + 1) * 128, c0:c0 + 512],
                           waits=(wop_free[buf], guard[0]))
                for i in range(4):
                    m = tiles2[i]
                    w = [tw]
                    if k == 0:
                        w += list(at_ready) + [psa_free2[i], tok_psa_attn]
                    lhs = at_fm_all[:, m * 1024 + k * 128:m * 1024 + (k + 1) * 128]
                    rhs = wop[:, buf * 512:(buf + 1) * 512]
                    psl = PSA[:, i * 512:(i + 1) * 512]
                    if i == 3:
                        wop_free[buf] = E.op("t", "matmul", psl, lhs, rhs,
                                             start=(k == 0), stop=False,
                                             waits=w)
                    else:
                        E.op_("t", "matmul", psl, lhs, rhs,
                              start=(k == 0), stop=False, waits=w)
            for i in range(4):
                tok_done = E.op("t", "matmul", PSA[:, i * 512:(i + 1) * 512],
                                ones[0:1, :], bor[0:1, c0:c0 + 512],
                                start=False, stop=True, waits=(tok_bor,))
                eng = "v" if i % 2 == 0 else "s"
                meth = "tensor_copy" if eng == "v" else "copy"
                tev = E.op(eng, meth, dens[:, i * DM + c0:i * DM + c0 + 512],
                           PSA[:, i * 512:(i + 1) * 512], waits=(tok_done,))
                psa_free2[i] = tev
                dens_done[i][n] = tev
        tok_out_last = None
        for i in range(4):
            m = tiles2[i]
            dview = dens[:, i * DM:(i + 1) * DM]
            t1, t2 = ln_stats(E, dview, p_s1[:, m:m + 1], p_s2[:, m:m + 1],
                              sqp, waits=tuple(dens_done[i]))
            tr, tn = ln_chain(E, p_s1[:, m:m + 1], p_s2[:, m:m + 1],
                              p_r[:, m:m + 1], p_nmr[:, m:m + 1],
                              p_tmp[:, m:m + 1], DM, t1, t2)
            tx = E.dma("dy", "a_x0", x0s, xo[m * 128:(m + 1) * 128, :],
                       waits=(tok_x0_free,) + guard)
            tap = E.op("v", "tensor_scalar", x1s, dview, p_r[:, m:m + 1],
                       p_nmr[:, m:m + 1], ALU.mult, ALU.add,
                       waits=(tr, tn, tok_x1_free))
            tadd = E.op("v", "tensor_tensor", x1s, x1s, x0s, ALU.add,
                        waits=(tap, tx))
            tok_x0_free = tadd
            tok_x1_free = E.dma("dy", "a_out", x1_out[m * 128:(m + 1) * 128, :],
                                x1s, waits=(tadd,))
        E.wait("v", tok_x1_free)
        E.wait("s", tok_x1_free)
    E.barrier()


def emit_l0_ffn(c, x1i, w0, b0, cent, assign, cc_in, cc_out, x2_out):
    nc, E, ar = c.nc, c.E, c.ar
    PSA, PSB = c.PSA, c.PSB
    ones = c.ones
    s1a, s2a, ra, nmra, tmpa = c.s1a, c.s2a, c.ra, c.nmra, c.tmpa
    p_s1, p_s2, p_r, p_nmr, p_tmp = c.p_s1, c.p_s2, c.p_r, c.p_nmr, c.p_tmp
    NCLUST, THR, INV_TEMP = 64, 0.01, 10.0
    nmx, nmx10, dsum, drec = c.nm, c.nms, c.rsum, c.rrec

    ar.reset()
    ident32 = ar.take(128, 128, F32)
    ones64 = ar.take(64, 128, F32)
    iota64 = ar.take(64, 1, F32)
    cent_sb = ar.take(128, NKC * NCLUST, F32)
    lnbf = ar.take(128, 8 * DM, BF16)
    qmask = ar.take(128, 8, F32)
    qmx = ar.take(128, 1, F32)
    cmax = ar.take(128, NCLUST, F32)
    cmT = ar.take(64, 128, F32)
    cm_col = ar.take(64, 1, F32)
    cs_col = ar.take(64, 1, F32)
    cs_gt = ar.take(64, 1, F32)
    dots = ar.take(128, NCLUST, F32)
    x_st = ar.take(128, DM, F32)
    lnf32 = ar.take(128, DM, F32)
    lnfm32 = ar.take(128, DM, F32)
    dens = ar.take(128, 4 * DM, F32)
    b0pc = ar.take(1, 1024, BF16)
    sqs = dens[:, 0:DM]
    sqs3 = lnf32
    asg = lnf32[0:64, :].bitcast(I32)
    asgf = lnfm32[0:64, :]
    ohcs = x_st[0:64, :]
    rmask = lnfm32
    wpc = cent_sb[:, 0:2048].bitcast(BF16)[:, 0:4096]

    T32 = [PSB[:, 0:128], PSB[:, 512:640]]
    PSD = PSB[:, 1024:1024 + NCLUST]
    TP32 = TRot(E, T32, ident32)

    tms = E.op("g", "memset", ident32[0:128, :], 0.0)
    E.op("g", "affine_select", out=ident32[0:128, :], in_=ident32[0:128, :],
         compare_op=ALU.not_equal, fill=1.0, base=0,
         pattern=[[-1, 128]], channel_multiplier=1, waits=(tms,))
    tmo = E.op("g", "memset", ones64[0:64, :], 1.0)
    tio = E.op("g", "iota", iota64[0:64, 0:1], pattern=[[0, 1]], base=0,
               channel_multiplier=1, allow_small_or_imprecise_dtypes=True)
    tok_cmz = E.op("v", "memset", cmax[:, :], 0.0)
    tok_cent = E.dma("dy", "b_cent",
                     cent_sb[:, :].rearrange("p (c n) -> p c n", n=NCLUST),
                     cent.rearrange("(c p) n -> p c n", p=128))

    tok_xst_free = None
    tok_lnf_free = None
    tok_lnfm_free = None
    tok_psd_free = None
    tok_cmax = tok_cmz
    tok_ident_done = E.tok("g")
    for m in range(8):
        td = E.dma("dy", "b_x", x_st, x1i[m * 128:(m + 1) * 128, :],
                   waits=(tok_xst_free,))
        t1, t2 = ln_stats(E, x_st, s1a[:, m:m + 1], s2a[:, m:m + 1], sqs,
                          waits=(td,))
        tr, tn = ln_chain(E, s1a[:, m:m + 1], s2a[:, m:m + 1],
                          ra[:, m:m + 1], nmra[:, m:m + 1],
                          tmpa[:, m:m + 1], DM, t1, t2)
        tok_ap = E.op("v", "tensor_scalar", lnf32, x_st, ra[:, m:m + 1],
                      nmra[:, m:m + 1], ALU.mult, ALU.add,
                      waits=(tr, tn, tok_lnf_free))
        tok_xst_free = tok_ap
        last_tt = None
        for ch in range(NKC):
            tt, te = TP32.go(lnf32[:, ch * 128:(ch + 1) * 128],
                             lnfm32[:, ch * 128:(ch + 1) * 128],
                             waits=(tok_ap, tok_lnfm_free, tok_ident_done))
            last_tt = tt
        tok_lnf_free = last_tt
        fm_ready = (E.tok("v"), E.tok("s"))
        tok_bf = E.op("v", "tensor_copy", lnbf[:, m * DM:(m + 1) * DM],
                      lnfm32, waits=fm_ready)
        tok_dots = None
        for k in range(NKC):
            w = list(fm_ready) + [tok_cent]
            if k == 0:
                w.append(tok_psd_free)
            tok_dots = E.op("t", "matmul", PSD,
                            lnfm32[:, k * 128:(k + 1) * 128],
                            cent_sb[:, k * NCLUST:(k + 1) * NCLUST],
                            start=(k == 0), stop=(k == NKC - 1), waits=w)
        tok_lnfm_free = tok_dots
        tmx = E.op("v", "reduce_max", nmx[:, :], PSD, axis=AX,
                   negate=True, waits=(tok_dots,))
        tx10 = E.op("v", "tensor_scalar_mul", nmx10[:, :], nmx[:, :],
                    INV_TEMP, waits=(tmx,))
        tex = E.op("s", "activation", dots[:, :], PSD, AF.Exp,
                   bias=nmx10[:, 0:1], scale=INV_TEMP,
                   accum_out=dsum[:, 0:1], waits=(tx10,))
        tok_psd_free = tex
        trc = E.op("v", "reciprocal", drec[:, :], dsum[:, :], waits=(tex,))
        tdn = E.op("v", "tensor_scalar", dots[:, :], dots[:, :],
                   drec[:, 0:1], None, ALU.mult, waits=(trc, tex))
        tqm = E.op("v", "reduce_max", qmx[:, 0:1], dots[:, :], axis=AX,
                   waits=(tdn,))
        E.op("v", "tensor_single_scalar", qmask[:, m:m + 1], qmx[:, 0:1],
             THR, ALU.is_gt, waits=(tqm,))
        tok_cmax = E.op("v", "tensor_tensor", cmax[:, :], cmax[:, :],
                        dots[:, :], ALU.max, waits=(tok_cmax, tdn))

    tt, te = TP32.go(cmax[:, 0:64], cmT[0:64, 0:128], waits=(tok_cmax,))
    tcm = E.op("v", "reduce_max", cm_col[0:64, 0:1], cmT[0:64, 0:128],
               axis=AX, waits=(te,))
    tdm = E.dma("dg", "b_ccin", cc_in.ap(), cm_col[0:64, 0:1],
                waits=(tcm,), allow_slow_non_contiguous=True)
    E.wait("g", tdm)
    cc_inst = nc.gpsimd.collective_compute(
        "AllReduce", ALU.max,
        replica_groups=[list(range(8))],
        ins=[cc_in.ap().opt()], outs=[cc_out.ap().opt()],
    )
    tok_cc = E.inc(cc_inst, "g", 1)
    tcs = E.dma("dg", "b_ccout", cs_col[0:64, 0:1], cc_out.ap(),
                waits=(tok_cc,), allow_slow_non_contiguous=True)
    tgt = E.op("v", "tensor_single_scalar", cs_gt[0:64, 0:1],
               cs_col[0:64, 0:1], THR, ALU.is_gt, waits=(tcs,))
    tas = E.dma("dg", "b_asg", asg, assign.partition_broadcast(64),
                waits=(E.tok("t"), E.tok("v"), E.tok("s")))
    taf = E.op("g", "tensor_copy", asgf, asg, waits=(tas,))
    toh = E.op("v", "tensor_scalar", ohcs, asgf, iota64[0:64, 0:1],
               cs_gt[0:64, 0:1], ALU.is_equal, ALU.mult, waits=(taf, tgt))
    rmask_done = []
    for n in range(8):
        w = [toh]
        tmm = E.op("t", "matmul", PSB[:, 1536:2048],
                   ones64[0:64, :], ohcs[0:64, n * 512:(n + 1) * 512],
                   start=True, stop=True,
                   waits=w + ([rmask_done[n - 1]] if n >= 1 else []))
        tev = E.op("v", "tensor_copy", rmask[:, n * 512:(n + 1) * 512],
                   PSB[:, 1536:2048], waits=(tmm,))
        rmask_done.append(tev)
    tok_rmask = rmask_done[-1]

    E.wait("v", E.tok("s"))
    piece_free = [None] * 4
    b0_free = [None, None]
    psa_free = [None] * 4
    psb_free = [None] * 4
    tok_x3_free = tok_rmask
    guardB = (E.tok("t"), E.tok("v"), E.tok("s"))
    for grp in range(2):
        tiles = [grp * 4 + i for i in range(4)]
        dens_done = [[None] * 8 for _ in range(4)]
        for n2 in range(4):
            c0 = n2 * 1024
            for k in range(NKC):
                buf = k % 4
                tw = E.dma("dg", f"b_w{buf}", wpc[:, buf * 1024:(buf + 1) * 1024],
                           w0[k * 128:(k + 1) * 128, c0:c0 + 1024],
                           waits=(piece_free[buf],))
                for i in range(4):
                    m = tiles[i]
                    w = [tw]
                    if k == 0:
                        w += [psa_free[i], psb_free[i]] + list(guardB)
                        if i >= 2:
                            w.append(tok_rmask)
                    lhs = lnbf[:, m * DM + k * 128:m * DM + (k + 1) * 128]
                    E.op_("t", "matmul", PSA[:, i * 512:(i + 1) * 512], lhs,
                          wpc[:, buf * 1024:buf * 1024 + 512],
                          start=(k == 0), stop=False, waits=w)
                    if i == 3:
                        piece_free[buf] = E.op("t", "matmul",
                                               PSB[:, i * 512:(i + 1) * 512], lhs,
                                               wpc[:, buf * 1024 + 512:buf * 1024 + 1024],
                                               start=(k == 0), stop=False, waits=w)
                    else:
                        E.op_("t", "matmul", PSB[:, i * 512:(i + 1) * 512], lhs,
                              wpc[:, buf * 1024 + 512:buf * 1024 + 1024],
                              start=(k == 0), stop=False, waits=w)
            for half in range(2):
                c0h = c0 + half * 512
                bb = half
                tb0 = E.dma("dg", f"b_b0{bb}", b0pc[0:1, bb * 512:bb * 512 + 512],
                            b0[0:1, c0h:c0h + 512], waits=(b0_free[bb],))
                for i in range(4):
                    m = tiles[i]
                    PSX = PSA if half == 0 else PSB
                    tok_done = E.op("t", "matmul", PSX[:, i * 512:(i + 1) * 512],
                                    c.ones_bf[0:1, :],
                                    b0pc[0:1, bb * 512:bb * 512 + 512],
                                    start=False, stop=True, waits=(tb0,))
                    if i == 3:
                        b0_free[bb] = tok_done
                    tev = E.op("v", "scalar_tensor_tensor",
                               dens[:, i * DM + c0h:i * DM + c0h + 512],
                               PSX[:, i * 512:(i + 1) * 512],
                               qmask[:, m:m + 1], rmask[:, c0h:c0h + 512],
                               ALU.mult, ALU.mult, waits=(tok_done, tok_rmask))
                    if half == 0:
                        psa_free[i] = tev
                    else:
                        psb_free[i] = tev
                    dens_done[i][n2 * 2 + half] = tev
        tok_out_last = None
        for i in range(4):
            m = tiles[i]
            dview = dens[:, i * DM:(i + 1) * DM]
            t1, t2 = ln_stats(E, dview, p_s1[:, m:m + 1], p_s2[:, m:m + 1],
                              sqs3, waits=tuple(dens_done[i]))
            tr, tn = ln_chain(E, p_s1[:, m:m + 1], p_s2[:, m:m + 1],
                              p_r[:, m:m + 1], p_nmr[:, m:m + 1],
                              p_tmp[:, m:m + 1], DM, t1, t2)
            tx = E.dma("dy", "b_x3", x_st, x1i[m * 128:(m + 1) * 128, :],
                       waits=(tok_x3_free,))
            tap = E.op("v", "tensor_scalar", dview, dview, p_r[:, m:m + 1],
                       p_nmr[:, m:m + 1], ALU.mult, ALU.add, waits=(tr, tn))
            tadd = E.op("v", "tensor_tensor", dview, dview, x_st, ALU.add,
                        waits=(tap, tx))
            tok_x3_free = tadd
            tok_out_last = E.dma("dy", "b_out",
                                 x2_out[m * 128:(m + 1) * 128, :],
                                 dview, waits=(tadd,))
        E.wait("v", tok_out_last)
        E.wait("s", tok_out_last)
    E.barrier()


def emit_l1_attn(c, x2i, cw, cb, wq, wk, wv, bqc, bkc, bvr, wo2, bo2, pw, pb,
                 halo_sel, halo_mask, hg_in, hg_out, x3_out):
    nc, E, ar = c.nc, c.E, c.ar
    PSA, PSB = c.PSA, c.PSB
    ident, ones = c.ident, c.ones_bf
    s1a, s2a, ra, nmra, tmpa = c.s1a, c.s2a, c.ra, c.nmra, c.tmpa
    p_s1, p_s2, p_r, p_nmr, p_tmp = c.p_s1, c.p_s2, c.p_r, c.p_nmr, c.p_tmp
    CW = 1032
    NH = 16
    SCALE = 0.125
    NEG_INF = -1e9

    ar.reset()
    cmask = ar.take(128, 128, F32)
    qkv_tm = ar.take(128, 8 * 3072, BF16)
    ao_tm = ar.take(128, 8 * 1024, BF16)
    bqc_sb = ar.take(128, 8, F32)
    bkc_sb = ar.take(128, 8, F32)
    hmask_sb = ar.take(2, 1, F32)
    hsel_sb = ar.take(16, 2, F32)
    nm = ar.take(128, 2, F32)
    rsum = ar.take(128, 2, F32)
    rrec = ar.take(128, 2, F32)
    mseg = ar.mark()
    x_st = ar.take(128, DM, F32)
    sqs = ar.take(128, DM, F32)
    ln_tm = ar.take(128, DM, BF16)
    ln3fm = ar.take(128, NKC * CW, BF16)
    cpc = ar.take(128, 4 * 512, BF16)
    cbr = ar.take(1, 3072, BF16)
    hg_sb = x_st[0:16, :]
    lnh = ln_tm[0:2, :]
    x2h = qkv_tm[0:2, 0:8192].bitcast(F32)
    segA_end = ar.off
    ar.rewind(mseg)
    wq_sb = ar.take(128, 8 * 1024, BF16)
    wk_sb = ar.take(128, 8 * 1024, BF16)
    wv_sb = ar.take(128, 8 * 1024, BF16)
    wo2_sb = ar.take(128, 8 * 1024, BF16)
    q_fm = ar.take(128, 1024, BF16)
    k_fm = ar.take(128, 1024, BF16)
    v_fm = ar.take(128, 1024, BF16)
    qh_fm = ar.take(128, 1024, BF16)
    kh_fm = ar.take(128, 1024, BF16)
    vh_tm = ar.take(128, 1024, BF16)
    s_sb = ar.take(128, 256, F32)
    P = ar.take(128, 256, BF16)
    PT_sb = ar.take(128, 256, BF16)
    o_tm = ar.take(128, 1024, BF16)
    o_fm = ar.take(128, 1024, BF16)
    bvr_sb = ar.take(1, 1024, BF16)
    bo2_sb = ar.take(1, 1024, BF16)
    segB_end = ar.off
    ar.rewind(mseg)
    at_fm = ar.take(128, 4 * 1024, BF16)
    dens = ar.take(128, 4 * DM, F32)
    x_st4 = ar.take(128, DM, F32)
    sqs4 = ar.take(128, DM, F32)
    ppc = ar.take(128, 4 * 512, BF16)
    pbr = ar.take(1, DM, BF16)
    ar.off = max(ar.off, segA_end, segB_end)

    TRB = TRot(E, c.TVIEWS, ident)

    tcm = E.op("g", "memset", cmask[:, :], 0.0)
    E.op("g", "affine_select", out=cmask[:, :], in_=cmask[:, :],
         compare_op=ALU.is_ge, fill=NEG_INF, base=0,
         pattern=[[-1, 128]], channel_multiplier=1, waits=(tcm,))
    tok_cb = E.dma("dg", "c_cbias", cbr, cb)
    tok_hm = E.dma("dy", "c_hmask", hmask_sb[0:2, 0:1], halo_mask,
                   allow_slow_non_contiguous=True)
    tok_hs = E.dma("dy", "c_hsel", hsel_sb[0:16, 0:2], halo_sel,
                   allow_slow_non_contiguous=True)

    tok_xst_free = None
    tok_lntm_free = None
    for m in range(8):
        r0 = m * 128
        td = E.dma("dy", "c_x", x_st, x2i[r0:r0 + 128, :],
                   waits=(tok_xst_free,))
        t1, t2 = ln_stats(E, x_st, s1a[:, m:m + 1], s2a[:, m:m + 1], sqs,
                          waits=(td,))
        tr, tn = ln_chain(E, s1a[:, m:m + 1], s2a[:, m:m + 1],
                          ra[:, m:m + 1], nmra[:, m:m + 1],
                          tmpa[:, m:m + 1], DM, t1, t2)
        tok_ap = E.op("v", "tensor_scalar", ln_tm, x_st, ra[:, m:m + 1],
                      nmra[:, m:m + 1], ALU.mult, ALU.add,
                      waits=(tr, tn, tok_lntm_free))
        tok_xst_free = tok_ap
        last_tt = None
        for ch in range(NKC):
            tt, te = TRB.go(ln_tm[:, ch * 128:(ch + 1) * 128],
                            ln3fm[:, ch * CW + r0 + 1:ch * CW + r0 + 129],
                            waits=(tok_ap,))
            last_tt = tt
        tok_lntm_free = last_tt

    th1 = E.dma("dg", "c_hgin", hg_in.ap()[0:1, :], x2i[0:1, :])
    th2 = E.dma("dg", "c_hgin", hg_in.ap()[1:2, :], x2i[1023:1024, :])
    E.wait("g", th2)
    cc_inst = nc.gpsimd.collective_compute(
        "AllGather", ALU.bypass,
        replica_groups=[list(range(8))],
        ins=[hg_in.ap().opt()], outs=[hg_out.ap().opt()],
    )
    tok_ag = E.inc(cc_inst, "g", 1)
    thg = E.dma("dg", "c_hg", hg_sb[0:16, :], hg_out.ap(),
                waits=(tok_ag, tok_xst_free))
    ev_h = []
    for n in range(8):
        sl = n % 4
        w = [thg, tok_hs]
        tmm = E.op("t", "matmul", PSA[0:2, sl * 512:sl * 512 + 512],
                   hsel_sb[0:16, 0:2], hg_sb[0:16, n * 512:(n + 1) * 512],
                   start=True, stop=True,
                   waits=w + ([ev_h[n - 4]] if n >= 4 else []))
        tev = E.op("v", "tensor_copy", x2h[0:2, n * 512:(n + 1) * 512],
                   PSA[0:2, sl * 512:sl * 512 + 512], waits=(tmm,))
        ev_h.append(tev)
    t1, t2 = ln_stats(E, x2h[0:2, :], s1a[0:2, 8:9], s2a[0:2, 8:9],
                      sqs[0:2, :], waits=tuple(ev_h))
    tr, tn = ln_chain(E, s1a[0:2, 8:9], s2a[0:2, 8:9], ra[0:2, 8:9],
                      nmra[0:2, 8:9], tmpa[0:2, 8:9], DM, t1, t2)
    tok_lnh = E.op("v", "tensor_scalar", lnh[0:2, :], x2h[0:2, :],
                   ra[0:2, 8:9], nmra[0:2, 8:9], ALU.mult, ALU.add,
                   waits=(tr, tn, tok_lntm_free))
    tok_lnhm = E.op("v", "tensor_scalar", lnh[0:2, :], lnh[0:2, :],
                    hmask_sb[0:2, 0:1], None, ALU.mult,
                    waits=(tok_lnh, tok_hm))
    for ch in range(NKC):
        i2 = ch % 2
        po = c.TVIEWS[i2][0:128, 0:2]
        fr = TRB.free[i2]
        fr = () if fr is None else ((fr,) if isinstance(fr[0], str) else fr)
        tt = E.op("t", "transpose", po, lnh[0:2, ch * 128:(ch + 1) * 128],
                  ident[0:2, 0:2], waits=(tok_lnhm,) + fr)
        e1 = E.op("v", "tensor_copy", ln3fm[:, ch * CW:ch * CW + 1],
                  po[:, 0:1], waits=(tt,))
        e2 = E.op("s", "copy", ln3fm[:, ch * CW + 1025:ch * CW + 1026],
                  po[:, 1:2], waits=(tt,))
        TRB.free[i2] = (e1, e2)
    ph1_done = (E.tok("v"), E.tok("s"))
    def conv_slice(m):
        if m < 4:
            return PSA[:, m * 512:(m + 1) * 512]
        return PSB[:, (m - 4) * 512:(m - 3) * 512]

    piece_free = [None] * 4
    slice_free = [None] * 8
    for n in range(6):
        c0 = n * 512
        step = 0
        for k in range(NKC):
            for tap in range(3):
                buf = step % 4
                step += 1
                tw = E.dma("dg", f"c_cw{buf}", cpc[:, buf * 512:(buf + 1) * 512],
                           cw[tap][k * 128:(k + 1) * 128, c0:c0 + 512],
                           waits=(piece_free[buf],))
                for m in range(8):
                    w = [tw]
                    if step == 1:
                        w += list(ph1_done) + [slice_free[m]]
                    lhs = ln3fm[:, k * CW + m * 128 + tap:
                                k * CW + m * 128 + tap + 128]
                    rhs = cpc[:, buf * 512:(buf + 1) * 512]
                    tok = E.op("t", "matmul", conv_slice(m), lhs, rhs,
                               start=(step == 1), stop=False, waits=w)
                    if m == 7:
                        piece_free[buf] = tok
        for m in range(8):
            tok_done = E.op("t", "matmul", conv_slice(m), ones[0:1, :],
                            cbr[0:1, c0:c0 + 512], start=False, stop=True,
                            waits=(tok_cb,))
            eng = "v" if m % 2 == 0 else "s"
            meth = "tensor_copy" if eng == "v" else "copy"
            slice_free[m] = E.op(eng, meth,
                                 qkv_tm[:, m * 3072 + c0:m * 3072 + c0 + 512],
                                 conv_slice(m), waits=(tok_done,))
    E.barrier()

    tok_wq = E.dma("dg", "c_wq", wq_sb[:, :].rearrange("p (c n) -> p c n", n=1024),
                   wq.rearrange("(c p) n -> p c n", p=128))
    tok_wk = E.dma("dg", "c_wk", wk_sb[:, :].rearrange("p (c n) -> p c n", n=1024),
                   wk.rearrange("(c p) n -> p c n", p=128))
    tok_wv = E.dma("dg", "c_wv", wv_sb[:, :].rearrange("p (c n) -> p c n", n=1024),
                   wv.rearrange("(c p) n -> p c n", p=128))
    tok_wo2 = E.dma("dg", "c_wo2", wo2_sb[:, :].rearrange("p (c n) -> p c n", n=1024),
                    wo2.rearrange("(c p) n -> p c n", p=128))
    tok_bqc = E.dma("dy", "c_bqc", bqc_sb[:, :], bqc)
    tok_bkc = E.dma("dy", "c_bkc", bkc_sb[:, :], bkc)
    tok_bvr = E.dma("dg", "c_bvr", bvr_sb, bvr)
    tok_bo2 = E.dma("dg", "c_bo2", bo2_sb, bo2)

    PH = [PSA[:, 0:128], PSA[:, 512:640]]
    PV = [PSA[:, 1024:1536], PSA[:, 1536:2048]]
    PS = [PSB[:, 1024:1152], PSB[:, 1536:1664]]
    ph_free = [None, None]
    pv_free = [None, None]
    ps_free = [None, None]

    for m in range(8):
        base = m * 3072
        qf = []
        for c3 in range(8):
            tt, te = TRB.go(qkv_tm[:, base + c3 * 128:base + (c3 + 1) * 128],
                            q_fm[:, c3 * 128:(c3 + 1) * 128], waits=())
        for c3 in range(8):
            tt, te = TRB.go(qkv_tm[:, base + 1024 + c3 * 128:base + 1024 + (c3 + 1) * 128],
                            k_fm[:, c3 * 128:(c3 + 1) * 128], waits=())
        for c3 in range(8):
            tt, te = TRB.go(qkv_tm[:, base + 2048 + c3 * 128:base + 2048 + (c3 + 1) * 128],
                            v_fm[:, c3 * 128:(c3 + 1) * 128], waits=())
        fm_ready = (E.tok("v"), E.tok("s"))
        for (wsb, bias_sb, dst, tok_w, tok_b) in (
                (wq_sb, bqc_sb, qh_fm, tok_wq, tok_bqc),
                (wk_sb, bkc_sb, kh_fm, tok_wk, tok_bkc)):
            for hc in range(8):
                pview = PH[hc % 2]
                tok_mm = None
                for wc in range(8):
                    w = [tok_w] + list(fm_ready)
                    if wc == 0:
                        w.append(ph_free[hc % 2])
                    if wc == 7:
                        tok_mm = E.op("t", "matmul", pview[:, 0:128],
                                      wsb[:, wc * 1024 + hc * 128:wc * 1024 + (hc + 1) * 128],
                                      (q_fm if dst is qh_fm else k_fm)[:, wc * 128:(wc + 1) * 128],
                                      start=(wc == 0), stop=(wc == 7), waits=w)
                    else:
                        E.op_("t", "matmul", pview[:, 0:128],
                              wsb[:, wc * 1024 + hc * 128:wc * 1024 + (hc + 1) * 128],
                              (q_fm if dst is qh_fm else k_fm)[:, wc * 128:(wc + 1) * 128],
                              start=(wc == 0), stop=(wc == 7), waits=w)
                tok_ev = E.op("v", "tensor_scalar_add",
                              dst[:, hc * 128:(hc + 1) * 128],
                              pview[:, 0:128], bias_sb[:, hc:hc + 1],
                              waits=(tok_mm, tok_b))
                ph_free[hc % 2] = tok_ev
        for n2 in range(2):
            pview = PV[n2]
            tok_mm = None
            for wc in range(8):
                w = [tok_wv] + list(fm_ready)
                if wc == 0:
                    w.append(pv_free[n2])
                E.op_("t", "matmul", pview,
                      v_fm[:, wc * 128:(wc + 1) * 128],
                      wv_sb[:, wc * 1024 + n2 * 512:wc * 1024 + n2 * 512 + 512],
                      start=(wc == 0), stop=False, waits=w)
            tok_mm = E.op("t", "matmul", pview, ones[0:1, :],
                          bvr_sb[0:1, n2 * 512:n2 * 512 + 512],
                          start=False, stop=True, waits=(tok_bvr,))
            tok_ev = E.op("s", "copy", vh_tm[:, n2 * 512:n2 * 512 + 512],
                          pview, waits=(tok_mm,))
            pv_free[n2] = tok_ev
        vh_ready = pv_free[:]
        qh_ready = (E.tok("v"),)
        tok_o_ev = [None, None]
        exp_prev = [None, None]
        tt_prev = [None, None]
        ps_free2 = [ps_free[0], ps_free[1]]
        sc_tok = [None] * (NH + 1)

        def emit_score(h):
            hc = h // 2
            pb_ = (h % 2) * 64
            psc = PS[h % 2]
            return E.op("t", "matmul", psc[:, 0:128],
                        qh_fm[pb_:pb_ + 64, hc * 128:(hc + 1) * 128],
                        kh_fm[pb_:pb_ + 64, hc * 128:(hc + 1) * 128],
                        start=True, stop=True,
                        waits=list(qh_ready) + [ps_free2[h % 2]])

        sc_tok[0] = emit_score(0)
        for h in range(NH):
            par = h % 2
            psc = PS[par]
            if h + 1 < NH:
                sc_tok[h + 1] = emit_score(h + 1)
            tok_stt = E.op("v", "scalar_tensor_tensor",
                           s_sb[:, par * 128:(par + 1) * 128],
                           psc[:, 0:128], SCALE, cmask[:, :],
                           ALU.mult, ALU.add,
                           waits=(sc_tok[h], exp_prev[par]))
            ps_free2[par] = tok_stt
            tok_mx = E.op("v", "reduce_max", nm[:, par:par + 1],
                          s_sb[:, par * 128:(par + 1) * 128], axis=AX,
                          negate=True, waits=(tok_stt,))
            tok_exp = E.op("s", "activation", P[:, par * 128:(par + 1) * 128],
                           s_sb[:, par * 128:(par + 1) * 128], AF.Exp,
                           bias=nm[:, par:par + 1], scale=1.0,
                           accum_out=rsum[:, par:par + 1],
                           waits=(tok_mx, tt_prev[par]))
            exp_prev[par] = tok_exp
            tok_rr = E.op("v", "reciprocal", rrec[:, par:par + 1],
                          rsum[:, par:par + 1], waits=(tok_exp,))
            tt, te = TRB.go(P[:, par * 128:(par + 1) * 128],
                            PT_sb[:, par * 128:(par + 1) * 128],
                            waits=(tok_exp,))
            tt_prev[par] = tt
            tok_pv = E.op("t", "matmul", PH[par][:, 0:64],
                          PT_sb[:, par * 128:(par + 1) * 128],
                          vh_tm[:, h * 64:(h + 1) * 64],
                          start=True, stop=True,
                          waits=[te, ph_free[par]] + vh_ready)
            tok_oev = E.op("s", "activation", o_tm[:, h * 64:(h + 1) * 64],
                           PH[par][:, 0:64], AF.Copy,
                           scale=rrec[:, par:par + 1],
                           waits=(tok_pv, tok_rr))
            ph_free[par] = tok_oev
            tok_o_ev[par] = tok_oev
        for c3 in range(8):
            tt, te = TRB.go(o_tm[:, c3 * 128:(c3 + 1) * 128],
                            o_fm[:, c3 * 128:(c3 + 1) * 128],
                            waits=[tok_o_ev[0], tok_o_ev[1]])
        ofm_ready = (E.tok("v"), E.tok("s"))
        for n2 in range(2):
            pview = PV[n2]
            tok_mm = None
            for hc in range(8):
                w = [tok_wo2] + list(ofm_ready)
                if hc == 0:
                    w.append(pv_free[n2])
                E.op_("t", "matmul", pview,
                      o_fm[:, hc * 128:(hc + 1) * 128],
                      wo2_sb[:, hc * 1024 + n2 * 512:hc * 1024 + n2 * 512 + 512],
                      start=(hc == 0), stop=False, waits=w)
            tok_mm = E.op("t", "matmul", pview, ones[0:1, :],
                          bo2_sb[0:1, n2 * 512:n2 * 512 + 512],
                          start=False, stop=True, waits=(tok_bo2,))
            tok_ev = E.op("v", "tensor_copy",
                          ao_tm[:, m * 1024 + n2 * 512:m * 1024 + n2 * 512 + 512],
                          pview, waits=(tok_mm,))
            pv_free[n2] = tok_ev
    E.barrier()

    tok_pb = E.dma("dg", "c_pbias", pbr, pb)
    piece_free = [None] * 4
    psa_free = [None] * 4
    tok_x4_free = None
    for grp in range(2):
        tiles = [grp * 4 + i for i in range(4)]
        for i, m in enumerate(tiles):
            for c3 in range(8):
                TRB.go(ao_tm[:, m * 1024 + c3 * 128:m * 1024 + (c3 + 1) * 128],
                       at_fm[:, i * 1024 + c3 * 128:i * 1024 + (c3 + 1) * 128],
                       waits=())
        at_ready = (E.tok("v"), E.tok("s"))
        dens_done = [[None] * 8 for _ in range(4)]
        for n in range(8):
            c0 = n * 512
            for k in range(8):
                buf = k % 4
                tw = E.dma("dg", f"c_pw{buf}", ppc[:, buf * 512:(buf + 1) * 512],
                           pw[k * 128:(k + 1) * 128, c0:c0 + 512],
                           waits=(piece_free[buf],))
                for i in range(4):
                    w = [tw]
                    if k == 0:
                        w += list(at_ready) + [psa_free[i]]
                    if i == 3:
                        piece_free[buf] = E.op("t", "matmul",
                                               PSA[:, i * 512:(i + 1) * 512],
                                               at_fm[:, i * 1024 + k * 128:i * 1024 + (k + 1) * 128],
                                               ppc[:, buf * 512:(buf + 1) * 512],
                                               start=(k == 0), stop=False,
                                               waits=w)
                    else:
                        E.op_("t", "matmul", PSA[:, i * 512:(i + 1) * 512],
                              at_fm[:, i * 1024 + k * 128:i * 1024 + (k + 1) * 128],
                              ppc[:, buf * 512:(buf + 1) * 512],
                              start=(k == 0), stop=False, waits=w)
            for i in range(4):
                m = tiles[i]
                tok_done = E.op("t", "matmul", PSA[:, i * 512:(i + 1) * 512],
                                ones[0:1, :], pbr[0:1, c0:c0 + 512],
                                start=False, stop=True, waits=(tok_pb,))
                eng = "v" if i % 2 == 0 else "s"
                meth = "tensor_copy" if eng == "v" else "copy"
                tev = E.op(eng, meth, dens[:, i * DM + c0:i * DM + c0 + 512],
                           PSA[:, i * 512:(i + 1) * 512], waits=(tok_done,))
                psa_free[i] = tev
                dens_done[i][n] = tev
        tok_out_last = None
        for i in range(4):
            m = tiles[i]
            dview = dens[:, i * DM:(i + 1) * DM]
            t1, t2 = ln_stats(E, dview, p_s1[:, m:m + 1], p_s2[:, m:m + 1],
                              sqs4, waits=tuple(dens_done[i]))
            tr, tn = ln_chain(E, p_s1[:, m:m + 1], p_s2[:, m:m + 1],
                              p_r[:, m:m + 1], p_nmr[:, m:m + 1],
                              p_tmp[:, m:m + 1], DM, t1, t2)
            tx = E.dma("dy", "c_x4", x_st4, x2i[m * 128:(m + 1) * 128, :],
                       waits=(tok_x4_free,))
            tap = E.op("v", "tensor_scalar", dview, dview, p_r[:, m:m + 1],
                       p_nmr[:, m:m + 1], ALU.mult, ALU.add,
                       waits=(tr, tn))
            tadd = E.op("v", "tensor_tensor", dview, dview, x_st4, ALU.add,
                        waits=(tap, tx))
            tok_x4_free = tadd
            tok_out_last = E.dma("dy", "c_out", x3_out[m * 128:(m + 1) * 128, :],
                                 dview, waits=(tadd,))
        E.wait("v", tok_out_last)
        E.wait("s", tok_out_last)

    E.barrier()


def emit_l1_ffn(c, x3i, w1, w3, w2, a1, b1, a3, b3, a2, b2, out_ap):
    nc, E, ar = c.nc, c.E, c.ar
    PSA, PSB = c.PSA, c.PSB
    ident, ones = c.ident, c.ones
    s1a, s2a, ra, nmra, tmpa = c.s1a, c.s2a, c.ra, c.nmra, c.tmpa
    p_s1, p_s2, p_r, p_nmr, p_tmp = c.p_s1, c.p_s2, c.p_r, c.p_nmr, c.p_tmp
    HID = 11008
    NHC = 86
    HCS = [1024] * 10 + [768]
    SCALING = 2.0

    def spans(L):
        return [(0, 512), (512, L)] if L > 512 else [(0, L)]

    ar.reset()
    a1_sb = ar.take(128, 32 * 16, BF16)
    a3_sb = ar.take(128, 32 * 16, BF16)
    a2_sb = ar.take(128, NHC * 16, BF16)
    b2_sb = ar.take(16, DM, BF16)
    x_st = ar.take(128, DM, F32)
    ln_tm = ar.take(128, DM, BF16)
    lnfm = ar.take(128, 4 * DM, BF16)
    dens = ar.take(128, 4 * DM, F32)
    sqs = dens[:, 0:DM]
    sq_post = lnfm[:, 0:2 * DM].bitcast(F32)
    h1_sb = ar.take(128, 4 * 1024, BF16)
    sl_sb = ar.take(128, 4 * 1024, BF16)
    g_sb = ar.take(128, 4 * 1024, BF16)
    g_fm = ar.take(128, 4 * 1024, BF16)
    wpc = ar.take(128, 4 * 1024, BF16)
    w2pc = ar.take(128, 4 * 1024, BF16)
    b1s = ar.take(16, 1024, BF16)
    b3s = ar.take(16, 1024, BF16)
    xa_bf = ar.take(128, 16, BF16)
    xa1_fm = ar.take(16, 4 * 128, BF16)
    xa3_fm = ar.take(16, 4 * 128, BF16)
    ga2_sb = ar.take(128, 4 * 16, F32)
    ga2_bf = ar.take(128, 16, BF16)
    ga2_fm = ar.take(16, 4 * 128, BF16)

    TVIEWS = c.TVIEWS
    PH = [PSA[:, 0:1024], PSA[:, 1024:2048],
          PSB[:, 0:1024], PSB[:, 1024:2048]]
    PGA = PSA[:, 0:16]

    TR = TRot(E, TVIEWS, ident)

    E.dma("dg", "d_a1", a1_sb[:, :].rearrange("p (c n) -> p c n", n=16),
          a1.rearrange("(c p) n -> p c n", p=128))
    E.dma("dg", "d_a3", a3_sb[:, :].rearrange("p (c n) -> p c n", n=16),
          a3.rearrange("(c p) n -> p c n", p=128))
    E.dma("dg", "d_a2", a2_sb[:, :].rearrange("p (c n) -> p c n", n=16),
          a2.rearrange("(c p) n -> p c n", p=128))
    E.dma("dg", "d_b2", b2_sb[0:16, :], b2)
    tok_a1 = E.tok("D:d_a1")
    tok_a3 = E.tok("D:d_a3")
    tok_a2 = E.tok("D:d_a2")
    tok_b2 = E.tok("D:d_b2")
    E.barrier()

    tok_xst_free = None
    tok_lntm_free = None
    tok_out_last = None
    tok_b1_used = None
    tok_b3_used = None
    for grp in range(2):
        tiles = [grp * 4 + i for i in range(4)]
        for i, m in enumerate(tiles):
            td = E.dma("dy", "d_x", x_st, x3i[m * 128:(m + 1) * 128, :],
                       waits=(tok_xst_free,))
            t1, t2 = ln_stats(E, x_st, s1a[:, m:m + 1], s2a[:, m:m + 1], sqs,
                              waits=(td,))
            tr, tn = ln_chain(E, s1a[:, m:m + 1], s2a[:, m:m + 1],
                              ra[:, m:m + 1], nmra[:, m:m + 1],
                              tmpa[:, m:m + 1], DM, t1, t2)
            tok_ap = E.op("v", "tensor_scalar", ln_tm, x_st, ra[:, m:m + 1],
                          nmra[:, m:m + 1], ALU.mult, ALU.add,
                          waits=(tr, tn, tok_lntm_free))
            tok_xst_free = tok_ap
            last_tt = None
            for c in range(NKC):
                tt, te = TR.go(ln_tm[:, c * 128:(c + 1) * 128],
                               lnfm[:, i * DM + c * 128:i * DM + (c + 1) * 128],
                               waits=(tok_ap,))
                last_tt = tt
            tok_lntm_free = last_tt
        lnfm_ready = (E.tok("v"), E.tok("s"))

        tok_pga_free = None
        for (asb, xfm, tok_a) in ((a1_sb, xa1_fm, tok_a1),
                                  (a3_sb, xa3_fm, tok_a3)):
            for i in range(4):
                tok_mm = None
                for k in range(NKC):
                    w = list(lnfm_ready) + [tok_a]
                    if k == 0:
                        w.append(tok_pga_free)
                    tok_mm = E.op("t", "matmul", PGA,
                                  lnfm[:, i * DM + k * 128:i * DM + (k + 1) * 128],
                                  asb[:, k * 16:(k + 1) * 16],
                                  start=(k == 0), stop=(k == 31), waits=w)
                tok_ev = E.op("s", "activation", xa_bf[:, :], PGA, AF.Copy,
                              scale=SCALING, waits=(tok_mm,))
                tok_pga_free = tok_ev
                tt, te = TR.go(xa_bf[:, :], xfm[0:16, i * 128:(i + 1) * 128],
                               waits=(tok_ev,))
        xa_ready = (E.tok("v"), E.tok("s"))
        tok_ga2_z = E.op("v", "memset", ga2_sb[:, :], 0.0)
        tok_ga2 = [tok_ga2_z] * 4

        E.wait("v", E.tok("s"))
        dens_tok = [[None] * 4 for _ in range(4)]
        ph_free = [xa_ready] * 4
        piece_free = [None] * 4
        w2_free = [None] * 4
        hc0 = 0
        for ci, L in enumerate(HCS):
            khs = L // 128
            tb1 = E.dma("dg", "d_b1s", b1s[0:16, 0:L], b1[0:16, hc0:hc0 + L],
                        waits=(tok_b1_used,))
            h1_done = [None] * 4
            for k in range(NKC):
                buf = k % 4
                tw = E.dma("dg", f"d_wp{buf}", wpc[:, buf * 1024:buf * 1024 + L],
                           w1[k * 128:(k + 1) * 128, hc0:hc0 + L],
                           waits=(piece_free[buf],))
                for i in range(4):
                    w = [tw]
                    if k == 0:
                        w += list(lnfm_ready) + list(ph_free[i])
                    for si, (s0, s1_) in enumerate(spans(L)):
                        last = (i == 3 and si == len(spans(L)) - 1)
                        if last:
                            piece_free[buf] = E.op(
                                "t", "matmul", PH[i][:, s0:s1_],
                                lnfm[:, i * DM + k * 128:i * DM + (k + 1) * 128],
                                wpc[:, buf * 1024 + s0:buf * 1024 + s1_],
                                start=(k == 0), stop=False, waits=w)
                        else:
                            E.op_("t", "matmul", PH[i][:, s0:s1_],
                                  lnfm[:, i * DM + k * 128:i * DM + (k + 1) * 128],
                                  wpc[:, buf * 1024 + s0:buf * 1024 + s1_],
                                  start=(k == 0), stop=False, waits=w)
            for i in range(4):
                tok = None
                for (s0, s1_) in spans(L):
                    tok = E.op("t", "matmul", PH[i][:, s0:s1_],
                               xa1_fm[0:16, i * 128:(i + 1) * 128],
                               b1s[0:16, s0:s1_],
                               start=False, stop=True,
                               waits=[tb1] + list(xa_ready))
                if i == 3:
                    tok_b1_used = tok
                eng = "v" if i % 2 == 0 else "s"
                meth = "tensor_copy" if eng == "v" else "copy"
                h1_done[i] = E.op(eng, meth, h1_sb[:, i * 1024:i * 1024 + L],
                                  PH[i][0:128, 0:L], waits=(tok,))
                ph_free[i] = (h1_done[i],)
            sl_done = [None] * 4
            for i in range(4):
                sl_done[i] = E.op("s", "activation",
                                  sl_sb[:, i * 1024:i * 1024 + L],
                                  h1_sb[:, i * 1024:i * 1024 + L], AF.Silu,
                                  waits=(h1_done[i],))
            tb3 = E.dma("dg", "d_b3s", b3s[0:16, 0:L], b3[0:16, hc0:hc0 + L],
                        waits=(tok_b3_used,))
            g_done = [None] * 4
            for k in range(NKC):
                buf = k % 4
                tw = E.dma("dg", f"d_wp{buf}", wpc[:, buf * 1024:buf * 1024 + L],
                           w3[k * 128:(k + 1) * 128, hc0:hc0 + L],
                           waits=(piece_free[buf],))
                for i in range(4):
                    w = [tw]
                    if k == 0:
                        w += list(ph_free[i])
                    for si, (s0, s1_) in enumerate(spans(L)):
                        last = (i == 3 and si == len(spans(L)) - 1)
                        if last:
                            piece_free[buf] = E.op(
                                "t", "matmul", PH[i][:, s0:s1_],
                                lnfm[:, i * DM + k * 128:i * DM + (k + 1) * 128],
                                wpc[:, buf * 1024 + s0:buf * 1024 + s1_],
                                start=(k == 0), stop=False, waits=w)
                        else:
                            E.op_("t", "matmul", PH[i][:, s0:s1_],
                                  lnfm[:, i * DM + k * 128:i * DM + (k + 1) * 128],
                                  wpc[:, buf * 1024 + s0:buf * 1024 + s1_],
                                  start=(k == 0), stop=False, waits=w)
            for i in range(4):
                tok = None
                for (s0, s1_) in spans(L):
                    tok = E.op("t", "matmul", PH[i][:, s0:s1_],
                               xa3_fm[0:16, i * 128:(i + 1) * 128],
                               b3s[0:16, s0:s1_],
                               start=False, stop=True,
                               waits=[tb3] + list(xa_ready))
                if i == 3:
                    tok_b3_used = tok
                g_done[i] = E.op("v", "tensor_tensor",
                                 g_sb[:, i * 1024:i * 1024 + L],
                                 PH[i][0:128, 0:L],
                                 sl_sb[:, i * 1024:i * 1024 + L],
                                 ALU.mult, waits=(tok, sl_done[i]))
                ph_free[i] = (g_done[i],)
            gfm_tok = [None] * 4
            for i in range(4):
                for kc in range(khs):
                    TR.go(g_sb[:, i * 1024 + kc * 128:i * 1024 + (kc + 1) * 128],
                          g_fm[:, i * 1024 + kc * 128:i * 1024 + (kc + 1) * 128],
                          waits=tuple(g_done))
                gfm_tok[i] = (E.tok("v"), E.tok("s"))
            for i in range(4):
                tok = None
                for kc in range(khs):
                    w = list(gfm_tok[i]) + [tok_a2]
                    if kc == 0:
                        w += [tok_pga_free] + list(ph_free[0])
                    kglob = (hc0 // 128) + kc
                    tok = E.op("t", "matmul", PGA,
                               g_fm[:, i * 1024 + kc * 128:i * 1024 + (kc + 1) * 128],
                               a2_sb[:, kglob * 16:(kglob + 1) * 16],
                               start=(kc == 0), stop=(kc == khs - 1), waits=w)
                tok_ga2[i] = E.op("v", "tensor_tensor",
                                  ga2_sb[:, i * 16:(i + 1) * 16],
                                  ga2_sb[:, i * 16:(i + 1) * 16], PGA,
                                  ALU.add, waits=(tok, tok_ga2[i]))
                tok_pga_free = tok_ga2[i]
            for dmq in range(4):
                c0 = dmq * 1024
                tokmm = [None] * 4
                for kc in range(khs):
                    buf = kc % 4
                    kglob = (hc0 // 128) + kc
                    tw2 = E.dma("dg", f"d_w2{buf}",
                                w2pc[:, buf * 1024:(buf + 1) * 1024],
                                w2[kglob * 128:(kglob + 1) * 128, c0:c0 + 1024],
                                waits=(w2_free[buf],))
                    for i in range(4):
                        w = [tw2] + list(gfm_tok[i])
                        if kc == 0:
                            w += [dens_tok[i][dmq], tok_pga_free] + list(ph_free[i])
                        for n5 in range(2):
                            need = (n5 == 1 and (i == 3 or kc == khs - 1))
                            if need:
                                tok = E.op("t", "matmul",
                                           PH[i][:, n5 * 512:(n5 + 1) * 512],
                                           g_fm[:, i * 1024 + kc * 128:i * 1024 + (kc + 1) * 128],
                                           w2pc[:, buf * 1024 + n5 * 512:buf * 1024 + (n5 + 1) * 512],
                                           start=(kc == 0), stop=(kc == khs - 1),
                                           waits=w)
                                if kc == khs - 1:
                                    tokmm[i] = tok
                                if i == 3:
                                    w2_free[buf] = tok
                            else:
                                E.op_("t", "matmul",
                                      PH[i][:, n5 * 512:(n5 + 1) * 512],
                                      g_fm[:, i * 1024 + kc * 128:i * 1024 + (kc + 1) * 128],
                                      w2pc[:, buf * 1024 + n5 * 512:buf * 1024 + (n5 + 1) * 512],
                                      start=(kc == 0), stop=(kc == khs - 1),
                                      waits=w)
                for i in range(4):
                    eng = "v" if i % 2 == 0 else "s"
                    dsl = dens[:, i * DM + c0:i * DM + c0 + 1024]
                    if ci == 0:
                        meth = "tensor_copy" if eng == "v" else "copy"
                        tev = E.op(eng, meth, dsl, PH[i],
                                   waits=(tokmm[i], dens_tok[i][dmq]))
                    else:
                        tev = E.op("v", "tensor_tensor", dsl, dsl, PH[i],
                                   ALU.add, waits=(tokmm[i], dens_tok[i][dmq]))
                    dens_tok[i][dmq] = tev
                    ph_free[i] = (tev,)
            hc0 += L

        all_dens = tuple(t for row in dens_tok for t in row)
        prev_tt = None
        for i in range(4):
            m = tiles[i]
            tcv = E.op("s", "activation", ga2_bf[:, :],
                       ga2_sb[:, i * 16:(i + 1) * 16],
                       AF.Copy, scale=SCALING, waits=(tok_ga2[i], prev_tt))
            tt, te = TR.go(ga2_bf[:, :], ga2_fm[0:16, i * 128:(i + 1) * 128],
                           waits=(tcv,) + all_dens)
            prev_tt = tt
            for dmq in range(4):
                tok = None
                for n5 in range(2):
                    c0 = dmq * 1024 + n5 * 512
                    tok = E.op("t", "matmul",
                               PH[i][:, n5 * 512:(n5 + 1) * 512],
                               ga2_fm[0:16, i * 128:(i + 1) * 128],
                               b2_sb[0:16, c0:c0 + 512],
                               start=True, stop=True,
                               waits=[te, tok_b2] + list(ph_free[i]))
                tev = E.op("v", "tensor_tensor",
                           dens[:, i * DM + dmq * 1024:i * DM + (dmq + 1) * 1024],
                           dens[:, i * DM + dmq * 1024:i * DM + (dmq + 1) * 1024],
                           PH[i], ALU.add, waits=(tok, dens_tok[i][dmq]))
                dens_tok[i][dmq] = tev
                ph_free[i] = (tev,)
            dview = dens[:, i * DM:(i + 1) * DM]
            t1, t2 = ln_stats(E, dview, p_s1[:, m:m + 1], p_s2[:, m:m + 1],
                              sq_post, waits=tuple(dens_tok[i]))
            tr, tn = ln_chain(E, p_s1[:, m:m + 1], p_s2[:, m:m + 1],
                              p_r[:, m:m + 1], p_nmr[:, m:m + 1],
                              p_tmp[:, m:m + 1], DM, t1, t2)
            tx = E.dma("dy", "d_x", x_st, x3i[m * 128:(m + 1) * 128, :],
                       waits=(tok_xst_free,))
            tap = E.op("v", "tensor_scalar", dview, dview, p_r[:, m:m + 1],
                       p_nmr[:, m:m + 1], ALU.mult, ALU.add, waits=(tr, tn))
            tadd = E.op("v", "tensor_tensor", dview, dview, x_st, ALU.add,
                        waits=(tap, tx))
            tok_xst_free = tadd
            tok_out_last = E.dma("dy", "d_out", out_ap[m * 128:(m + 1) * 128, :],
                                 dview, waits=(tadd,))
        E.barrier()

    E.barrier()




def build_fused():
    nc = bass.Bass("TRN2", target_bir_lowering=False)

    xo = nc.dram_tensor("xo", [1024, DM], F32, kind="ExternalInput").ap()
    xr = nc.dram_tensor("xr", [1024, DM], F32, kind="ExternalInput").ap()
    wqkv = nc.dram_tensor("wqkv", [DM, 3072], F32, kind="ExternalInput").ap()
    bqkv = nc.dram_tensor("bqkv", [1, 3072], F32, kind="ExternalInput").ap()
    wo = nc.dram_tensor("wo", [1024, DM], F32, kind="ExternalInput").ap()
    bo = nc.dram_tensor("bo", [1, DM], F32, kind="ExternalInput").ap()
    w0 = nc.dram_tensor("w0", [DM, DM], F32, kind="ExternalInput").ap()
    b0 = nc.dram_tensor("b0", [1, DM], F32, kind="ExternalInput").ap()
    cent = nc.dram_tensor("cent", [DM, 64], F32, kind="ExternalInput").ap()
    assign = nc.dram_tensor("assign", [1, DM], I32, kind="ExternalInput").ap()
    cw = [nc.dram_tensor(f"cw{t}", [DM, 3072], F32, kind="ExternalInput").ap()
          for t in range(3)]
    cb = nc.dram_tensor("cb", [1, 3072], F32, kind="ExternalInput").ap()
    wq = nc.dram_tensor("wq", [1024, 1024], F32, kind="ExternalInput").ap()
    wk = nc.dram_tensor("wk", [1024, 1024], F32, kind="ExternalInput").ap()
    wv = nc.dram_tensor("wv", [1024, 1024], F32, kind="ExternalInput").ap()
    bqc = nc.dram_tensor("bqc", [128, 8], F32, kind="ExternalInput").ap()
    bkc = nc.dram_tensor("bkc", [128, 8], F32, kind="ExternalInput").ap()
    bvr = nc.dram_tensor("bvr", [1, 1024], F32, kind="ExternalInput").ap()
    wo2 = nc.dram_tensor("wo2", [1024, 1024], F32, kind="ExternalInput").ap()
    bo2 = nc.dram_tensor("bo2", [1, 1024], F32, kind="ExternalInput").ap()
    pw = nc.dram_tensor("pw", [1024, DM], F32, kind="ExternalInput").ap()
    pb = nc.dram_tensor("pb", [1, DM], F32, kind="ExternalInput").ap()
    halo_sel = nc.dram_tensor("halo_sel", [16, 2], F32, kind="ExternalInput").ap()
    halo_mask = nc.dram_tensor("halo_mask", [2, 1], F32, kind="ExternalInput").ap()
    w1 = nc.dram_tensor("w1", [DM, 11008], F32, kind="ExternalInput").ap()
    w3 = nc.dram_tensor("w3", [DM, 11008], F32, kind="ExternalInput").ap()
    w2 = nc.dram_tensor("w2", [11008, DM], F32, kind="ExternalInput").ap()
    a1 = nc.dram_tensor("a1", [DM, 16], F32, kind="ExternalInput").ap()
    b1 = nc.dram_tensor("b1", [16, 11008], F32, kind="ExternalInput").ap()
    a3 = nc.dram_tensor("a3", [DM, 16], F32, kind="ExternalInput").ap()
    b3 = nc.dram_tensor("b3", [16, 11008], F32, kind="ExternalInput").ap()
    a2 = nc.dram_tensor("a2", [11008, 16], F32, kind="ExternalInput").ap()
    b2 = nc.dram_tensor("b2", [16, DM], F32, kind="ExternalInput").ap()
    out = nc.dram_tensor("out", [1024, DM], F32, kind="ExternalOutput").ap()

    xs1 = nc.dram_tensor("xs1", [1024, DM], F32)
    xs2 = nc.dram_tensor("xs2", [1024, DM], F32)
    xs3 = nc.dram_tensor("xs3", [1024, DM], F32)
    cc_in = nc.dram_tensor("cc_in", [64, 1], F32)
    cc_out = nc.dram_tensor("cc_out", [64, 1], F32)
    hg_in = nc.dram_tensor("hg_in", [2, DM], F32)
    hg_out = nc.dram_tensor("hg_out", [16, DM], F32)

    c = make_ctx(nc)
    c.E.barrier()
    emit_l0_attn(c, xo, xr, wqkv, bqkv, wo, bo, xs1.ap())
    emit_l0_ffn(c, xs1.ap(), w0, b0, cent, assign, cc_in, cc_out, xs2.ap())
    emit_l1_attn(c, xs2.ap(), cw, cb, wq, wk, wv, bqc, bkc, bvr, wo2, bo2,
                 pw, pb, halo_sel, halo_mask, hg_in, hg_out, xs3.ap())
    emit_l1_ffn(c, xs3.ap(), w1, w3, w2, a1, b1, a3, b3, a2, b2, out)
    c.E.wait("y", *c.E.dma_totals())
    return nc


def host_inputs_fused(d):
    x0 = np.ascontiguousarray(d["L0_attn_x"])
    w0T = np.ascontiguousarray(d["L0_ffn_weight"].T)
    centT = np.ascontiguousarray(d["L0_ffn_centroids"].T)
    cwf = d["L1_attn_conv_w"]
    shared = {
        "wqkv": np.ascontiguousarray(d["L0_attn_Wqkv"]),
        "bqkv": np.ascontiguousarray(d["L0_attn_bqkv"][None, :]),
        "wo": np.ascontiguousarray(d["L0_attn_Wo"]),
        "bo": np.ascontiguousarray(d["L0_attn_bo"][None, :]),
        "w0": w0T,
        "b0": np.ascontiguousarray(d["L0_ffn_bias"][None, :]),
        "cent": centT,
        "assign": np.ascontiguousarray(d["L0_ffn_assignments"][None, :]),
        "cw0": np.ascontiguousarray(cwf[0]),
        "cw1": np.ascontiguousarray(cwf[1]),
        "cw2": np.ascontiguousarray(cwf[2]),
        "cb": np.ascontiguousarray(d["L1_attn_conv_b"][None, :]),
        "wq": np.ascontiguousarray(d["L1_attn_wq"].reshape(1024, 1024)),
        "wk": np.ascontiguousarray(d["L1_attn_wk"].reshape(1024, 1024)),
        "wv": np.ascontiguousarray(d["L1_attn_wv"].reshape(1024, 1024)),
        "bqc": np.ascontiguousarray(d["L1_attn_bq"].reshape(8, 128).T),
        "bkc": np.ascontiguousarray(d["L1_attn_bk"].reshape(8, 128).T),
        "bvr": np.ascontiguousarray(d["L1_attn_bv"].reshape(1, 1024)),
        "wo2": np.ascontiguousarray(d["L1_attn_wo"].reshape(1024, 1024)),
        "bo2": np.ascontiguousarray(d["L1_attn_bo"][None, :]),
        "pw": np.ascontiguousarray(d["L1_attn_proj_w"]),
        "pb": np.ascontiguousarray(d["L1_attn_proj_b"][None, :]),
        "w1": np.ascontiguousarray(d["L1_ffn_w1"].T),
        "w3": np.ascontiguousarray(d["L1_ffn_w3"].T),
        "w2": np.ascontiguousarray(d["L1_ffn_w2"].T),
        "a1": np.ascontiguousarray(d["L1_ffn_lora_A1"].T),
        "b1": np.ascontiguousarray(d["L1_ffn_lora_B1"].T),
        "a3": np.ascontiguousarray(d["L1_ffn_lora_A3"].T),
        "b3": np.ascontiguousarray(d["L1_ffn_lora_B3"].T),
        "a2": np.ascontiguousarray(d["L1_ffn_lora_A2"].T),
        "b2": np.ascontiguousarray(d["L1_ffn_lora_B2"].T),
    }
    maps = []
    for cid in range(8):
        b, h = cid // 2, cid % 2
        hs = np.zeros((16, 2), np.float32)
        hm = np.zeros((2, 1), np.float32)
        if h == 1:
            hs[2 * cid - 1, 0] = 1.0
            hm[0, 0] = 1.0
        if h == 0:
            hs[2 * cid + 2, 1] = 1.0
            hm[1, 0] = 1.0
        m = dict(shared)
        m["xo"] = np.ascontiguousarray(x0[b, h * 1024:(h + 1) * 1024])
        m["xr"] = np.ascontiguousarray(x0[b, (1 - h) * 1024:(2 - h) * 1024])
        m["halo_sel"] = hs
        m["halo_mask"] = hm
        maps.append(m)
    return maps


def gather_out_fused(results):
    out = np.zeros((4, 2048, 4096), np.float32)
    for cid in range(8):
        b, h = cid // 2, cid % 2
        out[b, h * 1024:(h + 1) * 1024] = results[cid]["out"]
    return out




from concourse.bass_utils import run_bass_kernel_spmd


def kernel(**inputs):
    nc = build_fused()
    res = run_bass_kernel_spmd(nc, host_inputs_fused(inputs),
                               core_ids=list(range(8)))
    return gather_out_fused(res.results)


# version 09
# speedup vs baseline: 1.1731x; 1.0167x over previous
import numpy as np
import concourse.bass as bass
import concourse.mybir as mybir

F32 = mybir.dt.float32
BF16 = mybir.dt.bfloat16
I32 = mybir.dt.int32
U8 = mybir.dt.uint8
AF = mybir.ActivationFunctionType
ALU = mybir.AluOpType
AX = mybir.AxisListType.X

DM = 4096
NKC = DM // 128


class Emit:

    ENG = ("t", "v", "s", "g")

    def __init__(self, nc):
        self.nc = nc
        self.e = {"t": nc.tensor, "v": nc.vector, "s": nc.scalar,
                  "g": nc.gpsimd, "y": nc.sync,
                  "dg": nc.gpsimd, "dy": nc.sync}
        self.sem = {k: nc.alloc_semaphore("sem_" + k) for k in self.ENG}
        self.cnt = {k: 0 for k in self.ENG}
        self.hw = {}

    def _dsem(self, name):
        key = "D:" + name
        if key not in self.sem:
            self.sem[key] = self.nc.alloc_semaphore("sem_" + name)
            self.cnt[key] = 0
        return key

    def wait(self, consumer, *toks):
        for tok in toks:
            if tok is None:
                continue
            k, v = tok
            if v <= 0 or self.hw.get((consumer, k), 0) >= v:
                continue
            self.e[consumer].wait_ge(self.sem[k], v)
            self.hw[(consumer, k)] = v

    def inc(self, inst, key, n=1):
        inst.then_inc(self.sem[key], n)
        self.cnt[key] += n
        return (key, self.cnt[key])

    def tok(self, key):
        return (key, self.cnt[key])

    def op(self, eng, name, *args, waits=(), **kw):
        self.wait(eng, *[w for w in waits if w])
        inst = getattr(self.e[eng], name)(*args, **kw)
        return self.inc(inst, eng, 1)

    def op_(self, eng, name, *args, waits=(), **kw):
        self.wait(eng, *[w for w in waits if w])
        getattr(self.e[eng], name)(*args, **kw)
        return None

    def dma(self, q, stream, out, in_, *, waits=(), **kw):
        key = self._dsem(stream)
        self.wait(q, *[w for w in waits if w])
        inst = self.e[q].dma_start(out, in_, **kw)
        return self.inc(inst, key, 16)

    def barrier(self):
        toks = [self.tok(k) for k in self.cnt]
        for consumer in ("t", "v", "s", "g", "y"):
            self.wait(consumer, *toks)

    def dma_totals(self):
        return [self.tok(k) for k in self.cnt if k.startswith("D:")]


class Arena:

    def __init__(self, nc, name, nbytes):
        self.t = nc.alloc_sbuf_tensor(name, [128, nbytes], U8)
        self.nbytes = nbytes
        self.off = 0

    def reset(self):
        self.off = 0

    def mark(self):
        return self.off

    def rewind(self, m):
        self.off = m

    def view(self, off, part, free, dtype):
        sz = free * np.dtype(mybir.dt.np(dtype)).itemsize
        assert off + sz <= self.nbytes
        return self.t.ap()[0:part, off:off + sz].bitcast(dtype)

    def take(self, part, free, dtype):
        sz = free * np.dtype(mybir.dt.np(dtype)).itemsize
        off = (self.off + 511) & ~511
        assert off + sz <= self.nbytes, f"arena overflow {off}+{sz}>{self.nbytes}"
        self.off = off + sz
        return self.t.ap()[0:part, off:off + sz].bitcast(dtype)


def ln_stats(E, x_ap, s1_ap, s2_ap, sq_ap, waits=()):
    t1 = E.op("v", "reduce_sum", s1_ap, x_ap, axis=AX, waits=waits)
    t2 = E.op("s", "activation", sq_ap, x_ap, AF.Square, accum_out=s2_ap,
              waits=waits)
    return t1, t2


def ln_chain(E, s1_ap, s2_ap, r_ap, nmr_ap, tmp_ap, D, tok_s1, tok_s2):
    t1 = E.op("v", "tensor_tensor", tmp_ap, s1_ap, s1_ap, ALU.mult,
              waits=(tok_s1,))
    t2 = E.op("v", "scalar_tensor_tensor", tmp_ap, tmp_ap, -1.0 / D, s2_ap,
              ALU.mult, ALU.add, waits=(t1, tok_s2))
    t3 = E.op("v", "tensor_scalar", tmp_ap, tmp_ap, 1.0 / D, 1e-5,
              ALU.mult, ALU.add, waits=(t2,))
    sd = E.op("s", "activation", r_ap, tmp_ap, AF.Sqrt, waits=(t3,))
    rt = E.op("v", "reciprocal", r_ap, r_ap, waits=(sd,))
    nm = E.op("v", "scalar_tensor_tensor", nmr_ap, s1_ap, -1.0 / D, r_ap,
              ALU.mult, ALU.mult, waits=(rt, tok_s1))
    return rt, nm


class TPool:

    def __init__(self, E, psts, ident):
        self.E = E
        self.psts = psts
        self.free = [None] * len(psts)
        self.ident = ident
        self.i = 0
        self.flip = 0

    def transpose(self, src_ap, dst_ap, waits=(), evac_eng=None):
        E = self.E
        i = self.i
        self.i = (i + 1) % len(self.psts)
        eng = evac_eng or ("v" if self.flip == 0 else "s")
        self.flip ^= 1
        po = self.psts[i][0:src_ap.shape[1], 0:src_ap.shape[0]]
        tt = E.op("t", "transpose", po, src_ap, self.ident,
                  waits=tuple(waits) + (self.free[i],))
        meth = "copy" if eng == "s" else "tensor_copy"
        te = E.op(eng, meth, dst_ap, po, waits=(tt,))
        self.free[i] = te
        return tt, te




class TRot:

    def __init__(self, E, views, ident):
        self.E = E
        self.views = views
        self.ident = ident
        self.free = [None] * len(views)
        self.i = 0
        self.flip = 0

    def go(self, src_ap, dst_ap, waits=()):
        E = self.E
        i = self.i
        self.i = (i + 1) % len(self.views)
        eng = "v" if self.flip == 0 else "s"
        self.flip ^= 1
        p = src_ap.shape[0]
        po = self.views[i][0:src_ap.shape[1], 0:p]
        fr = self.free[i]
        if fr is None:
            fr = ()
        elif isinstance(fr[0], str):
            fr = (fr,)
        tt = E.op("t", "transpose", po, src_ap, self.ident[0:p, 0:p],
                  waits=tuple(w for w in waits if w) + fr)
        meth = "copy" if eng == "s" else "tensor_copy"
        te = E.op(eng, meth, dst_ap, po, waits=(tt,))
        self.free[i] = te
        return tt, te


class Ctx:
    pass


def make_ctx(nc):
    c = Ctx()
    c.nc = nc
    c.E = Emit(nc)
    E = c.E
    c.ident = nc.alloc_sbuf_tensor("ident", [128, 128], BF16)
    c.ones = nc.alloc_sbuf_tensor("ones", [1, 128], F32)
    c.ones_bf = nc.alloc_sbuf_tensor("ones_bf", [1, 128], BF16)
    c.s1a = nc.alloc_sbuf_tensor("s1a", [128, 16], F32)
    c.s2a = nc.alloc_sbuf_tensor("s2a", [128, 16], F32)
    c.ra = nc.alloc_sbuf_tensor("ra", [128, 16], F32)
    c.nmra = nc.alloc_sbuf_tensor("nmra", [128, 16], F32)
    c.tmpa = nc.alloc_sbuf_tensor("tmpa", [128, 16], F32)
    c.p_s1 = nc.alloc_sbuf_tensor("p_s1", [128, 16], F32)
    c.p_s2 = nc.alloc_sbuf_tensor("p_s2", [128, 16], F32)
    c.p_r = nc.alloc_sbuf_tensor("p_r", [128, 16], F32)
    c.p_nmr = nc.alloc_sbuf_tensor("p_nmr", [128, 16], F32)
    c.p_tmp = nc.alloc_sbuf_tensor("p_tmp", [128, 16], F32)
    c.nm = nc.alloc_sbuf_tensor("nm", [128, 1], F32)
    c.nms = nc.alloc_sbuf_tensor("nms", [128, 1], F32)
    c.rsum = nc.alloc_sbuf_tensor("rsum", [128, 1], F32)
    c.rrec = nc.alloc_sbuf_tensor("rrec", [128, 1], F32)
    c.PSA = nc.alloc_psum_tensor("PSA", [128, 2048], F32)
    c.PSB = nc.alloc_psum_tensor("PSB", [128, 2048], F32)
    c.TVIEWS = [c.PSB[:, 0:512].bitcast(BF16)[:, 0:128],
                c.PSB[:, 512:1024].bitcast(BF16)[:, 0:128]]
    rem = nc.sbuf_bytes_remaining
    c.ar = Arena(nc, "arena", rem - 1536)
    tms = E.op("g", "memset", c.ident[:, :], 0.0)
    E.op("g", "affine_select", out=c.ident[:, :], in_=c.ident[:, :],
         compare_op=ALU.not_equal, fill=1.0, base=0,
         pattern=[[-1, 128]], channel_multiplier=1, waits=(tms,))
    E.op("g", "memset", c.ones[0:1, :], 1.0)
    E.op("g", "memset", c.ones_bf[0:1, :], 1.0)
    return c


def emit_l0_attn(c, xo, xr, wqkv, bqkv, bqkvc, wo, bo, x1_out):
    nc, E, ar = c.nc, c.E, c.ar
    PSA, PSB = c.PSA, c.PSB
    ident, ones = c.ident, c.ones_bf
    s1a, s2a, ra, nmra, tmpa = c.s1a, c.s2a, c.ra, c.nmra, c.tmpa
    p_s1, p_s2, p_r, p_nmr, p_tmp = c.p_s1, c.p_s2, c.p_r, c.p_nmr, c.p_tmp
    nm, nms, rsum, rrec = c.nm, c.nms, c.rsum, c.rrec

    ar.reset()
    K_fm = ar.take(128, 8 * 2048, BF16)
    V_tm = ar.take(128, 16 * 1024, BF16)
    Q_fm = ar.take(128, 8 * 1024, BF16)
    at_fm_all = ar.take(128, 8 * 1024, BF16)
    m0 = ar.mark()
    x_st = ar.take(128, DM, F32)
    ln_tm = ar.take(128, DM, BF16)
    lnfm = ar.take(128, 4 * DM, BF16)
    sqs = ar.take(128, DM, F32)
    wpc = ar.take(128, 4 * 768, BF16)
    bqcol = ar.take(128, 16, F32)
    bqv = ar.take(1, 3072, BF16)
    gend = ar.off
    ar.rewind(m0)
    P = ar.take(128, 2048, BF16)
    PTs = ar.take(128, 2048, BF16)
    at_tm = ar.take(128, 1024, BF16)
    aend = ar.off
    ar.rewind(m0)
    x0s = ar.take(128, DM, F32)
    x1s = ar.take(128, DM, F32)
    sqp = ar.take(128, DM, F32)
    wop = ar.take(128, 4 * 512, BF16)
    bor = ar.take(1, DM, BF16)
    dens = ar.view(0, 128, 4 * DM, F32)
    ar.off = max(ar.off, gend, aend)

    PSO = [PSB[:, 1024:1536], PSB[:, 1536:2048]]
    TP = TRot(E, c.TVIEWS, ident)
    tok_bqv = E.dma("dg", "a_bias1", bqv, bqkv)

    groups = [
        (xo, [0, 1, 2, 3], 0, True),
        (xo, [4, 5, 6, 7], 4, True),
        (xr, [0, 1, 2, 3], 8, False),
        (xr, [4, 5, 6, 7], 12, False),
    ]
    tok_bqc = E.dma("dy", "a_bqc", bqcol[:, :], bqkvc)
    OF_SLOTS = [PSA[:, j * 512:(j + 1) * 512] for j in range(4)] + \
               [PSB[:, 1024:1536], PSB[:, 1536:2048]]
    slot_free = [None] * 6
    tok_xst_free = None
    tok_lntm_free = None
    piece_free = [None] * 4
    psa_free = [None] * 4
    jglob = 0
    for (src_, tiles, t16b, own) in groups:
        for i, tile in enumerate(tiles):
            j = jglob
            jglob += 1
            jj = j % 16
            td = E.dma("dy", "a_x", x_st, src_[tile * 128:(tile + 1) * 128, :],
                       waits=(tok_xst_free,))
            t1, t2 = ln_stats(E, x_st, s1a[:, jj:jj + 1], s2a[:, jj:jj + 1],
                              sqs, waits=(td,))
            tr, tn = ln_chain(E, s1a[:, jj:jj + 1], s2a[:, jj:jj + 1],
                              ra[:, jj:jj + 1], nmra[:, jj:jj + 1],
                              tmpa[:, jj:jj + 1], DM, t1, t2)
            tok_apply = E.op("v", "tensor_scalar", ln_tm, x_st,
                             ra[:, jj:jj + 1], nmra[:, jj:jj + 1],
                             ALU.mult, ALU.add,
                             waits=(tr, tn, tok_lntm_free))
            tok_xst_free = tok_apply
            last_tt = None
            for ch in range(NKC):
                tt, te = TP.go(ln_tm[:, ch * 128:(ch + 1) * 128],
                               lnfm[:, ch * 512 + i * 128:ch * 512 + (i + 1) * 128],
                               waits=(tok_apply,))
                last_tt = tt
            tok_lntm_free = last_tt
        lnfm_ready = (E.tok("v"), E.tok("s"))

        of_list = list(range(0, 16)) if own else list(range(8, 16))
        waves = [of_list[x:x + 6] for x in range(0, len(of_list), 6)]
        for wave in waves:
            nof = len(wave)
            cw0 = wave[0] * 128
            cw1 = (wave[-1] + 1) * 128
            for k in range(NKC):
                buf = k % 4
                tw = E.dma("dg", f"a_w{buf}", wpc[:, buf * 768:buf * 768 + cw1 - cw0],
                           wqkv[k * 128:(k + 1) * 128, cw0:cw1],
                           waits=(piece_free[buf],))
                for ci in range(nof):
                    w = [tw]
                    if k == 0:
                        w += list(lnfm_ready) + [slot_free[ci]]
                    if ci == nof - 1:
                        piece_free[buf] = E.op(
                            "t", "matmul", OF_SLOTS[ci],
                            wpc[:, buf * 768 + ci * 128:buf * 768 + (ci + 1) * 128],
                            lnfm[:, k * 512:(k + 1) * 512],
                            start=(k == 0), stop=(k == NKC - 1), waits=w)
                    else:
                        E.op_("t", "matmul", OF_SLOTS[ci],
                              wpc[:, buf * 768 + ci * 128:buf * 768 + (ci + 1) * 128],
                              lnfm[:, k * 512:(k + 1) * 512],
                              start=(k == 0), stop=(k == NKC - 1), waits=w)
            tok_wave = E.tok("t")
            for ci, of in enumerate(wave):
                if of < 8:
                    dst = Q_fm[:, of * 1024 + (t16b // 4) * 512:
                               of * 1024 + (t16b // 4) * 512 + 512]
                else:
                    dst = K_fm[:, (of - 8) * 2048 + t16b * 128:
                               (of - 8) * 2048 + t16b * 128 + 512]
                tev = E.op("v", "tensor_scalar_add", dst, OF_SLOTS[ci],
                           bqcol[:, of:of + 1], waits=(tok_wave, tok_bqc))
                slot_free[ci] = tev

        for n in (4, 5):
            c0 = n * 512
            for k in range(NKC):
                buf = k % 4
                tw = E.dma("dg", f"a_w{buf}", wpc[:, buf * 768:buf * 768 + 512],
                           wqkv[k * 128:(k + 1) * 128, c0:c0 + 512],
                           waits=(piece_free[buf],))
                for i in range(4):
                    w = [tw]
                    if k == 0:
                        w += list(lnfm_ready) + [psa_free[i], slot_free[i]]
                    lhs = lnfm[:, k * 512 + i * 128:k * 512 + (i + 1) * 128]
                    rhs = wpc[:, buf * 768:buf * 768 + 512]
                    psl = PSA[:, i * 512:(i + 1) * 512]
                    if i == 3:
                        piece_free[buf] = E.op("t", "matmul", psl, lhs, rhs,
                                               start=(k == 0), stop=False,
                                               waits=w)
                    else:
                        E.op_("t", "matmul", psl, lhs, rhs,
                              start=(k == 0), stop=False, waits=w)
            for i in range(4):
                t16 = t16b + i
                tok_done = E.op("t", "matmul", PSA[:, i * 512:(i + 1) * 512],
                                ones[0:1, :], bqv[0:1, c0:c0 + 512],
                                start=False, stop=True, waits=(tok_bqv,))
                ev = E.op("v", "tensor_copy",
                          V_tm[:, t16 * 1024 + (n - 4) * 512:t16 * 1024 + (n - 4) * 512 + 512],
                          PSA[:, i * 512:(i + 1) * 512], waits=(tok_done,))
                psa_free[i] = ev
                slot_free[i] = ev

    E.barrier()
    tok_psa_attn = None
    tok_pso_free = [None, None]
    for m in range(8):
        tok_scores = None
        for ch in range(8):
            for kk in range(4):
                w = []
                if ch == 0:
                    w = [tok_psa_attn, psa_free[kk]]
                lhs = Q_fm[:, ch * 1024 + m * 128:ch * 1024 + (m + 1) * 128]
                rhs = K_fm[:, ch * 2048 + kk * 512:ch * 2048 + (kk + 1) * 512]
                psl = PSA[:, kk * 512:(kk + 1) * 512]
                if ch == 7 and kk == 3:
                    tok_scores = E.op("t", "matmul", psl, lhs, rhs,
                                      start=(ch == 0), stop=(ch == 7), waits=w)
                else:
                    E.op_("t", "matmul", psl, lhs, rhs,
                          start=(ch == 0), stop=(ch == 7), waits=w)
        psa_free = [None] * 4
        tmax = E.op("v", "reduce_max", nm[:, :], PSA[:, :], axis=AX,
                    negate=True, waits=(tok_scores,))
        tnms = E.op("v", "tensor_scalar_mul", nms[:, :], nm[:, :], 1.0 / 32.0,
                    waits=(tmax,))
        tok_exp = E.op("s", "activation", P, PSA[:, :], AF.Exp,
                       bias=nms[:, 0:1], scale=1.0 / 32.0,
                       accum_out=rsum[:, 0:1], waits=(tnms,))
        tok_psa_attn = tok_exp
        trr = E.op("v", "reciprocal", rrec[:, :], rsum[:, :], waits=(tok_exp,))
        tok_pv = None
        for ch in range(16):
            tt, te = TP.go(P[:, ch * 128:(ch + 1) * 128],
                           PTs[:, ch * 128:(ch + 1) * 128],
                           waits=(tok_exp,))
            for n2 in range(2):
                w = [te]
                if ch == 0:
                    w += [tok_pso_free[0], tok_pso_free[1]]
                psl = PSO[n2]
                lhs = PTs[:, ch * 128:(ch + 1) * 128]
                rhs = V_tm[:, ch * 1024 + n2 * 512:ch * 1024 + (n2 + 1) * 512]
                if ch == 15 and n2 == 1:
                    tok_pv = E.op("t", "matmul", psl, lhs, rhs,
                                  start=(ch == 0), stop=(ch == 15), waits=w)
                else:
                    E.op_("t", "matmul", psl, lhs, rhs,
                          start=(ch == 0), stop=(ch == 15), waits=w)
        tok_at = E.op("s", "activation", at_tm, PSB[:, 1024:2048], AF.Copy,
                      scale=rrec[:, 0:1], waits=(tok_pv, trr))
        for ch in range(8):
            TP.go(at_tm[:, ch * 128:(ch + 1) * 128],
                  at_fm_all[:, m * 1024 + ch * 128:m * 1024 + (ch + 1) * 128],
                  waits=(tok_at,))
        tok_pso_free = [tok_at, tok_at]

    guard = (E.tok("t"), E.tok("v"), E.tok("s"))
    tok_bor = E.dma("dg", "a_bias2", bor, bo, waits=guard)
    at_ready = guard
    wop_free = [None] * 4
    psa_free2 = [None] * 4
    tok_x0_free = None
    tok_x1_free = None
    for g2 in range(2):
        tiles2 = [g2 * 4 + i for i in range(4)]
        dens_done = [[None] * 8 for _ in range(4)]
        for n in range(8):
            c0 = n * 512
            for k in range(8):
                buf = k % 4
                tw = E.dma("dg", f"a_wo{buf}", wop[:, buf * 512:(buf + 1) * 512],
                           wo[k * 128:(k + 1) * 128, c0:c0 + 512],
                           waits=(wop_free[buf], guard[0]))
                for i in range(4):
                    m = tiles2[i]
                    w = [tw]
                    if k == 0:
                        w += list(at_ready) + [psa_free2[i], tok_psa_attn]
                    lhs = at_fm_all[:, m * 1024 + k * 128:m * 1024 + (k + 1) * 128]
                    rhs = wop[:, buf * 512:(buf + 1) * 512]
                    psl = PSA[:, i * 512:(i + 1) * 512]
                    if i == 3:
                        wop_free[buf] = E.op("t", "matmul", psl, lhs, rhs,
                                             start=(k == 0), stop=False,
                                             waits=w)
                    else:
                        E.op_("t", "matmul", psl, lhs, rhs,
                              start=(k == 0), stop=False, waits=w)
            for i in range(4):
                tok_done = E.op("t", "matmul", PSA[:, i * 512:(i + 1) * 512],
                                ones[0:1, :], bor[0:1, c0:c0 + 512],
                                start=False, stop=True, waits=(tok_bor,))
                eng = "v" if i % 2 == 0 else "s"
                meth = "tensor_copy" if eng == "v" else "copy"
                tev = E.op(eng, meth, dens[:, i * DM + c0:i * DM + c0 + 512],
                           PSA[:, i * 512:(i + 1) * 512], waits=(tok_done,))
                psa_free2[i] = tev
                dens_done[i][n] = tev
        tok_out_last = None
        for i in range(4):
            m = tiles2[i]
            dview = dens[:, i * DM:(i + 1) * DM]
            t1, t2 = ln_stats(E, dview, p_s1[:, m:m + 1], p_s2[:, m:m + 1],
                              sqp, waits=tuple(dens_done[i]))
            tr, tn = ln_chain(E, p_s1[:, m:m + 1], p_s2[:, m:m + 1],
                              p_r[:, m:m + 1], p_nmr[:, m:m + 1],
                              p_tmp[:, m:m + 1], DM, t1, t2)
            tx = E.dma("dy", "a_x0", x0s, xo[m * 128:(m + 1) * 128, :],
                       waits=(tok_x0_free,) + guard)
            tap = E.op("v", "tensor_scalar", x1s, dview, p_r[:, m:m + 1],
                       p_nmr[:, m:m + 1], ALU.mult, ALU.add,
                       waits=(tr, tn, tok_x1_free))
            tadd = E.op("v", "tensor_tensor", x1s, x1s, x0s, ALU.add,
                        waits=(tap, tx))
            tok_x0_free = tadd
            tok_x1_free = E.dma("dy", "a_out", x1_out[m * 128:(m + 1) * 128, :],
                                x1s, waits=(tadd,))
        E.wait("v", tok_x1_free)
        E.wait("s", tok_x1_free)
    E.barrier()


def emit_l0_ffn(c, x1i, w0, b0, cent, assign, cc_in, cc_out, x2_out):
    nc, E, ar = c.nc, c.E, c.ar
    PSA, PSB = c.PSA, c.PSB
    ones = c.ones
    s1a, s2a, ra, nmra, tmpa = c.s1a, c.s2a, c.ra, c.nmra, c.tmpa
    p_s1, p_s2, p_r, p_nmr, p_tmp = c.p_s1, c.p_s2, c.p_r, c.p_nmr, c.p_tmp
    NCLUST, THR, INV_TEMP = 64, 0.01, 10.0
    nmx, nmx10, dsum, drec = c.nm, c.nms, c.rsum, c.rrec

    ar.reset()
    ident32 = ar.take(128, 128, F32)
    ones64 = ar.take(64, 128, F32)
    iota64 = ar.take(64, 1, F32)
    cent_sb = ar.take(128, NKC * NCLUST, F32)
    lnbf = ar.take(128, 8 * DM, BF16)
    qmask = ar.take(128, 8, F32)
    qmx = ar.take(128, 1, F32)
    cmax = ar.take(128, NCLUST, F32)
    cmT = ar.take(64, 128, F32)
    cm_col = ar.take(64, 1, F32)
    cs_col = ar.take(64, 1, F32)
    cs_gt = ar.take(64, 1, F32)
    dots = ar.take(128, NCLUST, F32)
    x_st = ar.take(128, DM, F32)
    lnf32 = ar.take(128, DM, F32)
    lnfm32 = ar.take(128, DM, F32)
    dens = ar.take(128, 4 * DM, F32)
    b0pc = ar.take(1, 1024, BF16)
    sqs = dens[:, 0:DM]
    sqs3 = lnf32
    asg = lnf32[0:64, :].bitcast(I32)
    asgf = lnfm32[0:64, :]
    ohcs = x_st[0:64, :]
    rmask = lnfm32
    wpc = cent_sb[:, 0:2048].bitcast(BF16)[:, 0:4096]

    T32 = [PSB[:, 0:128], PSB[:, 512:640]]
    PSD = PSB[:, 1024:1024 + NCLUST]
    TP32 = TRot(E, T32, ident32)

    tms = E.op("g", "memset", ident32[0:128, :], 0.0)
    E.op("g", "affine_select", out=ident32[0:128, :], in_=ident32[0:128, :],
         compare_op=ALU.not_equal, fill=1.0, base=0,
         pattern=[[-1, 128]], channel_multiplier=1, waits=(tms,))
    tmo = E.op("g", "memset", ones64[0:64, :], 1.0)
    tio = E.op("g", "iota", iota64[0:64, 0:1], pattern=[[0, 1]], base=0,
               channel_multiplier=1, allow_small_or_imprecise_dtypes=True)
    tok_cmz = E.op("v", "memset", cmax[:, :], 0.0)
    tok_cent = E.dma("dy", "b_cent",
                     cent_sb[:, :].rearrange("p (c n) -> p c n", n=NCLUST),
                     cent.rearrange("(c p) n -> p c n", p=128))

    tok_xst_free = None
    tok_lnf_free = None
    tok_lnfm_free = None
    tok_psd_free = None
    tok_cmax = tok_cmz
    tok_ident_done = E.tok("g")
    for m in range(8):
        td = E.dma("dy", "b_x", x_st, x1i[m * 128:(m + 1) * 128, :],
                   waits=(tok_xst_free,))
        t1, t2 = ln_stats(E, x_st, s1a[:, m:m + 1], s2a[:, m:m + 1], sqs,
                          waits=(td,))
        tr, tn = ln_chain(E, s1a[:, m:m + 1], s2a[:, m:m + 1],
                          ra[:, m:m + 1], nmra[:, m:m + 1],
                          tmpa[:, m:m + 1], DM, t1, t2)
        tok_ap = E.op("v", "tensor_scalar", lnf32, x_st, ra[:, m:m + 1],
                      nmra[:, m:m + 1], ALU.mult, ALU.add,
                      waits=(tr, tn, tok_lnf_free))
        tok_xst_free = tok_ap
        last_tt = None
        for ch in range(NKC):
            tt, te = TP32.go(lnf32[:, ch * 128:(ch + 1) * 128],
                             lnfm32[:, ch * 128:(ch + 1) * 128],
                             waits=(tok_ap, tok_lnfm_free, tok_ident_done))
            last_tt = tt
        tok_lnf_free = last_tt
        fm_ready = (E.tok("v"), E.tok("s"))
        tok_bf = E.op("v", "tensor_copy", lnbf[:, m * DM:(m + 1) * DM],
                      lnfm32, waits=fm_ready)
        tok_dots = None
        for k in range(NKC):
            w = list(fm_ready) + [tok_cent]
            if k == 0:
                w.append(tok_psd_free)
            tok_dots = E.op("t", "matmul", PSD,
                            lnfm32[:, k * 128:(k + 1) * 128],
                            cent_sb[:, k * NCLUST:(k + 1) * NCLUST],
                            start=(k == 0), stop=(k == NKC - 1), waits=w)
        tok_lnfm_free = tok_dots
        tmx = E.op("v", "reduce_max", nmx[:, :], PSD, axis=AX,
                   negate=True, waits=(tok_dots,))
        tx10 = E.op("v", "tensor_scalar_mul", nmx10[:, :], nmx[:, :],
                    INV_TEMP, waits=(tmx,))
        tex = E.op("s", "activation", dots[:, :], PSD, AF.Exp,
                   bias=nmx10[:, 0:1], scale=INV_TEMP,
                   accum_out=dsum[:, 0:1], waits=(tx10,))
        tok_psd_free = tex
        trc = E.op("v", "reciprocal", drec[:, :], dsum[:, :], waits=(tex,))
        tdn = E.op("v", "tensor_scalar", dots[:, :], dots[:, :],
                   drec[:, 0:1], None, ALU.mult, waits=(trc, tex))
        tqm = E.op("v", "reduce_max", qmx[:, 0:1], dots[:, :], axis=AX,
                   waits=(tdn,))
        E.op("v", "tensor_single_scalar", qmask[:, m:m + 1], qmx[:, 0:1],
             THR, ALU.is_gt, waits=(tqm,))
        tok_cmax = E.op("v", "tensor_tensor", cmax[:, :], cmax[:, :],
                        dots[:, :], ALU.max, waits=(tok_cmax, tdn))

    tt, te = TP32.go(cmax[:, 0:64], cmT[0:64, 0:128], waits=(tok_cmax,))
    tcm = E.op("v", "reduce_max", cm_col[0:64, 0:1], cmT[0:64, 0:128],
               axis=AX, waits=(te,))
    tdm = E.dma("dg", "b_ccin", cc_in.ap(), cm_col[0:64, 0:1],
                waits=(tcm,), allow_slow_non_contiguous=True)
    E.wait("g", tdm)
    cc_inst = nc.gpsimd.collective_compute(
        "AllReduce", ALU.max,
        replica_groups=[list(range(8))],
        ins=[cc_in.ap().opt()], outs=[cc_out.ap().opt()],
    )
    tok_cc = E.inc(cc_inst, "g", 1)
    tcs = E.dma("dg", "b_ccout", cs_col[0:64, 0:1], cc_out.ap(),
                waits=(tok_cc,), allow_slow_non_contiguous=True)
    tgt = E.op("v", "tensor_single_scalar", cs_gt[0:64, 0:1],
               cs_col[0:64, 0:1], THR, ALU.is_gt, waits=(tcs,))
    tas = E.dma("dg", "b_asg", asg, assign.partition_broadcast(64),
                waits=(E.tok("t"), E.tok("v"), E.tok("s")))
    taf = E.op("g", "tensor_copy", asgf, asg, waits=(tas,))
    toh = E.op("v", "tensor_scalar", ohcs, asgf, iota64[0:64, 0:1],
               cs_gt[0:64, 0:1], ALU.is_equal, ALU.mult, waits=(taf, tgt))
    rmask_done = []
    for n in range(8):
        w = [toh]
        tmm = E.op("t", "matmul", PSB[:, 1536:2048],
                   ones64[0:64, :], ohcs[0:64, n * 512:(n + 1) * 512],
                   start=True, stop=True,
                   waits=w + ([rmask_done[n - 1]] if n >= 1 else []))
        tev = E.op("v", "tensor_copy", rmask[:, n * 512:(n + 1) * 512],
                   PSB[:, 1536:2048], waits=(tmm,))
        rmask_done.append(tev)
    tok_rmask = rmask_done[-1]

    E.wait("v", E.tok("s"))
    piece_free = [None] * 4
    b0_free = [None, None]
    psa_free = [None] * 4
    psb_free = [None] * 4
    tok_x3_free = tok_rmask
    guardB = (E.tok("t"), E.tok("v"), E.tok("s"))
    for grp in range(2):
        tiles = [grp * 4 + i for i in range(4)]
        dens_done = [[None] * 8 for _ in range(4)]
        for n2 in range(4):
            c0 = n2 * 1024
            for k in range(NKC):
                buf = k % 4
                tw = E.dma("dg", f"b_w{buf}", wpc[:, buf * 1024:(buf + 1) * 1024],
                           w0[k * 128:(k + 1) * 128, c0:c0 + 1024],
                           waits=(piece_free[buf],))
                for i in range(4):
                    m = tiles[i]
                    w = [tw]
                    if k == 0:
                        w += [psa_free[i], psb_free[i]] + list(guardB)
                        if i >= 2:
                            w.append(tok_rmask)
                    lhs = lnbf[:, m * DM + k * 128:m * DM + (k + 1) * 128]
                    E.op_("t", "matmul", PSA[:, i * 512:(i + 1) * 512], lhs,
                          wpc[:, buf * 1024:buf * 1024 + 512],
                          start=(k == 0), stop=False, waits=w)
                    if i == 3:
                        piece_free[buf] = E.op("t", "matmul",
                                               PSB[:, i * 512:(i + 1) * 512], lhs,
                                               wpc[:, buf * 1024 + 512:buf * 1024 + 1024],
                                               start=(k == 0), stop=False, waits=w)
                    else:
                        E.op_("t", "matmul", PSB[:, i * 512:(i + 1) * 512], lhs,
                              wpc[:, buf * 1024 + 512:buf * 1024 + 1024],
                              start=(k == 0), stop=False, waits=w)
            for half in range(2):
                c0h = c0 + half * 512
                bb = half
                tb0 = E.dma("dg", f"b_b0{bb}", b0pc[0:1, bb * 512:bb * 512 + 512],
                            b0[0:1, c0h:c0h + 512], waits=(b0_free[bb],))
                for i in range(4):
                    m = tiles[i]
                    PSX = PSA if half == 0 else PSB
                    tok_done = E.op("t", "matmul", PSX[:, i * 512:(i + 1) * 512],
                                    c.ones_bf[0:1, :],
                                    b0pc[0:1, bb * 512:bb * 512 + 512],
                                    start=False, stop=True, waits=(tb0,))
                    if i == 3:
                        b0_free[bb] = tok_done
                    tev = E.op("v", "scalar_tensor_tensor",
                               dens[:, i * DM + c0h:i * DM + c0h + 512],
                               PSX[:, i * 512:(i + 1) * 512],
                               qmask[:, m:m + 1], rmask[:, c0h:c0h + 512],
                               ALU.mult, ALU.mult, waits=(tok_done, tok_rmask))
                    if half == 0:
                        psa_free[i] = tev
                    else:
                        psb_free[i] = tev
                    dens_done[i][n2 * 2 + half] = tev
        tok_out_last = None
        for i in range(4):
            m = tiles[i]
            dview = dens[:, i * DM:(i + 1) * DM]
            t1, t2 = ln_stats(E, dview, p_s1[:, m:m + 1], p_s2[:, m:m + 1],
                              sqs3, waits=tuple(dens_done[i]))
            tr, tn = ln_chain(E, p_s1[:, m:m + 1], p_s2[:, m:m + 1],
                              p_r[:, m:m + 1], p_nmr[:, m:m + 1],
                              p_tmp[:, m:m + 1], DM, t1, t2)
            tx = E.dma("dy", "b_x3", x_st, x1i[m * 128:(m + 1) * 128, :],
                       waits=(tok_x3_free,))
            tap = E.op("v", "tensor_scalar", dview, dview, p_r[:, m:m + 1],
                       p_nmr[:, m:m + 1], ALU.mult, ALU.add, waits=(tr, tn))
            tadd = E.op("v", "tensor_tensor", dview, dview, x_st, ALU.add,
                        waits=(tap, tx))
            tok_x3_free = tadd
            tok_out_last = E.dma("dy", "b_out",
                                 x2_out[m * 128:(m + 1) * 128, :],
                                 dview, waits=(tadd,))
        E.wait("v", tok_out_last)
        E.wait("s", tok_out_last)
    E.barrier()


def emit_l1_attn(c, x2i, cw, cb, wq, wk, wv, bqc, bkc, bvr, wo2, bo2, pw, pb,
                 halo_sel, halo_mask, hg_in, hg_out, x3_out):
    nc, E, ar = c.nc, c.E, c.ar
    PSA, PSB = c.PSA, c.PSB
    ident, ones = c.ident, c.ones_bf
    s1a, s2a, ra, nmra, tmpa = c.s1a, c.s2a, c.ra, c.nmra, c.tmpa
    p_s1, p_s2, p_r, p_nmr, p_tmp = c.p_s1, c.p_s2, c.p_r, c.p_nmr, c.p_tmp
    CW = 1032
    NH = 16
    SCALE = 0.125
    NEG_INF = -1e9

    ar.reset()
    cmask = ar.take(128, 128, F32)
    qkv_tm = ar.take(128, 8 * 3072, BF16)
    ao_tm = ar.take(128, 8 * 1024, BF16)
    bqc_sb = ar.take(128, 8, F32)
    bkc_sb = ar.take(128, 8, F32)
    hmask_sb = ar.take(2, 1, F32)
    hsel_sb = ar.take(16, 2, F32)
    nm = ar.take(128, 2, F32)
    rsum = ar.take(128, 2, F32)
    rrec = ar.take(128, 2, F32)
    mseg = ar.mark()
    x_st = ar.take(128, DM, F32)
    sqs = ar.take(128, DM, F32)
    ln_tm = ar.take(128, DM, BF16)
    ln3fm = ar.take(128, NKC * CW, BF16)
    cpc = ar.take(128, 4 * 512, BF16)
    cbr = ar.take(1, 3072, BF16)
    hg_sb = x_st[0:16, :]
    lnh = ln_tm[0:2, :]
    x2h = qkv_tm[0:2, 0:8192].bitcast(F32)
    segA_end = ar.off
    ar.rewind(mseg)
    wq_sb = ar.take(128, 8 * 1024, BF16)
    wk_sb = ar.take(128, 8 * 1024, BF16)
    wv_sb = ar.take(128, 8 * 1024, BF16)
    wo2_sb = ar.take(128, 8 * 1024, BF16)
    q_fm = ar.take(128, 1024, BF16)
    k_fm = ar.take(128, 1024, BF16)
    v_fm = ar.take(128, 1024, BF16)
    qh_fm = ar.take(128, 1024, BF16)
    kh_fm = ar.take(128, 1024, BF16)
    vh_tm = ar.take(128, 1024, BF16)
    s_sb = ar.take(128, 256, F32)
    P = ar.take(128, 256, BF16)
    PT_sb = ar.take(128, 256, BF16)
    o_tm = ar.take(128, 1024, BF16)
    o_fm = ar.take(128, 1024, BF16)
    bvr_sb = ar.take(1, 1024, BF16)
    bo2_sb = ar.take(1, 1024, BF16)
    segB_end = ar.off
    ar.rewind(mseg)
    at_fm = ar.take(128, 4 * 1024, BF16)
    dens = ar.take(128, 4 * DM, F32)
    x_st4 = ar.take(128, DM, F32)
    sqs4 = ar.take(128, DM, F32)
    ppc = ar.take(128, 4 * 512, BF16)
    pbr = ar.take(1, DM, BF16)
    ar.off = max(ar.off, segA_end, segB_end)

    TRB = TRot(E, c.TVIEWS, ident)

    tcm = E.op("g", "memset", cmask[:, :], 0.0)
    E.op("g", "affine_select", out=cmask[:, :], in_=cmask[:, :],
         compare_op=ALU.is_ge, fill=NEG_INF, base=0,
         pattern=[[-1, 128]], channel_multiplier=1, waits=(tcm,))
    tok_cb = E.dma("dg", "c_cbias", cbr, cb)
    tok_hm = E.dma("dy", "c_hmask", hmask_sb[0:2, 0:1], halo_mask,
                   allow_slow_non_contiguous=True)
    tok_hs = E.dma("dy", "c_hsel", hsel_sb[0:16, 0:2], halo_sel,
                   allow_slow_non_contiguous=True)

    tok_xst_free = None
    tok_lntm_free = None
    for m in range(8):
        r0 = m * 128
        td = E.dma("dy", "c_x", x_st, x2i[r0:r0 + 128, :],
                   waits=(tok_xst_free,))
        t1, t2 = ln_stats(E, x_st, s1a[:, m:m + 1], s2a[:, m:m + 1], sqs,
                          waits=(td,))
        tr, tn = ln_chain(E, s1a[:, m:m + 1], s2a[:, m:m + 1],
                          ra[:, m:m + 1], nmra[:, m:m + 1],
                          tmpa[:, m:m + 1], DM, t1, t2)
        tok_ap = E.op("v", "tensor_scalar", ln_tm, x_st, ra[:, m:m + 1],
                      nmra[:, m:m + 1], ALU.mult, ALU.add,
                      waits=(tr, tn, tok_lntm_free))
        tok_xst_free = tok_ap
        last_tt = None
        for ch in range(NKC):
            tt, te = TRB.go(ln_tm[:, ch * 128:(ch + 1) * 128],
                            ln3fm[:, ch * CW + r0 + 1:ch * CW + r0 + 129],
                            waits=(tok_ap,))
            last_tt = tt
        tok_lntm_free = last_tt

    th1 = E.dma("dg", "c_hgin", hg_in.ap()[0:1, :], x2i[0:1, :])
    th2 = E.dma("dg", "c_hgin", hg_in.ap()[1:2, :], x2i[1023:1024, :])
    E.wait("g", th2)
    cc_inst = nc.gpsimd.collective_compute(
        "AllGather", ALU.bypass,
        replica_groups=[list(range(8))],
        ins=[hg_in.ap().opt()], outs=[hg_out.ap().opt()],
    )
    tok_ag = E.inc(cc_inst, "g", 1)
    thg = E.dma("dg", "c_hg", hg_sb[0:16, :], hg_out.ap(),
                waits=(tok_ag, tok_xst_free))
    ev_h = []
    for n in range(8):
        sl = n % 4
        w = [thg, tok_hs]
        tmm = E.op("t", "matmul", PSA[0:2, sl * 512:sl * 512 + 512],
                   hsel_sb[0:16, 0:2], hg_sb[0:16, n * 512:(n + 1) * 512],
                   start=True, stop=True,
                   waits=w + ([ev_h[n - 4]] if n >= 4 else []))
        tev = E.op("v", "tensor_copy", x2h[0:2, n * 512:(n + 1) * 512],
                   PSA[0:2, sl * 512:sl * 512 + 512], waits=(tmm,))
        ev_h.append(tev)
    t1, t2 = ln_stats(E, x2h[0:2, :], s1a[0:2, 8:9], s2a[0:2, 8:9],
                      sqs[0:2, :], waits=tuple(ev_h))
    tr, tn = ln_chain(E, s1a[0:2, 8:9], s2a[0:2, 8:9], ra[0:2, 8:9],
                      nmra[0:2, 8:9], tmpa[0:2, 8:9], DM, t1, t2)
    tok_lnh = E.op("v", "tensor_scalar", lnh[0:2, :], x2h[0:2, :],
                   ra[0:2, 8:9], nmra[0:2, 8:9], ALU.mult, ALU.add,
                   waits=(tr, tn, tok_lntm_free))
    tok_lnhm = E.op("v", "tensor_scalar", lnh[0:2, :], lnh[0:2, :],
                    hmask_sb[0:2, 0:1], None, ALU.mult,
                    waits=(tok_lnh, tok_hm))
    for ch in range(NKC):
        i2 = ch % 2
        po = c.TVIEWS[i2][0:128, 0:2]
        fr = TRB.free[i2]
        fr = () if fr is None else ((fr,) if isinstance(fr[0], str) else fr)
        tt = E.op("t", "transpose", po, lnh[0:2, ch * 128:(ch + 1) * 128],
                  ident[0:2, 0:2], waits=(tok_lnhm,) + fr)
        e1 = E.op("v", "tensor_copy", ln3fm[:, ch * CW:ch * CW + 1],
                  po[:, 0:1], waits=(tt,))
        e2 = E.op("s", "copy", ln3fm[:, ch * CW + 1025:ch * CW + 1026],
                  po[:, 1:2], waits=(tt,))
        TRB.free[i2] = (e1, e2)
    ph1_done = (E.tok("v"), E.tok("s"))
    def conv_slice(m):
        if m < 4:
            return PSA[:, m * 512:(m + 1) * 512]
        return PSB[:, (m - 4) * 512:(m - 3) * 512]

    piece_free = [None] * 4
    slice_free = [None] * 8
    for n in range(6):
        c0 = n * 512
        step = 0
        for k in range(NKC):
            for tap in range(3):
                buf = step % 4
                step += 1
                tw = E.dma("dg", f"c_cw{buf}", cpc[:, buf * 512:(buf + 1) * 512],
                           cw[tap][k * 128:(k + 1) * 128, c0:c0 + 512],
                           waits=(piece_free[buf],))
                for m in range(8):
                    w = [tw]
                    if step == 1:
                        w += list(ph1_done) + [slice_free[m]]
                    lhs = ln3fm[:, k * CW + m * 128 + tap:
                                k * CW + m * 128 + tap + 128]
                    rhs = cpc[:, buf * 512:(buf + 1) * 512]
                    tok = E.op("t", "matmul", conv_slice(m), lhs, rhs,
                               start=(step == 1), stop=False, waits=w)
                    if m == 7:
                        piece_free[buf] = tok
        for m in range(8):
            tok_done = E.op("t", "matmul", conv_slice(m), ones[0:1, :],
                            cbr[0:1, c0:c0 + 512], start=False, stop=True,
                            waits=(tok_cb,))
            eng = "v" if m % 2 == 0 else "s"
            meth = "tensor_copy" if eng == "v" else "copy"
            slice_free[m] = E.op(eng, meth,
                                 qkv_tm[:, m * 3072 + c0:m * 3072 + c0 + 512],
                                 conv_slice(m), waits=(tok_done,))
    E.barrier()

    tok_wq = E.dma("dg", "c_wq", wq_sb[:, :].rearrange("p (c n) -> p c n", n=1024),
                   wq.rearrange("(c p) n -> p c n", p=128))
    tok_wk = E.dma("dg", "c_wk", wk_sb[:, :].rearrange("p (c n) -> p c n", n=1024),
                   wk.rearrange("(c p) n -> p c n", p=128))
    tok_wv = E.dma("dg", "c_wv", wv_sb[:, :].rearrange("p (c n) -> p c n", n=1024),
                   wv.rearrange("(c p) n -> p c n", p=128))
    tok_wo2 = E.dma("dg", "c_wo2", wo2_sb[:, :].rearrange("p (c n) -> p c n", n=1024),
                    wo2.rearrange("(c p) n -> p c n", p=128))
    tok_bqc = E.dma("dy", "c_bqc", bqc_sb[:, :], bqc)
    tok_bkc = E.dma("dy", "c_bkc", bkc_sb[:, :], bkc)
    tok_bvr = E.dma("dg", "c_bvr", bvr_sb, bvr)
    tok_bo2 = E.dma("dg", "c_bo2", bo2_sb, bo2)

    PH = [PSA[:, 0:128], PSA[:, 512:640]]
    PV = [PSA[:, 1024:1536], PSA[:, 1536:2048]]
    PS = [PSB[:, 1024:1152], PSB[:, 1536:1664]]
    ph_free = [None, None]
    pv_free = [None, None]
    ps_free = [None, None]

    for m in range(8):
        base = m * 3072
        qf = []
        for c3 in range(8):
            tt, te = TRB.go(qkv_tm[:, base + c3 * 128:base + (c3 + 1) * 128],
                            q_fm[:, c3 * 128:(c3 + 1) * 128], waits=())
        for c3 in range(8):
            tt, te = TRB.go(qkv_tm[:, base + 1024 + c3 * 128:base + 1024 + (c3 + 1) * 128],
                            k_fm[:, c3 * 128:(c3 + 1) * 128], waits=())
        for c3 in range(8):
            tt, te = TRB.go(qkv_tm[:, base + 2048 + c3 * 128:base + 2048 + (c3 + 1) * 128],
                            v_fm[:, c3 * 128:(c3 + 1) * 128], waits=())
        fm_ready = (E.tok("v"), E.tok("s"))
        for (wsb, bias_sb, dst, tok_w, tok_b) in (
                (wq_sb, bqc_sb, qh_fm, tok_wq, tok_bqc),
                (wk_sb, bkc_sb, kh_fm, tok_wk, tok_bkc)):
            for hc in range(8):
                pview = PH[hc % 2]
                tok_mm = None
                for wc in range(8):
                    w = [tok_w] + list(fm_ready)
                    if wc == 0:
                        w.append(ph_free[hc % 2])
                    if wc == 7:
                        tok_mm = E.op("t", "matmul", pview[:, 0:128],
                                      wsb[:, wc * 1024 + hc * 128:wc * 1024 + (hc + 1) * 128],
                                      (q_fm if dst is qh_fm else k_fm)[:, wc * 128:(wc + 1) * 128],
                                      start=(wc == 0), stop=(wc == 7), waits=w)
                    else:
                        E.op_("t", "matmul", pview[:, 0:128],
                              wsb[:, wc * 1024 + hc * 128:wc * 1024 + (hc + 1) * 128],
                              (q_fm if dst is qh_fm else k_fm)[:, wc * 128:(wc + 1) * 128],
                              start=(wc == 0), stop=(wc == 7), waits=w)
                tok_ev = E.op("v", "tensor_scalar_add",
                              dst[:, hc * 128:(hc + 1) * 128],
                              pview[:, 0:128], bias_sb[:, hc:hc + 1],
                              waits=(tok_mm, tok_b))
                ph_free[hc % 2] = tok_ev
        for n2 in range(2):
            pview = PV[n2]
            tok_mm = None
            for wc in range(8):
                w = [tok_wv] + list(fm_ready)
                if wc == 0:
                    w.append(pv_free[n2])
                E.op_("t", "matmul", pview,
                      v_fm[:, wc * 128:(wc + 1) * 128],
                      wv_sb[:, wc * 1024 + n2 * 512:wc * 1024 + n2 * 512 + 512],
                      start=(wc == 0), stop=False, waits=w)
            tok_mm = E.op("t", "matmul", pview, ones[0:1, :],
                          bvr_sb[0:1, n2 * 512:n2 * 512 + 512],
                          start=False, stop=True, waits=(tok_bvr,))
            tok_ev = E.op("s", "copy", vh_tm[:, n2 * 512:n2 * 512 + 512],
                          pview, waits=(tok_mm,))
            pv_free[n2] = tok_ev
        vh_ready = pv_free[:]
        qh_ready = (E.tok("v"),)
        tok_o_ev = [None, None]
        exp_prev = [None, None]
        tt_prev = [None, None]
        ps_free2 = [ps_free[0], ps_free[1]]
        sc_tok = [None] * (NH + 1)

        def emit_score(h):
            hc = h // 2
            pb_ = (h % 2) * 64
            psc = PS[h % 2]
            return E.op("t", "matmul", psc[:, 0:128],
                        qh_fm[pb_:pb_ + 64, hc * 128:(hc + 1) * 128],
                        kh_fm[pb_:pb_ + 64, hc * 128:(hc + 1) * 128],
                        start=True, stop=True,
                        waits=list(qh_ready) + [ps_free2[h % 2]])

        sc_tok[0] = emit_score(0)
        for h in range(NH):
            par = h % 2
            psc = PS[par]
            if h + 1 < NH:
                sc_tok[h + 1] = emit_score(h + 1)
            tok_stt = E.op("v", "scalar_tensor_tensor",
                           s_sb[:, par * 128:(par + 1) * 128],
                           psc[:, 0:128], SCALE, cmask[:, :],
                           ALU.mult, ALU.add,
                           waits=(sc_tok[h], exp_prev[par]))
            ps_free2[par] = tok_stt
            tok_mx = E.op("v", "reduce_max", nm[:, par:par + 1],
                          s_sb[:, par * 128:(par + 1) * 128], axis=AX,
                          negate=True, waits=(tok_stt,))
            tok_exp = E.op("s", "activation", P[:, par * 128:(par + 1) * 128],
                           s_sb[:, par * 128:(par + 1) * 128], AF.Exp,
                           bias=nm[:, par:par + 1], scale=1.0,
                           accum_out=rsum[:, par:par + 1],
                           waits=(tok_mx, tt_prev[par]))
            exp_prev[par] = tok_exp
            tok_rr = E.op("v", "reciprocal", rrec[:, par:par + 1],
                          rsum[:, par:par + 1], waits=(tok_exp,))
            tt, te = TRB.go(P[:, par * 128:(par + 1) * 128],
                            PT_sb[:, par * 128:(par + 1) * 128],
                            waits=(tok_exp,))
            tt_prev[par] = tt
            tok_pv = E.op("t", "matmul", PH[par][:, 0:64],
                          PT_sb[:, par * 128:(par + 1) * 128],
                          vh_tm[:, h * 64:(h + 1) * 64],
                          start=True, stop=True,
                          waits=[te, ph_free[par]] + vh_ready)
            tok_oev = E.op("s", "activation", o_tm[:, h * 64:(h + 1) * 64],
                           PH[par][:, 0:64], AF.Copy,
                           scale=rrec[:, par:par + 1],
                           waits=(tok_pv, tok_rr))
            ph_free[par] = tok_oev
            tok_o_ev[par] = tok_oev
        for c3 in range(8):
            tt, te = TRB.go(o_tm[:, c3 * 128:(c3 + 1) * 128],
                            o_fm[:, c3 * 128:(c3 + 1) * 128],
                            waits=[tok_o_ev[0], tok_o_ev[1]])
        ofm_ready = (E.tok("v"), E.tok("s"))
        for n2 in range(2):
            pview = PV[n2]
            tok_mm = None
            for hc in range(8):
                w = [tok_wo2] + list(ofm_ready)
                if hc == 0:
                    w.append(pv_free[n2])
                E.op_("t", "matmul", pview,
                      o_fm[:, hc * 128:(hc + 1) * 128],
                      wo2_sb[:, hc * 1024 + n2 * 512:hc * 1024 + n2 * 512 + 512],
                      start=(hc == 0), stop=False, waits=w)
            tok_mm = E.op("t", "matmul", pview, ones[0:1, :],
                          bo2_sb[0:1, n2 * 512:n2 * 512 + 512],
                          start=False, stop=True, waits=(tok_bo2,))
            tok_ev = E.op("v", "tensor_copy",
                          ao_tm[:, m * 1024 + n2 * 512:m * 1024 + n2 * 512 + 512],
                          pview, waits=(tok_mm,))
            pv_free[n2] = tok_ev
    E.barrier()

    tok_pb = E.dma("dg", "c_pbias", pbr, pb)
    piece_free = [None] * 4
    psa_free = [None] * 4
    tok_x4_free = None
    for grp in range(2):
        tiles = [grp * 4 + i for i in range(4)]
        for i, m in enumerate(tiles):
            for c3 in range(8):
                TRB.go(ao_tm[:, m * 1024 + c3 * 128:m * 1024 + (c3 + 1) * 128],
                       at_fm[:, i * 1024 + c3 * 128:i * 1024 + (c3 + 1) * 128],
                       waits=())
        at_ready = (E.tok("v"), E.tok("s"))
        dens_done = [[None] * 8 for _ in range(4)]
        for n in range(8):
            c0 = n * 512
            for k in range(8):
                buf = k % 4
                tw = E.dma("dg", f"c_pw{buf}", ppc[:, buf * 512:(buf + 1) * 512],
                           pw[k * 128:(k + 1) * 128, c0:c0 + 512],
                           waits=(piece_free[buf],))
                for i in range(4):
                    w = [tw]
                    if k == 0:
                        w += list(at_ready) + [psa_free[i]]
                    if i == 3:
                        piece_free[buf] = E.op("t", "matmul",
                                               PSA[:, i * 512:(i + 1) * 512],
                                               at_fm[:, i * 1024 + k * 128:i * 1024 + (k + 1) * 128],
                                               ppc[:, buf * 512:(buf + 1) * 512],
                                               start=(k == 0), stop=False,
                                               waits=w)
                    else:
                        E.op_("t", "matmul", PSA[:, i * 512:(i + 1) * 512],
                              at_fm[:, i * 1024 + k * 128:i * 1024 + (k + 1) * 128],
                              ppc[:, buf * 512:(buf + 1) * 512],
                              start=(k == 0), stop=False, waits=w)
            for i in range(4):
                m = tiles[i]
                tok_done = E.op("t", "matmul", PSA[:, i * 512:(i + 1) * 512],
                                ones[0:1, :], pbr[0:1, c0:c0 + 512],
                                start=False, stop=True, waits=(tok_pb,))
                eng = "v" if i % 2 == 0 else "s"
                meth = "tensor_copy" if eng == "v" else "copy"
                tev = E.op(eng, meth, dens[:, i * DM + c0:i * DM + c0 + 512],
                           PSA[:, i * 512:(i + 1) * 512], waits=(tok_done,))
                psa_free[i] = tev
                dens_done[i][n] = tev
        tok_out_last = None
        for i in range(4):
            m = tiles[i]
            dview = dens[:, i * DM:(i + 1) * DM]
            t1, t2 = ln_stats(E, dview, p_s1[:, m:m + 1], p_s2[:, m:m + 1],
                              sqs4, waits=tuple(dens_done[i]))
            tr, tn = ln_chain(E, p_s1[:, m:m + 1], p_s2[:, m:m + 1],
                              p_r[:, m:m + 1], p_nmr[:, m:m + 1],
                              p_tmp[:, m:m + 1], DM, t1, t2)
            tx = E.dma("dy", "c_x4", x_st4, x2i[m * 128:(m + 1) * 128, :],
                       waits=(tok_x4_free,))
            tap = E.op("v", "tensor_scalar", dview, dview, p_r[:, m:m + 1],
                       p_nmr[:, m:m + 1], ALU.mult, ALU.add,
                       waits=(tr, tn))
            tadd = E.op("v", "tensor_tensor", dview, dview, x_st4, ALU.add,
                        waits=(tap, tx))
            tok_x4_free = tadd
            tok_out_last = E.dma("dy", "c_out", x3_out[m * 128:(m + 1) * 128, :],
                                 dview, waits=(tadd,))
        E.wait("v", tok_out_last)
        E.wait("s", tok_out_last)

    E.barrier()


def emit_l1_ffn(c, x3i, w1, w3, w2, a1, b1, a3, b3, a2, b2, out_ap):
    nc, E, ar = c.nc, c.E, c.ar
    PSA, PSB = c.PSA, c.PSB
    ident, ones = c.ident, c.ones
    s1a, s2a, ra, nmra, tmpa = c.s1a, c.s2a, c.ra, c.nmra, c.tmpa
    p_s1, p_s2, p_r, p_nmr, p_tmp = c.p_s1, c.p_s2, c.p_r, c.p_nmr, c.p_tmp
    HID = 11008
    NHC = 86
    HCS = [1024] * 10 + [768]
    SCALING = 2.0

    def spans(L):
        return [(0, 512), (512, L)] if L > 512 else [(0, L)]

    ar.reset()
    a1_sb = ar.take(128, 32 * 16, BF16)
    a3_sb = ar.take(128, 32 * 16, BF16)
    a2_sb = ar.take(128, NHC * 16, BF16)
    b2_sb = ar.take(16, DM, BF16)
    x_st = ar.take(128, DM, F32)
    ln_tm = ar.take(128, DM, BF16)
    lnfm = ar.take(128, 4 * DM, BF16)
    dens = ar.take(128, 4 * DM, F32)
    sqs = dens[:, 0:DM]
    sq_post = lnfm[:, 0:2 * DM].bitcast(F32)
    h1_sb = ar.take(128, 4 * 1024, BF16)
    sl_sb = ar.take(128, 4 * 1024, BF16)
    g_sb = ar.take(128, 4 * 1024, BF16)
    g_fm = ar.take(128, 4 * 1024, BF16)
    wpc = ar.take(128, 4 * 1024, BF16)
    w2pc = ar.take(128, 4 * 1024, BF16)
    b1s = ar.take(16, 1024, BF16)
    b3s = ar.take(16, 1024, BF16)
    xa_bf = ar.take(128, 16, BF16)
    xa1_fm = ar.take(16, 4 * 128, BF16)
    xa3_fm = ar.take(16, 4 * 128, BF16)
    ga2_sb = ar.take(128, 4 * 16, F32)
    ga2_bf = ar.take(128, 16, BF16)
    ga2_fm = ar.take(16, 4 * 128, BF16)

    TVIEWS = c.TVIEWS
    PH = [PSA[:, 0:1024], PSA[:, 1024:2048],
          PSB[:, 0:1024], PSB[:, 1024:2048]]
    PGA = PSA[:, 0:16]

    TR = TRot(E, TVIEWS, ident)

    E.dma("dg", "d_a1", a1_sb[:, :].rearrange("p (c n) -> p c n", n=16),
          a1.rearrange("(c p) n -> p c n", p=128))
    E.dma("dg", "d_a3", a3_sb[:, :].rearrange("p (c n) -> p c n", n=16),
          a3.rearrange("(c p) n -> p c n", p=128))
    E.dma("dg", "d_a2", a2_sb[:, :].rearrange("p (c n) -> p c n", n=16),
          a2.rearrange("(c p) n -> p c n", p=128))
    E.dma("dg", "d_b2", b2_sb[0:16, :], b2)
    tok_a1 = E.tok("D:d_a1")
    tok_a3 = E.tok("D:d_a3")
    tok_a2 = E.tok("D:d_a2")
    tok_b2 = E.tok("D:d_b2")
    E.barrier()

    tok_xst_free = None
    tok_lntm_free = None
    tok_out_last = None
    tok_b1_used = None
    tok_b3_used = None
    for grp in range(2):
        tiles = [grp * 4 + i for i in range(4)]
        for i, m in enumerate(tiles):
            td = E.dma("dy", "d_x", x_st, x3i[m * 128:(m + 1) * 128, :],
                       waits=(tok_xst_free,))
            t1, t2 = ln_stats(E, x_st, s1a[:, m:m + 1], s2a[:, m:m + 1], sqs,
                              waits=(td,))
            tr, tn = ln_chain(E, s1a[:, m:m + 1], s2a[:, m:m + 1],
                              ra[:, m:m + 1], nmra[:, m:m + 1],
                              tmpa[:, m:m + 1], DM, t1, t2)
            tok_ap = E.op("v", "tensor_scalar", ln_tm, x_st, ra[:, m:m + 1],
                          nmra[:, m:m + 1], ALU.mult, ALU.add,
                          waits=(tr, tn, tok_lntm_free))
            tok_xst_free = tok_ap
            last_tt = None
            for c in range(NKC):
                tt, te = TR.go(ln_tm[:, c * 128:(c + 1) * 128],
                               lnfm[:, i * DM + c * 128:i * DM + (c + 1) * 128],
                               waits=(tok_ap,))
                last_tt = tt
            tok_lntm_free = last_tt
        lnfm_ready = (E.tok("v"), E.tok("s"))

        tok_pga_free = None
        for (asb, xfm, tok_a) in ((a1_sb, xa1_fm, tok_a1),
                                  (a3_sb, xa3_fm, tok_a3)):
            for i in range(4):
                tok_mm = None
                for k in range(NKC):
                    w = list(lnfm_ready) + [tok_a]
                    if k == 0:
                        w.append(tok_pga_free)
                    tok_mm = E.op("t", "matmul", PGA,
                                  lnfm[:, i * DM + k * 128:i * DM + (k + 1) * 128],
                                  asb[:, k * 16:(k + 1) * 16],
                                  start=(k == 0), stop=(k == 31), waits=w)
                tok_ev = E.op("s", "activation", xa_bf[:, :], PGA, AF.Copy,
                              scale=SCALING, waits=(tok_mm,))
                tok_pga_free = tok_ev
                tt, te = TR.go(xa_bf[:, :], xfm[0:16, i * 128:(i + 1) * 128],
                               waits=(tok_ev,))
        xa_ready = (E.tok("v"), E.tok("s"))
        tok_ga2_z = E.op("v", "memset", ga2_sb[:, :], 0.0)
        tok_ga2 = [tok_ga2_z] * 4

        E.wait("v", E.tok("s"))
        dens_tok = [[None] * 4 for _ in range(4)]
        ph_free = [xa_ready] * 4
        piece_free = [None] * 4
        w2_free = [None] * 4
        hc0 = 0
        for ci, L in enumerate(HCS):
            khs = L // 128
            tb1 = E.dma("dg", "d_b1s", b1s[0:16, 0:L], b1[0:16, hc0:hc0 + L],
                        waits=(tok_b1_used,))
            h1_done = [None] * 4
            for k in range(NKC):
                buf = k % 4
                tw = E.dma("dg", f"d_wp{buf}", wpc[:, buf * 1024:buf * 1024 + L],
                           w1[k * 128:(k + 1) * 128, hc0:hc0 + L],
                           waits=(piece_free[buf],))
                for i in range(4):
                    w = [tw]
                    if k == 0:
                        w += list(lnfm_ready) + list(ph_free[i])
                    for si, (s0, s1_) in enumerate(spans(L)):
                        last = (i == 3 and si == len(spans(L)) - 1)
                        if last:
                            piece_free[buf] = E.op(
                                "t", "matmul", PH[i][:, s0:s1_],
                                lnfm[:, i * DM + k * 128:i * DM + (k + 1) * 128],
                                wpc[:, buf * 1024 + s0:buf * 1024 + s1_],
                                start=(k == 0), stop=False, waits=w)
                        else:
                            E.op_("t", "matmul", PH[i][:, s0:s1_],
                                  lnfm[:, i * DM + k * 128:i * DM + (k + 1) * 128],
                                  wpc[:, buf * 1024 + s0:buf * 1024 + s1_],
                                  start=(k == 0), stop=False, waits=w)
            for i in range(4):
                tok = None
                for (s0, s1_) in spans(L):
                    tok = E.op("t", "matmul", PH[i][:, s0:s1_],
                               xa1_fm[0:16, i * 128:(i + 1) * 128],
                               b1s[0:16, s0:s1_],
                               start=False, stop=True,
                               waits=[tb1] + list(xa_ready))
                if i == 3:
                    tok_b1_used = tok
                eng = "v" if i % 2 == 0 else "s"
                meth = "tensor_copy" if eng == "v" else "copy"
                h1_done[i] = E.op(eng, meth, h1_sb[:, i * 1024:i * 1024 + L],
                                  PH[i][0:128, 0:L], waits=(tok,))
                ph_free[i] = (h1_done[i],)
            sl_done = [None] * 4
            for i in range(4):
                sl_done[i] = E.op("s", "activation",
                                  sl_sb[:, i * 1024:i * 1024 + L],
                                  h1_sb[:, i * 1024:i * 1024 + L], AF.Silu,
                                  waits=(h1_done[i],))
            tb3 = E.dma("dg", "d_b3s", b3s[0:16, 0:L], b3[0:16, hc0:hc0 + L],
                        waits=(tok_b3_used,))
            g_done = [None] * 4
            for k in range(NKC):
                buf = k % 4
                tw = E.dma("dg", f"d_wp{buf}", wpc[:, buf * 1024:buf * 1024 + L],
                           w3[k * 128:(k + 1) * 128, hc0:hc0 + L],
                           waits=(piece_free[buf],))
                for i in range(4):
                    w = [tw]
                    if k == 0:
                        w += list(ph_free[i])
                    for si, (s0, s1_) in enumerate(spans(L)):
                        last = (i == 3 and si == len(spans(L)) - 1)
                        if last:
                            piece_free[buf] = E.op(
                                "t", "matmul", PH[i][:, s0:s1_],
                                lnfm[:, i * DM + k * 128:i * DM + (k + 1) * 128],
                                wpc[:, buf * 1024 + s0:buf * 1024 + s1_],
                                start=(k == 0), stop=False, waits=w)
                        else:
                            E.op_("t", "matmul", PH[i][:, s0:s1_],
                                  lnfm[:, i * DM + k * 128:i * DM + (k + 1) * 128],
                                  wpc[:, buf * 1024 + s0:buf * 1024 + s1_],
                                  start=(k == 0), stop=False, waits=w)
            for i in range(4):
                tok = None
                for (s0, s1_) in spans(L):
                    tok = E.op("t", "matmul", PH[i][:, s0:s1_],
                               xa3_fm[0:16, i * 128:(i + 1) * 128],
                               b3s[0:16, s0:s1_],
                               start=False, stop=True,
                               waits=[tb3] + list(xa_ready))
                if i == 3:
                    tok_b3_used = tok
                g_done[i] = E.op("v", "tensor_tensor",
                                 g_sb[:, i * 1024:i * 1024 + L],
                                 PH[i][0:128, 0:L],
                                 sl_sb[:, i * 1024:i * 1024 + L],
                                 ALU.mult, waits=(tok, sl_done[i]))
                ph_free[i] = (g_done[i],)
            gfm_tok = [None] * 4
            for i in range(4):
                for kc in range(khs):
                    TR.go(g_sb[:, i * 1024 + kc * 128:i * 1024 + (kc + 1) * 128],
                          g_fm[:, i * 1024 + kc * 128:i * 1024 + (kc + 1) * 128],
                          waits=tuple(g_done))
                gfm_tok[i] = (E.tok("v"), E.tok("s"))
            for i in range(4):
                tok = None
                for kc in range(khs):
                    w = list(gfm_tok[i]) + [tok_a2]
                    if kc == 0:
                        w += [tok_pga_free] + list(ph_free[0])
                    kglob = (hc0 // 128) + kc
                    tok = E.op("t", "matmul", PGA,
                               g_fm[:, i * 1024 + kc * 128:i * 1024 + (kc + 1) * 128],
                               a2_sb[:, kglob * 16:(kglob + 1) * 16],
                               start=(kc == 0), stop=(kc == khs - 1), waits=w)
                tok_ga2[i] = E.op("v", "tensor_tensor",
                                  ga2_sb[:, i * 16:(i + 1) * 16],
                                  ga2_sb[:, i * 16:(i + 1) * 16], PGA,
                                  ALU.add, waits=(tok, tok_ga2[i]))
                tok_pga_free = tok_ga2[i]
            for dmq in range(4):
                c0 = dmq * 1024
                tokmm = [None] * 4
                for kc in range(khs):
                    buf = kc % 4
                    kglob = (hc0 // 128) + kc
                    tw2 = E.dma("dg", f"d_w2{buf}",
                                w2pc[:, buf * 1024:(buf + 1) * 1024],
                                w2[kglob * 128:(kglob + 1) * 128, c0:c0 + 1024],
                                waits=(w2_free[buf],))
                    for i in range(4):
                        w = [tw2] + list(gfm_tok[i])
                        if kc == 0:
                            w += [dens_tok[i][dmq], tok_pga_free] + list(ph_free[i])
                        for n5 in range(2):
                            need = (n5 == 1 and (i == 3 or kc == khs - 1))
                            if need:
                                tok = E.op("t", "matmul",
                                           PH[i][:, n5 * 512:(n5 + 1) * 512],
                                           g_fm[:, i * 1024 + kc * 128:i * 1024 + (kc + 1) * 128],
                                           w2pc[:, buf * 1024 + n5 * 512:buf * 1024 + (n5 + 1) * 512],
                                           start=(kc == 0), stop=(kc == khs - 1),
                                           waits=w)
                                if kc == khs - 1:
                                    tokmm[i] = tok
                                if i == 3:
                                    w2_free[buf] = tok
                            else:
                                E.op_("t", "matmul",
                                      PH[i][:, n5 * 512:(n5 + 1) * 512],
                                      g_fm[:, i * 1024 + kc * 128:i * 1024 + (kc + 1) * 128],
                                      w2pc[:, buf * 1024 + n5 * 512:buf * 1024 + (n5 + 1) * 512],
                                      start=(kc == 0), stop=(kc == khs - 1),
                                      waits=w)
                for i in range(4):
                    eng = "v" if i % 2 == 0 else "s"
                    dsl = dens[:, i * DM + c0:i * DM + c0 + 1024]
                    if ci == 0:
                        meth = "tensor_copy" if eng == "v" else "copy"
                        tev = E.op(eng, meth, dsl, PH[i],
                                   waits=(tokmm[i], dens_tok[i][dmq]))
                    else:
                        tev = E.op("v", "tensor_tensor", dsl, dsl, PH[i],
                                   ALU.add, waits=(tokmm[i], dens_tok[i][dmq]))
                    dens_tok[i][dmq] = tev
                    ph_free[i] = (tev,)
            hc0 += L

        all_dens = tuple(t for row in dens_tok for t in row)
        prev_tt = None
        for i in range(4):
            m = tiles[i]
            tcv = E.op("s", "activation", ga2_bf[:, :],
                       ga2_sb[:, i * 16:(i + 1) * 16],
                       AF.Copy, scale=SCALING, waits=(tok_ga2[i], prev_tt))
            tt, te = TR.go(ga2_bf[:, :], ga2_fm[0:16, i * 128:(i + 1) * 128],
                           waits=(tcv,) + all_dens)
            prev_tt = tt
            for dmq in range(4):
                tok = None
                for n5 in range(2):
                    c0 = dmq * 1024 + n5 * 512
                    tok = E.op("t", "matmul",
                               PH[i][:, n5 * 512:(n5 + 1) * 512],
                               ga2_fm[0:16, i * 128:(i + 1) * 128],
                               b2_sb[0:16, c0:c0 + 512],
                               start=True, stop=True,
                               waits=[te, tok_b2] + list(ph_free[i]))
                tev = E.op("v", "tensor_tensor",
                           dens[:, i * DM + dmq * 1024:i * DM + (dmq + 1) * 1024],
                           dens[:, i * DM + dmq * 1024:i * DM + (dmq + 1) * 1024],
                           PH[i], ALU.add, waits=(tok, dens_tok[i][dmq]))
                dens_tok[i][dmq] = tev
                ph_free[i] = (tev,)
            dview = dens[:, i * DM:(i + 1) * DM]
            t1, t2 = ln_stats(E, dview, p_s1[:, m:m + 1], p_s2[:, m:m + 1],
                              sq_post, waits=tuple(dens_tok[i]))
            tr, tn = ln_chain(E, p_s1[:, m:m + 1], p_s2[:, m:m + 1],
                              p_r[:, m:m + 1], p_nmr[:, m:m + 1],
                              p_tmp[:, m:m + 1], DM, t1, t2)
            tx = E.dma("dy", "d_x", x_st, x3i[m * 128:(m + 1) * 128, :],
                       waits=(tok_xst_free,))
            tap = E.op("v", "tensor_scalar", dview, dview, p_r[:, m:m + 1],
                       p_nmr[:, m:m + 1], ALU.mult, ALU.add, waits=(tr, tn))
            tadd = E.op("v", "tensor_tensor", dview, dview, x_st, ALU.add,
                        waits=(tap, tx))
            tok_xst_free = tadd
            tok_out_last = E.dma("dy", "d_out", out_ap[m * 128:(m + 1) * 128, :],
                                 dview, waits=(tadd,))
        E.barrier()

    E.barrier()




def build_fused():
    nc = bass.Bass("TRN2", target_bir_lowering=False)

    xo = nc.dram_tensor("xo", [1024, DM], F32, kind="ExternalInput").ap()
    xr = nc.dram_tensor("xr", [1024, DM], F32, kind="ExternalInput").ap()
    wqkv = nc.dram_tensor("wqkv", [DM, 3072], F32, kind="ExternalInput").ap()
    bqkv = nc.dram_tensor("bqkv", [1, 3072], F32, kind="ExternalInput").ap()
    bqkvc = nc.dram_tensor("bqkvc", [128, 16], F32, kind="ExternalInput").ap()
    wo = nc.dram_tensor("wo", [1024, DM], F32, kind="ExternalInput").ap()
    bo = nc.dram_tensor("bo", [1, DM], F32, kind="ExternalInput").ap()
    w0 = nc.dram_tensor("w0", [DM, DM], F32, kind="ExternalInput").ap()
    b0 = nc.dram_tensor("b0", [1, DM], F32, kind="ExternalInput").ap()
    cent = nc.dram_tensor("cent", [DM, 64], F32, kind="ExternalInput").ap()
    assign = nc.dram_tensor("assign", [1, DM], I32, kind="ExternalInput").ap()
    cw = [nc.dram_tensor(f"cw{t}", [DM, 3072], F32, kind="ExternalInput").ap()
          for t in range(3)]
    cb = nc.dram_tensor("cb", [1, 3072], F32, kind="ExternalInput").ap()
    wq = nc.dram_tensor("wq", [1024, 1024], F32, kind="ExternalInput").ap()
    wk = nc.dram_tensor("wk", [1024, 1024], F32, kind="ExternalInput").ap()
    wv = nc.dram_tensor("wv", [1024, 1024], F32, kind="ExternalInput").ap()
    bqc = nc.dram_tensor("bqc", [128, 8], F32, kind="ExternalInput").ap()
    bkc = nc.dram_tensor("bkc", [128, 8], F32, kind="ExternalInput").ap()
    bvr = nc.dram_tensor("bvr", [1, 1024], F32, kind="ExternalInput").ap()
    wo2 = nc.dram_tensor("wo2", [1024, 1024], F32, kind="ExternalInput").ap()
    bo2 = nc.dram_tensor("bo2", [1, 1024], F32, kind="ExternalInput").ap()
    pw = nc.dram_tensor("pw", [1024, DM], F32, kind="ExternalInput").ap()
    pb = nc.dram_tensor("pb", [1, DM], F32, kind="ExternalInput").ap()
    halo_sel = nc.dram_tensor("halo_sel", [16, 2], F32, kind="ExternalInput").ap()
    halo_mask = nc.dram_tensor("halo_mask", [2, 1], F32, kind="ExternalInput").ap()
    w1 = nc.dram_tensor("w1", [DM, 11008], F32, kind="ExternalInput").ap()
    w3 = nc.dram_tensor("w3", [DM, 11008], F32, kind="ExternalInput").ap()
    w2 = nc.dram_tensor("w2", [11008, DM], F32, kind="ExternalInput").ap()
    a1 = nc.dram_tensor("a1", [DM, 16], F32, kind="ExternalInput").ap()
    b1 = nc.dram_tensor("b1", [16, 11008], F32, kind="ExternalInput").ap()
    a3 = nc.dram_tensor("a3", [DM, 16], F32, kind="ExternalInput").ap()
    b3 = nc.dram_tensor("b3", [16, 11008], F32, kind="ExternalInput").ap()
    a2 = nc.dram_tensor("a2", [11008, 16], F32, kind="ExternalInput").ap()
    b2 = nc.dram_tensor("b2", [16, DM], F32, kind="ExternalInput").ap()
    out = nc.dram_tensor("out", [1024, DM], F32, kind="ExternalOutput").ap()

    xs1 = nc.dram_tensor("xs1", [1024, DM], F32)
    xs2 = nc.dram_tensor("xs2", [1024, DM], F32)
    xs3 = nc.dram_tensor("xs3", [1024, DM], F32)
    cc_in = nc.dram_tensor("cc_in", [64, 1], F32)
    cc_out = nc.dram_tensor("cc_out", [64, 1], F32)
    hg_in = nc.dram_tensor("hg_in", [2, DM], F32)
    hg_out = nc.dram_tensor("hg_out", [16, DM], F32)

    c = make_ctx(nc)
    c.E.barrier()
    emit_l0_attn(c, xo, xr, wqkv, bqkv, bqkvc, wo, bo, xs1.ap())
    emit_l0_ffn(c, xs1.ap(), w0, b0, cent, assign, cc_in, cc_out, xs2.ap())
    emit_l1_attn(c, xs2.ap(), cw, cb, wq, wk, wv, bqc, bkc, bvr, wo2, bo2,
                 pw, pb, halo_sel, halo_mask, hg_in, hg_out, xs3.ap())
    emit_l1_ffn(c, xs3.ap(), w1, w3, w2, a1, b1, a3, b3, a2, b2, out)
    c.E.wait("y", *c.E.dma_totals())
    return nc


def host_inputs_fused(d):
    x0 = np.ascontiguousarray(d["L0_attn_x"])
    w0T = np.ascontiguousarray(d["L0_ffn_weight"].T)
    centT = np.ascontiguousarray(d["L0_ffn_centroids"].T)
    cwf = d["L1_attn_conv_w"]
    shared = {
        "wqkv": np.ascontiguousarray(d["L0_attn_Wqkv"]),
        "bqkv": np.ascontiguousarray(d["L0_attn_bqkv"][None, :]),
        "bqkvc": np.ascontiguousarray(d["L0_attn_bqkv"][0:2048].reshape(16, 128).T),
        "wo": np.ascontiguousarray(d["L0_attn_Wo"]),
        "bo": np.ascontiguousarray(d["L0_attn_bo"][None, :]),
        "w0": w0T,
        "b0": np.ascontiguousarray(d["L0_ffn_bias"][None, :]),
        "cent": centT,
        "assign": np.ascontiguousarray(d["L0_ffn_assignments"][None, :]),
        "cw0": np.ascontiguousarray(cwf[0]),
        "cw1": np.ascontiguousarray(cwf[1]),
        "cw2": np.ascontiguousarray(cwf[2]),
        "cb": np.ascontiguousarray(d["L1_attn_conv_b"][None, :]),
        "wq": np.ascontiguousarray(d["L1_attn_wq"].reshape(1024, 1024)),
        "wk": np.ascontiguousarray(d["L1_attn_wk"].reshape(1024, 1024)),
        "wv": np.ascontiguousarray(d["L1_attn_wv"].reshape(1024, 1024)),
        "bqc": np.ascontiguousarray(d["L1_attn_bq"].reshape(8, 128).T),
        "bkc": np.ascontiguousarray(d["L1_attn_bk"].reshape(8, 128).T),
        "bvr": np.ascontiguousarray(d["L1_attn_bv"].reshape(1, 1024)),
        "wo2": np.ascontiguousarray(d["L1_attn_wo"].reshape(1024, 1024)),
        "bo2": np.ascontiguousarray(d["L1_attn_bo"][None, :]),
        "pw": np.ascontiguousarray(d["L1_attn_proj_w"]),
        "pb": np.ascontiguousarray(d["L1_attn_proj_b"][None, :]),
        "w1": np.ascontiguousarray(d["L1_ffn_w1"].T),
        "w3": np.ascontiguousarray(d["L1_ffn_w3"].T),
        "w2": np.ascontiguousarray(d["L1_ffn_w2"].T),
        "a1": np.ascontiguousarray(d["L1_ffn_lora_A1"].T),
        "b1": np.ascontiguousarray(d["L1_ffn_lora_B1"].T),
        "a3": np.ascontiguousarray(d["L1_ffn_lora_A3"].T),
        "b3": np.ascontiguousarray(d["L1_ffn_lora_B3"].T),
        "a2": np.ascontiguousarray(d["L1_ffn_lora_A2"].T),
        "b2": np.ascontiguousarray(d["L1_ffn_lora_B2"].T),
    }
    maps = []
    for cid in range(8):
        b, h = cid // 2, cid % 2
        hs = np.zeros((16, 2), np.float32)
        hm = np.zeros((2, 1), np.float32)
        if h == 1:
            hs[2 * cid - 1, 0] = 1.0
            hm[0, 0] = 1.0
        if h == 0:
            hs[2 * cid + 2, 1] = 1.0
            hm[1, 0] = 1.0
        m = dict(shared)
        m["xo"] = np.ascontiguousarray(x0[b, h * 1024:(h + 1) * 1024])
        m["xr"] = np.ascontiguousarray(x0[b, (1 - h) * 1024:(2 - h) * 1024])
        m["halo_sel"] = hs
        m["halo_mask"] = hm
        maps.append(m)
    return maps


def gather_out_fused(results):
    out = np.zeros((4, 2048, 4096), np.float32)
    for cid in range(8):
        b, h = cid // 2, cid % 2
        out[b, h * 1024:(h + 1) * 1024] = results[cid]["out"]
    return out




from concourse.bass_utils import run_bass_kernel_spmd


def kernel(**inputs):
    nc = build_fused()
    res = run_bass_kernel_spmd(nc, host_inputs_fused(inputs),
                               core_ids=list(range(8)))
    return gather_out_fused(res.results)
